# version 07
# baseline (speedup 1.0000x reference)
import contextlib
import math

import numpy as np
import ml_dtypes

import concourse.bass as bass
import concourse.mybir as mybir
from concourse.bass_utils import run_bass_kernel_spmd

F32 = mybir.dt.float32
BF16 = mybir.dt.bfloat16
AF = mybir.ActivationFunctionType
ALU = mybir.AluOpType
AX = mybir.AxisListType
BF16NP = ml_dtypes.bfloat16

NEG = -3.0e38


class Op:
    __slots__ = ("engine", "sem", "val", "kind")

    def __init__(self, engine, sem, val, kind):
        self.engine = engine
        self.sem = sem
        self.val = val
        self.kind = kind


class Buf:
    __slots__ = ("name", "last_write", "readers", "dma_sem", "dma_writes", "dma_base")

    def __init__(self, name):
        self.name = name
        self.last_write = None
        self.readers = []
        self.dma_sem = None
        self.dma_writes = 0
        self.dma_base = 0


ENGINES = ("sync", "tensor", "vector", "scalar", "gpsimd")


class Prog:
    def __init__(self, nc, es):
        self.nc = nc
        self.es = es
        self.sems = {e: es.enter_context(nc.semaphore(f"S_{e}")) for e in ENGINES}
        self.counts = {e: 0 for e in ENGINES}
        self.items = {e: [] for e in ENGINES}
        self.clock = {e: {} for e in ENGINES}
        self.out_bufs = []
        self.dma_bufs = []
        self.sem_pool = []
        self._nsem = len(ENGINES)

    def new_sem(self, name):
        self._nsem += 1
        assert self._nsem < 200, "semaphore budget exceeded"
        return self.es.enter_context(self.nc.semaphore(name))

    def buf(self, name):
        return Buf(name)

    def _wait(self, engine, sem, val):
        ck = self.clock[engine]
        k = id(sem)
        if ck.get(k, -1) >= val:
            return
        ck[k] = val
        self.items[engine].append(("wait", sem, val))

    def _dep(self, engine, dep):
        if dep is None:
            return
        if dep.engine == engine and engine == "tensor" and dep.kind == "compute":
            return
        self._wait(engine, dep.sem, dep.val)

    def _pre(self, engine, reads, writes, extra):
        deps = []
        for b in reads:
            deps.append(b.last_write)
        for b in writes:
            deps.extend(b.readers)
            deps.append(b.last_write)
        deps.extend(extra)
        for d in deps:
            self._dep(engine, d)

    def _post(self, op, reads, writes):
        for b in reads:
            b.readers.append(op)
        for b in writes:
            b.last_write = op
            b.readers = []

    def op(self, engine, fn, reads=(), writes=(), extra=(), tag=""):
        self._pre(engine, reads, writes, extra)
        sem = self.sems[engine]
        self.counts[engine] += 1
        val = self.counts[engine]
        self.items[engine].append(("emit", fn, [(sem, 1)], tag))
        op = Op(engine, sem, val, "compute")
        self._post(op, reads, writes)
        return op

    def dma(self, engine, out_ap, in_ap, reads=(), writes=(), extra=(), tag="", out_dram=False):
        assert len(writes) == 1
        wb = writes[0]
        if wb.dma_sem is None:
            if self.sem_pool:
                wb.dma_sem, wb.dma_base = self.sem_pool.pop()
            else:
                wb.dma_sem = self.new_sem(f"D{self._nsem}_{wb.name}")
                wb.dma_base = 0
            self.dma_bufs.append(wb)
        self._pre(engine, reads, writes, extra)
        wb.dma_writes += 1
        val = 16 * (wb.dma_base + wb.dma_writes)
        sem = wb.dma_sem

        def fn(eng, out_ap=out_ap, in_ap=in_ap):
            return eng.dma_start(out=out_ap, in_=in_ap)

        self.items[engine].append(("emit", fn, [(sem, 16)], tag))
        op = Op(engine, sem, val, "dma")
        self._post(op, reads, writes)
        if out_dram and wb not in self.out_bufs:
            self.out_bufs.append(wb)
        return op

    def finalize(self):
        for e in ("sync", "gpsimd"):
            for b in self.out_bufs:
                if b.dma_sem is not None:
                    self._wait(e, b.dma_sem, 16 * (b.dma_base + b.dma_writes))
        with self.nc.Block() as block:
            for e in ENGINES:
                if not self.items[e]:
                    continue
                items = self.items[e]

                def mk(items=items):
                    def section(eng):
                        for it in items:
                            if it[0] == "wait":
                                eng.wait_ge(it[1], it[2])
                            else:
                                ins = it[1](eng)
                                for (sem, k) in it[2]:
                                    ins.then_inc(sem, k)
                    return section
                getattr(block, e)(mk())


def ceil_div(a, b):
    return (a + b - 1) // b


def tiles(total, step):
    out = []
    o = 0
    while o < total:
        out.append((o, min(step, total - o)))
        o += step
    return out


class LaunchCtx:

    def __init__(self, name):
        self.nc = bass.Bass("TRN2", target_bir_lowering=False, debug=False)
        self.es = contextlib.ExitStack()
        self.prog = None
        self.name = name
        self._uniq = 0
        self._cb = {}

    def cbias(self, val, parts=128):
        if val not in self._cb:
            t, b = self.sbuf(f"cb{len(self._cb)}", [128, 1], F32)
            self.prog.op("vector", lambda v, t=t, val=val: v.memset(t[:], float(val)),
                         reads=[], writes=[b])
            self._cb[val] = (t, b)
        t, b = self._cb[val]
        return t[0:parts, 0:1], b

    def start(self):
        self.prog = Prog(self.nc, self.es)
        return self.prog

    def uniq(self):
        self._uniq += 1
        return self._uniq

    def sbuf(self, name, shape, dtype):
        t = self.es.enter_context(self.nc.sbuf_tensor(name, shape, dtype))
        return t, self.prog.buf(name)

    def psum(self, name, shape, dtype=F32):
        t = self.es.enter_context(self.nc.psum_tensor(name, shape, dtype))
        return t, self.prog.buf(name)

    io = None

    def din(self, name, shape, dtype=F32):
        if self.io is not None and name in self.io:
            return self.io[name]
        return self.nc.dram_tensor(name, shape, dtype, kind="ExternalInput").ap()

    def dout(self, name, shape, dtype=F32):
        if self.io is not None and name in self.io:
            return self.io[name], self.prog.buf(name + f"_p{self.uniq()}")
        ap = self.nc.dram_tensor(name, shape, dtype, kind="ExternalOutput").ap()
        return ap, self.prog.buf(name)

    def close(self):
        self.es.close()


def run_launch(ctx, in_maps, n_cores=8):
    res = run_bass_kernel_spmd(ctx.nc, in_maps, core_ids=list(range(n_cores)))
    return res.results


def emit_mm_acc(prog, psum_ap, pairs, reads, writes, dt_note=""):
    def fn(t):
        last = None
        n = len(pairs)
        for i, (l, r) in enumerate(pairs):
            last = t.matmul(psum_ap, l, r, start=(i == 0), stop=(i == n - 1))
        return last
    return prog.op("tensor", fn, reads=reads, writes=writes, tag="mm" + dt_note)


def emit_transpose(prog, psum_ap, in_ap, ident_ap, reads, writes):
    def fn(t):
        return t.transpose(psum_ap, in_ap, ident_ap)
    return prog.op("tensor", fn, reads=reads, writes=writes, tag="tr")


def emit(prog, engine, fn, reads, writes, tag=""):
    return prog.op(engine, fn, reads=reads, writes=writes, tag=tag)


def _f32c(a):
    return np.ascontiguousarray(a, dtype=np.float32)


def stage_vec_cols(v, nchunks=None):
    v = np.asarray(v, np.float32)
    d = v.shape[0]
    if nchunks is None:
        nchunks = ceil_div(d, 128)
    pad = nchunks * 128 - d
    if pad:
        v = np.concatenate([v, np.zeros(pad, np.float32)])
    return _f32c(v.reshape(nchunks, 128).T)




def stage_ktiles(wT, kc=8):
    wT = np.asarray(wT, np.float32)
    K, N = wT.shape
    assert K == kc * 128 and N % 128 == 0
    a = wT.reshape(kc, 128, N // 128, 128).transpose(2, 1, 0, 3)
    return _f32c(a.reshape(N, kc * 128))


def stage_ktiles3(w, kc=8):
    w = np.asarray(w, np.float32)
    E, K, N = w.shape
    return _f32c(np.stack([stage_ktiles(w[e], kc) for e in range(E)]).reshape(E * N, kc * 128))


CORES = [(c // 4, c % 4) for c in range(8)]
SH = 2048


def build_A1(n_tiles=5, dbg_skip=(), ctx=None, io=None):
    standalone = ctx is None
    if standalone:
        ctx = LaunchCtx("A1")
        p = ctx.start()
    else:
        p = ctx.prog
    ctx.io = io
    nc = ctx.nc

    TIN = 2176
    xin = ctx.din("xin", [1024, TIN])
    w_in_T = ctx.din("w_in_T", [4480, 1024])
    wln1c = ctx.din("wln1c", [128, 8])
    dtbc = ctx.din("dtbc", [32, 1])
    bconvc = ctx.din("bconvc", [128, 18])
    wconvc = ctx.din("wconvc", [128, 18 * 4])
    d_xbc, b_xbc = ctx.dout("xbc", [2304, SH], BF16)
    d_dt, b_dt = ctx.dout("dt", [32, SH])
    d_z, b_z = ctx.dout("z", [2048, SH])
    d_astat, b_astat = ctx.dout("astat", [2, SH])

    xt, b_xt = ctx.sbuf("xt", [128, 8 * 512], F32)
    hf, b_hf = ctx.sbuf("hf", [128, 8 * 512], F32)
    hb, b_hb = ctx.sbuf("hb", [128, 8 * 512], BF16)
    wst0, b_wst0 = ctx.sbuf("wst0", [128, 1024], F32)
    wst1, b_wst1 = ctx.sbuf("wst1", [128, 1024], F32)
    wbuf, b_wbuf = ctx.sbuf("wbuf", [128, 35 * 1024], BF16)
    wdtf, b_wdtf = ctx.sbuf("wdtf", [128, 8 * 32], F32)
    wln1, b_wln1 = ctx.sbuf("wln1", [128, 8], F32)
    dtb, b_dtb = ctx.sbuf("dtb", [32, 1], F32)
    bconv, b_bconv = ctx.sbuf("bconv", [128, 18], F32)
    wconv, b_wconv = ctx.sbuf("wconv", [128, 72], F32)
    onesc, b_onesc = ctx.sbuf("onesc", [128, 1], F32)
    ones1, b_ones1 = ctx.sbuf("ones1", [1, 128], F32)
    xraw, b_xraw = ctx.sbuf("xraw", [128, 515], F32)
    carry, b_carry = ctx.sbuf("carry", [128, 64], F32)
    convt, b_convt = ctx.sbuf("convt", [128, 512], F32)
    convt2, b_convt2 = ctx.sbuf("convt2", [128, 512], F32)
    xbcs, b_xbcs = ctx.sbuf("xbcs", [128, 512], BF16)
    zt, b_zt = ctx.sbuf("zt", [128, 512], F32)
    dtt, b_dtt = ctx.sbuf("dtt", [32, 512], F32)
    dtt2, b_dtt2 = ctx.sbuf("dtt2", [32, 512], F32)
    dtt3, b_dtt3 = ctx.sbuf("dtt3", [32, 512], F32)
    sq, b_sq = ctx.sbuf("sq", [128, 512], F32)
    srow, b_srow = ctx.sbuf("srow", [1, 6 * 512], F32)

    ps1, b_ps1 = ctx.psum("ps1", [1, 512])
    ps2, b_ps2 = ctx.psum("ps2", [1, 512])
    pbm, b_pbm = ctx.psum("pbm", [128, 512])
    pbs, b_pbs = ctx.psum("pbs", [128, 512])
    pp0, b_pp0 = ctx.psum("pp0", [128, 512])
    pp1, b_pp1 = ctx.psum("pp1", [128, 512])
    pdt, b_pdt = ctx.psum("pdt", [32, 512])

    emit(p, "vector", lambda v: v.memset(onesc[:], 1.0), [], [b_onesc])
    emit(p, "vector", lambda v: v.memset(ones1[:], 1.0), [], [b_ones1])
    p.dma("sync", wln1[:], wln1c[:, :], writes=[b_wln1])
    p.dma("sync", dtb[:], dtbc[:, :], writes=[b_dtb])
    p.dma("sync", bconv[:], bconvc[:, :], writes=[b_bconv])
    p.dma("sync", wconv[:], wconvc[:, :], writes=[b_wconv])
    emit(p, "vector", lambda v: v.memset(carry[:], 0.0), [], [b_carry])

    for oc in range(35):
        wst, b_wst = (wst0, b_wst0) if oc % 2 == 0 else (wst1, b_wst1)
        p.dma("sync", wst[:, 0:1024], w_in_T[oc * 128:(oc + 1) * 128, :],
              writes=[b_wst], tag=f"wld{oc}")
        emit(p, "vector", lambda v, oc=oc, wst=wst: v.tensor_copy(
            wbuf[:, oc * 1024:(oc + 1) * 1024], wst[:, 0:1024]),
            [b_wst], [b_wbuf], tag=f"wcast{oc}")
        if oc == 34:
            emit(p, "vector", lambda v, wst=wst: v.tensor_copy(
                wdtf[:].rearrange("p (k c) -> p k c", k=8),
                wst[:, 0:1024].rearrange("p (k c) -> p k c", k=8)[:, :, 0:32]),
                [b_wst], [b_wdtf])

    tlist = ([(0, 128)] + [(128 + i * 512, 512) for i in range(4)])[:n_tiles]
    prev_tt = None
    for ti, (o, tt) in enumerate(tlist):
        halo = (ti == 0)
        oo = o - 128
        for c in range(8):
            p.dma("sync", xt[:, c * 512: c * 512 + tt], xin[c * 128:(c + 1) * 128, o:o + tt],
                  writes=[b_xt], tag=f"x{ti}c{c}")
        if "stats" in dbg_skip:
            continue
        def mm_s1(t, tt=tt):
            last = None
            for c in range(8):
                last = t.matmul(ps1[:, 0:tt], onesc[:, :], xt[:, c * 512: c * 512 + tt],
                                start=(c == 0), stop=(c == 7))
            return last
        p.op("tensor", mm_s1, reads=[b_onesc, b_xt], writes=[b_ps1])
        for c in range(8):
            emit(p, "scalar", lambda s, c=c, tt=tt: s.activation(
                out=sq[:, 0:tt], in_=xt[:, c * 512: c * 512 + tt], func=AF.Square),
                [b_xt], [b_sq])
            emit(p, "tensor", lambda t, c=c, tt=tt: t.matmul(
                ps2[:, 0:tt], onesc[:, :], sq[:, 0:tt], start=(c == 0), stop=(c == 7)),
                [b_onesc, b_sq], [b_ps2])
        M, R, SS, T1, T2, T3 = (srow[0:1, i * 512:i * 512 + tt] for i in range(6))
        emit(p, "vector", lambda v, M=M, tt=tt: v.tensor_scalar_mul(M, ps1[0:1, 0:tt], 1.0 / 1024.0), [b_ps1], [b_srow])
        emit(p, "vector", lambda v, T1=T1, tt=tt: v.tensor_scalar_mul(T1, ps2[0:1, 0:tt], 1.0 / 1024.0), [b_ps2], [b_srow])
        emit(p, "scalar", lambda s, M=M, T2=T2: s.activation(out=T2, in_=M, func=AF.Square), [b_srow], [b_srow])
        emit(p, "vector", lambda v, T1=T1, T2=T2: v.tensor_sub(T1, T1, T2), [b_srow], [b_srow])
        cb5, b_cb5 = ctx.cbias(1e-5, 1)
        emit(p, "scalar", lambda s, T1=T1, T2=T2, cb5=cb5: s.activation(out=T2, in_=T1, func=AF.Sqrt, bias=cb5), [b_srow, b_cb5], [b_srow])
        emit(p, "vector", lambda v, T2=T2, R=R: v.reciprocal(R, T2), [b_srow], [b_srow])
        emit(p, "scalar", lambda s, R=R, T2=T2: s.activation(out=T2, in_=R, func=AF.Square), [b_srow], [b_srow])
        emit(p, "vector", lambda v, T1=T1, T2=T2, T3=T3: v.tensor_mul(T3, T1, T2), [b_srow], [b_srow])
        cb6, b_cb6 = ctx.cbias(1e-6, 1)
        emit(p, "scalar", lambda s, T3=T3, cb6=cb6: s.activation(out=T3, in_=T3, func=AF.Sqrt, bias=cb6), [b_srow, b_cb6], [b_srow])
        emit(p, "vector", lambda v, T3=T3: v.reciprocal(T3, T3), [b_srow], [b_srow])
        emit(p, "vector", lambda v, R=R, T3=T3, SS=SS: v.tensor_mul(SS, R, T3), [b_srow], [b_srow])
        if not halo:
            p.dma("gpsimd", d_astat[0:1, oo:oo + tt], M, reads=[b_srow], writes=[b_astat], out_dram=True)
            p.dma("gpsimd", d_astat[1:2, oo:oo + tt], R, reads=[b_srow], writes=[b_astat], out_dram=True)
        emit(p, "tensor", lambda t, M=M, tt=tt: t.matmul(pbm[:, 0:tt], ones1[0:1, :], M, start=True, stop=True),
             [b_ones1, b_srow], [b_pbm])
        emit(p, "tensor", lambda t, SS=SS, tt=tt: t.matmul(pbs[:, 0:tt], ones1[0:1, :], SS, start=True, stop=True),
             [b_ones1, b_srow], [b_pbs])
        for c in range(8):
            sl = slice(c * 512, c * 512 + tt)
            emit(p, "vector", lambda v, c=c, sl=sl, tt=tt: v.tensor_sub(hf[:, sl], xt[:, sl], pbm[:, 0:tt]),
                 [b_xt, b_pbm], [b_hf])
            emit(p, "vector", lambda v, c=c, sl=sl, tt=tt: v.scalar_tensor_tensor(
                hb[:, sl], hf[:, sl], wln1[:, c:c + 1], pbs[:, 0:tt], ALU.mult, ALU.mult),
                [b_hf, b_wln1, b_pbs], [b_hb])
            emit(p, "vector", lambda v, c=c, sl=sl, tt=tt: v.scalar_tensor_tensor(
                hf[:, sl], hf[:, sl], wln1[:, c:c + 1], pbs[:, 0:tt], ALU.mult, ALU.mult),
                [b_hf, b_wln1, b_pbs], [b_hf])
        if "proj" in dbg_skip:
            prev_tt = tt
            continue
        oc_list = list(range(16, 34)) + [34] + ([] if halo else list(range(16)))
        if "z" in dbg_skip:
            oc_list = [oc for oc in oc_list if oc >= 16]
        if "dt" in dbg_skip:
            oc_list = [oc for oc in oc_list if oc != 34]
        if "xbc" in dbg_skip:
            oc_list = [oc for oc in oc_list if not (16 <= oc < 34)]
        for oc in oc_list:
            if oc == 34:
                def mmdt(t, tt=tt):
                    last = None
                    for kc in range(8):
                        last = t.matmul(pdt[:, 0:tt], wdtf[:, kc * 32:(kc + 1) * 32],
                                        hf[:, kc * 512: kc * 512 + tt], start=(kc == 0), stop=(kc == 7))
                    return last
                p.op("tensor", mmdt, reads=[b_wdtf, b_hf], writes=[b_pdt])
                emit(p, "vector", lambda v, tt=tt: v.tensor_scalar_add(dtt[:, 0:tt], pdt[:, 0:tt], dtb[:, 0:1]),
                     [b_pdt, b_dtb], [b_dtt])
                emit(p, "vector", lambda v, tt=tt: v.tensor_scalar_mul(dtt2[:, 0:tt], dtt[:, 0:tt], -1.0),
                     [b_dtt], [b_dtt2])
                emit(p, "vector", lambda v, tt=tt: v.tensor_tensor(dtt2[:, 0:tt], dtt2[:, 0:tt], dtt[:, 0:tt], ALU.min),
                     [b_dtt2, b_dtt], [b_dtt2])
                emit(p, "scalar", lambda s, tt=tt: s.activation(out=dtt2[:, 0:tt], in_=dtt2[:, 0:tt], func=AF.Exp),
                     [b_dtt2], [b_dtt2])
                cbo, b_cbo = ctx.cbias(1.0, 32)
                emit(p, "scalar", lambda s, tt=tt, cbo=cbo: s.activation(out=dtt2[:, 0:tt], in_=dtt2[:, 0:tt], func=AF.Ln, bias=cbo),
                     [b_dtt2, b_cbo], [b_dtt2])
                emit(p, "vector", lambda v, tt=tt: v.tensor_scalar(dtt3[:, 0:tt], dtt[:, 0:tt], 0.0, None, ALU.max),
                     [b_dtt], [b_dtt3])
                emit(p, "vector", lambda v, tt=tt: v.tensor_add(dtt3[:, 0:tt], dtt3[:, 0:tt], dtt2[:, 0:tt]),
                     [b_dtt3, b_dtt2], [b_dtt3])
                if not halo:
                    p.dma("gpsimd", d_dt[:, oo:oo + tt], dtt3[:, 0:tt], reads=[b_dtt3], writes=[b_dt], out_dram=True)
                continue
            pp, b_pp = (pp0, b_pp0) if oc % 2 == 0 else (pp1, b_pp1)
            def mmoc(t, oc=oc, tt=tt, pp=pp):
                last = None
                for kc in range(8):
                    last = t.matmul(pp[:, 0:tt], wbuf[:, oc * 1024 + kc * 128: oc * 1024 + (kc + 1) * 128],
                                    hb[:, kc * 512: kc * 512 + tt], start=(kc == 0), stop=(kc == 7))
                return last
            p.op("tensor", mmoc, reads=[b_wbuf, b_hb], writes=[b_pp])
            if oc < 16:
                emit(p, "vector", lambda v, pp=pp, tt=tt: v.tensor_copy(zt[:, 0:tt], pp[:, 0:tt]),
                     [b_pp], [b_zt])
                p.dma("gpsimd", d_z[oc * 128:(oc + 1) * 128, oo:oo + tt], zt[:, 0:tt],
                      reads=[b_zt], writes=[b_z], out_dram=True)
            else:
                c = oc - 16
                emit(p, "vector", lambda v, c=c: v.tensor_copy(xraw[:, 0:3], carry[:, 3 * c:3 * c + 3]),
                     [b_carry], [b_xraw])
                emit(p, "vector", lambda v, tt=tt, pp=pp: v.tensor_copy(xraw[:, 3:3 + tt], pp[:, 0:tt]),
                     [b_pp], [b_xraw])
                emit(p, "vector", lambda v, c=c, tt=tt: v.tensor_copy(carry[:, 3 * c:3 * c + 3], xraw[:, tt:tt + 3]),
                     [b_xraw], [b_carry])
                if not halo:
                    emit(p, "vector", lambda v, c=c, tt=tt: v.tensor_scalar_mul(
                        convt[:, 0:tt], xraw[:, 0:tt], wconv[:, c * 4: c * 4 + 1]),
                        [b_xraw, b_wconv], [b_convt])
                    emit(p, "vector", lambda v, c=c, tt=tt: v.scalar_tensor_tensor(
                        convt2[:, 0:tt], xraw[:, 1:1 + tt], wconv[:, c * 4 + 1: c * 4 + 2], convt[:, 0:tt],
                        ALU.mult, ALU.add), [b_xraw, b_wconv, b_convt], [b_convt2])
                    emit(p, "vector", lambda v, c=c, tt=tt: v.scalar_tensor_tensor(
                        convt[:, 0:tt], xraw[:, 2:2 + tt], wconv[:, c * 4 + 2: c * 4 + 3], convt2[:, 0:tt],
                        ALU.mult, ALU.add), [b_xraw, b_wconv, b_convt2], [b_convt])
                    emit(p, "vector", lambda v, c=c, tt=tt: v.scalar_tensor_tensor(
                        convt2[:, 0:tt], xraw[:, 3:3 + tt], wconv[:, c * 4 + 3: c * 4 + 4], convt[:, 0:tt],
                        ALU.mult, ALU.add), [b_xraw, b_wconv, b_convt], [b_convt2])
                    emit(p, "scalar", lambda s, c=c, tt=tt: s.activation(
                        out=xbcs[:, 0:tt], in_=convt2[:, 0:tt], func=AF.Silu, bias=bconv[:, c:c + 1]),
                        [b_convt2, b_bconv], [b_xbcs])
                    p.dma("gpsimd", d_xbc[c * 128:(c + 1) * 128, oo:oo + tt], xbcs[:, 0:tt],
                          reads=[b_xbcs], writes=[b_xbc], out_dram=True)
        prev_tt = tt
    ctx.io = None
    if standalone:
        p.finalize()
    return ctx


def stage_A1(I):
    x = I["L0_attn_hidden_states"]
    xfm = np.ascontiguousarray(x.transpose(0, 2, 1), dtype=np.float32)
    w_in_T = np.zeros((1024, 4480), np.float32)
    w_in_T[:, 0:4384] = np.asarray(I["L0_attn_w_in"], np.float32).T
    w_in_T = _f32c(stage_ktiles(np.concatenate([w_in_T], axis=0), kc=8)) if False else stage_ktiles(w_in_T)
    wln1c = stage_vec_cols(I["L0_attn_w_ln1"], 8)
    dtbc = _f32c(np.asarray(I["L0_attn_dt_bias"], np.float32).reshape(32, 1))
    bconvc = stage_vec_cols(I["L0_attn_b_conv"], 18)
    wc = np.asarray(I["L0_attn_w_conv"], np.float32)
    wconvc = _f32c(wc.reshape(18, 128, 4).transpose(1, 0, 2).reshape(128, 72))
    in_maps = []
    for (b, k) in CORES:
        lo = 2048 * k - 128
        if lo < 0:
            xin = np.concatenate([np.zeros((1024, 128), np.float32), xfm[b][:, 0:2048]], axis=1)
        else:
            xin = np.ascontiguousarray(xfm[b][:, lo:lo + 2176])
        in_maps.append(dict(xin=xin, w_in_T=w_in_T, wln1c=wln1c, dtbc=dtbc,
                            bconvc=bconvc, wconvc=wconvc))
    return in_maps


def build_A2(ctx=None, io=None):
    standalone = ctx is None
    if standalone:
        ctx = LaunchCtx("A2")
        p = ctx.start()
    else:
        p = ctx.prog
    ctx.io = io
    nc = ctx.nc
    S = 8192
    HS = 4096
    NCH = 32

    xs_d = ctx.din("xs", [512, S], BF16)
    B_d = ctx.din("Bf", [128, S], BF16)
    C_d = ctx.din("Cf", [128, S], BF16)
    dth_d = ctx.din("dth", [8, S])
    alog_d = ctx.din("alog", [8, 1])
    dd_d = ctx.din("dd", [1, 8])
    idf_d = ctx.din("idf", [128, 128])
    idb_d = ctx.din("idb", [128, 128], BF16)
    maskT_d = ctx.din("maskT", [128, 128])
    selpair_d = ctx.din("selpair", [8, 128])
    sel8_d = ctx.din("sel8", [8, 128])
    y_d, b_y = ctx.dout("y", [512, S], BF16)

    xs, b_xs = ctx.sbuf("s_xs", [128, 4 * HS], BF16)
    Bf, b_Bf = ctx.sbuf("s_Bf", [128, HS], BF16)
    Cf, b_Cf = ctx.sbuf("s_Cf", [128, HS], BF16)
    dth, b_dth = ctx.sbuf("s_dth", [8, HS], F32)
    acs, b_acs = ctx.sbuf("acs", [8, HS], F32)
    dtds, b_dtds = ctx.sbuf("dtds", [8, HS], F32)
    alast, b_alast = ctx.sbuf("s_alast", [8, NCH], F32)
    expal, b_expal = ctx.sbuf("expal", [8, NCH], F32)
    selrhs, b_selrhs = ctx.sbuf("selrhs", [8, 4 * NCH], F32)
    selpair, b_selpair = ctx.sbuf("s_selpair", [8, 4 * NCH], F32)
    sel8, b_sel8 = ctx.sbuf("s_sel8", [8, 128], F32)
    acol, b_acol = ctx.sbuf("acol", [8, 1], F32)
    ddr, b_ddr = ctx.sbuf("ddr", [1, 8], F32)
    dcols, b_dcols = ctx.sbuf("dcols", [128, 8], F32)
    deccols, b_deccols = ctx.sbuf("deccols", [128, 4 * NCH], F32)
    ones8, b_ones8 = ctx.sbuf("ones8", [8, 128], F32)
    ones1, b_ones1 = ctx.sbuf("ones1", [1, 128], F32)
    idf, b_idf = ctx.sbuf("s_idf", [128, 128], F32)
    idb, b_idb = ctx.sbuf("s_idb", [128, 128], BF16)
    maskT, b_maskT = ctx.sbuf("s_maskT", [128, 128], F32)
    Sst, b_Sst = ctx.sbuf("Sst", [128, 512], F32)
    Sbf, b_Sbf = ctx.sbuf("Sbf", [128, 512], BF16)
    acsT, b_acsT = ctx.sbuf("acsT", [128, 8], F32)
    dtT, b_dtT = ctx.sbuf("dtT", [128, 8], F32)
    dtdsT, b_dtdsT = ctx.sbuf("dtdsT", [128, 8], F32)
    rows, b_rows = ctx.sbuf("rows", [1, 8 * 128], F32)
    xtm, b_xtm = ctx.sbuf("xtm", [128, 128], BF16)
    btm, b_btm = ctx.sbuf("btm", [128, 128], BF16)
    x2, b_x2 = ctx.sbuf("x2", [128, 128], BF16)
    cbt, b_cbt = ctx.sbuf("cbt", [128, 128], F32)
    g1, b_g1 = ctx.sbuf("g1", [128, 128], F32)
    g2, b_g2 = ctx.sbuf("g2", [128, 128], F32)
    g3, b_g3 = ctx.sbuf("g3", [128, 128], F32)
    gb, b_gb = ctx.sbuf("gb", [128, 128], BF16)
    ebq, b_ebq = ctx.sbuf("ebq", [128, 128], F32)
    cph, b_cph = ctx.sbuf("cph", [128, 128], BF16)
    ysb, b_ysb = ctx.sbuf("ysb", [128, 2 * 128], BF16)

    pMISC, b_pMISC = ctx.psum("pMISC", [128, 512])
    pTb, b_pTb = ctx.psum("pTb", [128, 1024], BF16)
    pBCQ, b_pBCQ = ctx.psum("pBCQ", [128, 128])
    pCB, b_pCB = ctx.psum("pCB", [128, 128])
    pYa, b_pYa = ctx.psum("pYa", [128, 128])
    pYb, b_pYb = ctx.psum("pYb", [128, 128])
    pS, b_pS = ctx.psum("pS", [128, 128])

    p.dma("sync", acol[:], alog_d[:, :], writes=[b_acol])
    p.dma("sync", ddr[:], dd_d[:, :], writes=[b_ddr])
    p.dma("sync", idf[:], idf_d[:, :], writes=[b_idf])
    p.dma("sync", idb[:], idb_d[:, :], writes=[b_idb])
    p.dma("sync", maskT[:], maskT_d[:, :], writes=[b_maskT])
    p.dma("sync", selpair[:], selpair_d[:, :], writes=[b_selpair])
    p.dma("sync", sel8[:], sel8_d[:, :], writes=[b_sel8])
    emit(p, "vector", lambda v: v.memset(ones8[:], 1.0), [], [b_ones8])
    emit(p, "vector", lambda v: v.memset(ones1[:], 1.0), [], [b_ones1])
    emit(p, "vector", lambda v: v.memset(Sst[:], 0.0), [], [b_Sst])
    emit(p, "scalar", lambda s: s.activation(out=acol[:], in_=acol[:], func=AF.Exp), [b_acol], [b_acol])
    emit(p, "vector", lambda v: v.tensor_scalar_mul(acol[:], acol[:], -1.0), [b_acol], [b_acol])
    emit(p, "tensor", lambda t: t.matmul(pMISC[:, 0:8], ones1[0:1, :], ddr[0:1, :], start=True, stop=True),
         [b_ones1, b_ddr], [b_pMISC])
    emit(p, "vector", lambda v: v.tensor_copy(dcols[:], pMISC[:, 0:8]), [b_pMISC], [b_dcols])

    for half in range(2):
        HOFF = half * HS
        for pp in range(4):
            p.dma("sync", xs[:, pp * HS:(pp + 1) * HS], xs_d[pp * 128:(pp + 1) * 128, HOFF:HOFF + HS], writes=[b_xs])
        p.dma("sync", Bf[:], B_d[:, HOFF:HOFF + HS], writes=[b_Bf])
        p.dma("sync", Cf[:], C_d[:, HOFF:HOFF + HS], writes=[b_Cf])
        p.dma("sync", dth[:], dth_d[:, HOFF:HOFF + HS], writes=[b_dth])

        emit(p, "vector", lambda v: v.tensor_scalar_mul(dtds[:], dth[:], acol[:, 0:1]), [b_dth, b_acol], [b_dtds])
        for ci in range(NCH):
            sl = slice(ci * 128, (ci + 1) * 128)
            emit(p, "vector", lambda v, sl=sl: v.tensor_tensor_scan(
                acs[:, sl], ones8[:, :], dtds[:, sl], 0.0, ALU.mult, ALU.add), [b_ones8, b_dtds], [b_acs])
        for ci in range(NCH):
            sl = slice(ci * 128, (ci + 1) * 128)
            emit(p, "vector", lambda v, sl=sl, ci=ci: v.tensor_scalar(
                dtds[:, sl], acs[:, sl], -1.0, acs[:, ci * 128 + 127: ci * 128 + 128], ALU.mult, ALU.add),
                [b_acs], [b_dtds])
        emit(p, "scalar", lambda s: s.activation(out=dtds[:], in_=dtds[:], func=AF.Exp), [b_dtds], [b_dtds])
        emit(p, "vector", lambda v: v.tensor_mul(dtds[:], dth[:], dtds[:]), [b_dth, b_dtds], [b_dtds])
        emit(p, "vector", lambda v: v.tensor_copy(alast[:], acs[:, 127::128]), [b_acs], [b_alast])
        emit(p, "scalar", lambda s: s.activation(out=expal[:], in_=alast[:], func=AF.Exp), [b_alast], [b_expal])
        for pp in range(4):
            emit(p, "vector", lambda v, pp=pp: v.tensor_copy(selrhs[:, pp * NCH:(pp + 1) * NCH], expal[:]),
                 [b_expal], [b_selrhs])
        emit(p, "vector", lambda v: v.tensor_mul(selrhs[:], selrhs[:], selpair[:]), [b_selrhs, b_selpair], [b_selrhs])
        emit(p, "tensor", lambda t: t.matmul(pMISC[:, 256:256 + 4 * NCH], sel8[:, :], selrhs[:, :], start=True, stop=True),
             [b_sel8, b_selrhs], [b_pMISC])
        emit(p, "vector", lambda v: v.tensor_copy(deccols[:], pMISC[:, 256:256 + 4 * NCH]), [b_pMISC], [b_deccols])

        for ci in range(NCH):
            sl = slice(ci * 128, (ci + 1) * 128)
            gci = HOFF + ci * 128
            for (src, b_src, dst, b_dst) in ((acs, b_acs, acsT, b_acsT), (dth, b_dth, dtT, b_dtT), (dtds, b_dtds, dtdsT, b_dtdsT)):
                emit_transpose(p, pMISC[0:128, 0:8], src[:, sl], idf[0:8, 0:8], [b_src, b_idf], [b_pMISC])
                emit(p, "vector", lambda v, dst=dst: v.tensor_copy(dst[:], pMISC[0:128, 0:8]), [b_pMISC], [b_dst])
            for h in range(8):
                emit_transpose(p, pMISC[0:1, 128:256], acsT[:, h:h + 1], idf[:, :], [b_acsT, b_idf], [b_pMISC])
                emit(p, "vector", lambda v, h=h: v.tensor_copy(rows[0:1, h * 128:(h + 1) * 128], pMISC[0:1, 128:256]),
                     [b_pMISC], [b_rows])
            emit(p, "tensor", lambda t, sl=sl: t.matmul(pCB[:, :], Bf[:, sl], Cf[:, sl], start=True, stop=True),
                 [b_Bf, b_Cf], [b_pCB])
            emit(p, "vector", lambda v: v.tensor_copy(cbt[:], pCB[:, :]), [b_pCB], [b_cbt])
            emit_transpose(p, pTb[:, 0:128], Bf[:, sl], idb[:, :], [b_Bf, b_idb], [b_pTb])
            emit(p, "vector", lambda v: v.tensor_copy(btm[:], pTb[:, 0:128]), [b_pTb], [b_btm])
            for pp in range(4):
                emit_transpose(p, pMISC[0:128, 128:256], Sst[:, pp * 128:(pp + 1) * 128], idf[:, :],
                               [b_Sst, b_idf], [b_pMISC])
                emit(p, "vector", lambda v, pp=pp: v.tensor_copy(Sbf[:, pp * 128:(pp + 1) * 128], pMISC[0:128, 128:256]),
                     [b_pMISC], [b_Sbf])
                xsl = slice(pp * HS + ci * 128, pp * HS + (ci + 1) * 128)
                emit_transpose(p, pTb[:, 128:256], xs[:, xsl], idb[:, :], [b_xs, b_idb], [b_pTb])
                emit(p, "vector", lambda v: v.tensor_copy(xtm[:], pTb[:, 128:256]), [b_pTb], [b_xtm])
                for hf in range(2):
                    h = pp * 2 + hf
                    emit(p, "vector", lambda v, hf=hf, h=h: v.tensor_scalar_mul(
                        x2[:, hf * 64:(hf + 1) * 64], xtm[:, hf * 64:(hf + 1) * 64], dtdsT[:, h:h + 1]),
                        [b_xtm, b_dtdsT], [b_x2])
                pY, b_pY = (pYa, b_pYa) if pp % 2 == 0 else (pYb, b_pYb)
                for hf in range(2):
                    h = pp * 2 + hf
                    emit(p, "tensor", lambda t, h=h: t.matmul(
                        pBCQ[:, :], ones1[0:1, :], rows[0:1, h * 128:(h + 1) * 128], start=True, stop=True),
                        [b_ones1, b_rows], [b_pBCQ])
                    emit(p, "vector", lambda v, h=h: v.scalar_tensor_tensor(
                        g1[:], pBCQ[:, :], acsT[:, h:h + 1], maskT[:], ALU.subtract, ALU.min),
                        [b_pBCQ, b_acsT, b_maskT], [b_g1])
                    emit(p, "scalar", lambda s: s.activation(out=g2[:], in_=g1[:], func=AF.Exp), [b_g1], [b_g2])
                    emit(p, "vector", lambda v, h=h: v.scalar_tensor_tensor(
                        g3[:], g2[:], dtT[:, h:h + 1], cbt[:], ALU.mult, ALU.mult),
                        [b_g2, b_dtT, b_cbt], [b_g3])
                    emit(p, "vector", lambda v, h=h: v.scalar_tensor_tensor(
                        gb[:], idf[:, :], dcols[:, h:h + 1], g3[:], ALU.mult, ALU.add),
                        [b_idf, b_dcols, b_g3], [b_gb])
                    emit(p, "tensor", lambda t, hf=hf, pY=pY: t.matmul(
                        pY[hf * 64:(hf + 1) * 64, :], xtm[:, hf * 64:(hf + 1) * 64], gb[:, :],
                        start=True, stop=False),
                        [b_xtm, b_gb], [b_pY])
                    emit(p, "scalar", lambda s: s.activation(out=ebq[:], in_=pBCQ[:, :], func=AF.Exp), [b_pBCQ], [b_ebq])
                    emit(p, "vector", lambda v, sl=sl: v.tensor_tensor(cph[:], Cf[:, sl], ebq[:], ALU.mult),
                         [b_Cf, b_ebq], [b_cph])
                    emit(p, "tensor", lambda t, pp=pp, hf=hf, pY=pY: t.matmul(
                        pY[hf * 64:(hf + 1) * 64, :], Sbf[:, pp * 128 + hf * 64: pp * 128 + (hf + 1) * 64], cph[:, :],
                        start=False, stop=(hf == 1)),
                        [b_Sbf, b_cph], [b_pY])
                emit(p, "tensor", lambda t: t.matmul(pS[:, :], x2[:, :], btm[:, :], start=True, stop=True),
                     [b_btm, b_x2], [b_pS])
                emit(p, "vector", lambda v, pp=pp, ci=ci: v.scalar_tensor_tensor(
                    Sst[:, pp * 128:(pp + 1) * 128], Sst[:, pp * 128:(pp + 1) * 128],
                    deccols[:, pp * NCH + ci: pp * NCH + ci + 1], pS[:, :], ALU.mult, ALU.add),
                    [b_Sst, b_deccols, b_pS], [b_Sst])
                ysl = slice((pp % 2) * 128, (pp % 2) * 128 + 128)
                emit(p, "vector", lambda v, ysl=ysl, pY=pY: v.tensor_copy(ysb[:, ysl], pY[:, :]), [b_pY], [b_ysb])
                p.dma("gpsimd", y_d[pp * 128:(pp + 1) * 128, gci:gci + 128], ysb[:, ysl],
                      reads=[b_ysb], writes=[b_y], out_dram=True)
    ctx.io = None
    if standalone:
        p.finalize()
    return ctx


def stage_A2(I, a1_res):
    NEGv = NEG
    idf = np.eye(128, dtype=np.float32)
    idb = idf.astype(BF16NP)
    sq = np.arange(128)
    maskT = np.where(sq[:, None] <= sq[None, :], 0.0, NEGv).astype(np.float32)
    selpair = np.zeros((8, 128), np.float32)
    for h in range(8):
        selpair[h, (h // 2) * 32:(h // 2) * 32 + 32] = 1.0
    sel8 = np.zeros((8, 128), np.float32)
    for h in range(8):
        sel8[h, (h % 2) * 64:(h % 2) * 64 + 64] = 1.0
    alog_full = np.asarray(I["L0_attn_A_log"], np.float32)
    d_full = np.asarray(I["L0_attn_D"], np.float32)
    xall = {}
    dtall = {}
    for b in range(2):
        xall[b] = np.concatenate([np.asarray(a1_res[b * 4 + k]["xbc"]) for k in range(4)], axis=1)
        dtall[b] = np.concatenate([np.asarray(a1_res[b * 4 + k]["dt"]) for k in range(4)], axis=1)
    in_maps = []
    for (b, k) in CORES:
        in_maps.append(dict(
            xs=np.ascontiguousarray(xall[b][512 * k:512 * k + 512]),
            Bf=np.ascontiguousarray(xall[b][2048:2176]),
            Cf=np.ascontiguousarray(xall[b][2176:2304]),
            dth=np.ascontiguousarray(dtall[b][8 * k:8 * k + 8]),
            alog=np.ascontiguousarray(alog_full[8 * k:8 * k + 8].reshape(8, 1)),
            dd=np.ascontiguousarray(d_full[8 * k:8 * k + 8].reshape(1, 8)),
            idf=idf, idb=idb, maskT=maskT, selpair=selpair, sel8=sel8,
        ))
    return in_maps


def build_A3(ctx=None, io=None):
    standalone = ctx is None
    if standalone:
        ctx = LaunchCtx("A3")
        p = ctx.start()
    else:
        p = ctx.prog
    ctx.io = io
    nc = ctx.nc

    x_d = ctx.din("xin3", [1024, SH])
    astat_d = ctx.din("astat3", [2, SH])
    z_d = ctx.din("zin", [2048, SH])
    y_d = ctx.din("yin", [2048, SH], BF16)
    wmn_d = ctx.din("wmnc", [128, 16])
    wout_d = ctx.din("wout_T", [1024, 2048])
    wln2_d = ctx.din("wln2c", [128, 8])
    wg_d = ctx.din("wg_T", [4096, 1024])
    wu_d = ctx.din("wu_T", [4096, 1024])
    wd_d = ctx.din("wd_T", [1024, 4096])
    x1_d, b_x1 = ctx.dout("x1", [1024, SH])

    xt, b_xt = ctx.sbuf("xt", [128, 8 * 512], F32)
    asm, b_asm = ctx.sbuf("asm", [1, 512], F32)
    asr, b_asr = ctx.sbuf("asr", [1, 512], F32)
    zt0, b_zt0 = ctx.sbuf("zt0", [128, 512], F32)
    zt1, b_zt1 = ctx.sbuf("zt1", [128, 512], F32)
    yt, b_yt = ctx.sbuf("yt", [128, 16 * 512], BF16)
    zs, b_zs = ctx.sbuf("zs", [128, 512], F32)
    ygf, b_ygf = ctx.sbuf("ygf", [128, 512], F32)
    sq, b_sq = ctx.sbuf("sq", [128, 512], F32)
    ygb, b_ygb = ctx.sbuf("ygb", [128, 16 * 512], BF16)
    ynb, b_ynb = ctx.sbuf("ynb", [128, 16 * 512], BF16)
    hid, b_hid = ctx.sbuf("hid", [128, 8 * 512], F32)
    h2, b_h2 = ctx.sbuf("h2", [128, 8 * 512], BF16)
    gs, b_gs = ctx.sbuf("gs", [128, 32 * 512], BF16)
    wmn, b_wmn = ctx.sbuf("wmn", [128, 16], F32)
    wln2, b_wln2 = ctx.sbuf("wln2", [128, 8], F32)
    wob, b_wob = ctx.sbuf("wob", [128, 16 * 1024], BF16)
    wst0, b_wst0 = ctx.sbuf("wst0", [128, 1024], F32)
    wst1, b_wst1 = ctx.sbuf("wst1", [128, 1024], F32)
    wcb0, b_wcb0 = ctx.sbuf("wcb0", [128, 1024], BF16)
    wcb1, b_wcb1 = ctx.sbuf("wcb1", [128, 1024], BF16)
    onesc, b_onesc = ctx.sbuf("onesc", [128, 1], F32)
    ones1, b_ones1 = ctx.sbuf("ones1", [1, 128], F32)
    srow, b_srow = ctx.sbuf("srow", [1, 6 * 512], F32)
    x1s, b_x1s = ctx.sbuf("x1s", [128, 512], F32)
    tvec, b_tvec = ctx.sbuf("tvec", [128, 512], F32)

    ps2, b_ps2 = ctx.psum("ps2", [1, 512])
    pbm, b_pbm = ctx.psum("pbm", [128, 512])
    pbr, b_pbr = ctx.psum("pbr", [128, 512])
    pbn, b_pbn = ctx.psum("pbn", [128, 512])
    pp0, b_pp0 = ctx.psum("pp0", [128, 512])
    pp1, b_pp1 = ctx.psum("pp1", [128, 512])

    emit(p, "vector", lambda v: v.memset(onesc[:], 1.0), [], [b_onesc])
    emit(p, "vector", lambda v: v.memset(ones1[:], 1.0), [], [b_ones1])
    p.dma("sync", wmn[:], wmn_d[:, :], writes=[b_wmn])
    p.dma("sync", wln2[:], wln2_d[:, :], writes=[b_wln2])
    for oc in range(8):
        wst, b_wst = (wst0, b_wst0) if oc % 2 == 0 else (wst1, b_wst1)
        for hh in range(2):
            p.dma("sync", wst[:, :], wout_d[oc * 128:(oc + 1) * 128, hh * 1024:(hh + 1) * 1024],
                  writes=[b_wst])
            emit(p, "vector", lambda v, oc=oc, hh=hh, wst=wst: v.tensor_copy(
                wob[:, oc * 2048 + hh * 1024: oc * 2048 + (hh + 1) * 1024], wst[:, :]),
                [b_wst], [b_wob])

    cb5, b_cb5 = ctx.cbias(1e-5, 1)
    cb6, b_cb6 = ctx.cbias(1e-6, 1)

    for ti in range(4):
        o = ti * 512
        for c in range(8):
            p.dma("sync", xt[:, c * 512:(c + 1) * 512], x_d[c * 128:(c + 1) * 128, o:o + 512],
                  writes=[b_xt])
        p.dma("sync", asm[:], astat_d[0:1, o:o + 512], writes=[b_asm])
        p.dma("sync", asr[:], astat_d[1:2, o:o + 512], writes=[b_asr])
        for c in range(16):
            p.dma("sync", yt[:, c * 512:(c + 1) * 512], y_d[c * 128:(c + 1) * 128, o:o + 512],
                  writes=[b_yt])
        for c in range(16):
            zt, b_zt = (zt0, b_zt0) if c % 2 == 0 else (zt1, b_zt1)
            p.dma("sync", zt[:], z_d[c * 128:(c + 1) * 128, o:o + 512], writes=[b_zt])
            emit(p, "scalar", lambda s, zt=zt: s.activation(out=zs[:], in_=zt[:], func=AF.Silu),
                 [b_zt], [b_zs])
            emit(p, "vector", lambda v, c=c: v.tensor_tensor(ygf[:], yt[:, c * 512:(c + 1) * 512], zs[:], ALU.mult),
                 [b_yt, b_zs], [b_ygf])
            emit(p, "vector", lambda v, c=c: v.tensor_copy(ygb[:, c * 512:(c + 1) * 512], ygf[:]),
                 [b_ygf], [b_ygb])
            emit(p, "scalar", lambda s: s.activation(out=sq[:], in_=ygf[:], func=AF.Square), [b_ygf], [b_sq])
            emit(p, "tensor", lambda t, c=c: t.matmul(ps2[:, :], onesc[:, :], sq[:, :],
                                                      start=(c == 0), stop=(c == 15)),
                 [b_onesc, b_sq], [b_ps2])
        R0 = srow[0:1, 0:512]
        emit(p, "vector", lambda v: v.tensor_scalar_mul(R0, ps2[0:1, :], 1.0 / 2048.0), [b_ps2], [b_srow])
        emit(p, "scalar", lambda s, cb5=cb5: s.activation(out=R0, in_=R0, func=AF.Sqrt, bias=cb5), [b_srow, b_cb5], [b_srow])
        emit(p, "vector", lambda v: v.reciprocal(R0, R0), [b_srow], [b_srow])
        emit(p, "tensor", lambda t: t.matmul(pbn[:, :], ones1[0:1, :], R0, start=True, stop=True),
             [b_ones1, b_srow], [b_pbn])
        for c in range(16):
            emit(p, "vector", lambda v, c=c: v.scalar_tensor_tensor(
                ynb[:, c * 512:(c + 1) * 512], ygb[:, c * 512:(c + 1) * 512], wmn[:, c:c + 1], pbn[:, :],
                ALU.mult, ALU.mult), [b_ygb, b_wmn, b_pbn], [b_ynb])
        emit(p, "tensor", lambda t: t.matmul(pbm[:, :], ones1[0:1, :], asm[0:1, :], start=True, stop=True),
             [b_ones1, b_asm], [b_pbm])
        emit(p, "tensor", lambda t: t.matmul(pbr[:, :], ones1[0:1, :], asr[0:1, :], start=True, stop=True),
             [b_ones1, b_asr], [b_pbr])
        for oc in range(8):
            pp, b_pp = (pp0, b_pp0) if oc % 2 == 0 else (pp1, b_pp1)
            def mmwo(t, oc=oc, pp=pp):
                last = None
                for kc in range(16):
                    last = t.matmul(pp[:, :], wob[:, oc * 2048 + kc * 128:(oc * 2048 + (kc + 1) * 128)],
                                    ynb[:, kc * 512:(kc + 1) * 512], start=(kc == 0), stop=(kc == 15))
                return last
            p.op("tensor", mmwo, reads=[b_wob, b_ynb], writes=[b_pp])
            emit(p, "vector", lambda v, oc=oc: v.tensor_sub(tvec[:], xt[:, oc * 512:(oc + 1) * 512], pbm[:, :]),
                 [b_xt, b_pbm], [b_tvec])
            emit(p, "vector", lambda v: v.tensor_mul(tvec[:], tvec[:], pbr[:, :]), [b_tvec, b_pbr], [b_tvec])
            emit(p, "vector", lambda v, oc=oc, pp=pp: v.tensor_add(hid[:, oc * 512:(oc + 1) * 512], tvec[:], pp[:, :]),
                 [b_tvec, b_pp], [b_hid])
        for oc in range(8):
            emit(p, "scalar", lambda s, oc=oc: s.activation(out=sq[:], in_=hid[:, oc * 512:(oc + 1) * 512], func=AF.Square),
                 [b_hid], [b_sq])
            emit(p, "tensor", lambda t, oc=oc: t.matmul(ps2[:, :], onesc[:, :], sq[:, :],
                                                        start=(oc == 0), stop=(oc == 7)),
                 [b_onesc, b_sq], [b_ps2])
        R1 = srow[0:1, 512:1024]
        emit(p, "vector", lambda v: v.tensor_scalar_mul(R1, ps2[0:1, :], 1.0 / 1024.0), [b_ps2], [b_srow])
        emit(p, "scalar", lambda s, cb6=cb6: s.activation(out=R1, in_=R1, func=AF.Sqrt, bias=cb6), [b_srow, b_cb6], [b_srow])
        emit(p, "vector", lambda v: v.reciprocal(R1, R1), [b_srow], [b_srow])
        emit(p, "tensor", lambda t: t.matmul(pbn[:, :], ones1[0:1, :], R1, start=True, stop=True),
             [b_ones1, b_srow], [b_pbn])
        for oc in range(8):
            emit(p, "vector", lambda v, oc=oc: v.scalar_tensor_tensor(
                h2[:, oc * 512:(oc + 1) * 512], hid[:, oc * 512:(oc + 1) * 512], wln2[:, oc:oc + 1], pbn[:, :],
                ALU.mult, ALU.mult), [b_hid, b_wln2, b_pbn], [b_h2])
        for oc in range(32):
            wst, b_wst = (wst0, b_wst0) if oc % 2 == 0 else (wst1, b_wst1)
            wcb, b_wcb = (wcb0, b_wcb0) if oc % 2 == 0 else (wcb1, b_wcb1)
            p.dma("sync", wst[:, :], wg_d[oc * 128:(oc + 1) * 128, :], writes=[b_wst])
            emit(p, "vector", lambda v, wst=wst, wcb=wcb: v.tensor_copy(wcb[:], wst[:]), [b_wst], [b_wcb])
            pp, b_pp = (pp0, b_pp0) if oc % 2 == 0 else (pp1, b_pp1)
            def mmg(t, wcb=wcb, pp=pp):
                last = None
                for kc in range(8):
                    last = t.matmul(pp[:, :], wcb[:, kc * 128:(kc + 1) * 128],
                                    h2[:, kc * 512:(kc + 1) * 512], start=(kc == 0), stop=(kc == 7))
                return last
            p.op("tensor", mmg, reads=[b_wcb, b_h2], writes=[b_pp])
            emit(p, "scalar", lambda s, oc=oc, pp=pp: s.activation(
                out=gs[:, oc * 512:(oc + 1) * 512], in_=pp[:, :], func=AF.Silu), [b_pp], [b_gs])
        for oc in range(32):
            wst, b_wst = (wst0, b_wst0) if oc % 2 == 0 else (wst1, b_wst1)
            wcb, b_wcb = (wcb0, b_wcb0) if oc % 2 == 0 else (wcb1, b_wcb1)
            p.dma("sync", wst[:, :], wu_d[oc * 128:(oc + 1) * 128, :], writes=[b_wst])
            emit(p, "vector", lambda v, wst=wst, wcb=wcb: v.tensor_copy(wcb[:], wst[:]), [b_wst], [b_wcb])
            pp, b_pp = (pp0, b_pp0) if oc % 2 == 0 else (pp1, b_pp1)
            def mmu(t, wcb=wcb, pp=pp):
                last = None
                for kc in range(8):
                    last = t.matmul(pp[:, :], wcb[:, kc * 128:(kc + 1) * 128],
                                    h2[:, kc * 512:(kc + 1) * 512], start=(kc == 0), stop=(kc == 7))
                return last
            p.op("tensor", mmu, reads=[b_wcb, b_h2], writes=[b_pp])
            emit(p, "vector", lambda v, oc=oc, pp=pp: v.tensor_tensor(
                gs[:, oc * 512:(oc + 1) * 512], pp[:, :], gs[:, oc * 512:(oc + 1) * 512], ALU.mult),
                [b_pp, b_gs], [b_gs])
        for oc in range(8):
            wst, b_wst = (wst0, b_wst0) if oc % 2 == 0 else (wst1, b_wst1)
            wcb, b_wcb = (wcb0, b_wcb0) if oc % 2 == 0 else (wcb1, b_wcb1)
            for hh in range(4):
                p.dma("sync", wst[:, :], wd_d[oc * 128:(oc + 1) * 128, hh * 1024:(hh + 1) * 1024],
                      writes=[b_wst])
                emit(p, "vector", lambda v, wst=wst, wcb=wcb, hh=hh: v.tensor_copy(wcb[:], wst[:]), [b_wst], [b_wcb])
                pp, b_pp = (pp0, b_pp0) if oc % 2 == 0 else (pp1, b_pp1)
                def mmd(t, wcb=wcb, pp=pp, hh=hh, oc=oc):
                    last = None
                    for kc in range(8):
                        gk = hh * 8 + kc
                        last = t.matmul(pp[:, :], wcb[:, kc * 128:(kc + 1) * 128],
                                        gs[:, gk * 512:(gk + 1) * 512], start=(hh == 0 and kc == 0),
                                        stop=(hh == 3 and kc == 7))
                    return last
                p.op("tensor", mmd, reads=[b_wcb, b_gs], writes=[b_pp])
            pp, b_pp = (pp0, b_pp0) if oc % 2 == 0 else (pp1, b_pp1)
            emit(p, "vector", lambda v, oc=oc, pp=pp: v.tensor_add(x1s[:], pp[:, :], hid[:, oc * 512:(oc + 1) * 512]),
                 [b_pp, b_hid], [b_x1s])
            emit(p, "vector", lambda v, oc=oc: v.tensor_add(x1s[:], x1s[:], xt[:, oc * 512:(oc + 1) * 512]),
                 [b_x1s, b_xt], [b_x1s])
            p.dma("gpsimd", x1_d[oc * 128:(oc + 1) * 128, o:o + 512], x1s[:], reads=[b_x1s], writes=[b_x1], out_dram=True)
    ctx.io = None
    if standalone:
        p.finalize()
    return ctx


def stage_A3(I, a1_res, a2_res):
    wmnc = stage_vec_cols(I["L0_attn_w_mnorm"], 16)
    wout_T = stage_ktiles(np.asarray(I["L0_attn_w_out"], np.float32).T, kc=16)
    wln2c = stage_vec_cols(I["L0_attn_w_ln2"], 8)
    wg_T = stage_ktiles(np.asarray(I["L0_attn_w_gate"], np.float32).T)
    wu_T = stage_ktiles(np.asarray(I["L0_attn_w_up"], np.float32).T)
    wd_T = stage_ktiles(np.asarray(I["L0_attn_w_down"], np.float32).T, kc=32)
    x = I["L0_attn_hidden_states"]
    xfm = np.ascontiguousarray(x.transpose(0, 2, 1), dtype=np.float32)
    yfm = {}
    for b in range(2):
        yfm[b] = np.concatenate([np.asarray(a2_res[b * 4 + k]["y"]) for k in range(4)], axis=0)
    in_maps = []
    for c, (b, k) in enumerate(CORES):
        sl = slice(2048 * k, 2048 * k + 2048)
        in_maps.append(dict(
            xin3=np.ascontiguousarray(xfm[b][:, sl]),
            astat3=np.ascontiguousarray(np.asarray(a1_res[c]["astat"])),
            zin=np.ascontiguousarray(np.asarray(a1_res[c]["z"])),
            yin=np.ascontiguousarray(yfm[b][:, sl]),
            wmnc=wmnc, wout_T=wout_T, wln2c=wln2c, wg_T=wg_T, wu_T=wu_T, wd_T=wd_T,
        ))
    return in_maps


def build_B1(ctx=None, io=None):
    standalone = ctx is None
    if standalone:
        ctx = LaunchCtx("B1")
        p = ctx.start()
    else:
        p = ctx.prog
    ctx.io = io
    nc = ctx.nc

    x_d = ctx.din("x1in", [1024, SH])
    wq_d = ctx.din("wq", [1024, 1024])
    wk_d = ctx.din("wk", [1024, 1024])
    wv_d = ctx.din("wv", [1024, 1024])
    bq_d = ctx.din("bqc", [128, 8])
    bk_d = ctx.din("bkc", [128, 8])
    bv_d = ctx.din("bvr", [1, 1024])
    g1_d = ctx.din("g1c", [128, 8])
    b1_d = ctx.din("b1c", [128, 8])
    lnb_d, b_lnb = ctx.dout("lnb", [1024, SH])
    q_d, b_q = ctx.dout("q", [1024, SH], BF16)
    k_d, b_k = ctx.dout("k", [1024, SH], BF16)
    v_d, b_v = ctx.dout("vtm", [SH, 1024], BF16)

    xt, b_xt = ctx.sbuf("xt", [128, 8 * 512], F32)
    lnf, b_lnf = ctx.sbuf("lnf", [128, 8 * 512], F32)
    hnb, b_hnb = ctx.sbuf("hnb", [128, 8 * 512], BF16)
    sq, b_sq = ctx.sbuf("sq", [128, 512], F32)
    tv, b_tv = ctx.sbuf("tv", [128, 512], F32)
    wqb, b_wqb = ctx.sbuf("wqb", [128, 8 * 1024], BF16)
    wkb, b_wkb = ctx.sbuf("wkb", [128, 8 * 1024], BF16)
    wvb, b_wvb = ctx.sbuf("wvb", [128, 8 * 1024], BF16)
    wst0, b_wst0 = ctx.sbuf("wst0", [128, 1024], F32)
    wst1, b_wst1 = ctx.sbuf("wst1", [128, 1024], F32)
    bqc, b_bqc = ctx.sbuf("bqc_s", [128, 8], F32)
    bkc, b_bkc = ctx.sbuf("bkc_s", [128, 8], F32)
    bvr, b_bvr = ctx.sbuf("bvr_s", [1, 1024], F32)
    g1c, b_g1c = ctx.sbuf("g1c_s", [128, 8], F32)
    b1c, b_b1c = ctx.sbuf("b1c_s", [128, 8], F32)
    onesc, b_onesc = ctx.sbuf("onesc", [128, 1], F32)
    ones1, b_ones1 = ctx.sbuf("ones1", [1, 128], F32)
    srow, b_srow = ctx.sbuf("srow", [1, 6 * 512], F32)
    outs, b_outs = ctx.sbuf("outs", [128, 512], BF16)
    vts, b_vts = ctx.sbuf("vts", [128, 512], BF16)

    ps1, b_ps1 = ctx.psum("ps1", [1, 512])
    ps2, b_ps2 = ctx.psum("ps2", [1, 512])
    pbm, b_pbm = ctx.psum("pbm", [128, 512])
    pbr, b_pbr = ctx.psum("pbr", [128, 512])
    pp0, b_pp0 = ctx.psum("pp0", [128, 512])
    pp1, b_pp1 = ctx.psum("pp1", [128, 512])

    emit(p, "vector", lambda v: v.memset(onesc[:], 1.0), [], [b_onesc])
    emit(p, "vector", lambda v: v.memset(ones1[:], 1.0), [], [b_ones1])
    for (dst, b_dst, src) in ((bqc, b_bqc, bq_d), (bkc, b_bkc, bk_d), (g1c, b_g1c, g1_d), (b1c, b_b1c, b1_d)):
        p.dma("sync", dst[:], src[:, :], writes=[b_dst])
    p.dma("sync", bvr[:], bv_d[:, :], writes=[b_bvr])
    for (wsb, b_wsb, wd) in ((wqb, b_wqb, wq_d), (wkb, b_wkb, wk_d), (wvb, b_wvb, wv_d)):
        for oc in range(8):
            wst, b_wst = (wst0, b_wst0) if oc % 2 == 0 else (wst1, b_wst1)
            p.dma("sync", wst[:, :], wd[oc * 128:(oc + 1) * 128, :], writes=[b_wst])
            emit(p, "vector", lambda v, wsb=wsb, oc=oc, wst=wst: v.tensor_copy(
                wsb[:, oc * 1024:(oc + 1) * 1024], wst[:]), [b_wst], [b_wsb])

    cb5, b_cb5 = ctx.cbias(1e-5, 1)

    for ti in range(4):
        o = ti * 512
        for c in range(8):
            p.dma("sync", xt[:, c * 512:(c + 1) * 512], x_d[c * 128:(c + 1) * 128, o:o + 512], writes=[b_xt])

        def norm_pass(src_sb, b_src, R0off):
            def mm1(t):
                last = None
                for c in range(8):
                    last = t.matmul(ps1[:, :], onesc[:, :], src_sb[:, c * 512:(c + 1) * 512],
                                    start=(c == 0), stop=(c == 7))
                return last
            p.op("tensor", mm1, reads=[b_onesc, b_src], writes=[b_ps1])
            for c in range(8):
                emit(p, "scalar", lambda s, c=c: s.activation(out=sq[:], in_=src_sb[:, c * 512:(c + 1) * 512], func=AF.Square),
                     [b_src], [b_sq])
                emit(p, "tensor", lambda t, c=c: t.matmul(ps2[:, :], onesc[:, :], sq[:, :], start=(c == 0), stop=(c == 7)),
                     [b_onesc, b_sq], [b_ps2])
            M = srow[0:1, R0off:R0off + 512]
            R = srow[0:1, R0off + 512:R0off + 1024]
            T1 = srow[0:1, R0off + 1024:R0off + 1536]
            emit(p, "vector", lambda v, M=M: v.tensor_scalar_mul(M, ps1[0:1, :], 1.0 / 1024.0), [b_ps1], [b_srow])
            emit(p, "vector", lambda v, T1=T1: v.tensor_scalar_mul(T1, ps2[0:1, :], 1.0 / 1024.0), [b_ps2], [b_srow])
            emit(p, "scalar", lambda s, M=M, R=R: s.activation(out=R, in_=M, func=AF.Square), [b_srow], [b_srow])
            emit(p, "vector", lambda v, T1=T1, R=R: v.tensor_sub(T1, T1, R), [b_srow], [b_srow])
            emit(p, "scalar", lambda s, T1=T1, cb5=cb5: s.activation(out=T1, in_=T1, func=AF.Sqrt, bias=cb5), [b_srow, b_cb5], [b_srow])
            emit(p, "vector", lambda v, T1=T1, R=R: v.reciprocal(R, T1), [b_srow], [b_srow])
            emit(p, "tensor", lambda t, M=M: t.matmul(pbm[:, :], ones1[0:1, :], M, start=True, stop=True),
                 [b_ones1, b_srow], [b_pbm])
            emit(p, "tensor", lambda t, R=R: t.matmul(pbr[:, :], ones1[0:1, :], R, start=True, stop=True),
                 [b_ones1, b_srow], [b_pbr])

        norm_pass(xt, b_xt, 0)
        for c in range(8):
            sl = slice(c * 512, (c + 1) * 512)
            emit(p, "vector", lambda v, sl=sl: v.tensor_sub(lnf[:, sl], xt[:, sl], pbm[:, :]), [b_xt, b_pbm], [b_lnf])
            emit(p, "vector", lambda v, sl=sl: v.tensor_mul(lnf[:, sl], lnf[:, sl], pbr[:, :]), [b_lnf, b_pbr], [b_lnf])
            p.dma("gpsimd", lnb_d[c * 128:(c + 1) * 128, o:o + 512], lnf[:, sl], reads=[b_lnf], writes=[b_lnb], out_dram=True)
        norm_pass(lnf, b_lnf, 1536)
        for c in range(8):
            sl = slice(c * 512, (c + 1) * 512)
            emit(p, "vector", lambda v, sl=sl: v.tensor_sub(tv[:], lnf[:, sl], pbm[:, :]), [b_lnf, b_pbm], [b_tv])
            emit(p, "vector", lambda v: v.tensor_mul(tv[:], tv[:], pbr[:, :]), [b_tv, b_pbr], [b_tv])
            emit(p, "vector", lambda v, c=c, sl=sl: v.tensor_scalar(
                hnb[:, sl], tv[:], g1c[:, c:c + 1], b1c[:, c:c + 1], ALU.mult, ALU.add),
                [b_tv, b_g1c, b_b1c], [b_hnb])
        for (wsb, b_wsb, bc, b_bc, dram, b_dram) in ((wqb, b_wqb, bqc, b_bqc, q_d, b_q), (wkb, b_wkb, bkc, b_bkc, k_d, b_k)):
            for oc in range(8):
                pp, b_pp = (pp0, b_pp0) if oc % 2 == 0 else (pp1, b_pp1)
                def mmp(t, wsb=wsb, oc=oc, pp=pp):
                    last = None
                    for kc in range(8):
                        last = t.matmul(pp[:, :], wsb[:, oc * 1024 + kc * 128: oc * 1024 + (kc + 1) * 128],
                                        hnb[:, kc * 512:(kc + 1) * 512], start=(kc == 0), stop=(kc == 7))
                    return last
                p.op("tensor", mmp, reads=[b_wsb, b_hnb], writes=[b_pp])
                emit(p, "vector", lambda v, oc=oc, pp=pp, bc=bc: v.tensor_scalar_add(outs[:], pp[:, :], bc[:, oc:oc + 1]),
                     [b_pp, b_bc], [b_outs])
                p.dma("gpsimd", dram[oc * 128:(oc + 1) * 128, o:o + 512], outs[:], reads=[b_outs], writes=[b_dram], out_dram=True)
        for sc in range(4):
            so = o + sc * 128
            for oc in range(8):
                pp, b_pp = (pp0, b_pp0) if oc % 2 == 0 else (pp1, b_pp1)
                def mmv(t, sc=sc, oc=oc, pp=pp):
                    last = None
                    for kc in range(8):
                        last = t.matmul(pp[:, 0:128], hnb[:, kc * 512 + sc * 128: kc * 512 + (sc + 1) * 128],
                                        wvb[:, oc * 1024 + kc * 128: oc * 1024 + (kc + 1) * 128],
                                        start=(kc == 0), stop=False)
                    last = t.matmul(pp[:, 0:128], ones1[0:1, 0:128], bvr[0:1, oc * 128:(oc + 1) * 128],
                                    start=False, stop=True)
                    return last
                p.op("tensor", mmv, reads=[b_hnb, b_wvb, b_ones1, b_bvr], writes=[b_pp])
                emit(p, "vector", lambda v, pp=pp: v.tensor_copy(vts[:, 0:128], pp[:, 0:128]), [b_pp], [b_vts])
                p.dma("gpsimd", v_d[so:so + 128, oc * 128:(oc + 1) * 128], vts[:, 0:128], reads=[b_vts], writes=[b_v], out_dram=True)
    ctx.io = None
    if standalone:
        p.finalize()
    return ctx


def stage_B1(I, x1_parts):
    wq = stage_ktiles(I["L0_ffn_wq"])
    wk = stage_ktiles(I["L0_ffn_wk"])
    wv = stage_ktiles(I["L0_ffn_wv"])
    bqc = stage_vec_cols(I["L0_ffn_bq"], 8)
    bkc = stage_vec_cols(I["L0_ffn_bk"], 8)
    bvr = _f32c(np.asarray(I["L0_ffn_bv"], np.float32).reshape(1, 1024))
    g1c = stage_vec_cols(I["L0_ffn_ln1_g"], 8)
    b1c = stage_vec_cols(I["L0_ffn_ln1_b"], 8)
    in_maps = []
    for c, (b, k) in enumerate(CORES):
        in_maps.append(dict(x1in=np.ascontiguousarray(x1_parts[c]),
                            wq=wq, wk=wk, wv=wv, bqc=bqc, bkc=bkc, bvr=bvr, g1c=g1c, b1c=b1c))
    return in_maps


def build_B2(ctx=None, io=None):
    standalone = ctx is None
    if standalone:
        ctx = LaunchCtx("B2")
        p = ctx.start()
    else:
        p = ctx.prog
    ctx.io = io
    nc = ctx.nc
    S = 8192

    q_dd = ctx.din("qin", [1024, SH], BF16)
    kf_d = ctx.din("kf", [1024, S], BF16)
    vf_d = ctx.din("vf", [S, 1024], BF16)
    lnb_d = ctx.din("lnbin", [1024, SH])
    x1_d = ctx.din("x1in2", [1024, SH])
    wo_d = ctx.din("wo", [1024, 1024])
    bo_d = ctx.din("boc", [128, 8])
    g2_d = ctx.din("g2c", [128, 8])
    b2_d = ctx.din("b2c", [128, 8])
    gw_d = ctx.din("gatew", [1024, 8])
    gb_d = ctx.din("gatebc", [8, 1])
    w1_d = ctx.din("w1m", [8 * 4096, 1024])
    b1_d = ctx.din("b1m", [8, 4096])
    w2_d = ctx.din("w2m", [8 * 1024, 4096])
    b2m_d = ctx.din("b2m", [8, 1024])
    esel_d = ctx.din("esel", [8, 1024])
    idf_d = ctx.din("idf2", [128, 128])
    x2_d, b_x2 = ctx.dout("x2", [1024, SH])
    od_d = ctx.nc.dram_tensor("oscr", [1024, SH], BF16, kind="Internal").ap()
    b_od = p.buf("oscr")
    ar_d = ctx.nc.dram_tensor("arscr", [1024, SH], F32, kind="Internal").ap()
    b_ar = p.buf("arscr")

    kx, b_kx = ctx.sbuf("kx", [65, S], BF16)
    vx, b_vx = ctx.sbuf("vx", [128, 64 * 65], BF16)
    qx, b_qx = ctx.sbuf("qx", [65, SH], BF16)
    ptb, b_ptb = ctx.sbuf("ptb", [128, 2 * 512], BF16)
    sq, b_sq = ctx.sbuf("sq", [128, 512], F32)
    lbuf, b_lbuf = ctx.sbuf("lbuf", [128, 512], F32)
    bcs, b_bcs = ctx.sbuf("bcs", [64, 512], F32)
    osb, b_osb = ctx.sbuf("osb", [64, 512], BF16)
    ob0, b_ob0 = ctx.sbuf("ob0", [128, 512], BF16)
    ob1, b_ob1 = ctx.sbuf("ob1", [128, 512], BF16)
    art, b_art = ctx.sbuf("art", [128, 8 * 512], F32)
    hn2, b_hn2 = ctx.sbuf("hn2", [128, 8 * SH], BF16)
    hn2f, b_hn2f = ctx.sbuf("hn2f", [128, 512], F32)
    selfm, b_selfm = ctx.sbuf("selfm", [8, SH], BF16)
    lgs, b_lgs = ctx.sbuf("lgs", [8, 512], F32)
    lgt, b_lgt = ctx.sbuf("lgt", [128, 8], F32)
    selt, b_selt = ctx.sbuf("selt", [128, 8], F32)
    maxc, b_maxc = ctx.sbuf("maxc", [128, 1], F32)
    midb, b_midb = ctx.sbuf("midb", [128, 8 * SH], BF16)
    outacc, b_outacc = ctx.sbuf("outacc", [128, 8 * SH], BF16)
    mskb, b_mskb = ctx.sbuf("mskb", [128, 4 * 512], BF16)
    wst0, b_wst0 = ctx.sbuf("wst0", [128, 1024], F32)
    wst1, b_wst1 = ctx.sbuf("wst1", [128, 1024], F32)
    wcb0, b_wcb0 = ctx.sbuf("wcb0", [128, 1024], BF16)
    wcb1, b_wcb1 = ctx.sbuf("wcb1", [128, 1024], BF16)
    boc, b_boc = ctx.sbuf("boc_s", [128, 8], F32)
    g2c, b_g2c = ctx.sbuf("g2c_s", [128, 8], F32)
    b2c, b_b2c = ctx.sbuf("b2c_s", [128, 8], F32)
    gwc, b_gwc = ctx.sbuf("gwc", [128, 8 * 8], F32)
    gbc, b_gbc = ctx.sbuf("gbc", [8, 1], F32)
    b1s, b_b1s = ctx.sbuf("b1s", [8, 128], BF16)
    b1st, b_b1st = ctx.sbuf("b1st", [8, 128], F32)
    ones8b, b_ones8b = ctx.sbuf("ones8b", [8, 512], BF16)
    eself, b_eself = ctx.sbuf("eself", [8, 1024], F32)
    b2f, b_b2f = ctx.sbuf("b2f", [8, 1024], BF16)
    esel, b_esel = ctx.sbuf("esel_s", [8, 1024], BF16)
    idf, b_idf = ctx.sbuf("idf_s", [128, 128], F32)
    onesc, b_onesc = ctx.sbuf("onesc", [128, 1], F32)
    ones1, b_ones1 = ctx.sbuf("ones1", [1, 128], F32)
    ones64, b_ones64 = ctx.sbuf("ones64", [128, 64], F32)
    srow, b_srow = ctx.sbuf("srow", [1, 4 * 512], F32)
    tv, b_tv = ctx.sbuf("tv", [128, 512], F32)
    xts, b_xts = ctx.sbuf("xts", [128, 512], F32)

    pst0, b_pst0 = ctx.psum("pst0", [128, 512])
    pst1, b_pst1 = ctx.psum("pst1", [128, 512])
    po, b_po = ctx.psum("po", [65, 512])
    pbc, b_pbc = ctx.psum("pbc", [64, 512])
    pm, b_pm = ctx.psum("pm", [128, 512])
    pw0, b_pw0 = ctx.psum("pw0", [128, 512])
    pw1, b_pw1 = ctx.psum("pw1", [128, 512])
    ps1, b_ps1 = pbc[0:1, :], b_pbc

    emit(p, "vector", lambda v: v.memset(onesc[:], 1.0), [], [b_onesc])
    emit(p, "vector", lambda v: v.memset(ones1[:], 1.0), [], [b_ones1])
    emit(p, "vector", lambda v: v.memset(ones64[:], 1.0), [], [b_ones64])
    emit(p, "vector", lambda v: v.memset(kx[64:65, :], 1.0), [], [b_kx])
    emit(p, "vector", lambda v: v.memset(vx[:, 64::65], 1.0), [], [b_vx])
    emit(p, "vector", lambda v: v.memset(outacc[:], 0.0), [], [b_outacc])
    for (dst, b_dst, src) in ((boc, b_boc, bo_d), (g2c, b_g2c, g2_d), (b2c, b_b2c, b2_d),
                              (gbc, b_gbc, gb_d), (idf, b_idf, idf_d)):
        p.dma("sync", dst[:], src[:, :], writes=[b_dst])
    p.dma("sync", wst0[0:8, 0:1024], b2m_d[:, :], writes=[b_wst0])
    emit(p, "vector", lambda v: v.tensor_copy(b2f[:], wst0[0:8, 0:1024]), [b_wst0], [b_b2f])
    p.dma("sync", wst1[0:8, 0:1024], esel_d[:, :], writes=[b_wst1])
    emit(p, "vector", lambda v: v.tensor_copy(esel[:], wst1[0:8, 0:1024]), [b_wst1], [b_esel])
    for kc in range(8):
        p.dma("sync", gwc[:, kc * 8:(kc + 1) * 8], gw_d[kc * 128:(kc + 1) * 128, :], writes=[b_gwc])
    emit(p, "vector", lambda v: v.memset(ones8b[:], 1.0), [], [b_ones8b])
    p.dma("sync", eself[:], esel_d[:, :], writes=[b_eself])

    cb5, b_cb5 = ctx.cbias(1e-5, 1)
    cb0125, b_cb0125 = ctx.cbias(0.125, 128)

    for h in range(16):
        p.dma("sync", kx[0:64, :], kf_d[64 * h:64 * h + 64, :], writes=[b_kx])
        p.dma("sync", vx[:, :].rearrange("p (s c) -> p s c", c=65)[:, :, 0:64],
              vf_d[:, 64 * h:64 * h + 64].rearrange("(s p) c -> p s c", p=128), writes=[b_vx])
        p.dma("sync", qx[0:64, :], q_dd[64 * h:64 * h + 64, :], writes=[b_qx])
        emit(p, "vector", lambda v: v.memset(srow[0:1, 0:1], 0.0), [], [b_srow])
        for sc in range(16):
            emit(p, "scalar", lambda s, sc=sc: s.activation(out=sq[0:64, 0:512], in_=kx[0:64, sc * 512:(sc + 1) * 512], func=AF.Square),
                 [b_kx], [b_sq])
            emit(p, "tensor", lambda t, sc=sc: t.matmul(ps1[:, :], onesc[0:64, :], sq[0:64, 0:512], start=True, stop=True),
                 [b_onesc, b_sq], [b_ps1])
            emit(p, "vector", lambda v, sc=sc: v.reduce_max(srow[0:1, 2:3], ps1[0:1, :], axis=AX.X),
                 [b_ps1], [b_srow])
            emit(p, "vector", lambda v: v.tensor_tensor(srow[0:1, 0:1], srow[0:1, 0:1], srow[0:1, 2:3], ALU.max),
                 [b_srow], [b_srow])
        emit(p, "scalar", lambda s: s.activation(out=srow[0:1, 0:1], in_=srow[0:1, 0:1], func=AF.Sqrt), [b_srow], [b_srow])
        QN = srow[0:1, 512:1024]
        for qt in range(4):
            emit(p, "scalar", lambda s, qt=qt: s.activation(out=sq[0:64, 0:512], in_=qx[0:64, qt * 512:(qt + 1) * 512], func=AF.Square),
                 [b_qx], [b_sq])
            emit(p, "tensor", lambda t: t.matmul(ps1[:, :], onesc[0:64, :], sq[0:64, 0:512], start=True, stop=True),
                 [b_onesc, b_sq], [b_ps1])
            emit(p, "scalar", lambda s, QN=QN: s.activation(out=QN, in_=ps1[0:1, :], func=AF.Sqrt),
                 [b_ps1], [b_srow])
            emit(p, "vector", lambda v, QN=QN: v.tensor_single_scalar(QN, QN, srow[0:1, 0:1], ALU.mult),
                 [b_srow], [b_srow])
            emit(p, "vector", lambda v, QN=QN: v.tensor_scalar_mul(QN, QN, -1.0), [b_srow], [b_srow])
            emit(p, "tensor", lambda t, QN=QN: t.matmul(po[64:65, :], ones1[0:1, 0:1], QN,
                                                        start=True, stop=True), [b_ones1, b_srow], [b_po])
            emit(p, "vector", lambda v, qt=qt: v.tensor_copy(qx[64:65, qt * 512:(qt + 1) * 512], po[64:65, :]),
                 [b_po], [b_qx])
        for qt in range(4):
            for sc in range(64):
                pst, b_pst = (pst0, b_pst0) if sc % 2 == 0 else (pst1, b_pst1)
                emit(p, "tensor", lambda t, sc=sc, qt=qt, pst=pst: t.matmul(
                    pst[:, :], kx[:, sc * 128:(sc + 1) * 128], qx[:, qt * 512:(qt + 1) * 512],
                    start=True, stop=True), [b_kx, b_qx], [b_pst])
                pt_sl = slice((sc % 2) * 512, (sc % 2) * 512 + 512)
                emit(p, "scalar", lambda s, pst=pst, pt_sl=pt_sl, cb0125=cb0125: s.activation(
                    out=ptb[:, pt_sl], in_=pst[:, :], func=AF.Exp, scale=cb0125),
                    [b_pst, b_cb0125], [b_ptb])
                emit(p, "tensor", lambda t, sc=sc, pt_sl=pt_sl: t.matmul(
                    po[:, :], vx[:, sc * 65:(sc + 1) * 65], ptb[:, pt_sl],
                    start=(sc == 0), stop=(sc == 63)), [b_vx, b_ptb], [b_po])
            emit(p, "vector", lambda v: v.reciprocal(lbuf[64:65, 0:512], po[64:65, :]), [b_po], [b_lbuf])
            emit(p, "tensor", lambda t: t.matmul(pbc[:, :], ones64[64:65, 0:64], lbuf[64:65, 0:512], start=True, stop=True),
                 [b_ones64, b_lbuf], [b_pbc])
            emit(p, "vector", lambda v: v.tensor_copy(bcs[:], pbc[:, :]), [b_pbc], [b_bcs])
            emit(p, "vector", lambda v: v.tensor_tensor(osb[:], po[0:64, :], bcs[:], ALU.mult), [b_po, b_bcs], [b_osb])
            p.dma("gpsimd", od_d[64 * h:64 * h + 64, qt * 512:(qt + 1) * 512], osb[:], reads=[b_osb], writes=[b_od])

    for ti in range(4):
        o = ti * 512
        for oc in range(8):
            pw, b_pw = (pw0, b_pw0) if oc % 2 == 0 else (pw1, b_pw1)
            wst, b_wst = (wst0, b_wst0) if oc % 2 == 0 else (wst1, b_wst1)
            wcb, b_wcb = (wcb0, b_wcb0) if oc % 2 == 0 else (wcb1, b_wcb1)
            p.dma("sync", wst[:, :], wo_d[oc * 128:(oc + 1) * 128, :], writes=[b_wst])
            emit(p, "vector", lambda v, wst=wst, wcb=wcb: v.tensor_copy(wcb[:], wst[:]), [b_wst], [b_wcb])
            for kc in range(8):
                obx, b_obx = (ob0, b_ob0) if kc % 2 == 0 else (ob1, b_ob1)
                p.dma("sync", obx[:], od_d[kc * 128:(kc + 1) * 128, o:o + 512],
                      reads=[b_od], writes=[b_obx])
                emit(p, "tensor", lambda t, kc=kc, pw=pw, wcb=wcb, obx=obx: t.matmul(
                    pw[:, :], wcb[:, kc * 128:(kc + 1) * 128], obx[:],
                    start=(kc == 0), stop=(kc == 7)), [b_wcb, b_obx], [b_pw])
            p.dma("sync", xts[:], lnb_d[oc * 128:(oc + 1) * 128, o:o + 512], writes=[b_xts])
            emit(p, "vector", lambda v, oc=oc, pw=pw: v.tensor_scalar_add(tv[:], pw[:, :], boc[:, oc:oc + 1]),
                 [b_pw, b_boc], [b_tv])
            emit(p, "vector", lambda v, oc=oc: v.tensor_add(art[:, oc * 512:(oc + 1) * 512], tv[:], xts[:]),
                 [b_tv, b_xts], [b_art])
            p.dma("gpsimd", ar_d[oc * 128:(oc + 1) * 128, o:o + 512], art[:, oc * 512:(oc + 1) * 512],
                  reads=[b_art], writes=[b_ar])
        def mm1(t):
            last = None
            for c in range(8):
                last = t.matmul(ps1[:, :], onesc[:, :], art[:, c * 512:(c + 1) * 512], start=(c == 0), stop=(c == 7))
            return last
        p.op("tensor", mm1, reads=[b_onesc, b_art], writes=[b_ps1])
        M = srow[0:1, 512:1024]
        emit(p, "vector", lambda v, M=M: v.tensor_scalar_mul(M, ps1[0:1, :], 1.0 / 1024.0), [b_ps1], [b_srow])
        for c in range(8):
            emit(p, "scalar", lambda s, c=c: s.activation(out=sq[:], in_=art[:, c * 512:(c + 1) * 512], func=AF.Square),
                 [b_art], [b_sq])
            emit(p, "tensor", lambda t, c=c: t.matmul(ps1[:, :], onesc[:, :], sq[:, :], start=(c == 0), stop=(c == 7)),
                 [b_onesc, b_sq], [b_ps1])
        T1 = srow[0:1, 1024:1536]
        R = srow[0:1, 1536:2048]
        emit(p, "vector", lambda v, T1=T1: v.tensor_scalar_mul(T1, ps1[0:1, :], 1.0 / 1024.0), [b_ps1], [b_srow])
        emit(p, "scalar", lambda s, M=M, R=R: s.activation(out=R, in_=M, func=AF.Square), [b_srow], [b_srow])
        emit(p, "vector", lambda v, T1=T1, R=R: v.tensor_sub(T1, T1, R), [b_srow], [b_srow])
        emit(p, "scalar", lambda s, T1=T1, cb5=cb5: s.activation(out=T1, in_=T1, func=AF.Sqrt, bias=cb5), [b_srow, b_cb5], [b_srow])
        emit(p, "vector", lambda v, T1=T1, R=R: v.reciprocal(R, T1), [b_srow], [b_srow])
        emit(p, "tensor", lambda t, M=M: t.matmul(pm[:, :], ones1[0:1, :], M, start=True, stop=True),
             [b_ones1, b_srow], [b_pm])
        emit(p, "tensor", lambda t, R=R: t.matmul(pw0[:, :], ones1[0:1, :], R, start=True, stop=True),
             [b_ones1, b_srow], [b_pw0])
        for c in range(8):
            emit(p, "vector", lambda v, c=c: v.tensor_sub(hn2f[:], art[:, c * 512:(c + 1) * 512], pm[:, :]),
                 [b_art, b_pm], [b_hn2f])
            emit(p, "vector", lambda v: v.tensor_mul(hn2f[:], hn2f[:], pw0[:, :]), [b_hn2f, b_pw0], [b_hn2f])
            emit(p, "vector", lambda v, c=c: v.tensor_scalar(hn2f[:], hn2f[:], g2c[:, c:c + 1], b2c[:, c:c + 1],
                                                             ALU.mult, ALU.add), [b_hn2f, b_g2c, b_b2c], [b_hn2f])
            emit(p, "vector", lambda v, c=c, o=o: v.tensor_copy(hn2[:, c * SH + o: c * SH + o + 512], hn2f[:]),
                 [b_hn2f], [b_hn2])
            emit(p, "tensor", lambda t, c=c: t.matmul(pst0[0:8, :], gwc[:, c * 8:(c + 1) * 8], hn2f[:, :],
                                                      start=(c == 0), stop=(c == 7)), [b_gwc, b_hn2f], [b_pst0])
        emit(p, "vector", lambda v: v.tensor_scalar_add(lgs[:, :], pst0[0:8, :], gbc[:, 0:1]), [b_pst0, b_gbc], [b_lgs])
        for cc in range(4):
            emit_transpose(p, pm[0:128, 0:8], lgs[:, cc * 128:(cc + 1) * 128], idf[0:8, 0:8], [b_lgs, b_idf], [b_pm])
            emit(p, "vector", lambda v: v.tensor_copy(lgt[:], pm[0:128, 0:8]), [b_pm], [b_lgt])
            emit(p, "vector", lambda v: v.reduce_max(maxc[:], lgt[:], axis=AX.X), [b_lgt], [b_maxc])
            emit(p, "vector", lambda v: v.tensor_scalar(selt[:], lgt[:], maxc[:, 0:1], None, ALU.is_ge),
                 [b_lgt, b_maxc], [b_selt])
            emit_transpose(p, pm[0:8, 8:136], selt[:, :], idf[:, :], [b_selt, b_idf], [b_pm])
            emit(p, "vector", lambda v, o=o, cc=cc: v.tensor_copy(selfm[:, o + cc * 128: o + (cc + 1) * 128], pm[0:8, 8:136]),
                 [b_pm], [b_selfm])

    for e in range(8):
        for t4 in range(4):
            emit(p, "tensor", lambda t, e=e, t4=t4: t.matmul(
                pm[:, :], esel[:, e * 128:(e + 1) * 128], selfm[:, t4 * 512:(t4 + 1) * 512], start=True, stop=True),
                [b_esel, b_selfm], [b_pm])
            emit(p, "vector", lambda v, t4=t4: v.tensor_copy(mskb[:, t4 * 512:(t4 + 1) * 512], pm[:, :]),
                 [b_pm], [b_mskb])
        for jb in range(4):
            for jc in range(8):
                j = jb * 8 + jc
                wst, b_wst = (wst0, b_wst0) if jc % 2 == 0 else (wst1, b_wst1)
                wcb, b_wcb = (wcb0, b_wcb0) if jc % 2 == 0 else (wcb1, b_wcb1)
                p.dma("sync", wst[:, :], w1_d[(e * 32 + j) * 128:(e * 32 + j + 1) * 128, :], writes=[b_wst])
                emit(p, "vector", lambda v, wst=wst, wcb=wcb: v.tensor_copy(wcb[:], wst[:]), [b_wst], [b_wcb])
                p.dma("sync", b1st[:], b1_d[:, j * 128:(j + 1) * 128], writes=[b_b1st])
                emit(p, "vector", lambda v, e=e: v.tensor_scalar_mul(b1s[:], b1st[:], eself[:, e * 128:e * 128 + 1]),
                     [b_b1st, b_eself], [b_b1s])
                for t4 in range(4):
                    pw, b_pw = (pw0, b_pw0) if t4 % 2 == 0 else (pw1, b_pw1)
                    def mmw1b(t, wcb=wcb, t4=t4, pw=pw, e=e, j=j):
                        last = None
                        for kc in range(8):
                            last = t.matmul(pw[:, :], wcb[:, kc * 128:(kc + 1) * 128],
                                            hn2[:, kc * SH + t4 * 512: kc * SH + t4 * 512 + 512],
                                            start=(kc == 0), stop=False)
                        last = t.matmul(pw[:, :], b1s[0:8, :], ones8b[0:8, :],
                                        start=False, stop=True)
                        return last
                    _ = None
                    p.op("tensor", mmw1b, reads=[b_wcb, b_hn2, b_b1s, b_ones8b], writes=[b_pw])
                    emit(p, "vector", lambda v, jc=jc, t4=t4, pw=pw: v.scalar_tensor_tensor(
                        midb[:, jc * SH + t4 * 512: jc * SH + t4 * 512 + 512], pw[:, :], 0.0,
                        mskb[:, t4 * 512:(t4 + 1) * 512], ALU.max, ALU.mult),
                        [b_pw, b_mskb], [b_midb])
            for oc in range(8):
                wst, b_wst = (wst0, b_wst0) if oc % 2 == 0 else (wst1, b_wst1)
                wcb, b_wcb = (wcb0, b_wcb0) if oc % 2 == 0 else (wcb1, b_wcb1)
                p.dma("sync", wst[:, :], w2_d[e * 1024 + oc * 128: e * 1024 + (oc + 1) * 128, jb * 1024:(jb + 1) * 1024],
                      writes=[b_wst])
                emit(p, "vector", lambda v, wst=wst, wcb=wcb: v.tensor_copy(wcb[:], wst[:]), [b_wst], [b_wcb])
                for t4 in range(4):
                    pw, b_pw = (pw0, b_pw0) if t4 % 2 == 0 else (pw1, b_pw1)
                    def mmw2(t, wcb=wcb, t4=t4, pw=pw):
                        last = None
                        for kc in range(8):
                            last = t.matmul(pw[:, :], wcb[:, kc * 128:(kc + 1) * 128],
                                            midb[:, kc * SH + t4 * 512: kc * SH + t4 * 512 + 512],
                                            start=(kc == 0), stop=(kc == 7))
                        return last
                    p.op("tensor", mmw2, reads=[b_wcb, b_midb], writes=[b_pw])
                    emit(p, "vector", lambda v, oc=oc, t4=t4, pw=pw: v.tensor_tensor(
                        outacc[:, oc * SH + t4 * 512: oc * SH + t4 * 512 + 512], pw[:, :],
                        outacc[:, oc * SH + t4 * 512: oc * SH + t4 * 512 + 512], ALU.add),
                        [b_pw, b_outacc], [b_outacc])

    for ti in range(4):
        o = ti * 512
        for oc in range(8):
            emit(p, "tensor", lambda t, oc=oc, o=o: t.matmul(
                pm[:, :], b2f[:, oc * 128:(oc + 1) * 128], selfm[:, o:o + 512], start=True, stop=True),
                [b_b2f, b_selfm], [b_pm])
            p.dma("sync", xts[:], x1_d[oc * 128:(oc + 1) * 128, o:o + 512], writes=[b_xts])
            p.dma("sync", tv[:], ar_d[oc * 128:(oc + 1) * 128, o:o + 512], reads=[b_ar], writes=[b_tv])
            emit(p, "vector", lambda v, oc=oc, o=o: v.tensor_add(sq[:], pm[:, :],
                 outacc[:, oc * SH + o: oc * SH + o + 512]), [b_pm, b_outacc], [b_sq])
            emit(p, "vector", lambda v: v.tensor_add(sq[:], sq[:], tv[:]), [b_sq, b_tv], [b_sq])
            emit(p, "vector", lambda v: v.tensor_add(sq[:], sq[:], xts[:]), [b_sq, b_xts], [b_sq])
            p.dma("gpsimd", x2_d[oc * 128:(oc + 1) * 128, o:o + 512], sq[:], reads=[b_sq], writes=[b_x2], out_dram=True)
    ctx.io = None
    if standalone:
        p.finalize()
    return ctx


def stage_B2(I, b1_res, x1_parts):
    kf = {}
    vf = {}
    for b in range(2):
        kf[b] = np.concatenate([np.asarray(b1_res[b * 4 + k]["k"]) for k in range(4)], axis=1)
        vf[b] = np.concatenate([np.asarray(b1_res[b * 4 + k]["vtm"]) for k in range(4)], axis=0)
    wo = stage_ktiles(I["L0_ffn_wo"])
    boc = stage_vec_cols(I["L0_ffn_bo"], 8)
    g2c = stage_vec_cols(I["L0_ffn_ln2_g"], 8)
    b2c = stage_vec_cols(I["L0_ffn_ln2_b"], 8)
    gatew = _f32c(I["L0_ffn_gate_w"])
    gatebc = _f32c(np.asarray(I["L0_ffn_gate_b"], np.float32).reshape(8, 1))
    w1m = stage_ktiles3(I["L0_ffn_w1"], kc=8)
    b1m = _f32c(I["L0_ffn_b1"])
    w2m = stage_ktiles3(I["L0_ffn_w2"], kc=32)
    b2m = _f32c(I["L0_ffn_b2"])
    erow = np.zeros((8, 4096), np.float32)
    esel = np.zeros((8, 1024), np.float32)
    for e in range(8):
        erow[e, e * 512:(e + 1) * 512] = 1.0
        esel[e, e * 128:(e + 1) * 128] = 1.0
    idf2 = np.eye(128, dtype=np.float32)
    in_maps = []
    for c, (b, k) in enumerate(CORES):
        in_maps.append(dict(
            qin=np.asarray(b1_res[c]["q"]), kf=kf[b], vf=vf[b],
            lnbin=np.asarray(b1_res[c]["lnb"]), x1in2=np.ascontiguousarray(x1_parts[c]),
            wo=wo, boc=boc, g2c=g2c, b2c=b2c, gatew=gatew, gatebc=gatebc,
            w1m=w1m, b1m=b1m, w2m=w2m, b2m=b2m, esel=esel, idf2=idf2))
    return in_maps


def build_C1(ctx=None, io=None):
    standalone = ctx is None
    if standalone:
        ctx = LaunchCtx("C1")
        p = ctx.start()
    else:
        p = ctx.prog
    ctx.io = io
    nc = ctx.nc

    x_d = ctx.din("x2in", [1024, SH])
    wq_d = ctx.din("wqT", [1024, 1024])
    wk_d = ctx.din("wkT", [1024, 1024])
    wv_d = ctx.din("wvT", [1024, 1024])
    q_d, b_q = ctx.dout("q", [1024, SH], BF16)
    k_d, b_k = ctx.dout("k", [1024, SH], BF16)
    v_d, b_v = ctx.dout("vtm", [SH, 1024], BF16)

    xt, b_xt = ctx.sbuf("xt", [128, 8 * 512], F32)
    lnb, b_lnb = ctx.sbuf("lnb", [128, 8 * 512], BF16)
    sq, b_sq = ctx.sbuf("sq", [128, 512], F32)
    tv, b_tv = ctx.sbuf("tv", [128, 512], F32)
    wqb, b_wqb = ctx.sbuf("wqb", [128, 8 * 1024], BF16)
    wkb, b_wkb = ctx.sbuf("wkb", [128, 8 * 1024], BF16)
    wvb, b_wvb = ctx.sbuf("wvb", [128, 8 * 1024], BF16)
    wst0, b_wst0 = ctx.sbuf("wst0", [128, 1024], F32)
    wst1, b_wst1 = ctx.sbuf("wst1", [128, 1024], F32)
    onesc, b_onesc = ctx.sbuf("onesc", [128, 1], F32)
    ones1, b_ones1 = ctx.sbuf("ones1", [1, 128], F32)
    srow, b_srow = ctx.sbuf("srow", [1, 4 * 512], F32)
    outs, b_outs = ctx.sbuf("outs", [128, 512], BF16)
    vts, b_vts = ctx.sbuf("vts", [128, 512], BF16)

    ps1, b_ps1 = ctx.psum("ps1", [1, 512])
    ps2, b_ps2 = ctx.psum("ps2", [1, 512])
    pbm, b_pbm = ctx.psum("pbm", [128, 512])
    pbr, b_pbr = ctx.psum("pbr", [128, 512])
    pp0, b_pp0 = ctx.psum("pp0", [128, 512])
    pp1, b_pp1 = ctx.psum("pp1", [128, 512])

    emit(p, "vector", lambda v: v.memset(onesc[:], 1.0), [], [b_onesc])
    emit(p, "vector", lambda v: v.memset(ones1[:], 1.0), [], [b_ones1])
    for (wsb, b_wsb, wd) in ((wqb, b_wqb, wq_d), (wkb, b_wkb, wk_d), (wvb, b_wvb, wv_d)):
        for oc in range(8):
            wst, b_wst = (wst0, b_wst0) if oc % 2 == 0 else (wst1, b_wst1)
            p.dma("sync", wst[:, :], wd[oc * 128:(oc + 1) * 128, :], writes=[b_wst])
            emit(p, "vector", lambda v, wsb=wsb, oc=oc, wst=wst: v.tensor_copy(
                wsb[:, oc * 1024:(oc + 1) * 1024], wst[:]), [b_wst], [b_wsb])

    cb5, b_cb5 = ctx.cbias(1e-5, 1)

    for ti in range(4):
        o = ti * 512
        for c in range(8):
            p.dma("sync", xt[:, c * 512:(c + 1) * 512], x_d[c * 128:(c + 1) * 128, o:o + 512], writes=[b_xt])
        def mm1(t):
            last = None
            for c in range(8):
                last = t.matmul(ps1[:, :], onesc[:, :], xt[:, c * 512:(c + 1) * 512], start=(c == 0), stop=(c == 7))
            return last
        p.op("tensor", mm1, reads=[b_onesc, b_xt], writes=[b_ps1])
        for c in range(8):
            emit(p, "scalar", lambda s, c=c: s.activation(out=sq[:], in_=xt[:, c * 512:(c + 1) * 512], func=AF.Square),
                 [b_xt], [b_sq])
            emit(p, "tensor", lambda t, c=c: t.matmul(ps2[:, :], onesc[:, :], sq[:, :], start=(c == 0), stop=(c == 7)),
                 [b_onesc, b_sq], [b_ps2])
        M, R, T1 = srow[0:1, 0:512], srow[0:1, 512:1024], srow[0:1, 1024:1536]
        emit(p, "vector", lambda v, M=M: v.tensor_scalar_mul(M, ps1[0:1, :], 1.0 / 1024.0), [b_ps1], [b_srow])
        emit(p, "vector", lambda v, T1=T1: v.tensor_scalar_mul(T1, ps2[0:1, :], 1.0 / 1024.0), [b_ps2], [b_srow])
        emit(p, "scalar", lambda s, M=M, R=R: s.activation(out=R, in_=M, func=AF.Square), [b_srow], [b_srow])
        emit(p, "vector", lambda v, T1=T1, R=R: v.tensor_sub(T1, T1, R), [b_srow], [b_srow])
        emit(p, "scalar", lambda s, T1=T1, cb5=cb5: s.activation(out=T1, in_=T1, func=AF.Sqrt, bias=cb5), [b_srow, b_cb5], [b_srow])
        emit(p, "vector", lambda v, T1=T1, R=R: v.reciprocal(R, T1), [b_srow], [b_srow])
        emit(p, "tensor", lambda t, M=M: t.matmul(pbm[:, :], ones1[0:1, :], M, start=True, stop=True),
             [b_ones1, b_srow], [b_pbm])
        emit(p, "tensor", lambda t, R=R: t.matmul(pbr[:, :], ones1[0:1, :], R, start=True, stop=True),
             [b_ones1, b_srow], [b_pbr])
        for c in range(8):
            sl = slice(c * 512, (c + 1) * 512)
            emit(p, "vector", lambda v, sl=sl: v.tensor_sub(tv[:], xt[:, sl], pbm[:, :]), [b_xt, b_pbm], [b_tv])
            emit(p, "vector", lambda v, sl=sl: v.tensor_tensor(lnb[:, sl], tv[:], pbr[:, :], ALU.mult),
                 [b_tv, b_pbr], [b_lnb])
        for (wsb, b_wsb, dram, b_dram) in ((wqb, b_wqb, q_d, b_q), (wkb, b_wkb, k_d, b_k)):
            for oc in range(8):
                pp, b_pp = (pp0, b_pp0) if oc % 2 == 0 else (pp1, b_pp1)
                def mmp(t, wsb=wsb, oc=oc, pp=pp):
                    last = None
                    for kc in range(8):
                        last = t.matmul(pp[:, :], wsb[:, oc * 1024 + kc * 128: oc * 1024 + (kc + 1) * 128],
                                        lnb[:, kc * 512:(kc + 1) * 512], start=(kc == 0), stop=(kc == 7))
                    return last
                p.op("tensor", mmp, reads=[b_wsb, b_lnb], writes=[b_pp])
                emit(p, "vector", lambda v, pp=pp: v.tensor_copy(outs[:], pp[:, :]), [b_pp], [b_outs])
                p.dma("gpsimd", dram[oc * 128:(oc + 1) * 128, o:o + 512], outs[:], reads=[b_outs], writes=[b_dram], out_dram=True)
        for sc in range(4):
            so = o + sc * 128
            for oc in range(8):
                pp, b_pp = (pp0, b_pp0) if oc % 2 == 0 else (pp1, b_pp1)
                def mmv(t, sc=sc, oc=oc, pp=pp):
                    last = None
                    for kc in range(8):
                        last = t.matmul(pp[:, 0:128], lnb[:, kc * 512 + sc * 128: kc * 512 + (sc + 1) * 128],
                                        wvb[:, oc * 1024 + kc * 128: oc * 1024 + (kc + 1) * 128],
                                        start=(kc == 0), stop=(kc == 7))
                    return last
                p.op("tensor", mmv, reads=[b_lnb, b_wvb], writes=[b_pp])
                emit(p, "vector", lambda v, pp=pp: v.tensor_copy(vts[:, 0:128], pp[:, 0:128]), [b_pp], [b_vts])
                p.dma("gpsimd", v_d[so:so + 128, oc * 128:(oc + 1) * 128], vts[:, 0:128], reads=[b_vts], writes=[b_v], out_dram=True)
    ctx.io = None
    if standalone:
        p.finalize()
    return ctx


def stage_C1(I, x2_parts):
    wqT = stage_ktiles(np.asarray(I["L1_attn_Wq"], np.float32).T)
    wkT = stage_ktiles(np.asarray(I["L1_attn_Wk"], np.float32).T)
    wvT = stage_ktiles(np.asarray(I["L1_attn_Wv"], np.float32).T)
    in_maps = []
    for c, (b, k) in enumerate(CORES):
        in_maps.append(dict(x2in=np.ascontiguousarray(x2_parts[c]), wqT=wqT, wkT=wkT, wvT=wvT))
    return in_maps


def build_C2(ctx=None, io=None):
    standalone = ctx is None
    if standalone:
        ctx = LaunchCtx("C2")
        p = ctx.start()
    else:
        p = ctx.prog
    ctx.io = io
    nc = ctx.nc
    S = 8192

    q_dd = ctx.din("qn", [256, S], BF16)
    qp_d = ctx.din("qp", [256, S], BF16)
    kf_d = ctx.din("kn", [256, S], BF16)
    kp_d = ctx.din("kp", [256, S], BF16)
    vf_d = ctx.din("vf", [S, 256], BF16)
    cosk_d = ctx.din("cosk", [128, S], BF16)
    sink_d = ctx.din("sink", [128, S], BF16)
    dmask_d = ctx.din("dmask", [128, 4 * 512])
    o_d, b_o = ctx.dout("oh", [256, S], BF16)

    kx, b_kx = ctx.sbuf("kx", [65, S], BF16)
    vx, b_vx = ctx.sbuf("vx", [128, 64 * 65], BF16)
    qx, b_qx = ctx.sbuf("qx", [65, S], BF16)
    rp, b_rp = ctx.sbuf("rp", [64, S], BF16)
    rp2, b_rp2 = ctx.sbuf("rp2", [64, S], BF16)
    cosk, b_cosk = ctx.sbuf("cosk_s", [128, S], BF16)
    sink, b_sink = ctx.sbuf("sink_s", [128, S], BF16)
    dmask, b_dmask = ctx.sbuf("dmask_s", [128, 4 * 512], F32)
    ptb, b_ptb = ctx.sbuf("ptb", [128, 2 * 512], BF16)
    sq, b_sq = ctx.sbuf("sq", [128, 512], F32)
    tvm, b_tvm = ctx.sbuf("tvm", [128, 512], F32)
    lbuf, b_lbuf = ctx.sbuf("lbuf", [128, 512], F32)
    bcs, b_bcs = ctx.sbuf("bcs", [64, 512], F32)
    osb, b_osb = ctx.sbuf("osb", [64, 512], BF16)
    onesc, b_onesc = ctx.sbuf("onesc", [128, 1], F32)
    ones1, b_ones1 = ctx.sbuf("ones1", [1, 128], F32)
    ones64, b_ones64 = ctx.sbuf("ones64", [128, 64], F32)
    srow, b_srow = ctx.sbuf("srow", [1, 4 * 512], F32)

    pst0, b_pst0 = ctx.psum("pst0", [128, 512])
    pst1, b_pst1 = ctx.psum("pst1", [128, 512])
    po, b_po = ctx.psum("po", [65, 512])
    pbc, b_pbc = ctx.psum("pbc", [64, 512])
    ps1, b_ps1 = ctx.psum("ps1", [1, 512])

    emit(p, "vector", lambda v: v.memset(onesc[:], 1.0), [], [b_onesc])
    emit(p, "vector", lambda v: v.memset(ones1[:], 1.0), [], [b_ones1])
    emit(p, "vector", lambda v: v.memset(ones64[:], 1.0), [], [b_ones64])
    emit(p, "vector", lambda v: v.memset(kx[64:65, :], 1.0), [], [b_kx])
    emit(p, "vector", lambda v: v.memset(vx[:, 64::65], 1.0), [], [b_vx])
    p.dma("sync", cosk[:], cosk_d[:, :], writes=[b_cosk])
    p.dma("sync", sink[:], sink_d[:, :], writes=[b_sink])
    p.dma("sync", dmask[:], dmask_d[:, :], writes=[b_dmask])
    cb0125, b_cb0125 = ctx.cbias(0.125, 128)

    for h in range(4):
        hh = 0
        p.dma("sync", rp[:, :], kf_d[64 * h:64 * h + 64, :], writes=[b_rp])
        p.dma("sync", rp2[:, :], kp_d[64 * h:64 * h + 64, :], writes=[b_rp2])
        emit(p, "vector", lambda v: v.tensor_tensor(kx[0:64, :], rp[:, :], cosk[0:64, :], ALU.mult),
             [b_rp, b_cosk], [b_kx])
        emit(p, "vector", lambda v: v.tensor_tensor(rp2[:, :], rp2[:, :], sink[0:64, :], ALU.mult),
             [b_rp2, b_sink], [b_rp2])
        emit(p, "vector", lambda v: v.tensor_tensor(kx[0:64, :], kx[0:64, :], rp2[:, :], ALU.add),
             [b_kx, b_rp2], [b_kx])
        p.dma("sync", rp[:, :], q_dd[64 * h:64 * h + 64, :], writes=[b_rp])
        p.dma("sync", rp2[:, :], qp_d[64 * h:64 * h + 64, :], writes=[b_rp2])
        emit(p, "vector", lambda v: v.tensor_tensor(qx[0:64, :], rp[:, :], cosk[0:64, :], ALU.mult),
             [b_rp, b_cosk], [b_qx])
        emit(p, "vector", lambda v: v.tensor_tensor(rp2[:, :], rp2[:, :], sink[0:64, :], ALU.mult),
             [b_rp2, b_sink], [b_rp2])
        emit(p, "vector", lambda v: v.tensor_tensor(qx[0:64, :], qx[0:64, :], rp2[:, :], ALU.add),
             [b_qx, b_rp2], [b_qx])
        p.dma("sync", vx[:, :].rearrange("p (s c) -> p s c", c=65)[:, :, 0:64],
              vf_d[:, 64 * h:64 * h + 64].rearrange("(s p) c -> p s c", p=128), writes=[b_vx])
        emit(p, "vector", lambda v: v.memset(srow[0:1, 0:1], 0.0), [], [b_srow])
        for sc in range(16):
            emit(p, "scalar", lambda s, sc=sc: s.activation(out=sq[0:64, 0:512], in_=kx[0:64, sc * 512:(sc + 1) * 512], func=AF.Square),
                 [b_kx], [b_sq])
            emit(p, "tensor", lambda t: t.matmul(ps1[:, :], onesc[0:64, :], sq[0:64, 0:512], start=True, stop=True),
                 [b_onesc, b_sq], [b_ps1])
            emit(p, "vector", lambda v: v.reduce_max(srow[0:1, 2:3], ps1[0:1, :], axis=AX.X), [b_ps1], [b_srow])
            emit(p, "vector", lambda v: v.tensor_tensor(srow[0:1, 0:1], srow[0:1, 0:1], srow[0:1, 2:3], ALU.max),
                 [b_srow], [b_srow])
        emit(p, "scalar", lambda s: s.activation(out=srow[0:1, 0:1], in_=srow[0:1, 0:1], func=AF.Sqrt), [b_srow], [b_srow])
        QN = srow[0:1, 512:1024]
        for qt in range(16):
            emit(p, "scalar", lambda s, qt=qt: s.activation(out=sq[0:64, 0:512], in_=qx[0:64, qt * 512:(qt + 1) * 512], func=AF.Square),
                 [b_qx], [b_sq])
            emit(p, "tensor", lambda t: t.matmul(ps1[:, :], onesc[0:64, :], sq[0:64, 0:512], start=True, stop=True),
                 [b_onesc, b_sq], [b_ps1])
            emit(p, "scalar", lambda s, QN=QN: s.activation(out=QN, in_=ps1[0:1, :], func=AF.Sqrt), [b_ps1], [b_srow])
            emit(p, "vector", lambda v, QN=QN: v.tensor_single_scalar(QN, QN, srow[0:1, 0:1], ALU.mult), [b_srow], [b_srow])
            emit(p, "vector", lambda v, QN=QN: v.tensor_scalar_mul(QN, QN, -1.0), [b_srow], [b_srow])
            emit(p, "tensor", lambda t, QN=QN: t.matmul(po[64:65, :], ones1[0:1, 0:1], QN, start=True, stop=True),
                 [b_ones1, b_srow], [b_po])
            emit(p, "vector", lambda v, qt=qt: v.tensor_copy(qx[64:65, qt * 512:(qt + 1) * 512], po[64:65, :]),
                 [b_po], [b_qx])
            q0 = qt * 512
            nsc = (q0 + 512) // 128
            for sc in range(nsc):
                pst, b_pst = (pst0, b_pst0) if sc % 2 == 0 else (pst1, b_pst1)
                emit(p, "tensor", lambda t, sc=sc, qt=qt, pst=pst: t.matmul(
                    pst[:, :], kx[:, sc * 128:(sc + 1) * 128], qx[:, qt * 512:(qt + 1) * 512],
                    start=True, stop=True), [b_kx, b_qx], [b_pst])
                pt_sl = slice((sc % 2) * 512, (sc % 2) * 512 + 512)
                d = sc - q0 // 128
                if d >= 0:
                    emit(p, "vector", lambda v, pst=pst, d=d: v.tensor_tensor(
                        tvm[:], pst[:, :], dmask[:, d * 512:(d + 1) * 512], ALU.add), [b_pst, b_dmask], [b_tvm])
                    emit(p, "scalar", lambda s, pt_sl=pt_sl, cb0125=cb0125: s.activation(
                        out=ptb[:, pt_sl], in_=tvm[:], func=AF.Exp, scale=cb0125),
                        [b_tvm, b_cb0125], [b_ptb])
                else:
                    emit(p, "scalar", lambda s, pst=pst, pt_sl=pt_sl, cb0125=cb0125: s.activation(
                        out=ptb[:, pt_sl], in_=pst[:, :], func=AF.Exp, scale=cb0125),
                        [b_pst, b_cb0125], [b_ptb])
                emit(p, "tensor", lambda t, sc=sc, pt_sl=pt_sl, nsc=nsc: t.matmul(
                    po[:, :], vx[:, sc * 65:(sc + 1) * 65], ptb[:, pt_sl],
                    start=(sc == 0), stop=(sc == nsc - 1)), [b_vx, b_ptb], [b_po])
            emit(p, "vector", lambda v: v.reciprocal(lbuf[64:65, 0:512], po[64:65, :]), [b_po], [b_lbuf])
            emit(p, "tensor", lambda t: t.matmul(pbc[:, :], ones64[64:65, 0:64], lbuf[64:65, 0:512], start=True, stop=True),
                 [b_ones64, b_lbuf], [b_pbc])
            emit(p, "vector", lambda v: v.tensor_copy(bcs[:], pbc[:, :]), [b_pbc], [b_bcs])
            emit(p, "vector", lambda v: v.tensor_tensor(osb[:], po[0:64, :], bcs[:], ALU.mult), [b_po, b_bcs], [b_osb])
            p.dma("gpsimd", o_d[64 * h:64 * h + 64, qt * 512:(qt + 1) * 512], osb[:], reads=[b_osb], writes=[b_o], out_dram=True)
    ctx.io = None
    if standalone:
        p.finalize()
    return ctx


def build_C3(ctx=None, io=None):
    standalone = ctx is None
    if standalone:
        ctx = LaunchCtx("C3")
        p = ctx.start()
    else:
        p = ctx.prog
    ctx.io = io
    nc = ctx.nc
    oin_d = ctx.din("oin", [1024, SH], BF16)
    x2_d = ctx.din("x2in3", [1024, SH])
    wo_d = ctx.din("woT", [1024, 1024])
    x3_d, b_x3 = ctx.dout("x3", [1024, SH])

    ob, b_ob = ctx.sbuf("ob", [128, 8 * 512], BF16)
    wob, b_wob = ctx.sbuf("wob", [128, 8 * 1024], BF16)
    wst0, b_wst0 = ctx.sbuf("wst0", [128, 1024], F32)
    wst1, b_wst1 = ctx.sbuf("wst1", [128, 1024], F32)
    xts, b_xts = ctx.sbuf("xts", [128, 512], F32)
    x3s, b_x3s = ctx.sbuf("x3s", [128, 512], F32)
    pw0, b_pw0 = ctx.psum("pw0", [128, 512])
    pw1, b_pw1 = ctx.psum("pw1", [128, 512])

    for oc in range(8):
        wst, b_wst = (wst0, b_wst0) if oc % 2 == 0 else (wst1, b_wst1)
        p.dma("sync", wst[:, :], wo_d[oc * 128:(oc + 1) * 128, :], writes=[b_wst])
        emit(p, "vector", lambda v, oc=oc, wst=wst: v.tensor_copy(wob[:, oc * 1024:(oc + 1) * 1024], wst[:]),
             [b_wst], [b_wob])
    for ti in range(4):
        o = ti * 512
        for c in range(8):
            p.dma("sync", ob[:, c * 512:(c + 1) * 512], oin_d[c * 128:(c + 1) * 128, o:o + 512], writes=[b_ob])
        for oc in range(8):
            pw, b_pw = (pw0, b_pw0) if oc % 2 == 0 else (pw1, b_pw1)
            def mmwo(t, oc=oc, pw=pw):
                last = None
                for kc in range(8):
                    last = t.matmul(pw[:, :], wob[:, oc * 1024 + kc * 128: oc * 1024 + (kc + 1) * 128],
                                    ob[:, kc * 512:(kc + 1) * 512], start=(kc == 0), stop=(kc == 7))
                return last
            p.op("tensor", mmwo, reads=[b_wob, b_ob], writes=[b_pw])
            p.dma("sync", xts[:], x2_d[oc * 128:(oc + 1) * 128, o:o + 512], writes=[b_xts])
            emit(p, "vector", lambda v, pw=pw: v.tensor_add(x3s[:], pw[:, :], xts[:]), [b_pw, b_xts], [b_x3s])
            p.dma("gpsimd", x3_d[oc * 128:(oc + 1) * 128, o:o + 512], x3s[:], reads=[b_x3s], writes=[b_x3], out_dram=True)
    ctx.io = None
    if standalone:
        p.finalize()
    return ctx


def rope_tables(S):
    DC = 64
    pos = np.arange(S, dtype=np.float32)
    inv = 10000.0 ** (-np.arange(0, DC, 2, dtype=np.float32) / DC)
    ang = pos[:, None] * inv[None, :]
    cos = np.cos(ang)
    sin = np.sin(ang)
    C = np.zeros((128, S), np.float32)
    Sg = np.zeros((128, S), np.float32)
    for r in range(128):
        d = r % 64
        i = d // 2
        C[r] = cos[:, i]
        Sg[r] = -sin[:, i] if d % 2 == 0 else sin[:, i]
    return C, Sg


def swap_pairs_rows(a):
    b = a.reshape(-1, 2, a.shape[1])
    return np.ascontiguousarray(b[:, ::-1, :].reshape(a.shape))


def stage_C2(I, c1_res):
    S = 8192
    kf = {}
    vf = {}
    qf = {}
    for b in range(2):
        kf[b] = np.concatenate([np.asarray(c1_res[b * 4 + k]["k"]) for k in range(4)], axis=1)
        vf[b] = np.concatenate([np.asarray(c1_res[b * 4 + k]["vtm"]) for k in range(4)], axis=0)
        qf[b] = np.concatenate([np.asarray(c1_res[b * 4 + k]["q"]) for k in range(4)], axis=1)
    C, Sg = rope_tables(S)
    Cb = np.ascontiguousarray(C[0:128].astype(BF16NP))
    Sb = np.ascontiguousarray(Sg[0:128].astype(BF16NP))
    sqr = np.arange(128)
    dmask = np.zeros((128, 4 * 512), np.float32)
    for d in range(4):
        qcols = np.arange(512)
        m = (sqr[:, None] + 128 * d) <= qcols[None, :]
        dmask[:, d * 512:(d + 1) * 512] = np.where(m, 0.0, NEG)
    in_maps = []
    for c, (b, k) in enumerate(CORES):
        rows = slice(256 * k, 256 * k + 256)
        in_maps.append(dict(
            qn=np.ascontiguousarray(qf[b][rows]), qp=swap_pairs_rows(np.ascontiguousarray(qf[b][rows])),
            kn=np.ascontiguousarray(kf[b][rows]), kp=swap_pairs_rows(np.ascontiguousarray(kf[b][rows])),
            vf=np.ascontiguousarray(vf[b][:, rows]),
            cosk=Cb, sink=Sb, dmask=dmask))
    return in_maps


def stage_C3(I, c2_res, x2_parts):
    woT = stage_ktiles(np.asarray(I["L1_attn_Wo"], np.float32).T)
    ofull = {}
    for b in range(2):
        ofull[b] = np.concatenate([np.asarray(c2_res[b * 4 + k]["oh"]) for k in range(4)], axis=0)
    in_maps = []
    for c, (b, k) in enumerate(CORES):
        sl = slice(2048 * k, 2048 * k + 2048)
        in_maps.append(dict(oin=np.ascontiguousarray(ofull[b][:, sl]),
                            x2in3=np.ascontiguousarray(x2_parts[c]), woT=woT))
    return in_maps


def build_D(ctx=None, io=None):
    standalone = ctx is None
    if standalone:
        ctx = LaunchCtx("D")
        p = ctx.start()
    else:
        p = ctx.prog
    ctx.io = io
    nc = ctx.nc

    x_d = ctx.din("x3in", [1024, SH])
    wr_d = ctx.din("wrT", [1024, 16])
    gu_d = ctx.din("gup", [16 * 4096, 1024])
    dn_d = ctx.din("dnp", [16 * 1024, 2048])
    wg_d = ctx.din("wgT2", [2048, 1024])
    wu_d = ctx.din("wuT2", [2048, 1024])
    wd_d = ctx.din("wdT2", [1024, 2048])
    esel_d = ctx.din("esel16", [16, 2048])
    idf_d = ctx.din("idf3", [128, 128])
    out_d, b_out = ctx.dout("out", [1024, SH])

    xt, b_xt = ctx.sbuf("xt", [128, 8 * 512], F32)
    lnd, b_lnd = ctx.sbuf("lnd", [128, 8 * SH], BF16)
    lnf, b_lnf = ctx.sbuf("lnf", [128, 512], F32)
    sq, b_sq = ctx.sbuf("sq", [128, 512], F32)
    outacc, b_outacc = ctx.sbuf("outacc", [128, 8 * SH], BF16)
    midb, b_midb = ctx.sbuf("midb", [128, 8 * SH], BF16)
    xin, b_xin = ctx.sbuf("xin", [128, 8 * 512], BF16)
    gsil, b_gsil = ctx.sbuf("gsil", [128, 512], BF16)
    scfm, b_scfm = ctx.sbuf("scfm", [16, SH], F32)
    lg, b_lg = ctx.sbuf("lg", [16, 512], F32)
    sg, b_sg = ctx.sbuf("sg", [16, 512], F32)
    lgt, b_lgt = ctx.sbuf("lgt", [128, 16], F32)
    selt, b_selt = ctx.sbuf("selt", [128, 16], F32)
    maxc, b_maxc = ctx.sbuf("maxc", [128, 1], F32)
    msk, b_msk = ctx.sbuf("msk", [128, 512], BF16)
    wrb, b_wrb = ctx.sbuf("wrb", [128, 8 * 16], F32)
    esel, b_esel = ctx.sbuf("esel_s", [16, 2048], F32)
    idf, b_idf = ctx.sbuf("idf_s", [128, 128], F32)
    wst0, b_wst0 = ctx.sbuf("wst0", [128, 1024], F32)
    wst1, b_wst1 = ctx.sbuf("wst1", [128, 1024], F32)
    wcb0, b_wcb0 = ctx.sbuf("wcb0", [128, 1024], BF16)
    wcb1, b_wcb1 = ctx.sbuf("wcb1", [128, 1024], BF16)
    wcb2, b_wcb2 = ctx.sbuf("wcb2", [128, 1024], BF16)
    wcb3, b_wcb3 = ctx.sbuf("wcb3", [128, 1024], BF16)
    onesc, b_onesc = ctx.sbuf("onesc", [128, 1], F32)
    ones1, b_ones1 = ctx.sbuf("ones1", [1, 128], F32)
    srow, b_srow = ctx.sbuf("srow", [1, 4 * 512], F32)
    os_, b_os = ctx.sbuf("os_", [128, 512], F32)

    ps1, b_ps1 = ctx.psum("ps1", [1, 512])
    ps2, b_ps2 = ctx.psum("ps2", [1, 512])
    pbm, b_pbm = ctx.psum("pbm", [128, 512])
    pbr, b_pbr = ctx.psum("pbr", [128, 512])
    pp0, b_pp0 = ctx.psum("pp0", [128, 512])
    pp1, b_pp1 = ctx.psum("pp1", [128, 512])
    pmk, b_pmk = ctx.psum("pmk", [128, 512])

    emit(p, "vector", lambda v: v.memset(onesc[:], 1.0), [], [b_onesc])
    emit(p, "vector", lambda v: v.memset(ones1[:], 1.0), [], [b_ones1])
    emit(p, "vector", lambda v: v.memset(outacc[:], 0.0), [], [b_outacc])
    p.dma("sync", esel[:], esel_d[:, :], writes=[b_esel])
    p.dma("sync", idf[:], idf_d[:, :], writes=[b_idf])
    for kc in range(8):
        p.dma("sync", wrb[:, kc * 16:(kc + 1) * 16], wr_d[kc * 128:(kc + 1) * 128, :], writes=[b_wrb])
    cb5, b_cb5 = ctx.cbias(1e-5, 1)

    for ti in range(4):
        o = ti * 512
        for c in range(8):
            p.dma("sync", xt[:, c * 512:(c + 1) * 512], x_d[c * 128:(c + 1) * 128, o:o + 512], writes=[b_xt])
        def mm1(t):
            last = None
            for c in range(8):
                last = t.matmul(ps1[:, :], onesc[:, :], xt[:, c * 512:(c + 1) * 512], start=(c == 0), stop=(c == 7))
            return last
        p.op("tensor", mm1, reads=[b_onesc, b_xt], writes=[b_ps1])
        for c in range(8):
            emit(p, "scalar", lambda s, c=c: s.activation(out=sq[:], in_=xt[:, c * 512:(c + 1) * 512], func=AF.Square),
                 [b_xt], [b_sq])
            emit(p, "tensor", lambda t, c=c: t.matmul(ps2[:, :], onesc[:, :], sq[:, :], start=(c == 0), stop=(c == 7)),
                 [b_onesc, b_sq], [b_ps2])
        M, R, T1 = srow[0:1, 0:512], srow[0:1, 512:1024], srow[0:1, 1024:1536]
        emit(p, "vector", lambda v, M=M: v.tensor_scalar_mul(M, ps1[0:1, :], 1.0 / 1024.0), [b_ps1], [b_srow])
        emit(p, "vector", lambda v, T1=T1: v.tensor_scalar_mul(T1, ps2[0:1, :], 1.0 / 1024.0), [b_ps2], [b_srow])
        emit(p, "scalar", lambda s, M=M, R=R: s.activation(out=R, in_=M, func=AF.Square), [b_srow], [b_srow])
        emit(p, "vector", lambda v, T1=T1, R=R: v.tensor_sub(T1, T1, R), [b_srow], [b_srow])
        emit(p, "scalar", lambda s, T1=T1, cb5=cb5: s.activation(out=T1, in_=T1, func=AF.Sqrt, bias=cb5), [b_srow, b_cb5], [b_srow])
        emit(p, "vector", lambda v, T1=T1, R=R: v.reciprocal(R, T1), [b_srow], [b_srow])
        emit(p, "tensor", lambda t, M=M: t.matmul(pbm[:, :], ones1[0:1, :], M, start=True, stop=True),
             [b_ones1, b_srow], [b_pbm])
        emit(p, "tensor", lambda t, R=R: t.matmul(pbr[:, :], ones1[0:1, :], R, start=True, stop=True),
             [b_ones1, b_srow], [b_pbr])
        for c in range(8):
            sl = slice(c * 512, (c + 1) * 512)
            emit(p, "vector", lambda v, sl=sl: v.tensor_sub(lnf[:], xt[:, sl], pbm[:, :]), [b_xt, b_pbm], [b_lnf])
            emit(p, "vector", lambda v: v.tensor_mul(lnf[:], lnf[:], pbr[:, :]), [b_lnf, b_pbr], [b_lnf])
            emit(p, "vector", lambda v, c=c, o=o: v.tensor_copy(lnd[:, c * SH + o: c * SH + o + 512], lnf[:]),
                 [b_lnf], [b_lnd])
            emit(p, "tensor", lambda t, c=c: t.matmul(pp0[0:16, :], wrb[:, c * 16:(c + 1) * 16], lnf[:, :],
                                                      start=(c == 0), stop=(c == 7)), [b_wrb, b_lnf], [b_pp0])
        emit(p, "vector", lambda v: v.tensor_copy(lg[:], pp0[0:16, :]), [b_pp0], [b_lg])
        emit(p, "scalar", lambda s: s.activation(out=sg[:], in_=lg[:], func=AF.Sigmoid), [b_lg], [b_sg])
        for cc in range(4):
            emit_transpose(p, pmk[0:128, 0:16], lg[:, cc * 128:(cc + 1) * 128], idf[0:16, 0:16], [b_lg, b_idf], [b_pmk])
            emit(p, "vector", lambda v: v.tensor_copy(lgt[:], pmk[0:128, 0:16]), [b_pmk], [b_lgt])
            emit(p, "vector", lambda v: v.reduce_max(maxc[:], lgt[:], axis=AX.X), [b_lgt], [b_maxc])
            emit(p, "vector", lambda v: v.tensor_scalar(selt[:], lgt[:], maxc[:, 0:1], None, ALU.is_ge),
                 [b_lgt, b_maxc], [b_selt])
            emit_transpose(p, pmk[0:16, 16:144], selt[:, :], idf[:, :], [b_selt, b_idf], [b_pmk])
            emit(p, "vector", lambda v, o=o, cc=cc: v.tensor_copy(scfm[:, o + cc * 128: o + (cc + 1) * 128], pmk[0:16, 16:144]),
                 [b_pmk], [b_scfm])
        emit(p, "vector", lambda v, o=o: v.tensor_mul(scfm[:, o:o + 512], scfm[:, o:o + 512], sg[:, :]),
             [b_scfm, b_sg], [b_scfm])

    for e in range(17):
        shared = (e == 16)
        for t4 in range(4):
            if not shared:
                emit(p, "tensor", lambda t, e=e, t4=t4: t.matmul(
                    pmk[:, :], esel[:, e * 128:(e + 1) * 128], scfm[:, t4 * 512:(t4 + 1) * 512], start=True, stop=True),
                    [b_esel, b_scfm], [b_pmk])
                emit(p, "vector", lambda v, t4=t4: v.tensor_copy(msk[:], pmk[:, :]), [b_pmk], [b_msk])
                for c in range(8):
                    emit(p, "vector", lambda v, c=c, t4=t4: v.tensor_tensor(
                        xin[:, c * 512:(c + 1) * 512], lnd[:, c * SH + t4 * 512: c * SH + t4 * 512 + 512],
                        msk[:], ALU.mult), [b_lnd, b_msk], [b_xin])
            else:
                for c in range(8):
                    emit(p, "vector", lambda v, c=c, t4=t4: v.tensor_copy(
                        xin[:, c * 512:(c + 1) * 512], lnd[:, c * SH + t4 * 512: c * SH + t4 * 512 + 512]),
                        [b_lnd], [b_xin])
            for jb in range(2):
                for jc in range(8):
                    jg = jb * 8 + jc
                    ju = 16 + jg
                    for (wcb, b_wcb, wst, b_wst, col) in ((wcb0, b_wcb0, wst0, b_wst0, jg), (wcb1, b_wcb1, wst1, b_wst1, ju)):
                        if shared:
                            wsrc = wg_d if col < 16 else wu_d
                            ccol = col % 16
                            p.dma("sync", wst[:, :], wsrc[ccol * 128:(ccol + 1) * 128, :], writes=[b_wst])
                        else:
                            p.dma("sync", wst[:, :], gu_d[(e * 32 + col) * 128:(e * 32 + col + 1) * 128, :],
                                  writes=[b_wst])
                        emit(p, "vector", lambda v, wcb=wcb, wst=wst: v.tensor_copy(wcb[:], wst[:]), [b_wst], [b_wcb])
                    def mmg(t, wcb=wcb0, t4=t4):
                        last = None
                        for kc in range(8):
                            last = t.matmul(pp0[:, :], wcb[:, kc * 128:(kc + 1) * 128],
                                            xin[:, kc * 512:(kc + 1) * 512], start=(kc == 0), stop=(kc == 7))
                        return last
                    p.op("tensor", mmg, reads=[b_wcb0, b_xin], writes=[b_pp0])
                    def mmu(t, wcb=wcb1, t4=t4):
                        last = None
                        for kc in range(8):
                            last = t.matmul(pp1[:, :], wcb[:, kc * 128:(kc + 1) * 128],
                                            xin[:, kc * 512:(kc + 1) * 512], start=(kc == 0), stop=(kc == 7))
                        return last
                    p.op("tensor", mmu, reads=[b_wcb1, b_xin], writes=[b_pp1])
                    emit(p, "scalar", lambda s: s.activation(out=gsil[:], in_=pp0[:, :], func=AF.Silu), [b_pp0], [b_gsil])
                    emit(p, "vector", lambda v, jc=jc, t4=t4: v.tensor_tensor(
                        midb[:, jc * SH + t4 * 512: jc * SH + t4 * 512 + 512], pp1[:, :], gsil[:], ALU.mult),
                        [b_pp1, b_gsil], [b_midb])
                for oc in range(8):
                    wst, b_wst = (wst0, b_wst0) if oc % 2 == 0 else (wst1, b_wst1)
                    wcb, b_wcb = (wcb2, b_wcb2) if oc % 2 == 0 else (wcb3, b_wcb3)
                    if shared:
                        p.dma("sync", wst[:, :], wd_d[oc * 128:(oc + 1) * 128, jb * 1024:(jb + 1) * 1024],
                              writes=[b_wst])
                    else:
                        p.dma("sync", wst[:, :], dn_d[e * 1024 + oc * 128: e * 1024 + (oc + 1) * 128, jb * 1024:(jb + 1) * 1024],
                              writes=[b_wst])
                    emit(p, "vector", lambda v, wcb=wcb, wst=wst: v.tensor_copy(wcb[:], wst[:]), [b_wst], [b_wcb])
                    pw, b_pw = (pp0, b_pp0) if oc % 2 == 0 else (pp1, b_pp1)
                    def mmd(t, wcb=wcb, t4=t4, pw=pw):
                        last = None
                        for kc in range(8):
                            last = t.matmul(pw[:, :], wcb[:, kc * 128:(kc + 1) * 128],
                                            midb[:, kc * SH + t4 * 512: kc * SH + t4 * 512 + 512],
                                            start=(kc == 0), stop=(kc == 7))
                        return last
                    p.op("tensor", mmd, reads=[b_wcb, b_midb], writes=[b_pw])
                    emit(p, "vector", lambda v, oc=oc, t4=t4, pw=pw: v.tensor_tensor(
                        outacc[:, oc * SH + t4 * 512: oc * SH + t4 * 512 + 512], pw[:, :],
                        outacc[:, oc * SH + t4 * 512: oc * SH + t4 * 512 + 512], ALU.add),
                        [b_pw, b_outacc], [b_outacc])

    for ti in range(4):
        o = ti * 512
        for oc in range(8):
            p.dma("sync", xt[:, 0:512], x_d[oc * 128:(oc + 1) * 128, o:o + 512], writes=[b_xt])
            emit(p, "vector", lambda v, oc=oc, o=o: v.tensor_add(os_[:], xt[:, 0:512],
                 outacc[:, oc * SH + o: oc * SH + o + 512]), [b_xt, b_outacc], [b_os])
            p.dma("gpsimd", out_d[oc * 128:(oc + 1) * 128, o:o + 512], os_[:], reads=[b_os], writes=[b_out], out_dram=True)
    ctx.io = None
    if standalone:
        p.finalize()
    return ctx


def stage_D(I, x3_parts):
    wrT = _f32c(np.asarray(I["L1_ffn_w_router"], np.float32).T)
    gup = stage_ktiles3(I["L1_ffn_gate_up_proj"], kc=8)
    dnp = stage_ktiles3(I["L1_ffn_down_proj"], kc=16)
    wgT2 = stage_ktiles(np.asarray(I["L1_ffn_w_gate"], np.float32).T)
    wuT2 = stage_ktiles(np.asarray(I["L1_ffn_w_up"], np.float32).T)
    wdT2 = stage_ktiles(np.asarray(I["L1_ffn_w_down"], np.float32).T, kc=16)
    esel16 = np.zeros((16, 2048), np.float32)
    for e in range(16):
        esel16[e, e * 128:(e + 1) * 128] = 1.0
    idf3 = np.eye(128, dtype=np.float32)
    in_maps = []
    for c, (b, k) in enumerate(CORES):
        in_maps.append(dict(x3in=np.ascontiguousarray(x3_parts[c]), wrT=wrT, gup=gup, dnp=dnp,
                            wgT2=wgT2, wuT2=wuT2, wdT2=wdT2, esel16=esel16, idf3=idf3))
    return in_maps


_CTX_CACHE = {}


def _get_ctx(name, builder):
    if name not in _CTX_CACHE:
        _CTX_CACHE[name] = builder()
    return _CTX_CACHE[name]


def kernel(**I):
    return kernel_mega(**I)


def kernel_multilaunch(**I):
    ctx = _get_ctx("A1", build_A1)
    a1 = run_launch(ctx, stage_A1(I))
    a1_res = [dict(xbc=np.asarray(r["xbc"]), dt=np.asarray(r["dt"]),
                   z=np.asarray(r["z"]), astat=np.asarray(r["astat"])) for r in a1]
    ctx = _get_ctx("A2", build_A2)
    a2 = run_launch(ctx, stage_A2(I, a1_res))
    a2_res = [dict(y=np.asarray(r["y"])) for r in a2]
    ctx = _get_ctx("A3", build_A3)
    a3 = run_launch(ctx, stage_A3(I, a1_res, a2_res))
    x1_parts = [np.asarray(r["x1"]) for r in a3]
    ctx = _get_ctx("B1", build_B1)
    b1 = run_launch(ctx, stage_B1(I, x1_parts))
    b1_res = [dict(q=np.asarray(r["q"]), k=np.asarray(r["k"]),
                   vtm=np.asarray(r["vtm"]), lnb=np.asarray(r["lnb"])) for r in b1]
    ctx = _get_ctx("B2", build_B2)
    b2 = run_launch(ctx, stage_B2(I, b1_res, x1_parts))
    x2_parts = [np.asarray(r["x2"]) for r in b2]
    ctx = _get_ctx("C1", build_C1)
    c1 = run_launch(ctx, stage_C1(I, x2_parts))
    c1_res = [dict(q=np.asarray(r["q"]), k=np.asarray(r["k"]), vtm=np.asarray(r["vtm"])) for r in c1]
    ctx = _get_ctx("C2", build_C2)
    c2 = run_launch(ctx, stage_C2(I, c1_res))
    c2_res = [dict(oh=np.asarray(r["oh"])) for r in c2]
    ctx = _get_ctx("C3", build_C3)
    c3 = run_launch(ctx, stage_C3(I, c2_res, x2_parts))
    x3_parts = [np.asarray(r["x3"]) for r in c3]
    ctx = _get_ctx("D", build_D)
    d = run_launch(ctx, stage_D(I, x3_parts))
    full = np.zeros((2, 8192, 1024), np.float32)
    for c, (b, k) in enumerate(CORES):
        full[b, 2048 * k:2048 * k + 2048] = np.asarray(d[c]["out"]).T
    return full


class MegaCtx(LaunchCtx):
    NP = 52800

    def __init__(self, name):
        super().__init__(name)
        self.mega = False

    def init_pools(self):
        p = self.prog
        self.cbt = self.es.enter_context(self.nc.sbuf_tensor("CBIAS", [128, 8], F32))
        self.poolf = self.es.enter_context(self.nc.sbuf_tensor("POOLF", [128, self.NP], F32))
        self.poolb_full = self.poolf[:, :].bitcast(BF16)
        self.psf = [self.es.enter_context(self.nc.psum_tensor(f"PSF{i}", [128, 512], F32)) for i in range(7)]
        self.psb = self.es.enter_context(self.nc.psum_tensor("PSB", [128, 1024], BF16))
        self.mega = True
        self.new_phase(first=True)

    def new_phase(self, first=False):
        if not first:
            self.prog.barrier()
            for b in self.prog.dma_bufs:
                self.prog.sem_pool.append((b.dma_sem, b.dma_base + b.dma_writes))
            self.prog.dma_bufs = []
        self.off = 0
        self.ps_i = 0
        self.ps_b_used = False

    def cbias(self, val, parts=128):
        if val not in self._cb:
            i = len(self._cb)
            t = self.cbt[:, i:i + 1]
            b = self.prog.buf(f"cb_{i}")
            self.prog.op("vector", lambda v, t=t, val=val: v.memset(t, float(val)),
                         reads=[], writes=[b])
            self._cb[val] = (t, b)
        t, b = self._cb[val]
        return t[0:parts, 0:1], b

    def sbuf(self, name, shape, dtype):
        if not self.mega:
            return super().sbuf(name, shape, dtype)
        parts, cols = shape
        esz = 4 if dtype == F32 else 2
        nbytes = ceil_div(cols * esz, 512) * 512
        assert self.off + nbytes <= self.NP * 4, f"pool overflow at {name}: {self.off}+{nbytes}"
        if dtype == F32:
            t = self.poolf[0:parts, self.off // 4: self.off // 4 + cols]
        else:
            t = self.poolb_full[0:parts, self.off // 2: self.off // 2 + cols]
        self.off += nbytes
        return t, self.prog.buf(name)

    def psum(self, name, shape, dtype=F32):
        if not self.mega:
            return super().psum(name, shape, dtype)
        parts, cols = shape
        if dtype == F32:
            assert self.ps_i < 7, f"psum pool overflow at {name}"
            t = self.psf[self.ps_i][0:parts, 0:cols]
            self.ps_i += 1
        else:
            assert not self.ps_b_used, f"bf16 psum overflow at {name}"
            t = self.psb[0:parts, 0:cols]
            self.ps_b_used = True
        return t, self.prog.buf(name)


def _prog_barrier(self):
    for e in ENGINES:
        for f in ENGINES:
            if f != e and self.counts[f] > 0:
                self._wait(e, self.sems[f], self.counts[f])
        for b in self.dma_bufs:
            self._wait(e, b.dma_sem, 16 * (b.dma_base + b.dma_writes))


Prog.barrier = _prog_barrier


from concourse.aot_env import allow_ici_all_to_all

GRP = [[0, 1, 2, 3], [4, 5, 6, 7]]


def build_mega():
    ctx = MegaCtx("MEGA")
    p = ctx.start()
    nc = ctx.nc
    ctx.init_pools()
    S = 8192
    I32 = mybir.dt.int32

    def scr(name, shape, dtype=F32):
        return nc.dram_tensor(name, shape, dtype).ap()

    xin_ap = nc.dram_tensor("xin", [1024, 2176], F32, kind="ExternalInput").ap()
    ix_xs = nc.dram_tensor("ix_xs", [128, 16], I32, kind="ExternalInput").ap()
    ix_dt = nc.dram_tensor("ix_dt", [8, 4], I32, kind="ExternalInput").ap()
    ix_y = nc.dram_tensor("ix_y", [128, 16], I32, kind="ExternalInput").ap()
    ix_qk = nc.dram_tensor("ix_qk", [128, 8], I32, kind="ExternalInput").ap()
    ix_v = nc.dram_tensor("ix_v", [128, 64], I32, kind="ExternalInput").ap()
    ix_o = nc.dram_tensor("ix_o", [128, 8], I32, kind="ExternalInput").ap()
    ix_bc = nc.dram_tensor("ix_bc", [128, 8], I32, kind="ExternalInput").ap()
    ix_kf = nc.dram_tensor("ix_kf", [128, 32], I32, kind="ExternalInput").ap()
    ix_sw = nc.dram_tensor("ix_sw", [128, 2], I32, kind="ExternalInput").ap()

    g_xbc = scr("g_xbc", [2304, SH], BF16)
    g_dt = scr("g_dt", [32, SH])
    g_z = scr("g_z", [2048, SH])
    g_ast = scr("g_ast", [2, SH])
    g_xbcg = scr("g_xbcg", [4 * 2304, SH], BF16)
    g_dtg = scr("g_dtg", [128, SH])
    g_xs = scr("g_xs", [512, S], BF16)
    g_dth = scr("g_dth", [8, S])
    g_Bf = scr("g_Bf", [128, S], BF16)
    g_Cf = scr("g_Cf", [128, S], BF16)
    g_y = scr("g_y", [512, S], BF16)
    g_yg = scr("g_yg", [2048, S], BF16)
    g_yin = scr("g_yin", [2048, SH], BF16)
    g_x1 = scr("g_x1", [1024, SH])
    g_lnb = scr("g_lnb", [1024, SH])
    g_q = scr("g_q", [1024, SH], BF16)
    g_k = scr("g_k", [1024, SH], BF16)
    g_v = scr("g_v", [SH, 1024], BF16)
    g_kg = scr("g_kg", [4096, SH], BF16)
    g_kf = scr("g_kf", [1024, S], BF16)
    g_vg = scr("g_vg", [S, 1024], BF16)
    g_x2 = scr("g_x2", [1024, SH])
    g_qg = scr("g_qg", [4096, SH], BF16)
    g_kg2 = scr("g_kg2", [4096, SH], BF16)
    g_qn = scr("g_qn", [256, S], BF16)
    g_qp = scr("g_qp", [256, S], BF16)
    g_kn = scr("g_kn", [256, S], BF16)
    g_kp = scr("g_kp", [256, S], BF16)
    g_vf2 = scr("g_vf2", [S, 256], BF16)
    g_oh = scr("g_oh", [256, S], BF16)
    g_og = scr("g_og", [1024, S], BF16)
    g_oin = scr("g_oin", [1024, SH], BF16)
    g_x3 = scr("g_x3", [1024, SH])

    BREG = {}

    def gb(name):
        if name not in BREG:
            BREG[name] = p.buf(name)
        return BREG[name]

    def cc(kind, in_ap, out_ap, tag, src, dst):
        def fn(g_eng, in_ap=in_ap, out_ap=out_ap, kind=kind):
            with allow_ici_all_to_all():
                return g_eng.collective_compute(
                    kind, ALU.bypass, replica_groups=GRP,
                    ins=[in_ap.opt()], outs=[out_ap.opt()])
        return p.op("gpsimd", fn, reads=[gb(src)], writes=[gb(dst)], tag=tag)

    def dcp(dst_ap, src_ap, tag, src, dst):
        return p.dma("sync", dst_ap, src_ap, reads=[gb(src)], writes=[gb(dst)], tag=tag)

    ixs = {}
    for nm, ap_, shp in (("ix_xs", ix_xs, [128, 16]), ("ix_dt", ix_dt, [8, 4]),
                         ("ix_y", ix_y, [128, 16]), ("ix_qk", ix_qk, [128, 8]),
                         ("ix_v", ix_v, [128, 64]), ("ix_o", ix_o, [128, 8]),
                         ("ix_bc", ix_bc, [128, 8]), ("ix_kf", ix_kf, [128, 32]),
                         ("ix_sw", ix_sw, [128, 2])):
        t = ctx.es.enter_context(nc.sbuf_tensor("s" + nm, shp, I32))
        b = p.buf("s" + nm)
        p.dma("sync", t[:, :], ap_[:, :], writes=[b])
        ixs[nm] = (t, b)

    STG = {"t": None}

    def stg_tiles():
        sb, bsb = ctx.sbuf(f"stg_b{ctx.uniq()}", [128, 8192], BF16)
        sf, bsf = ctx.sbuf(f"stg_f{ctx.uniq()}", [128, 2048], F32)
        STG["t"] = (sb, bsb, sf, bsf)

    def _raw_idma(out_ap, src_view, t, col, nparts, tag, reads, wbuf):
        def fn(g_eng, out_ap=out_ap, src_view=src_view, t=t, col=col, nparts=nparts):
            return g_eng.indirect_dma_start(
                out_ap, None, src_view,
                bass.IndirectOffsetOnAxis(t[0:nparts, col:col + 1], 0))
        wb = wbuf
        if wb.dma_sem is None:
            if p.sem_pool:
                wb.dma_sem, wb.dma_base = p.sem_pool.pop()
            else:
                wb.dma_sem = p.new_sem(f"D{p._nsem}_{wb.name}")
                wb.dma_base = 0
            p.dma_bufs.append(wb)
        p._pre("gpsimd", reads, [wb], ())
        wb.dma_writes += 1
        val = 16 * (wb.dma_base + wb.dma_writes)
        p.items["gpsimd"].append(("emit", fn, [(wb.dma_sem, 16)], tag))
        op = Op("gpsimd", wb.dma_sem, val, "dma")
        p._post(op, reads, [wb])
        return op

    def igather(out_ap, src_view, ixname, col, nparts, tag, srcb, dstb, f32=False):
        t, bix = ixs[ixname]
        sb, bsb, sf, bsf = STG["t"]
        cols = out_ap.shape[-1]
        stg, bstg = (sf, bsf) if f32 else (sb, bsb)
        _raw_idma(stg[0:nparts, 0:cols], src_view, t, col, nparts, tag + "_g",
                  [gb(srcb), bix], bstg)
        p.dma("sync", out_ap, stg[0:nparts, 0:cols], reads=[bstg], writes=[gb(dstb)], tag=tag + "_s")

    build_A1(ctx=ctx, io={"xin": xin_ap, "xbc": g_xbc, "dt": g_dt, "z": g_z, "astat": g_ast})
    ctx.new_phase()
    stg_tiles()
    cc("AllGather", g_xbc[:, :], g_xbcg[:, :], "cc_xbc", "g_xbc", "g_xbcg")
    cc("AllGather", g_dt[:, :], g_dtg[:, :], "cc_dt", "g_dt", "g_dtg")
    for gg in range(4):
        for blk in range(4):
            igather(g_xs[blk * 128:(blk + 1) * 128, gg * 2048:(gg + 1) * 2048],
                    g_xbcg[:, :], "ix_xs", gg * 4 + blk, 128, f"ig_xs{gg}_{blk}", "g_xbcg", "g_xs")
        igather(g_dth[0:8, gg * 2048:(gg + 1) * 2048], g_dtg[:, :], "ix_dt", gg, 8,
                f"ig_dt{gg}", "g_dtg", "g_dth", f32=True)
        igather(g_Bf[:, gg * 2048:(gg + 1) * 2048], g_xbcg[:, :], "ix_bc", gg * 2, 128,
                f"m_B{gg}", "g_xbcg", "g_Bf")
        igather(g_Cf[:, gg * 2048:(gg + 1) * 2048], g_xbcg[:, :], "ix_bc", gg * 2 + 1, 128,
                f"m_C{gg}", "g_xbcg", "g_Cf")
    ctx.new_phase()
    build_A2(ctx=ctx, io={"xs": g_xs, "Bf": g_Bf, "Cf": g_Cf, "dth": g_dth, "y": g_y})
    ctx.new_phase()
    stg_tiles()
    cc("AllGather", g_y[:, :], g_yg[:, :], "cc_y", "g_y", "g_yg")
    yv = g_yg.rearrange("r (q c) -> (r q) c", c=2048)
    for blk in range(16):
        igather(g_yin[blk * 128:(blk + 1) * 128, :], yv, "ix_y", blk, 128,
                f"ig_y{blk}", "g_yg", "g_yin")
    ctx.new_phase()
    build_A3(ctx=ctx, io={"xin3": xin_ap[:, 128:2176], "astat3": g_ast, "zin": g_z,
                          "yin": g_yin, "x1": g_x1})
    ctx.new_phase()
    build_B1(ctx=ctx, io={"x1in": g_x1, "lnb": g_lnb, "q": g_q, "k": g_k, "vtm": g_v})
    ctx.new_phase()
    stg_tiles()
    cc("AllGather", g_k[:, :], g_kg[:, :], "cc_k", "g_k", "g_kg")
    cc("AllGather", g_v[:, :], g_vg[:, :], "cc_v", "g_v", "g_vg")
    for gg in range(4):
        for blk in range(8):
            igather(g_kf[blk * 128:(blk + 1) * 128, gg * 2048:(gg + 1) * 2048], g_kg[:, :],
                    "ix_kf", gg * 8 + blk, 128, f"m_k{gg}_{blk}", "g_kg", "g_kf")
    ctx.new_phase()
    build_B2(ctx=ctx, io={"qin": g_q, "kf": g_kf, "vf": g_vg, "lnbin": g_lnb,
                          "x1in2": g_x1, "x2": g_x2})
    ctx.new_phase()
    build_C1(ctx=ctx, io={"x2in": g_x2, "q": g_q, "k": g_k, "vtm": g_v})
    ctx.new_phase()
    stg_tiles()
    cc("AllGather", g_q[:, :], g_qg[:, :], "cc_q2", "g_q", "g_qg")
    cc("AllGather", g_k[:, :], g_kg2[:, :], "cc_k2", "g_k", "g_kg2")
    cc("AllGather", g_v[:, :], g_vg[:, :], "cc_v2", "g_v", "g_vg")
    for gg in range(4):
        for blk in range(2):
            igather(g_qn[blk * 128:(blk + 1) * 128, gg * 2048:(gg + 1) * 2048], g_qg[:, :],
                    "ix_qk", gg * 2 + blk, 128, f"ig_q{gg}_{blk}", "g_qg", "g_qn")
            igather(g_kn[blk * 128:(blk + 1) * 128, gg * 2048:(gg + 1) * 2048], g_kg2[:, :],
                    "ix_qk", gg * 2 + blk, 128, f"ig_k{gg}_{blk}", "g_kg2", "g_kn")
    vv = g_vg.rearrange("r (q c) -> (r q) c", c=256)
    for blk in range(64):
        igather(g_vf2[blk * 128:(blk + 1) * 128, :], vv, "ix_v", blk, 128,
                f"ig_v{blk}", "g_vg", "g_vf2")
    ctx.new_phase()
    stg_tiles()
    for blk in range(2):
        igather(g_qp[blk * 128:(blk + 1) * 128, :], g_qn[:, :], "ix_sw", blk, 128,
                f"sw_q{blk}", "g_qn", "g_qp")
        igather(g_kp[blk * 128:(blk + 1) * 128, :], g_kn[:, :], "ix_sw", blk, 128,
                f"sw_k{blk}", "g_kn", "g_kp")
    ctx.new_phase()
    build_C2(ctx=ctx, io={"qn": g_qn, "qp": g_qp, "kn": g_kn, "kp": g_kp, "vf": g_vf2,
                          "oh": g_oh})
    ctx.new_phase()
    stg_tiles()
    cc("AllGather", g_oh[:, :], g_og[:, :], "cc_o", "g_oh", "g_og")
    ov = g_og.rearrange("r (q c) -> (r q) c", c=2048)
    for blk in range(8):
        igather(g_oin[blk * 128:(blk + 1) * 128, :], ov, "ix_o", blk, 128,
                f"ig_o{blk}", "g_og", "g_oin")
    ctx.new_phase()
    build_C3(ctx=ctx, io={"oin": g_oin, "x2in3": g_x2, "x3": g_x3})
    ctx.new_phase()
    build_D(ctx=ctx, io={"x3in": g_x3})
    p.finalize()
    return ctx


def stage_ix(k):
    pq = np.arange(128)
    ix_xs = np.zeros((128, 16), np.int32)
    for gg in range(4):
        for blk in range(4):
            ix_xs[:, gg * 4 + blk] = gg * 2304 + 512 * k + blk * 128 + pq
    ix_dt = np.zeros((8, 4), np.int32)
    for gg in range(4):
        ix_dt[:, gg] = gg * 32 + 8 * k + np.arange(8)
    ix_y = np.zeros((128, 16), np.int32)
    for blk in range(16):
        ix_y[:, blk] = (blk * 128 + pq) * 4 + k
    ix_qk = np.zeros((128, 8), np.int32)
    for gg in range(4):
        for blk in range(2):
            ix_qk[:, gg * 2 + blk] = gg * 1024 + 256 * k + blk * 128 + pq
    ix_v = np.zeros((128, 64), np.int32)
    for blk in range(64):
        ix_v[:, blk] = (blk * 128 + pq) * 4 + k
    ix_o = np.zeros((128, 8), np.int32)
    for blk in range(8):
        ix_o[:, blk] = (blk * 128 + pq) * 4 + k
    ix_bc = np.zeros((128, 8), np.int32)
    for gg in range(4):
        ix_bc[:, gg * 2] = gg * 2304 + 2048 + pq
        ix_bc[:, gg * 2 + 1] = gg * 2304 + 2176 + pq
    ix_kf = np.zeros((128, 32), np.int32)
    for gg in range(4):
        for blk in range(8):
            ix_kf[:, gg * 8 + blk] = gg * 1024 + blk * 128 + pq
    ix_sw = np.zeros((128, 2), np.int32)
    for blk in range(2):
        ix_sw[:, blk] = blk * 128 + (pq ^ 1)
    return dict(ix_xs=ix_xs, ix_dt=ix_dt, ix_y=ix_y, ix_qk=ix_qk, ix_v=ix_v, ix_o=ix_o,
                ix_bc=ix_bc, ix_kf=ix_kf, ix_sw=ix_sw)


def stage_mega(I):
    x = I["L0_attn_hidden_states"]
    xfm = np.ascontiguousarray(x.transpose(0, 2, 1), dtype=np.float32)
    w_in_T = np.zeros((1024, 4480), np.float32)
    w_in_T[:, 0:4384] = np.asarray(I["L0_attn_w_in"], np.float32).T
    w_in_T = _f32c(stage_ktiles(np.concatenate([w_in_T], axis=0), kc=8)) if False else stage_ktiles(w_in_T)
    wln1c = stage_vec_cols(I["L0_attn_w_ln1"], 8)
    dtbc = _f32c(np.asarray(I["L0_attn_dt_bias"], np.float32).reshape(32, 1))
    bconvc = stage_vec_cols(I["L0_attn_b_conv"], 18)
    wc = np.asarray(I["L0_attn_w_conv"], np.float32)
    wconvc = _f32c(wc.reshape(18, 128, 4).transpose(1, 0, 2).reshape(128, 72))
    idf = np.eye(128, dtype=np.float32)
    idb = idf.astype(BF16NP)
    sq = np.arange(128)
    maskT = np.where(sq[:, None] <= sq[None, :], 0.0, NEG).astype(np.float32)
    selpair = np.zeros((8, 128), np.float32)
    for h in range(8):
        selpair[h, (h // 2) * 32:(h // 2) * 32 + 32] = 1.0
    sel8 = np.zeros((8, 128), np.float32)
    for h in range(8):
        sel8[h, (h % 2) * 64:(h % 2) * 64 + 64] = 1.0
    alog_full = np.asarray(I["L0_attn_A_log"], np.float32)
    d_full = np.asarray(I["L0_attn_D"], np.float32)
    wmnc = stage_vec_cols(I["L0_attn_w_mnorm"], 16)
    wout_T = stage_ktiles(np.asarray(I["L0_attn_w_out"], np.float32).T, kc=16)
    wln2c = stage_vec_cols(I["L0_attn_w_ln2"], 8)
    wg_T = stage_ktiles(np.asarray(I["L0_attn_w_gate"], np.float32).T)
    wu_T = stage_ktiles(np.asarray(I["L0_attn_w_up"], np.float32).T)
    wd_T = stage_ktiles(np.asarray(I["L0_attn_w_down"], np.float32).T, kc=32)
    wq = stage_ktiles(I["L0_ffn_wq"]); wk = stage_ktiles(I["L0_ffn_wk"]); wv = stage_ktiles(I["L0_ffn_wv"])
    bqc = stage_vec_cols(I["L0_ffn_bq"], 8); bkc = stage_vec_cols(I["L0_ffn_bk"], 8)
    bvr = _f32c(np.asarray(I["L0_ffn_bv"], np.float32).reshape(1, 1024))
    g1c = stage_vec_cols(I["L0_ffn_ln1_g"], 8); b1c = stage_vec_cols(I["L0_ffn_ln1_b"], 8)
    wo = stage_ktiles(I["L0_ffn_wo"]); boc = stage_vec_cols(I["L0_ffn_bo"], 8)
    g2c = stage_vec_cols(I["L0_ffn_ln2_g"], 8); b2c = stage_vec_cols(I["L0_ffn_ln2_b"], 8)
    gatew = _f32c(I["L0_ffn_gate_w"])
    gatebc = _f32c(np.asarray(I["L0_ffn_gate_b"], np.float32).reshape(8, 1))
    w1m = stage_ktiles3(I["L0_ffn_w1"], kc=8)
    b1m = _f32c(I["L0_ffn_b1"])
    w2m = stage_ktiles3(I["L0_ffn_w2"], kc=32)
    b2m = _f32c(I["L0_ffn_b2"])
    erow = np.zeros((8, 4096), np.float32)
    esel = np.zeros((8, 1024), np.float32)
    for e in range(8):
        erow[e, e * 512:(e + 1) * 512] = 1.0
        esel[e, e * 128:(e + 1) * 128] = 1.0
    wqT = stage_ktiles(np.asarray(I["L1_attn_Wq"], np.float32).T)
    wkT = stage_ktiles(np.asarray(I["L1_attn_Wk"], np.float32).T)
    wvT = stage_ktiles(np.asarray(I["L1_attn_Wv"], np.float32).T)
    woT = stage_ktiles(np.asarray(I["L1_attn_Wo"], np.float32).T)
    C, Sg = rope_tables(8192)
    Cb = np.ascontiguousarray(C.astype(BF16NP))
    Sb = np.ascontiguousarray(Sg.astype(BF16NP))
    dmask = np.zeros((128, 4 * 512), np.float32)
    for d in range(4):
        qcols = np.arange(512)
        m = (sq[:, None] + 128 * d) <= qcols[None, :]
        dmask[:, d * 512:(d + 1) * 512] = np.where(m, 0.0, NEG)
    wrT = _f32c(np.asarray(I["L1_ffn_w_router"], np.float32).T)
    gup = stage_ktiles3(I["L1_ffn_gate_up_proj"], kc=8)
    dnp = stage_ktiles3(I["L1_ffn_down_proj"], kc=16)
    wgT2 = stage_ktiles(np.asarray(I["L1_ffn_w_gate"], np.float32).T)
    wuT2 = stage_ktiles(np.asarray(I["L1_ffn_w_up"], np.float32).T)
    wdT2 = stage_ktiles(np.asarray(I["L1_ffn_w_down"], np.float32).T, kc=16)
    esel16 = np.zeros((16, 2048), np.float32)
    for e in range(16):
        esel16[e, e * 128:(e + 1) * 128] = 1.0
    in_maps = []
    for c, (b, k) in enumerate(CORES):
        lo = 2048 * k - 128
        if lo < 0:
            xin = np.concatenate([np.zeros((1024, 128), np.float32), xfm[b][:, 0:2048]], axis=1)
        else:
            xin = np.ascontiguousarray(xfm[b][:, lo:lo + 2176])
        in_maps.append(dict(
            xin=xin, w_in_T=w_in_T, wln1c=wln1c, dtbc=dtbc, bconvc=bconvc, wconvc=wconvc,
            alog=np.ascontiguousarray(alog_full[8 * k:8 * k + 8].reshape(8, 1)),
            dd=np.ascontiguousarray(d_full[8 * k:8 * k + 8].reshape(1, 8)),
            idf=idf, idb=idb, maskT=maskT, selpair=selpair, sel8=sel8,
            wmnc=wmnc, wout_T=wout_T, wln2c=wln2c, wg_T=wg_T, wu_T=wu_T, wd_T=wd_T,
            wq=wq, wk=wk, wv=wv, bqc=bqc, bkc=bkc, bvr=bvr, g1c=g1c, b1c=b1c,
            wo=wo, boc=boc, g2c=g2c, b2c=b2c, gatew=gatew, gatebc=gatebc,
            w1m=w1m, b1m=b1m, w2m=w2m, b2m=b2m, esel=esel, idf2=idf,
            wqT=wqT, wkT=wkT, wvT=wvT, woT=woT, cosk=Cb, sink=Sb, dmask=dmask,
            wrT=wrT, gup=gup, dnp=dnp, wgT2=wgT2, wuT2=wuT2, wdT2=wdT2,
            esel16=esel16, idf3=idf, **stage_ix(k),
        ))
    return in_maps


def kernel_mega(**I):
    ctx = _get_ctx("MEGA", build_mega)
    res = run_launch(ctx, stage_mega(I))
    full = np.zeros((2, 8192, 1024), np.float32)
    for c, (b, k) in enumerate(CORES):
        full[b, 2048 * k:2048 * k + 2048] = np.asarray(res[c]["out"]).T
    return full


# version 08
# speedup vs baseline: 1.0120x; 1.0120x over previous
import contextlib
import math

import numpy as np
import ml_dtypes

import concourse.bass as bass
import concourse.mybir as mybir
from concourse.bass_utils import run_bass_kernel_spmd

F32 = mybir.dt.float32
BF16 = mybir.dt.bfloat16
AF = mybir.ActivationFunctionType
ALU = mybir.AluOpType
AX = mybir.AxisListType
BF16NP = ml_dtypes.bfloat16

NEG = -3.0e38


class Op:
    __slots__ = ("engine", "sem", "val", "kind")

    def __init__(self, engine, sem, val, kind):
        self.engine = engine
        self.sem = sem
        self.val = val
        self.kind = kind


class Buf:
    __slots__ = ("name", "last_write", "readers", "dma_sem", "dma_writes", "dma_base")

    def __init__(self, name):
        self.name = name
        self.last_write = None
        self.readers = []
        self.dma_sem = None
        self.dma_writes = 0
        self.dma_base = 0


ENGINES = ("sync", "tensor", "vector", "scalar", "gpsimd")


class Prog:
    def __init__(self, nc, es):
        self.nc = nc
        self.es = es
        self.sems = {e: es.enter_context(nc.semaphore(f"S_{e}")) for e in ENGINES}
        self.counts = {e: 0 for e in ENGINES}
        self.items = {e: [] for e in ENGINES}
        self.clock = {e: {} for e in ENGINES}
        self.out_bufs = []
        self.dma_bufs = []
        self.sem_pool = []
        self._nsem = len(ENGINES)

    def new_sem(self, name):
        self._nsem += 1
        assert self._nsem < 200, "semaphore budget exceeded"
        return self.es.enter_context(self.nc.semaphore(name))

    def buf(self, name):
        return Buf(name)

    def _wait(self, engine, sem, val):
        ck = self.clock[engine]
        k = id(sem)
        if ck.get(k, -1) >= val:
            return
        ck[k] = val
        self.items[engine].append(("wait", sem, val))

    def _dep(self, engine, dep):
        if dep is None:
            return
        if dep.engine == engine and engine == "tensor" and dep.kind == "compute":
            return
        self._wait(engine, dep.sem, dep.val)

    def _pre(self, engine, reads, writes, extra):
        deps = []
        for b in reads:
            deps.append(b.last_write)
        for b in writes:
            deps.extend(b.readers)
            deps.append(b.last_write)
        deps.extend(extra)
        for d in deps:
            self._dep(engine, d)

    def _post(self, op, reads, writes):
        for b in reads:
            b.readers.append(op)
        for b in writes:
            b.last_write = op
            b.readers = []

    def op(self, engine, fn, reads=(), writes=(), extra=(), tag=""):
        self._pre(engine, reads, writes, extra)
        sem = self.sems[engine]
        self.counts[engine] += 1
        val = self.counts[engine]
        self.items[engine].append(("emit", fn, [(sem, 1)], tag))
        op = Op(engine, sem, val, "compute")
        self._post(op, reads, writes)
        return op

    def dma(self, engine, out_ap, in_ap, reads=(), writes=(), extra=(), tag="", out_dram=False):
        assert len(writes) == 1
        wb = writes[0]
        if wb.dma_sem is None:
            if self.sem_pool:
                wb.dma_sem, wb.dma_base = self.sem_pool.pop()
            else:
                wb.dma_sem = self.new_sem(f"D{self._nsem}_{wb.name}")
                wb.dma_base = 0
            self.dma_bufs.append(wb)
        self._pre(engine, reads, writes, extra)
        wb.dma_writes += 1
        val = 16 * (wb.dma_base + wb.dma_writes)
        sem = wb.dma_sem

        def fn(eng, out_ap=out_ap, in_ap=in_ap):
            return eng.dma_start(out=out_ap, in_=in_ap)

        self.items[engine].append(("emit", fn, [(sem, 16)], tag))
        op = Op(engine, sem, val, "dma")
        self._post(op, reads, writes)
        if out_dram and wb not in self.out_bufs:
            self.out_bufs.append(wb)
        return op

    def finalize(self):
        for e in ("sync", "gpsimd"):
            for b in self.out_bufs:
                if b.dma_sem is not None:
                    self._wait(e, b.dma_sem, 16 * (b.dma_base + b.dma_writes))
        with self.nc.Block() as block:
            for e in ENGINES:
                if not self.items[e]:
                    continue
                items = self.items[e]

                def mk(items=items):
                    def section(eng):
                        for it in items:
                            if it[0] == "wait":
                                eng.wait_ge(it[1], it[2])
                            else:
                                ins = it[1](eng)
                                for (sem, k) in it[2]:
                                    ins.then_inc(sem, k)
                    return section
                getattr(block, e)(mk())


def ceil_div(a, b):
    return (a + b - 1) // b


def tiles(total, step):
    out = []
    o = 0
    while o < total:
        out.append((o, min(step, total - o)))
        o += step
    return out


class LaunchCtx:

    def __init__(self, name):
        self.nc = bass.Bass("TRN2", target_bir_lowering=False, debug=False)
        self.es = contextlib.ExitStack()
        self.prog = None
        self.name = name
        self._uniq = 0
        self._cb = {}

    def cbias(self, val, parts=128):
        if val not in self._cb:
            t, b = self.sbuf(f"cb{len(self._cb)}", [128, 1], F32)
            self.prog.op("vector", lambda v, t=t, val=val: v.memset(t[:], float(val)),
                         reads=[], writes=[b])
            self._cb[val] = (t, b)
        t, b = self._cb[val]
        return t[0:parts, 0:1], b

    def start(self):
        self.prog = Prog(self.nc, self.es)
        return self.prog

    def uniq(self):
        self._uniq += 1
        return self._uniq

    def sbuf(self, name, shape, dtype):
        t = self.es.enter_context(self.nc.sbuf_tensor(name, shape, dtype))
        return t, self.prog.buf(name)

    def psum(self, name, shape, dtype=F32):
        t = self.es.enter_context(self.nc.psum_tensor(name, shape, dtype))
        return t, self.prog.buf(name)

    io = None

    def din(self, name, shape, dtype=F32):
        if self.io is not None and name in self.io:
            return self.io[name]
        return self.nc.dram_tensor(name, shape, dtype, kind="ExternalInput").ap()

    def dout(self, name, shape, dtype=F32):
        if self.io is not None and name in self.io:
            return self.io[name], self.prog.buf(name + f"_p{self.uniq()}")
        ap = self.nc.dram_tensor(name, shape, dtype, kind="ExternalOutput").ap()
        return ap, self.prog.buf(name)

    def close(self):
        self.es.close()


def run_launch(ctx, in_maps, n_cores=8):
    res = run_bass_kernel_spmd(ctx.nc, in_maps, core_ids=list(range(n_cores)))
    return res.results


def emit_mm_acc(prog, psum_ap, pairs, reads, writes, dt_note=""):
    def fn(t):
        last = None
        n = len(pairs)
        for i, (l, r) in enumerate(pairs):
            last = t.matmul(psum_ap, l, r, start=(i == 0), stop=(i == n - 1))
        return last
    return prog.op("tensor", fn, reads=reads, writes=writes, tag="mm" + dt_note)


def emit_transpose(prog, psum_ap, in_ap, ident_ap, reads, writes):
    def fn(t):
        return t.transpose(psum_ap, in_ap, ident_ap)
    return prog.op("tensor", fn, reads=reads, writes=writes, tag="tr")


def emit(prog, engine, fn, reads, writes, tag=""):
    return prog.op(engine, fn, reads=reads, writes=writes, tag=tag)


def _f32c(a):
    return np.ascontiguousarray(a, dtype=np.float32)


def stage_vec_cols(v, nchunks=None):
    v = np.asarray(v, np.float32)
    d = v.shape[0]
    if nchunks is None:
        nchunks = ceil_div(d, 128)
    pad = nchunks * 128 - d
    if pad:
        v = np.concatenate([v, np.zeros(pad, np.float32)])
    return _f32c(v.reshape(nchunks, 128).T)




def stage_ktiles(wT, kc=8):
    wT = np.asarray(wT, np.float32)
    K, N = wT.shape
    assert K == kc * 128 and N % 128 == 0
    a = wT.reshape(kc, 128, N // 128, 128).transpose(2, 1, 0, 3)
    return _f32c(a.reshape(N, kc * 128))


def stage_ktiles3(w, kc=8):
    w = np.asarray(w, np.float32)
    E, K, N = w.shape
    return _f32c(np.stack([stage_ktiles(w[e], kc) for e in range(E)]).reshape(E * N, kc * 128))


CORES = [(c // 4, c % 4) for c in range(8)]
SH = 2048


def build_A1(n_tiles=5, dbg_skip=(), ctx=None, io=None):
    standalone = ctx is None
    if standalone:
        ctx = LaunchCtx("A1")
        p = ctx.start()
    else:
        p = ctx.prog
    ctx.io = io
    nc = ctx.nc

    TIN = 2176
    xin = ctx.din("xin", [1024, TIN])
    w_in_T = ctx.din("w_in_T", [4480, 1024])
    wln1c = ctx.din("wln1c", [128, 8])
    dtbc = ctx.din("dtbc", [32, 1])
    bconvc = ctx.din("bconvc", [128, 18])
    wconvc = ctx.din("wconvc", [128, 18 * 4])
    d_xbc, b_xbc = ctx.dout("xbc", [2304, SH], BF16)
    d_dt, b_dt = ctx.dout("dt", [32, SH])
    d_z, b_z = ctx.dout("z", [2048, SH])
    d_astat, b_astat = ctx.dout("astat", [2, SH])

    xt, b_xt = ctx.sbuf("xt", [128, 8 * 512], F32)
    hf, b_hf = ctx.sbuf("hf", [128, 8 * 512], F32)
    hb, b_hb = ctx.sbuf("hb", [128, 8 * 512], BF16)
    wst0, b_wst0 = ctx.sbuf("wst0", [128, 1024], F32)
    wst1, b_wst1 = ctx.sbuf("wst1", [128, 1024], F32)
    wbuf, b_wbuf = ctx.sbuf("wbuf", [128, 35 * 1024], BF16)
    wdtf, b_wdtf = ctx.sbuf("wdtf", [128, 8 * 32], F32)
    wln1, b_wln1 = ctx.sbuf("wln1", [128, 8], F32)
    dtb, b_dtb = ctx.sbuf("dtb", [32, 1], F32)
    bconv, b_bconv = ctx.sbuf("bconv", [128, 18], F32)
    wconv, b_wconv = ctx.sbuf("wconv", [128, 72], F32)
    onesc, b_onesc = ctx.sbuf("onesc", [128, 1], F32)
    ones1, b_ones1 = ctx.sbuf("ones1", [1, 128], F32)
    xraw, b_xraw = ctx.sbuf("xraw", [128, 515], F32)
    carry, b_carry = ctx.sbuf("carry", [128, 64], F32)
    convt, b_convt = ctx.sbuf("convt", [128, 512], F32)
    convt2, b_convt2 = ctx.sbuf("convt2", [128, 512], F32)
    xbcs, b_xbcs = ctx.sbuf("xbcs", [128, 512], BF16)
    zt, b_zt = ctx.sbuf("zt", [128, 512], F32)
    dtt, b_dtt = ctx.sbuf("dtt", [32, 512], F32)
    dtt2, b_dtt2 = ctx.sbuf("dtt2", [32, 512], F32)
    dtt3, b_dtt3 = ctx.sbuf("dtt3", [32, 512], F32)
    sq, b_sq = ctx.sbuf("sq", [128, 512], F32)
    srow, b_srow = ctx.sbuf("srow", [1, 6 * 512], F32)

    ps1, b_ps1 = ctx.psum("ps1", [1, 512])
    ps2, b_ps2 = ctx.psum("ps2", [1, 512])
    pbm, b_pbm = ctx.psum("pbm", [128, 512])
    pbs, b_pbs = ctx.psum("pbs", [128, 512])
    pp0, b_pp0 = ctx.psum("pp0", [128, 512])
    pp1, b_pp1 = ctx.psum("pp1", [128, 512])
    pdt, b_pdt = ctx.psum("pdt", [32, 512])

    emit(p, "vector", lambda v: v.memset(onesc[:], 1.0), [], [b_onesc])
    emit(p, "vector", lambda v: v.memset(ones1[:], 1.0), [], [b_ones1])
    p.dma("sync", wln1[:], wln1c[:, :], writes=[b_wln1])
    p.dma("sync", dtb[:], dtbc[:, :], writes=[b_dtb])
    p.dma("sync", bconv[:], bconvc[:, :], writes=[b_bconv])
    p.dma("sync", wconv[:], wconvc[:, :], writes=[b_wconv])
    emit(p, "vector", lambda v: v.memset(carry[:], 0.0), [], [b_carry])

    for oc in range(35):
        wst, b_wst = (wst0, b_wst0) if oc % 2 == 0 else (wst1, b_wst1)
        p.dma("sync", wst[:, 0:1024], w_in_T[oc * 128:(oc + 1) * 128, :],
              writes=[b_wst], tag=f"wld{oc}")
        emit(p, "vector", lambda v, oc=oc, wst=wst: v.tensor_copy(
            wbuf[:, oc * 1024:(oc + 1) * 1024], wst[:, 0:1024]),
            [b_wst], [b_wbuf], tag=f"wcast{oc}")
        if oc == 34:
            emit(p, "vector", lambda v, wst=wst: v.tensor_copy(
                wdtf[:].rearrange("p (k c) -> p k c", k=8),
                wst[:, 0:1024].rearrange("p (k c) -> p k c", k=8)[:, :, 0:32]),
                [b_wst], [b_wdtf])

    tlist = ([(0, 128)] + [(128 + i * 512, 512) for i in range(4)])[:n_tiles]
    prev_tt = None
    for ti, (o, tt) in enumerate(tlist):
        halo = (ti == 0)
        oo = o - 128
        for c in range(8):
            p.dma("sync", xt[:, c * 512: c * 512 + tt], xin[c * 128:(c + 1) * 128, o:o + tt],
                  writes=[b_xt], tag=f"x{ti}c{c}")
        if "stats" in dbg_skip:
            continue
        def mm_s1(t, tt=tt):
            last = None
            for c in range(8):
                last = t.matmul(ps1[:, 0:tt], onesc[:, :], xt[:, c * 512: c * 512 + tt],
                                start=(c == 0), stop=(c == 7))
            return last
        p.op("tensor", mm_s1, reads=[b_onesc, b_xt], writes=[b_ps1])
        for c in range(8):
            emit(p, "scalar", lambda s, c=c, tt=tt: s.activation(
                out=sq[:, 0:tt], in_=xt[:, c * 512: c * 512 + tt], func=AF.Square),
                [b_xt], [b_sq])
            emit(p, "tensor", lambda t, c=c, tt=tt: t.matmul(
                ps2[:, 0:tt], onesc[:, :], sq[:, 0:tt], start=(c == 0), stop=(c == 7)),
                [b_onesc, b_sq], [b_ps2])
        M, R, SS, T1, T2, T3 = (srow[0:1, i * 512:i * 512 + tt] for i in range(6))
        emit(p, "vector", lambda v, M=M, tt=tt: v.tensor_scalar_mul(M, ps1[0:1, 0:tt], 1.0 / 1024.0), [b_ps1], [b_srow])
        emit(p, "vector", lambda v, T1=T1, tt=tt: v.tensor_scalar_mul(T1, ps2[0:1, 0:tt], 1.0 / 1024.0), [b_ps2], [b_srow])
        emit(p, "scalar", lambda s, M=M, T2=T2: s.activation(out=T2, in_=M, func=AF.Square), [b_srow], [b_srow])
        emit(p, "vector", lambda v, T1=T1, T2=T2: v.tensor_sub(T1, T1, T2), [b_srow], [b_srow])
        cb5, b_cb5 = ctx.cbias(1e-5, 1)
        emit(p, "scalar", lambda s, T1=T1, T2=T2, cb5=cb5: s.activation(out=T2, in_=T1, func=AF.Sqrt, bias=cb5), [b_srow, b_cb5], [b_srow])
        emit(p, "vector", lambda v, T2=T2, R=R: v.reciprocal(R, T2), [b_srow], [b_srow])
        emit(p, "scalar", lambda s, R=R, T2=T2: s.activation(out=T2, in_=R, func=AF.Square), [b_srow], [b_srow])
        emit(p, "vector", lambda v, T1=T1, T2=T2, T3=T3: v.tensor_mul(T3, T1, T2), [b_srow], [b_srow])
        cb6, b_cb6 = ctx.cbias(1e-6, 1)
        emit(p, "scalar", lambda s, T3=T3, cb6=cb6: s.activation(out=T3, in_=T3, func=AF.Sqrt, bias=cb6), [b_srow, b_cb6], [b_srow])
        emit(p, "vector", lambda v, T3=T3: v.reciprocal(T3, T3), [b_srow], [b_srow])
        emit(p, "vector", lambda v, R=R, T3=T3, SS=SS: v.tensor_mul(SS, R, T3), [b_srow], [b_srow])
        if not halo:
            p.dma("gpsimd", d_astat[0:1, oo:oo + tt], M, reads=[b_srow], writes=[b_astat], out_dram=True)
            p.dma("gpsimd", d_astat[1:2, oo:oo + tt], R, reads=[b_srow], writes=[b_astat], out_dram=True)
        emit(p, "tensor", lambda t, M=M, tt=tt: t.matmul(pbm[:, 0:tt], ones1[0:1, :], M, start=True, stop=True),
             [b_ones1, b_srow], [b_pbm])
        emit(p, "tensor", lambda t, SS=SS, tt=tt: t.matmul(pbs[:, 0:tt], ones1[0:1, :], SS, start=True, stop=True),
             [b_ones1, b_srow], [b_pbs])
        for c in range(8):
            sl = slice(c * 512, c * 512 + tt)
            emit(p, "vector", lambda v, c=c, sl=sl, tt=tt: v.tensor_sub(hf[:, sl], xt[:, sl], pbm[:, 0:tt]),
                 [b_xt, b_pbm], [b_hf])
            emit(p, "vector", lambda v, c=c, sl=sl, tt=tt: v.scalar_tensor_tensor(
                hb[:, sl], hf[:, sl], wln1[:, c:c + 1], pbs[:, 0:tt], ALU.mult, ALU.mult),
                [b_hf, b_wln1, b_pbs], [b_hb])
            emit(p, "vector", lambda v, c=c, sl=sl, tt=tt: v.scalar_tensor_tensor(
                hf[:, sl], hf[:, sl], wln1[:, c:c + 1], pbs[:, 0:tt], ALU.mult, ALU.mult),
                [b_hf, b_wln1, b_pbs], [b_hf])
        if "proj" in dbg_skip:
            prev_tt = tt
            continue
        oc_list = list(range(16, 34)) + [34] + ([] if halo else list(range(16)))
        if "z" in dbg_skip:
            oc_list = [oc for oc in oc_list if oc >= 16]
        if "dt" in dbg_skip:
            oc_list = [oc for oc in oc_list if oc != 34]
        if "xbc" in dbg_skip:
            oc_list = [oc for oc in oc_list if not (16 <= oc < 34)]
        for oc in oc_list:
            if oc == 34:
                def mmdt(t, tt=tt):
                    last = None
                    for kc in range(8):
                        last = t.matmul(pdt[:, 0:tt], wdtf[:, kc * 32:(kc + 1) * 32],
                                        hf[:, kc * 512: kc * 512 + tt], start=(kc == 0), stop=(kc == 7))
                    return last
                p.op("tensor", mmdt, reads=[b_wdtf, b_hf], writes=[b_pdt])
                emit(p, "vector", lambda v, tt=tt: v.tensor_scalar_add(dtt[:, 0:tt], pdt[:, 0:tt], dtb[:, 0:1]),
                     [b_pdt, b_dtb], [b_dtt])
                emit(p, "vector", lambda v, tt=tt: v.tensor_scalar_mul(dtt2[:, 0:tt], dtt[:, 0:tt], -1.0),
                     [b_dtt], [b_dtt2])
                emit(p, "vector", lambda v, tt=tt: v.tensor_tensor(dtt2[:, 0:tt], dtt2[:, 0:tt], dtt[:, 0:tt], ALU.min),
                     [b_dtt2, b_dtt], [b_dtt2])
                emit(p, "scalar", lambda s, tt=tt: s.activation(out=dtt2[:, 0:tt], in_=dtt2[:, 0:tt], func=AF.Exp),
                     [b_dtt2], [b_dtt2])
                cbo, b_cbo = ctx.cbias(1.0, 32)
                emit(p, "scalar", lambda s, tt=tt, cbo=cbo: s.activation(out=dtt2[:, 0:tt], in_=dtt2[:, 0:tt], func=AF.Ln, bias=cbo),
                     [b_dtt2, b_cbo], [b_dtt2])
                emit(p, "vector", lambda v, tt=tt: v.tensor_scalar(dtt3[:, 0:tt], dtt[:, 0:tt], 0.0, None, ALU.max),
                     [b_dtt], [b_dtt3])
                emit(p, "vector", lambda v, tt=tt: v.tensor_add(dtt3[:, 0:tt], dtt3[:, 0:tt], dtt2[:, 0:tt]),
                     [b_dtt3, b_dtt2], [b_dtt3])
                if not halo:
                    p.dma("gpsimd", d_dt[:, oo:oo + tt], dtt3[:, 0:tt], reads=[b_dtt3], writes=[b_dt], out_dram=True)
                continue
            pp, b_pp = (pp0, b_pp0) if oc % 2 == 0 else (pp1, b_pp1)
            def mmoc(t, oc=oc, tt=tt, pp=pp):
                last = None
                for kc in range(8):
                    last = t.matmul(pp[:, 0:tt], wbuf[:, oc * 1024 + kc * 128: oc * 1024 + (kc + 1) * 128],
                                    hb[:, kc * 512: kc * 512 + tt], start=(kc == 0), stop=(kc == 7))
                return last
            p.op("tensor", mmoc, reads=[b_wbuf, b_hb], writes=[b_pp])
            if oc < 16:
                emit(p, "vector", lambda v, pp=pp, tt=tt: v.tensor_copy(zt[:, 0:tt], pp[:, 0:tt]),
                     [b_pp], [b_zt])
                p.dma("gpsimd", d_z[oc * 128:(oc + 1) * 128, oo:oo + tt], zt[:, 0:tt],
                      reads=[b_zt], writes=[b_z], out_dram=True)
            else:
                c = oc - 16
                emit(p, "vector", lambda v, c=c: v.tensor_copy(xraw[:, 0:3], carry[:, 3 * c:3 * c + 3]),
                     [b_carry], [b_xraw])
                emit(p, "vector", lambda v, tt=tt, pp=pp: v.tensor_copy(xraw[:, 3:3 + tt], pp[:, 0:tt]),
                     [b_pp], [b_xraw])
                emit(p, "vector", lambda v, c=c, tt=tt: v.tensor_copy(carry[:, 3 * c:3 * c + 3], xraw[:, tt:tt + 3]),
                     [b_xraw], [b_carry])
                if not halo:
                    emit(p, "vector", lambda v, c=c, tt=tt: v.tensor_scalar_mul(
                        convt[:, 0:tt], xraw[:, 0:tt], wconv[:, c * 4: c * 4 + 1]),
                        [b_xraw, b_wconv], [b_convt])
                    emit(p, "vector", lambda v, c=c, tt=tt: v.scalar_tensor_tensor(
                        convt2[:, 0:tt], xraw[:, 1:1 + tt], wconv[:, c * 4 + 1: c * 4 + 2], convt[:, 0:tt],
                        ALU.mult, ALU.add), [b_xraw, b_wconv, b_convt], [b_convt2])
                    emit(p, "vector", lambda v, c=c, tt=tt: v.scalar_tensor_tensor(
                        convt[:, 0:tt], xraw[:, 2:2 + tt], wconv[:, c * 4 + 2: c * 4 + 3], convt2[:, 0:tt],
                        ALU.mult, ALU.add), [b_xraw, b_wconv, b_convt2], [b_convt])
                    emit(p, "vector", lambda v, c=c, tt=tt: v.scalar_tensor_tensor(
                        convt2[:, 0:tt], xraw[:, 3:3 + tt], wconv[:, c * 4 + 3: c * 4 + 4], convt[:, 0:tt],
                        ALU.mult, ALU.add), [b_xraw, b_wconv, b_convt], [b_convt2])
                    emit(p, "scalar", lambda s, c=c, tt=tt: s.activation(
                        out=xbcs[:, 0:tt], in_=convt2[:, 0:tt], func=AF.Silu, bias=bconv[:, c:c + 1]),
                        [b_convt2, b_bconv], [b_xbcs])
                    p.dma("gpsimd", d_xbc[c * 128:(c + 1) * 128, oo:oo + tt], xbcs[:, 0:tt],
                          reads=[b_xbcs], writes=[b_xbc], out_dram=True)
        prev_tt = tt
    ctx.io = None
    if standalone:
        p.finalize()
    return ctx


def stage_A1(I):
    x = I["L0_attn_hidden_states"]
    xfm = np.ascontiguousarray(x.transpose(0, 2, 1), dtype=np.float32)
    w_in_T = np.zeros((1024, 4480), np.float32)
    w_in_T[:, 0:4384] = np.asarray(I["L0_attn_w_in"], np.float32).T
    w_in_T = _f32c(stage_ktiles(np.concatenate([w_in_T], axis=0), kc=8)) if False else stage_ktiles(w_in_T)
    wln1c = stage_vec_cols(I["L0_attn_w_ln1"], 8)
    dtbc = _f32c(np.asarray(I["L0_attn_dt_bias"], np.float32).reshape(32, 1))
    bconvc = stage_vec_cols(I["L0_attn_b_conv"], 18)
    wc = np.asarray(I["L0_attn_w_conv"], np.float32)
    wconvc = _f32c(wc.reshape(18, 128, 4).transpose(1, 0, 2).reshape(128, 72))
    in_maps = []
    for (b, k) in CORES:
        lo = 2048 * k - 128
        if lo < 0:
            xin = np.concatenate([np.zeros((1024, 128), np.float32), xfm[b][:, 0:2048]], axis=1)
        else:
            xin = np.ascontiguousarray(xfm[b][:, lo:lo + 2176])
        in_maps.append(dict(xin=xin, w_in_T=w_in_T, wln1c=wln1c, dtbc=dtbc,
                            bconvc=bconvc, wconvc=wconvc))
    return in_maps


def build_A2(ctx=None, io=None):
    standalone = ctx is None
    if standalone:
        ctx = LaunchCtx("A2")
        p = ctx.start()
    else:
        p = ctx.prog
    ctx.io = io
    nc = ctx.nc
    S = 8192
    HS = 4096
    NCH = 32

    xs_d = ctx.din("xs", [512, S], BF16)
    B_d = ctx.din("Bf", [128, S], BF16)
    C_d = ctx.din("Cf", [128, S], BF16)
    dth_d = ctx.din("dth", [8, S])
    alog_d = ctx.din("alog", [8, 1])
    dd_d = ctx.din("dd", [1, 8])
    idf_d = ctx.din("idf", [128, 128])
    idb_d = ctx.din("idb", [128, 128], BF16)
    maskT_d = ctx.din("maskT", [128, 128])
    selpair_d = ctx.din("selpair", [8, 128])
    sel8_d = ctx.din("sel8", [8, 128])
    y_d, b_y = ctx.dout("y", [512, S], BF16)

    xs, b_xs = ctx.sbuf("s_xs", [128, 4 * HS], BF16)
    Bf, b_Bf = ctx.sbuf("s_Bf", [128, HS], BF16)
    Cf, b_Cf = ctx.sbuf("s_Cf", [128, HS], BF16)
    dth, b_dth = ctx.sbuf("s_dth", [8, HS], F32)
    acs, b_acs = ctx.sbuf("acs", [8, HS], F32)
    dtds, b_dtds = ctx.sbuf("dtds", [8, HS], F32)
    alast, b_alast = ctx.sbuf("s_alast", [8, NCH], F32)
    expal, b_expal = ctx.sbuf("expal", [8, NCH], F32)
    selrhs, b_selrhs = ctx.sbuf("selrhs", [8, 4 * NCH], F32)
    selpair, b_selpair = ctx.sbuf("s_selpair", [8, 4 * NCH], F32)
    sel8, b_sel8 = ctx.sbuf("s_sel8", [8, 128], F32)
    acol, b_acol = ctx.sbuf("acol", [8, 1], F32)
    ddr, b_ddr = ctx.sbuf("ddr", [1, 8], F32)
    dcols, b_dcols = ctx.sbuf("dcols", [128, 8], F32)
    deccols, b_deccols = ctx.sbuf("deccols", [128, 4 * NCH], F32)
    ones8, b_ones8 = ctx.sbuf("ones8", [8, 128], F32)
    ones1, b_ones1 = ctx.sbuf("ones1", [1, 128], F32)
    idf, b_idf = ctx.sbuf("s_idf", [128, 128], F32)
    idb, b_idb = ctx.sbuf("s_idb", [128, 128], BF16)
    maskT, b_maskT = ctx.sbuf("s_maskT", [128, 128], F32)
    Sst, b_Sst = ctx.sbuf("Sst", [128, 512], F32)
    Sbf, b_Sbf = ctx.sbuf("Sbf", [128, 512], BF16)
    acsT, b_acsT = ctx.sbuf("acsT", [128, 8], F32)
    dtT, b_dtT = ctx.sbuf("dtT", [128, 8], F32)
    dtdsT, b_dtdsT = ctx.sbuf("dtdsT", [128, 8], F32)
    rows, b_rows = ctx.sbuf("rows", [1, 8 * 128], F32)
    xtm, b_xtm = ctx.sbuf("xtm", [128, 128], BF16)
    btm, b_btm = ctx.sbuf("btm", [128, 128], BF16)
    x2, b_x2 = ctx.sbuf("x2", [128, 128], BF16)
    cbt, b_cbt = ctx.sbuf("cbt", [128, 128], F32)
    g1, b_g1 = ctx.sbuf("g1", [128, 128], F32)
    g2, b_g2 = ctx.sbuf("g2", [128, 128], F32)
    g3, b_g3 = ctx.sbuf("g3", [128, 128], F32)
    gb, b_gb = ctx.sbuf("gb", [128, 128], BF16)
    ebq, b_ebq = ctx.sbuf("ebq", [128, 128], F32)
    cph, b_cph = ctx.sbuf("cph", [128, 128], BF16)
    ysb, b_ysb = ctx.sbuf("ysb", [128, 2 * 128], BF16)

    pMISC, b_pMISC = ctx.psum("pMISC", [128, 512])
    pTb, b_pTb = ctx.psum("pTb", [128, 1024], BF16)
    pBCQ, b_pBCQ = ctx.psum("pBCQ", [128, 128])
    pCB, b_pCB = ctx.psum("pCB", [128, 128])
    pYa, b_pYa = ctx.psum("pYa", [128, 128])
    pYb, b_pYb = ctx.psum("pYb", [128, 128])
    pS, b_pS = ctx.psum("pS", [128, 128])

    p.dma("sync", acol[:], alog_d[:, :], writes=[b_acol])
    p.dma("sync", ddr[:], dd_d[:, :], writes=[b_ddr])
    p.dma("sync", idf[:], idf_d[:, :], writes=[b_idf])
    p.dma("sync", idb[:], idb_d[:, :], writes=[b_idb])
    p.dma("sync", maskT[:], maskT_d[:, :], writes=[b_maskT])
    p.dma("sync", selpair[:], selpair_d[:, :], writes=[b_selpair])
    p.dma("sync", sel8[:], sel8_d[:, :], writes=[b_sel8])
    emit(p, "vector", lambda v: v.memset(ones8[:], 1.0), [], [b_ones8])
    emit(p, "vector", lambda v: v.memset(ones1[:], 1.0), [], [b_ones1])
    emit(p, "vector", lambda v: v.memset(Sst[:], 0.0), [], [b_Sst])
    emit(p, "scalar", lambda s: s.activation(out=acol[:], in_=acol[:], func=AF.Exp), [b_acol], [b_acol])
    emit(p, "vector", lambda v: v.tensor_scalar_mul(acol[:], acol[:], -1.0), [b_acol], [b_acol])
    emit(p, "tensor", lambda t: t.matmul(pMISC[:, 0:8], ones1[0:1, :], ddr[0:1, :], start=True, stop=True),
         [b_ones1, b_ddr], [b_pMISC])
    emit(p, "vector", lambda v: v.tensor_copy(dcols[:], pMISC[:, 0:8]), [b_pMISC], [b_dcols])

    for half in range(2):
        HOFF = half * HS
        for pp in range(4):
            p.dma("sync", xs[:, pp * HS:(pp + 1) * HS], xs_d[pp * 128:(pp + 1) * 128, HOFF:HOFF + HS], writes=[b_xs])
        p.dma("sync", Bf[:], B_d[:, HOFF:HOFF + HS], writes=[b_Bf])
        p.dma("sync", Cf[:], C_d[:, HOFF:HOFF + HS], writes=[b_Cf])
        p.dma("sync", dth[:], dth_d[:, HOFF:HOFF + HS], writes=[b_dth])

        emit(p, "vector", lambda v: v.tensor_scalar_mul(dtds[:], dth[:], acol[:, 0:1]), [b_dth, b_acol], [b_dtds])
        for ci in range(NCH):
            sl = slice(ci * 128, (ci + 1) * 128)
            emit(p, "vector", lambda v, sl=sl: v.tensor_tensor_scan(
                acs[:, sl], ones8[:, :], dtds[:, sl], 0.0, ALU.mult, ALU.add), [b_ones8, b_dtds], [b_acs])
        for ci in range(NCH):
            sl = slice(ci * 128, (ci + 1) * 128)
            emit(p, "vector", lambda v, sl=sl, ci=ci: v.tensor_scalar(
                dtds[:, sl], acs[:, sl], -1.0, acs[:, ci * 128 + 127: ci * 128 + 128], ALU.mult, ALU.add),
                [b_acs], [b_dtds])
        emit(p, "scalar", lambda s: s.activation(out=dtds[:], in_=dtds[:], func=AF.Exp), [b_dtds], [b_dtds])
        emit(p, "vector", lambda v: v.tensor_mul(dtds[:], dth[:], dtds[:]), [b_dth, b_dtds], [b_dtds])
        emit(p, "vector", lambda v: v.tensor_copy(alast[:], acs[:, 127::128]), [b_acs], [b_alast])
        emit(p, "scalar", lambda s: s.activation(out=expal[:], in_=alast[:], func=AF.Exp), [b_alast], [b_expal])
        for pp in range(4):
            emit(p, "vector", lambda v, pp=pp: v.tensor_copy(selrhs[:, pp * NCH:(pp + 1) * NCH], expal[:]),
                 [b_expal], [b_selrhs])
        emit(p, "vector", lambda v: v.tensor_mul(selrhs[:], selrhs[:], selpair[:]), [b_selrhs, b_selpair], [b_selrhs])
        emit(p, "tensor", lambda t: t.matmul(pMISC[:, 256:256 + 4 * NCH], sel8[:, :], selrhs[:, :], start=True, stop=True),
             [b_sel8, b_selrhs], [b_pMISC])
        emit(p, "vector", lambda v: v.tensor_copy(deccols[:], pMISC[:, 256:256 + 4 * NCH]), [b_pMISC], [b_deccols])

        for ci in range(NCH):
            sl = slice(ci * 128, (ci + 1) * 128)
            gci = HOFF + ci * 128
            for (src, b_src, dst, b_dst) in ((acs, b_acs, acsT, b_acsT), (dth, b_dth, dtT, b_dtT), (dtds, b_dtds, dtdsT, b_dtdsT)):
                emit_transpose(p, pMISC[0:128, 0:8], src[:, sl], idf[0:8, 0:8], [b_src, b_idf], [b_pMISC])
                emit(p, "vector", lambda v, dst=dst: v.tensor_copy(dst[:], pMISC[0:128, 0:8]), [b_pMISC], [b_dst])
            for h in range(8):
                emit_transpose(p, pMISC[0:1, 128:256], acsT[:, h:h + 1], idf[:, :], [b_acsT, b_idf], [b_pMISC])
                emit(p, "vector", lambda v, h=h: v.tensor_copy(rows[0:1, h * 128:(h + 1) * 128], pMISC[0:1, 128:256]),
                     [b_pMISC], [b_rows])
            emit(p, "tensor", lambda t, sl=sl: t.matmul(pCB[:, :], Bf[:, sl], Cf[:, sl], start=True, stop=True),
                 [b_Bf, b_Cf], [b_pCB])
            emit(p, "vector", lambda v: v.tensor_copy(cbt[:], pCB[:, :]), [b_pCB], [b_cbt])
            emit_transpose(p, pTb[:, 0:128], Bf[:, sl], idb[:, :], [b_Bf, b_idb], [b_pTb])
            emit(p, "vector", lambda v: v.tensor_copy(btm[:], pTb[:, 0:128]), [b_pTb], [b_btm])
            for pp in range(4):
                emit_transpose(p, pMISC[0:128, 128:256], Sst[:, pp * 128:(pp + 1) * 128], idf[:, :],
                               [b_Sst, b_idf], [b_pMISC])
                emit(p, "vector", lambda v, pp=pp: v.tensor_copy(Sbf[:, pp * 128:(pp + 1) * 128], pMISC[0:128, 128:256]),
                     [b_pMISC], [b_Sbf])
                xsl = slice(pp * HS + ci * 128, pp * HS + (ci + 1) * 128)
                emit_transpose(p, pTb[:, 128:256], xs[:, xsl], idb[:, :], [b_xs, b_idb], [b_pTb])
                emit(p, "vector", lambda v: v.tensor_copy(xtm[:], pTb[:, 128:256]), [b_pTb], [b_xtm])
                for hf in range(2):
                    h = pp * 2 + hf
                    emit(p, "vector", lambda v, hf=hf, h=h: v.tensor_scalar_mul(
                        x2[:, hf * 64:(hf + 1) * 64], xtm[:, hf * 64:(hf + 1) * 64], dtdsT[:, h:h + 1]),
                        [b_xtm, b_dtdsT], [b_x2])
                pY, b_pY = (pYa, b_pYa) if pp % 2 == 0 else (pYb, b_pYb)
                for hf in range(2):
                    h = pp * 2 + hf
                    emit(p, "tensor", lambda t, h=h: t.matmul(
                        pBCQ[:, :], ones1[0:1, :], rows[0:1, h * 128:(h + 1) * 128], start=True, stop=True),
                        [b_ones1, b_rows], [b_pBCQ])
                    emit(p, "vector", lambda v, h=h: v.scalar_tensor_tensor(
                        g1[:], pBCQ[:, :], acsT[:, h:h + 1], maskT[:], ALU.subtract, ALU.min),
                        [b_pBCQ, b_acsT, b_maskT], [b_g1])
                    emit(p, "scalar", lambda s: s.activation(out=g2[:], in_=g1[:], func=AF.Exp), [b_g1], [b_g2])
                    emit(p, "vector", lambda v, h=h: v.scalar_tensor_tensor(
                        g3[:], g2[:], dtT[:, h:h + 1], cbt[:], ALU.mult, ALU.mult),
                        [b_g2, b_dtT, b_cbt], [b_g3])
                    emit(p, "vector", lambda v, h=h: v.scalar_tensor_tensor(
                        gb[:], idf[:, :], dcols[:, h:h + 1], g3[:], ALU.mult, ALU.add),
                        [b_idf, b_dcols, b_g3], [b_gb])
                    emit(p, "tensor", lambda t, hf=hf, pY=pY: t.matmul(
                        pY[hf * 64:(hf + 1) * 64, :], xtm[:, hf * 64:(hf + 1) * 64], gb[:, :],
                        start=True, stop=False),
                        [b_xtm, b_gb], [b_pY])
                    emit(p, "scalar", lambda s: s.activation(out=ebq[:], in_=pBCQ[:, :], func=AF.Exp), [b_pBCQ], [b_ebq])
                    emit(p, "vector", lambda v, sl=sl: v.tensor_tensor(cph[:], Cf[:, sl], ebq[:], ALU.mult),
                         [b_Cf, b_ebq], [b_cph])
                    emit(p, "tensor", lambda t, pp=pp, hf=hf, pY=pY: t.matmul(
                        pY[hf * 64:(hf + 1) * 64, :], Sbf[:, pp * 128 + hf * 64: pp * 128 + (hf + 1) * 64], cph[:, :],
                        start=False, stop=(hf == 1)),
                        [b_Sbf, b_cph], [b_pY])
                emit(p, "tensor", lambda t: t.matmul(pS[:, :], x2[:, :], btm[:, :], start=True, stop=True),
                     [b_btm, b_x2], [b_pS])
                emit(p, "vector", lambda v, pp=pp, ci=ci: v.scalar_tensor_tensor(
                    Sst[:, pp * 128:(pp + 1) * 128], Sst[:, pp * 128:(pp + 1) * 128],
                    deccols[:, pp * NCH + ci: pp * NCH + ci + 1], pS[:, :], ALU.mult, ALU.add),
                    [b_Sst, b_deccols, b_pS], [b_Sst])
                ysl = slice((pp % 2) * 128, (pp % 2) * 128 + 128)
                emit(p, "vector", lambda v, ysl=ysl, pY=pY: v.tensor_copy(ysb[:, ysl], pY[:, :]), [b_pY], [b_ysb])
                p.dma("gpsimd", y_d[pp * 128:(pp + 1) * 128, gci:gci + 128], ysb[:, ysl],
                      reads=[b_ysb], writes=[b_y], out_dram=True)
    ctx.io = None
    if standalone:
        p.finalize()
    return ctx


def stage_A2(I, a1_res):
    NEGv = NEG
    idf = np.eye(128, dtype=np.float32)
    idb = idf.astype(BF16NP)
    sq = np.arange(128)
    maskT = np.where(sq[:, None] <= sq[None, :], 0.0, NEGv).astype(np.float32)
    selpair = np.zeros((8, 128), np.float32)
    for h in range(8):
        selpair[h, (h // 2) * 32:(h // 2) * 32 + 32] = 1.0
    sel8 = np.zeros((8, 128), np.float32)
    for h in range(8):
        sel8[h, (h % 2) * 64:(h % 2) * 64 + 64] = 1.0
    alog_full = np.asarray(I["L0_attn_A_log"], np.float32)
    d_full = np.asarray(I["L0_attn_D"], np.float32)
    xall = {}
    dtall = {}
    for b in range(2):
        xall[b] = np.concatenate([np.asarray(a1_res[b * 4 + k]["xbc"]) for k in range(4)], axis=1)
        dtall[b] = np.concatenate([np.asarray(a1_res[b * 4 + k]["dt"]) for k in range(4)], axis=1)
    in_maps = []
    for (b, k) in CORES:
        in_maps.append(dict(
            xs=np.ascontiguousarray(xall[b][512 * k:512 * k + 512]),
            Bf=np.ascontiguousarray(xall[b][2048:2176]),
            Cf=np.ascontiguousarray(xall[b][2176:2304]),
            dth=np.ascontiguousarray(dtall[b][8 * k:8 * k + 8]),
            alog=np.ascontiguousarray(alog_full[8 * k:8 * k + 8].reshape(8, 1)),
            dd=np.ascontiguousarray(d_full[8 * k:8 * k + 8].reshape(1, 8)),
            idf=idf, idb=idb, maskT=maskT, selpair=selpair, sel8=sel8,
        ))
    return in_maps


def build_A3(ctx=None, io=None):
    standalone = ctx is None
    if standalone:
        ctx = LaunchCtx("A3")
        p = ctx.start()
    else:
        p = ctx.prog
    ctx.io = io
    nc = ctx.nc

    x_d = ctx.din("xin3", [1024, SH])
    astat_d = ctx.din("astat3", [2, SH])
    z_d = ctx.din("zin", [2048, SH])
    y_d = ctx.din("yin", [2048, SH], BF16)
    wmn_d = ctx.din("wmnc", [128, 16])
    wout_d = ctx.din("wout_T", [1024, 2048])
    wln2_d = ctx.din("wln2c", [128, 8])
    wg_d = ctx.din("wg_T", [4096, 1024])
    wu_d = ctx.din("wu_T", [4096, 1024])
    wd_d = ctx.din("wd_T", [1024, 4096])
    x1_d, b_x1 = ctx.dout("x1", [1024, SH])

    xt, b_xt = ctx.sbuf("xt", [128, 8 * 512], F32)
    asm, b_asm = ctx.sbuf("asm", [1, 512], F32)
    asr, b_asr = ctx.sbuf("asr", [1, 512], F32)
    zt0, b_zt0 = ctx.sbuf("zt0", [128, 512], F32)
    zt1, b_zt1 = ctx.sbuf("zt1", [128, 512], F32)
    yt, b_yt = ctx.sbuf("yt", [128, 16 * 512], BF16)
    zs, b_zs = ctx.sbuf("zs", [128, 512], F32)
    ygf, b_ygf = ctx.sbuf("ygf", [128, 512], F32)
    sq, b_sq = ctx.sbuf("sq", [128, 512], F32)
    ygb, b_ygb = ctx.sbuf("ygb", [128, 16 * 512], BF16)
    ynb, b_ynb = ctx.sbuf("ynb", [128, 16 * 512], BF16)
    hid, b_hid = ctx.sbuf("hid", [128, 8 * 512], F32)
    h2, b_h2 = ctx.sbuf("h2", [128, 8 * 512], BF16)
    h2_scr = ctx.nc.dram_tensor("a3h2scr", [1024, SH], BF16).ap()
    b_h2s = p.buf("a3h2scr")
    hid_scr = ctx.nc.dram_tensor("a3hidscr", [1024, SH], F32).ap()
    b_hids = p.buf("a3hidscr")
    wmn, b_wmn = ctx.sbuf("wmn", [128, 16], F32)
    wln2, b_wln2 = ctx.sbuf("wln2", [128, 8], F32)
    wob, b_wob = ctx.sbuf("wob", [128, 16 * 1024], BF16)
    wst0, b_wst0 = ctx.sbuf("wst0", [128, 1024], F32)
    wst1, b_wst1 = ctx.sbuf("wst1", [128, 1024], F32)
    wcb0, b_wcb0 = ctx.sbuf("wcb0", [128, 1024], BF16)
    wcb1, b_wcb1 = ctx.sbuf("wcb1", [128, 1024], BF16)
    onesc, b_onesc = ctx.sbuf("onesc", [128, 1], F32)
    ones1, b_ones1 = ctx.sbuf("ones1", [1, 128], F32)
    srow, b_srow = ctx.sbuf("srow", [1, 6 * 512], F32)
    x1s, b_x1s = ctx.sbuf("x1s", [128, 512], F32)
    tvec, b_tvec = ctx.sbuf("tvec", [128, 512], F32)

    ps2, b_ps2 = ctx.psum("ps2", [1, 512])
    pbm, b_pbm = ctx.psum("pbm", [128, 512])
    pbr, b_pbr = ctx.psum("pbr", [128, 512])
    pbn, b_pbn = ctx.psum("pbn", [128, 512])
    pp0, b_pp0 = ctx.psum("pp0", [128, 512])
    pp1, b_pp1 = ctx.psum("pp1", [128, 512])

    emit(p, "vector", lambda v: v.memset(onesc[:], 1.0), [], [b_onesc])
    emit(p, "vector", lambda v: v.memset(ones1[:], 1.0), [], [b_ones1])
    p.dma("sync", wmn[:], wmn_d[:, :], writes=[b_wmn])
    p.dma("sync", wln2[:], wln2_d[:, :], writes=[b_wln2])
    for oc in range(8):
        wst, b_wst = (wst0, b_wst0) if oc % 2 == 0 else (wst1, b_wst1)
        for hh in range(2):
            p.dma("sync", wst[:, :], wout_d[oc * 128:(oc + 1) * 128, hh * 1024:(hh + 1) * 1024],
                  writes=[b_wst])
            emit(p, "vector", lambda v, oc=oc, hh=hh, wst=wst: v.tensor_copy(
                wob[:, oc * 2048 + hh * 1024: oc * 2048 + (hh + 1) * 1024], wst[:, :]),
                [b_wst], [b_wob])

    cb5, b_cb5 = ctx.cbias(1e-5, 1)
    cb6, b_cb6 = ctx.cbias(1e-6, 1)

    for ti in range(4):
        o = ti * 512
        for c in range(8):
            p.dma("sync", xt[:, c * 512:(c + 1) * 512], x_d[c * 128:(c + 1) * 128, o:o + 512],
                  writes=[b_xt])
        p.dma("sync", asm[:], astat_d[0:1, o:o + 512], writes=[b_asm])
        p.dma("sync", asr[:], astat_d[1:2, o:o + 512], writes=[b_asr])
        for c in range(16):
            p.dma("sync", yt[:, c * 512:(c + 1) * 512], y_d[c * 128:(c + 1) * 128, o:o + 512],
                  writes=[b_yt])
        for c in range(16):
            zt, b_zt = (zt0, b_zt0) if c % 2 == 0 else (zt1, b_zt1)
            p.dma("sync", zt[:], z_d[c * 128:(c + 1) * 128, o:o + 512], writes=[b_zt])
            emit(p, "scalar", lambda s, zt=zt: s.activation(out=zs[:], in_=zt[:], func=AF.Silu),
                 [b_zt], [b_zs])
            emit(p, "vector", lambda v, c=c: v.tensor_tensor(ygf[:], yt[:, c * 512:(c + 1) * 512], zs[:], ALU.mult),
                 [b_yt, b_zs], [b_ygf])
            emit(p, "vector", lambda v, c=c: v.tensor_copy(ygb[:, c * 512:(c + 1) * 512], ygf[:]),
                 [b_ygf], [b_ygb])
            emit(p, "scalar", lambda s: s.activation(out=sq[:], in_=ygf[:], func=AF.Square), [b_ygf], [b_sq])
            emit(p, "tensor", lambda t, c=c: t.matmul(ps2[:, :], onesc[:, :], sq[:, :],
                                                      start=(c == 0), stop=(c == 15)),
                 [b_onesc, b_sq], [b_ps2])
        R0 = srow[0:1, 0:512]
        emit(p, "vector", lambda v: v.tensor_scalar_mul(R0, ps2[0:1, :], 1.0 / 2048.0), [b_ps2], [b_srow])
        emit(p, "scalar", lambda s, cb5=cb5: s.activation(out=R0, in_=R0, func=AF.Sqrt, bias=cb5), [b_srow, b_cb5], [b_srow])
        emit(p, "vector", lambda v: v.reciprocal(R0, R0), [b_srow], [b_srow])
        emit(p, "tensor", lambda t: t.matmul(pbn[:, :], ones1[0:1, :], R0, start=True, stop=True),
             [b_ones1, b_srow], [b_pbn])
        for c in range(16):
            emit(p, "vector", lambda v, c=c: v.scalar_tensor_tensor(
                ynb[:, c * 512:(c + 1) * 512], ygb[:, c * 512:(c + 1) * 512], wmn[:, c:c + 1], pbn[:, :],
                ALU.mult, ALU.mult), [b_ygb, b_wmn, b_pbn], [b_ynb])
        emit(p, "tensor", lambda t: t.matmul(pbm[:, :], ones1[0:1, :], asm[0:1, :], start=True, stop=True),
             [b_ones1, b_asm], [b_pbm])
        emit(p, "tensor", lambda t: t.matmul(pbr[:, :], ones1[0:1, :], asr[0:1, :], start=True, stop=True),
             [b_ones1, b_asr], [b_pbr])
        for oc in range(8):
            pp, b_pp = (pp0, b_pp0) if oc % 2 == 0 else (pp1, b_pp1)
            def mmwo(t, oc=oc, pp=pp):
                last = None
                for kc in range(16):
                    last = t.matmul(pp[:, :], wob[:, oc * 2048 + kc * 128:(oc * 2048 + (kc + 1) * 128)],
                                    ynb[:, kc * 512:(kc + 1) * 512], start=(kc == 0), stop=(kc == 15))
                return last
            p.op("tensor", mmwo, reads=[b_wob, b_ynb], writes=[b_pp])
            emit(p, "vector", lambda v, oc=oc: v.tensor_sub(tvec[:], xt[:, oc * 512:(oc + 1) * 512], pbm[:, :]),
                 [b_xt, b_pbm], [b_tvec])
            emit(p, "vector", lambda v: v.tensor_mul(tvec[:], tvec[:], pbr[:, :]), [b_tvec, b_pbr], [b_tvec])
            emit(p, "vector", lambda v, oc=oc, pp=pp: v.tensor_add(hid[:, oc * 512:(oc + 1) * 512], tvec[:], pp[:, :]),
                 [b_tvec, b_pp], [b_hid])
        for oc in range(8):
            emit(p, "scalar", lambda s, oc=oc: s.activation(out=sq[:], in_=hid[:, oc * 512:(oc + 1) * 512], func=AF.Square),
                 [b_hid], [b_sq])
            emit(p, "tensor", lambda t, oc=oc: t.matmul(ps2[:, :], onesc[:, :], sq[:, :],
                                                        start=(oc == 0), stop=(oc == 7)),
                 [b_onesc, b_sq], [b_ps2])
        R1 = srow[0:1, 512:1024]
        emit(p, "vector", lambda v: v.tensor_scalar_mul(R1, ps2[0:1, :], 1.0 / 1024.0), [b_ps2], [b_srow])
        emit(p, "scalar", lambda s, cb6=cb6: s.activation(out=R1, in_=R1, func=AF.Sqrt, bias=cb6), [b_srow, b_cb6], [b_srow])
        emit(p, "vector", lambda v: v.reciprocal(R1, R1), [b_srow], [b_srow])
        emit(p, "tensor", lambda t: t.matmul(pbn[:, :], ones1[0:1, :], R1, start=True, stop=True),
             [b_ones1, b_srow], [b_pbn])
        for oc in range(8):
            emit(p, "vector", lambda v, oc=oc: v.scalar_tensor_tensor(
                h2[:, oc * 512:(oc + 1) * 512], hid[:, oc * 512:(oc + 1) * 512], wln2[:, oc:oc + 1], pbn[:, :],
                ALU.mult, ALU.mult), [b_hid, b_wln2, b_pbn], [b_h2])
            p.dma("gpsimd", h2_scr[oc * 128:(oc + 1) * 128, o:o + 512], h2[:, oc * 512:(oc + 1) * 512],
                  reads=[b_h2], writes=[b_h2s])
            p.dma("gpsimd", hid_scr[oc * 128:(oc + 1) * 128, o:o + 512], hid[:, oc * 512:(oc + 1) * 512],
                  reads=[b_hid], writes=[b_hids])

    if ctx.__class__.__name__ == "MegaCtx":
        ctx.new_phase()
    h2r, b_h2r = ctx.sbuf("h2r", [128, 8 * SH], BF16)
    midb, b_midb = ctx.sbuf("midb3", [128, 8 * SH], BF16)
    outac, b_outac = ctx.sbuf("outac3", [128, 8 * SH], BF16)
    gsil, b_gsil = ctx.sbuf("gsil3", [128, 512], BF16)
    w2st0, b_w2st0 = ctx.sbuf("w2st0", [128, 1024], F32)
    w2st1, b_w2st1 = ctx.sbuf("w2st1", [128, 1024], F32)
    w2cb0, b_w2cb0 = ctx.sbuf("w2cb0", [128, 1024], BF16)
    w2cb1, b_w2cb1 = ctx.sbuf("w2cb1", [128, 1024], BF16)
    w2cb2, b_w2cb2 = ctx.sbuf("w2cb2", [128, 1024], BF16)
    w2cb3, b_w2cb3 = ctx.sbuf("w2cb3", [128, 1024], BF16)
    xl2, b_xl2 = ctx.sbuf("xl2", [128, 512], F32)
    hl2, b_hl2 = ctx.sbuf("hl2", [128, 512], F32)
    x1s2, b_x1s2 = ctx.sbuf("x1s2", [128, 512], F32)
    if ctx.__class__.__name__ == "MegaCtx":
        pq0, b_pq0 = ctx.psum("pq0", [128, 512])
        pq1, b_pq1 = ctx.psum("pq1", [128, 512])
    else:
        pq0, b_pq0, pq1, b_pq1 = pp0, b_pp0, pp1, b_pp1
    for c in range(8):
        for t4 in range(4):
            p.dma("sync", h2r[:, c * SH + t4 * 512: c * SH + t4 * 512 + 512],
                  h2_scr[c * 128:(c + 1) * 128, t4 * 512:(t4 + 1) * 512], reads=[b_h2s], writes=[b_h2r])
    emit(p, "vector", lambda v: v.memset(outac[:], 0.0), [], [b_outac])
    for jb in range(4):
        for jc in range(8):
            ocg = jb * 8 + jc
            p.dma("sync", w2st0[:, :], wg_d[ocg * 128:(ocg + 1) * 128, :], writes=[b_w2st0])
            emit(p, "vector", lambda v: v.tensor_copy(w2cb0[:], w2st0[:]), [b_w2st0], [b_w2cb0])
            p.dma("sync", w2st1[:, :], wu_d[ocg * 128:(ocg + 1) * 128, :], writes=[b_w2st1])
            emit(p, "vector", lambda v: v.tensor_copy(w2cb1[:], w2st1[:]), [b_w2st1], [b_w2cb1])
            for t4 in range(4):
                def mmg(t, t4=t4):
                    last = None
                    for kc in range(8):
                        last = t.matmul(pq0[:, :], w2cb0[:, kc * 128:(kc + 1) * 128],
                                        h2r[:, kc * SH + t4 * 512: kc * SH + t4 * 512 + 512],
                                        start=(kc == 0), stop=(kc == 7))
                    return last
                p.op("tensor", mmg, reads=[b_w2cb0, b_h2r], writes=[b_pq0])
                def mmu(t, t4=t4):
                    last = None
                    for kc in range(8):
                        last = t.matmul(pq1[:, :], w2cb1[:, kc * 128:(kc + 1) * 128],
                                        h2r[:, kc * SH + t4 * 512: kc * SH + t4 * 512 + 512],
                                        start=(kc == 0), stop=(kc == 7))
                    return last
                p.op("tensor", mmu, reads=[b_w2cb1, b_h2r], writes=[b_pq1])
                emit(p, "scalar", lambda s: s.activation(out=gsil[:], in_=pq0[:, :], func=AF.Silu), [b_pq0], [b_gsil])
                emit(p, "vector", lambda v, jc=jc, t4=t4: v.tensor_tensor(
                    midb[:, jc * SH + t4 * 512: jc * SH + t4 * 512 + 512], pq1[:, :], gsil[:], ALU.mult),
                    [b_pq1, b_gsil], [b_midb])
        for oc in range(8):
            w2st, b_w2st = (w2st0, b_w2st0) if oc % 2 == 0 else (w2st1, b_w2st1)
            w2cb, b_w2cb = (w2cb2, b_w2cb2) if oc % 2 == 0 else (w2cb3, b_w2cb3)
            p.dma("sync", w2st[:, :], wd_d[oc * 128:(oc + 1) * 128, jb * 1024:(jb + 1) * 1024], writes=[b_w2st])
            emit(p, "vector", lambda v, w2st=w2st, w2cb=w2cb: v.tensor_copy(w2cb[:], w2st[:]), [b_w2st], [b_w2cb])
            for t4 in range(4):
                pq, b_pq = (pq0, b_pq0) if t4 % 2 == 0 else (pq1, b_pq1)
                def mmd(t, w2cb=w2cb, t4=t4, pq=pq):
                    last = None
                    for kc in range(8):
                        last = t.matmul(pq[:, :], w2cb[:, kc * 128:(kc + 1) * 128],
                                        midb[:, kc * SH + t4 * 512: kc * SH + t4 * 512 + 512],
                                        start=(kc == 0), stop=(kc == 7))
                    return last
                p.op("tensor", mmd, reads=[b_w2cb, b_midb], writes=[b_pq])
                emit(p, "vector", lambda v, oc=oc, t4=t4, pq=pq: v.tensor_tensor(
                    outac[:, oc * SH + t4 * 512: oc * SH + t4 * 512 + 512], pq[:, :],
                    outac[:, oc * SH + t4 * 512: oc * SH + t4 * 512 + 512], ALU.add),
                    [b_pq, b_outac], [b_outac])
    for ti in range(4):
        o = ti * 512
        for oc in range(8):
            p.dma("sync", xl2[:], x_d[oc * 128:(oc + 1) * 128, o:o + 512], writes=[b_xl2])
            p.dma("sync", hl2[:], hid_scr[oc * 128:(oc + 1) * 128, o:o + 512], reads=[b_hids], writes=[b_hl2])
            emit(p, "vector", lambda v, oc=oc, o=o: v.tensor_add(
                x1s2[:], hl2[:], outac[:, oc * SH + o: oc * SH + o + 512]), [b_hl2, b_outac], [b_x1s2])
            emit(p, "vector", lambda v: v.tensor_add(x1s2[:], x1s2[:], xl2[:]), [b_x1s2, b_xl2], [b_x1s2])
            p.dma("gpsimd", x1_d[oc * 128:(oc + 1) * 128, o:o + 512], x1s2[:], reads=[b_x1s2], writes=[b_x1], out_dram=True)
    ctx.io = None
    if standalone:
        p.finalize()
    return ctx


def stage_A3(I, a1_res, a2_res):
    wmnc = stage_vec_cols(I["L0_attn_w_mnorm"], 16)
    wout_T = stage_ktiles(np.asarray(I["L0_attn_w_out"], np.float32).T, kc=16)
    wln2c = stage_vec_cols(I["L0_attn_w_ln2"], 8)
    wg_T = stage_ktiles(np.asarray(I["L0_attn_w_gate"], np.float32).T)
    wu_T = stage_ktiles(np.asarray(I["L0_attn_w_up"], np.float32).T)
    wd_T = stage_ktiles(np.asarray(I["L0_attn_w_down"], np.float32).T, kc=32)
    x = I["L0_attn_hidden_states"]
    xfm = np.ascontiguousarray(x.transpose(0, 2, 1), dtype=np.float32)
    yfm = {}
    for b in range(2):
        yfm[b] = np.concatenate([np.asarray(a2_res[b * 4 + k]["y"]) for k in range(4)], axis=0)
    in_maps = []
    for c, (b, k) in enumerate(CORES):
        sl = slice(2048 * k, 2048 * k + 2048)
        in_maps.append(dict(
            xin3=np.ascontiguousarray(xfm[b][:, sl]),
            astat3=np.ascontiguousarray(np.asarray(a1_res[c]["astat"])),
            zin=np.ascontiguousarray(np.asarray(a1_res[c]["z"])),
            yin=np.ascontiguousarray(yfm[b][:, sl]),
            wmnc=wmnc, wout_T=wout_T, wln2c=wln2c, wg_T=wg_T, wu_T=wu_T, wd_T=wd_T,
        ))
    return in_maps


def build_B1(ctx=None, io=None):
    standalone = ctx is None
    if standalone:
        ctx = LaunchCtx("B1")
        p = ctx.start()
    else:
        p = ctx.prog
    ctx.io = io
    nc = ctx.nc

    x_d = ctx.din("x1in", [1024, SH])
    wq_d = ctx.din("wq", [1024, 1024])
    wk_d = ctx.din("wk", [1024, 1024])
    wv_d = ctx.din("wv", [1024, 1024])
    bq_d = ctx.din("bqc", [128, 8])
    bk_d = ctx.din("bkc", [128, 8])
    bv_d = ctx.din("bvr", [1, 1024])
    g1_d = ctx.din("g1c", [128, 8])
    b1_d = ctx.din("b1c", [128, 8])
    lnb_d, b_lnb = ctx.dout("lnb", [1024, SH])
    q_d, b_q = ctx.dout("q", [1024, SH], BF16)
    k_d, b_k = ctx.dout("k", [1024, SH], BF16)
    v_d, b_v = ctx.dout("vtm", [SH, 1024], BF16)

    xt, b_xt = ctx.sbuf("xt", [128, 8 * 512], F32)
    lnf, b_lnf = ctx.sbuf("lnf", [128, 8 * 512], F32)
    hnb, b_hnb = ctx.sbuf("hnb", [128, 8 * 512], BF16)
    sq, b_sq = ctx.sbuf("sq", [128, 512], F32)
    tv, b_tv = ctx.sbuf("tv", [128, 512], F32)
    wqb, b_wqb = ctx.sbuf("wqb", [128, 8 * 1024], BF16)
    wkb, b_wkb = ctx.sbuf("wkb", [128, 8 * 1024], BF16)
    wvb, b_wvb = ctx.sbuf("wvb", [128, 8 * 1024], BF16)
    wst0, b_wst0 = ctx.sbuf("wst0", [128, 1024], F32)
    wst1, b_wst1 = ctx.sbuf("wst1", [128, 1024], F32)
    bqc, b_bqc = ctx.sbuf("bqc_s", [128, 8], F32)
    bkc, b_bkc = ctx.sbuf("bkc_s", [128, 8], F32)
    bvr, b_bvr = ctx.sbuf("bvr_s", [1, 1024], F32)
    g1c, b_g1c = ctx.sbuf("g1c_s", [128, 8], F32)
    b1c, b_b1c = ctx.sbuf("b1c_s", [128, 8], F32)
    onesc, b_onesc = ctx.sbuf("onesc", [128, 1], F32)
    ones1, b_ones1 = ctx.sbuf("ones1", [1, 128], F32)
    srow, b_srow = ctx.sbuf("srow", [1, 6 * 512], F32)
    outs, b_outs = ctx.sbuf("outs", [128, 512], BF16)
    vts, b_vts = ctx.sbuf("vts", [128, 512], BF16)

    ps1, b_ps1 = ctx.psum("ps1", [1, 512])
    ps2, b_ps2 = ctx.psum("ps2", [1, 512])
    pbm, b_pbm = ctx.psum("pbm", [128, 512])
    pbr, b_pbr = ctx.psum("pbr", [128, 512])
    pp0, b_pp0 = ctx.psum("pp0", [128, 512])
    pp1, b_pp1 = ctx.psum("pp1", [128, 512])

    emit(p, "vector", lambda v: v.memset(onesc[:], 1.0), [], [b_onesc])
    emit(p, "vector", lambda v: v.memset(ones1[:], 1.0), [], [b_ones1])
    for (dst, b_dst, src) in ((bqc, b_bqc, bq_d), (bkc, b_bkc, bk_d), (g1c, b_g1c, g1_d), (b1c, b_b1c, b1_d)):
        p.dma("sync", dst[:], src[:, :], writes=[b_dst])
    p.dma("sync", bvr[:], bv_d[:, :], writes=[b_bvr])
    for (wsb, b_wsb, wd) in ((wqb, b_wqb, wq_d), (wkb, b_wkb, wk_d), (wvb, b_wvb, wv_d)):
        for oc in range(8):
            wst, b_wst = (wst0, b_wst0) if oc % 2 == 0 else (wst1, b_wst1)
            p.dma("sync", wst[:, :], wd[oc * 128:(oc + 1) * 128, :], writes=[b_wst])
            emit(p, "vector", lambda v, wsb=wsb, oc=oc, wst=wst: v.tensor_copy(
                wsb[:, oc * 1024:(oc + 1) * 1024], wst[:]), [b_wst], [b_wsb])

    cb5, b_cb5 = ctx.cbias(1e-5, 1)

    for ti in range(4):
        o = ti * 512
        for c in range(8):
            p.dma("sync", xt[:, c * 512:(c + 1) * 512], x_d[c * 128:(c + 1) * 128, o:o + 512], writes=[b_xt])

        def norm_pass(src_sb, b_src, R0off):
            def mm1(t):
                last = None
                for c in range(8):
                    last = t.matmul(ps1[:, :], onesc[:, :], src_sb[:, c * 512:(c + 1) * 512],
                                    start=(c == 0), stop=(c == 7))
                return last
            p.op("tensor", mm1, reads=[b_onesc, b_src], writes=[b_ps1])
            for c in range(8):
                emit(p, "scalar", lambda s, c=c: s.activation(out=sq[:], in_=src_sb[:, c * 512:(c + 1) * 512], func=AF.Square),
                     [b_src], [b_sq])
                emit(p, "tensor", lambda t, c=c: t.matmul(ps2[:, :], onesc[:, :], sq[:, :], start=(c == 0), stop=(c == 7)),
                     [b_onesc, b_sq], [b_ps2])
            M = srow[0:1, R0off:R0off + 512]
            R = srow[0:1, R0off + 512:R0off + 1024]
            T1 = srow[0:1, R0off + 1024:R0off + 1536]
            emit(p, "vector", lambda v, M=M: v.tensor_scalar_mul(M, ps1[0:1, :], 1.0 / 1024.0), [b_ps1], [b_srow])
            emit(p, "vector", lambda v, T1=T1: v.tensor_scalar_mul(T1, ps2[0:1, :], 1.0 / 1024.0), [b_ps2], [b_srow])
            emit(p, "scalar", lambda s, M=M, R=R: s.activation(out=R, in_=M, func=AF.Square), [b_srow], [b_srow])
            emit(p, "vector", lambda v, T1=T1, R=R: v.tensor_sub(T1, T1, R), [b_srow], [b_srow])
            emit(p, "scalar", lambda s, T1=T1, cb5=cb5: s.activation(out=T1, in_=T1, func=AF.Sqrt, bias=cb5), [b_srow, b_cb5], [b_srow])
            emit(p, "vector", lambda v, T1=T1, R=R: v.reciprocal(R, T1), [b_srow], [b_srow])
            emit(p, "tensor", lambda t, M=M: t.matmul(pbm[:, :], ones1[0:1, :], M, start=True, stop=True),
                 [b_ones1, b_srow], [b_pbm])
            emit(p, "tensor", lambda t, R=R: t.matmul(pbr[:, :], ones1[0:1, :], R, start=True, stop=True),
                 [b_ones1, b_srow], [b_pbr])

        norm_pass(xt, b_xt, 0)
        for c in range(8):
            sl = slice(c * 512, (c + 1) * 512)
            emit(p, "vector", lambda v, sl=sl: v.tensor_sub(lnf[:, sl], xt[:, sl], pbm[:, :]), [b_xt, b_pbm], [b_lnf])
            emit(p, "vector", lambda v, sl=sl: v.tensor_mul(lnf[:, sl], lnf[:, sl], pbr[:, :]), [b_lnf, b_pbr], [b_lnf])
            p.dma("gpsimd", lnb_d[c * 128:(c + 1) * 128, o:o + 512], lnf[:, sl], reads=[b_lnf], writes=[b_lnb], out_dram=True)
        norm_pass(lnf, b_lnf, 1536)
        for c in range(8):
            sl = slice(c * 512, (c + 1) * 512)
            emit(p, "vector", lambda v, sl=sl: v.tensor_sub(tv[:], lnf[:, sl], pbm[:, :]), [b_lnf, b_pbm], [b_tv])
            emit(p, "vector", lambda v: v.tensor_mul(tv[:], tv[:], pbr[:, :]), [b_tv, b_pbr], [b_tv])
            emit(p, "vector", lambda v, c=c, sl=sl: v.tensor_scalar(
                hnb[:, sl], tv[:], g1c[:, c:c + 1], b1c[:, c:c + 1], ALU.mult, ALU.add),
                [b_tv, b_g1c, b_b1c], [b_hnb])
        for (wsb, b_wsb, bc, b_bc, dram, b_dram) in ((wqb, b_wqb, bqc, b_bqc, q_d, b_q), (wkb, b_wkb, bkc, b_bkc, k_d, b_k)):
            for oc in range(8):
                pp, b_pp = (pp0, b_pp0) if oc % 2 == 0 else (pp1, b_pp1)
                def mmp(t, wsb=wsb, oc=oc, pp=pp):
                    last = None
                    for kc in range(8):
                        last = t.matmul(pp[:, :], wsb[:, oc * 1024 + kc * 128: oc * 1024 + (kc + 1) * 128],
                                        hnb[:, kc * 512:(kc + 1) * 512], start=(kc == 0), stop=(kc == 7))
                    return last
                p.op("tensor", mmp, reads=[b_wsb, b_hnb], writes=[b_pp])
                emit(p, "vector", lambda v, oc=oc, pp=pp, bc=bc: v.tensor_scalar_add(outs[:], pp[:, :], bc[:, oc:oc + 1]),
                     [b_pp, b_bc], [b_outs])
                p.dma("gpsimd", dram[oc * 128:(oc + 1) * 128, o:o + 512], outs[:], reads=[b_outs], writes=[b_dram], out_dram=True)
        for sc in range(4):
            so = o + sc * 128
            for oc in range(8):
                pp, b_pp = (pp0, b_pp0) if oc % 2 == 0 else (pp1, b_pp1)
                def mmv(t, sc=sc, oc=oc, pp=pp):
                    last = None
                    for kc in range(8):
                        last = t.matmul(pp[:, 0:128], hnb[:, kc * 512 + sc * 128: kc * 512 + (sc + 1) * 128],
                                        wvb[:, oc * 1024 + kc * 128: oc * 1024 + (kc + 1) * 128],
                                        start=(kc == 0), stop=False)
                    last = t.matmul(pp[:, 0:128], ones1[0:1, 0:128], bvr[0:1, oc * 128:(oc + 1) * 128],
                                    start=False, stop=True)
                    return last
                p.op("tensor", mmv, reads=[b_hnb, b_wvb, b_ones1, b_bvr], writes=[b_pp])
                emit(p, "vector", lambda v, pp=pp: v.tensor_copy(vts[:, 0:128], pp[:, 0:128]), [b_pp], [b_vts])
                p.dma("gpsimd", v_d[so:so + 128, oc * 128:(oc + 1) * 128], vts[:, 0:128], reads=[b_vts], writes=[b_v], out_dram=True)
    ctx.io = None
    if standalone:
        p.finalize()
    return ctx


def stage_B1(I, x1_parts):
    wq = stage_ktiles(I["L0_ffn_wq"])
    wk = stage_ktiles(I["L0_ffn_wk"])
    wv = stage_ktiles(I["L0_ffn_wv"])
    bqc = stage_vec_cols(I["L0_ffn_bq"], 8)
    bkc = stage_vec_cols(I["L0_ffn_bk"], 8)
    bvr = _f32c(np.asarray(I["L0_ffn_bv"], np.float32).reshape(1, 1024))
    g1c = stage_vec_cols(I["L0_ffn_ln1_g"], 8)
    b1c = stage_vec_cols(I["L0_ffn_ln1_b"], 8)
    in_maps = []
    for c, (b, k) in enumerate(CORES):
        in_maps.append(dict(x1in=np.ascontiguousarray(x1_parts[c]),
                            wq=wq, wk=wk, wv=wv, bqc=bqc, bkc=bkc, bvr=bvr, g1c=g1c, b1c=b1c))
    return in_maps


def build_B2(ctx=None, io=None):
    standalone = ctx is None
    if standalone:
        ctx = LaunchCtx("B2")
        p = ctx.start()
    else:
        p = ctx.prog
    ctx.io = io
    nc = ctx.nc
    S = 8192

    q_dd = ctx.din("qin", [1024, SH], BF16)
    kf_d = ctx.din("kf", [1024, S], BF16)
    vf_d = ctx.din("vf", [S, 1024], BF16)
    lnb_d = ctx.din("lnbin", [1024, SH])
    x1_d = ctx.din("x1in2", [1024, SH])
    wo_d = ctx.din("wo", [1024, 1024])
    bo_d = ctx.din("boc", [128, 8])
    g2_d = ctx.din("g2c", [128, 8])
    b2_d = ctx.din("b2c", [128, 8])
    gw_d = ctx.din("gatew", [1024, 8])
    gb_d = ctx.din("gatebc", [8, 1])
    w1_d = ctx.din("w1m", [8 * 4096, 1024])
    b1_d = ctx.din("b1m", [8, 4096])
    w2_d = ctx.din("w2m", [8 * 1024, 4096])
    b2m_d = ctx.din("b2m", [8, 1024])
    esel_d = ctx.din("esel", [8, 1024])
    idf_d = ctx.din("idf2", [128, 128])
    x2_d, b_x2 = ctx.dout("x2", [1024, SH])
    od_d = ctx.nc.dram_tensor("oscr", [1024, SH], BF16, kind="Internal").ap()
    b_od = p.buf("oscr")
    ar_d = ctx.nc.dram_tensor("arscr", [1024, SH], F32, kind="Internal").ap()
    b_ar = p.buf("arscr")

    kx, b_kx = ctx.sbuf("kx", [65, S], BF16)
    vx, b_vx = ctx.sbuf("vx", [128, 64 * 65], BF16)
    qx, b_qx = ctx.sbuf("qx", [65, SH], BF16)
    ptb, b_ptb = ctx.sbuf("ptb", [128, 2 * 512], BF16)
    sq, b_sq = ctx.sbuf("sq", [128, 512], F32)
    lbuf, b_lbuf = ctx.sbuf("lbuf", [128, 512], F32)
    bcs, b_bcs = ctx.sbuf("bcs", [64, 512], F32)
    osb, b_osb = ctx.sbuf("osb", [64, 512], BF16)
    ob0, b_ob0 = ctx.sbuf("ob0", [128, 512], BF16)
    ob1, b_ob1 = ctx.sbuf("ob1", [128, 512], BF16)
    art, b_art = ctx.sbuf("art", [128, 8 * 512], F32)
    hn2, b_hn2 = ctx.sbuf("hn2", [128, 8 * SH], BF16)
    hn2f, b_hn2f = ctx.sbuf("hn2f", [128, 512], F32)
    selfm, b_selfm = ctx.sbuf("selfm", [8, SH], BF16)
    lgs, b_lgs = ctx.sbuf("lgs", [8, 512], F32)
    lgt, b_lgt = ctx.sbuf("lgt", [128, 8], F32)
    selt, b_selt = ctx.sbuf("selt", [128, 8], F32)
    maxc, b_maxc = ctx.sbuf("maxc", [128, 1], F32)
    midb, b_midb = ctx.sbuf("midb", [128, 8 * SH], BF16)
    outacc, b_outacc = ctx.sbuf("outacc", [128, 8 * SH], BF16)
    mskb, b_mskb = ctx.sbuf("mskb", [128, 4 * 512], BF16)
    wst0, b_wst0 = ctx.sbuf("wst0", [128, 1024], F32)
    wst1, b_wst1 = ctx.sbuf("wst1", [128, 1024], F32)
    wcb0, b_wcb0 = ctx.sbuf("wcb0", [128, 1024], BF16)
    wcb1, b_wcb1 = ctx.sbuf("wcb1", [128, 1024], BF16)
    boc, b_boc = ctx.sbuf("boc_s", [128, 8], F32)
    g2c, b_g2c = ctx.sbuf("g2c_s", [128, 8], F32)
    b2c, b_b2c = ctx.sbuf("b2c_s", [128, 8], F32)
    gwc, b_gwc = ctx.sbuf("gwc", [128, 8 * 8], F32)
    gbc, b_gbc = ctx.sbuf("gbc", [8, 1], F32)
    b1s, b_b1s = ctx.sbuf("b1s", [8, 128], BF16)
    b1st, b_b1st = ctx.sbuf("b1st", [8, 128], F32)
    ones8b, b_ones8b = ctx.sbuf("ones8b", [8, 512], BF16)
    eself, b_eself = ctx.sbuf("eself", [8, 1024], F32)
    b2f, b_b2f = ctx.sbuf("b2f", [8, 1024], BF16)
    esel, b_esel = ctx.sbuf("esel_s", [8, 1024], BF16)
    idf, b_idf = ctx.sbuf("idf_s", [128, 128], F32)
    onesc, b_onesc = ctx.sbuf("onesc", [128, 1], F32)
    ones1, b_ones1 = ctx.sbuf("ones1", [1, 128], F32)
    ones64, b_ones64 = ctx.sbuf("ones64", [128, 64], F32)
    srow, b_srow = ctx.sbuf("srow", [1, 4 * 512], F32)
    tv, b_tv = ctx.sbuf("tv", [128, 512], F32)
    xts, b_xts = ctx.sbuf("xts", [128, 512], F32)

    pst0, b_pst0 = ctx.psum("pst0", [128, 512])
    pst1, b_pst1 = ctx.psum("pst1", [128, 512])
    po, b_po = ctx.psum("po", [65, 512])
    pbc, b_pbc = ctx.psum("pbc", [64, 512])
    pm, b_pm = ctx.psum("pm", [128, 512])
    pw0, b_pw0 = ctx.psum("pw0", [128, 512])
    pw1, b_pw1 = ctx.psum("pw1", [128, 512])
    ps1, b_ps1 = pbc[0:1, :], b_pbc

    emit(p, "vector", lambda v: v.memset(onesc[:], 1.0), [], [b_onesc])
    emit(p, "vector", lambda v: v.memset(ones1[:], 1.0), [], [b_ones1])
    emit(p, "vector", lambda v: v.memset(ones64[:], 1.0), [], [b_ones64])
    emit(p, "vector", lambda v: v.memset(kx[64:65, :], 1.0), [], [b_kx])
    emit(p, "vector", lambda v: v.memset(vx[:, 64::65], 1.0), [], [b_vx])
    emit(p, "vector", lambda v: v.memset(outacc[:], 0.0), [], [b_outacc])
    for (dst, b_dst, src) in ((boc, b_boc, bo_d), (g2c, b_g2c, g2_d), (b2c, b_b2c, b2_d),
                              (gbc, b_gbc, gb_d), (idf, b_idf, idf_d)):
        p.dma("sync", dst[:], src[:, :], writes=[b_dst])
    p.dma("sync", wst0[0:8, 0:1024], b2m_d[:, :], writes=[b_wst0])
    emit(p, "vector", lambda v: v.tensor_copy(b2f[:], wst0[0:8, 0:1024]), [b_wst0], [b_b2f])
    p.dma("sync", wst1[0:8, 0:1024], esel_d[:, :], writes=[b_wst1])
    emit(p, "vector", lambda v: v.tensor_copy(esel[:], wst1[0:8, 0:1024]), [b_wst1], [b_esel])
    for kc in range(8):
        p.dma("sync", gwc[:, kc * 8:(kc + 1) * 8], gw_d[kc * 128:(kc + 1) * 128, :], writes=[b_gwc])
    emit(p, "vector", lambda v: v.memset(ones8b[:], 1.0), [], [b_ones8b])
    p.dma("sync", eself[:], esel_d[:, :], writes=[b_eself])

    cb5, b_cb5 = ctx.cbias(1e-5, 1)
    cb0125, b_cb0125 = ctx.cbias(0.125, 128)

    for h in range(16):
        p.dma("sync", kx[0:64, :], kf_d[64 * h:64 * h + 64, :], writes=[b_kx])
        p.dma("sync", vx[:, :].rearrange("p (s c) -> p s c", c=65)[:, :, 0:64],
              vf_d[:, 64 * h:64 * h + 64].rearrange("(s p) c -> p s c", p=128), writes=[b_vx])
        p.dma("sync", qx[0:64, :], q_dd[64 * h:64 * h + 64, :], writes=[b_qx])
        emit(p, "vector", lambda v: v.memset(srow[0:1, 0:1], 0.0), [], [b_srow])
        for sc in range(16):
            emit(p, "scalar", lambda s, sc=sc: s.activation(out=sq[0:64, 0:512], in_=kx[0:64, sc * 512:(sc + 1) * 512], func=AF.Square),
                 [b_kx], [b_sq])
            emit(p, "tensor", lambda t, sc=sc: t.matmul(ps1[:, :], onesc[0:64, :], sq[0:64, 0:512], start=True, stop=True),
                 [b_onesc, b_sq], [b_ps1])
            emit(p, "vector", lambda v, sc=sc: v.reduce_max(srow[0:1, 2:3], ps1[0:1, :], axis=AX.X),
                 [b_ps1], [b_srow])
            emit(p, "vector", lambda v: v.tensor_tensor(srow[0:1, 0:1], srow[0:1, 0:1], srow[0:1, 2:3], ALU.max),
                 [b_srow], [b_srow])
        emit(p, "scalar", lambda s: s.activation(out=srow[0:1, 0:1], in_=srow[0:1, 0:1], func=AF.Sqrt), [b_srow], [b_srow])
        QN = srow[0:1, 512:1024]
        for qt in range(4):
            emit(p, "scalar", lambda s, qt=qt: s.activation(out=sq[0:64, 0:512], in_=qx[0:64, qt * 512:(qt + 1) * 512], func=AF.Square),
                 [b_qx], [b_sq])
            emit(p, "tensor", lambda t: t.matmul(ps1[:, :], onesc[0:64, :], sq[0:64, 0:512], start=True, stop=True),
                 [b_onesc, b_sq], [b_ps1])
            emit(p, "scalar", lambda s, QN=QN: s.activation(out=QN, in_=ps1[0:1, :], func=AF.Sqrt),
                 [b_ps1], [b_srow])
            emit(p, "vector", lambda v, QN=QN: v.tensor_single_scalar(QN, QN, srow[0:1, 0:1], ALU.mult),
                 [b_srow], [b_srow])
            emit(p, "vector", lambda v, QN=QN: v.tensor_scalar_mul(QN, QN, -1.0), [b_srow], [b_srow])
            emit(p, "tensor", lambda t, QN=QN: t.matmul(po[64:65, :], ones1[0:1, 0:1], QN,
                                                        start=True, stop=True), [b_ones1, b_srow], [b_po])
            emit(p, "vector", lambda v, qt=qt: v.tensor_copy(qx[64:65, qt * 512:(qt + 1) * 512], po[64:65, :]),
                 [b_po], [b_qx])
        for qt in range(4):
            for sc in range(64):
                pst, b_pst = (pst0, b_pst0) if sc % 2 == 0 else (pst1, b_pst1)
                emit(p, "tensor", lambda t, sc=sc, qt=qt, pst=pst: t.matmul(
                    pst[:, :], kx[:, sc * 128:(sc + 1) * 128], qx[:, qt * 512:(qt + 1) * 512],
                    start=True, stop=True), [b_kx, b_qx], [b_pst])
                pt_sl = slice((sc % 2) * 512, (sc % 2) * 512 + 512)
                emit(p, "scalar", lambda s, pst=pst, pt_sl=pt_sl, cb0125=cb0125: s.activation(
                    out=ptb[:, pt_sl], in_=pst[:, :], func=AF.Exp, scale=cb0125),
                    [b_pst, b_cb0125], [b_ptb])
                emit(p, "tensor", lambda t, sc=sc, pt_sl=pt_sl: t.matmul(
                    po[:, :], vx[:, sc * 65:(sc + 1) * 65], ptb[:, pt_sl],
                    start=(sc == 0), stop=(sc == 63)), [b_vx, b_ptb], [b_po])
            emit(p, "vector", lambda v: v.reciprocal(lbuf[64:65, 0:512], po[64:65, :]), [b_po], [b_lbuf])
            emit(p, "tensor", lambda t: t.matmul(pbc[:, :], ones64[64:65, 0:64], lbuf[64:65, 0:512], start=True, stop=True),
                 [b_ones64, b_lbuf], [b_pbc])
            emit(p, "vector", lambda v: v.tensor_copy(bcs[:], pbc[:, :]), [b_pbc], [b_bcs])
            emit(p, "vector", lambda v: v.tensor_tensor(osb[:], po[0:64, :], bcs[:], ALU.mult), [b_po, b_bcs], [b_osb])
            p.dma("gpsimd", od_d[64 * h:64 * h + 64, qt * 512:(qt + 1) * 512], osb[:], reads=[b_osb], writes=[b_od])

    for ti in range(4):
        o = ti * 512
        for oc in range(8):
            pw, b_pw = (pw0, b_pw0) if oc % 2 == 0 else (pw1, b_pw1)
            wst, b_wst = (wst0, b_wst0) if oc % 2 == 0 else (wst1, b_wst1)
            wcb, b_wcb = (wcb0, b_wcb0) if oc % 2 == 0 else (wcb1, b_wcb1)
            p.dma("sync", wst[:, :], wo_d[oc * 128:(oc + 1) * 128, :], writes=[b_wst])
            emit(p, "vector", lambda v, wst=wst, wcb=wcb: v.tensor_copy(wcb[:], wst[:]), [b_wst], [b_wcb])
            for kc in range(8):
                obx, b_obx = (ob0, b_ob0) if kc % 2 == 0 else (ob1, b_ob1)
                p.dma("sync", obx[:], od_d[kc * 128:(kc + 1) * 128, o:o + 512],
                      reads=[b_od], writes=[b_obx])
                emit(p, "tensor", lambda t, kc=kc, pw=pw, wcb=wcb, obx=obx: t.matmul(
                    pw[:, :], wcb[:, kc * 128:(kc + 1) * 128], obx[:],
                    start=(kc == 0), stop=(kc == 7)), [b_wcb, b_obx], [b_pw])
            p.dma("sync", xts[:], lnb_d[oc * 128:(oc + 1) * 128, o:o + 512], writes=[b_xts])
            emit(p, "vector", lambda v, oc=oc, pw=pw: v.tensor_scalar_add(tv[:], pw[:, :], boc[:, oc:oc + 1]),
                 [b_pw, b_boc], [b_tv])
            emit(p, "vector", lambda v, oc=oc: v.tensor_add(art[:, oc * 512:(oc + 1) * 512], tv[:], xts[:]),
                 [b_tv, b_xts], [b_art])
            p.dma("gpsimd", ar_d[oc * 128:(oc + 1) * 128, o:o + 512], art[:, oc * 512:(oc + 1) * 512],
                  reads=[b_art], writes=[b_ar])
        def mm1(t):
            last = None
            for c in range(8):
                last = t.matmul(ps1[:, :], onesc[:, :], art[:, c * 512:(c + 1) * 512], start=(c == 0), stop=(c == 7))
            return last
        p.op("tensor", mm1, reads=[b_onesc, b_art], writes=[b_ps1])
        M = srow[0:1, 512:1024]
        emit(p, "vector", lambda v, M=M: v.tensor_scalar_mul(M, ps1[0:1, :], 1.0 / 1024.0), [b_ps1], [b_srow])
        for c in range(8):
            emit(p, "scalar", lambda s, c=c: s.activation(out=sq[:], in_=art[:, c * 512:(c + 1) * 512], func=AF.Square),
                 [b_art], [b_sq])
            emit(p, "tensor", lambda t, c=c: t.matmul(ps1[:, :], onesc[:, :], sq[:, :], start=(c == 0), stop=(c == 7)),
                 [b_onesc, b_sq], [b_ps1])
        T1 = srow[0:1, 1024:1536]
        R = srow[0:1, 1536:2048]
        emit(p, "vector", lambda v, T1=T1: v.tensor_scalar_mul(T1, ps1[0:1, :], 1.0 / 1024.0), [b_ps1], [b_srow])
        emit(p, "scalar", lambda s, M=M, R=R: s.activation(out=R, in_=M, func=AF.Square), [b_srow], [b_srow])
        emit(p, "vector", lambda v, T1=T1, R=R: v.tensor_sub(T1, T1, R), [b_srow], [b_srow])
        emit(p, "scalar", lambda s, T1=T1, cb5=cb5: s.activation(out=T1, in_=T1, func=AF.Sqrt, bias=cb5), [b_srow, b_cb5], [b_srow])
        emit(p, "vector", lambda v, T1=T1, R=R: v.reciprocal(R, T1), [b_srow], [b_srow])
        emit(p, "tensor", lambda t, M=M: t.matmul(pm[:, :], ones1[0:1, :], M, start=True, stop=True),
             [b_ones1, b_srow], [b_pm])
        emit(p, "tensor", lambda t, R=R: t.matmul(pw0[:, :], ones1[0:1, :], R, start=True, stop=True),
             [b_ones1, b_srow], [b_pw0])
        for c in range(8):
            emit(p, "vector", lambda v, c=c: v.tensor_sub(hn2f[:], art[:, c * 512:(c + 1) * 512], pm[:, :]),
                 [b_art, b_pm], [b_hn2f])
            emit(p, "vector", lambda v: v.tensor_mul(hn2f[:], hn2f[:], pw0[:, :]), [b_hn2f, b_pw0], [b_hn2f])
            emit(p, "vector", lambda v, c=c: v.tensor_scalar(hn2f[:], hn2f[:], g2c[:, c:c + 1], b2c[:, c:c + 1],
                                                             ALU.mult, ALU.add), [b_hn2f, b_g2c, b_b2c], [b_hn2f])
            emit(p, "vector", lambda v, c=c, o=o: v.tensor_copy(hn2[:, c * SH + o: c * SH + o + 512], hn2f[:]),
                 [b_hn2f], [b_hn2])
            emit(p, "tensor", lambda t, c=c: t.matmul(pst0[0:8, :], gwc[:, c * 8:(c + 1) * 8], hn2f[:, :],
                                                      start=(c == 0), stop=(c == 7)), [b_gwc, b_hn2f], [b_pst0])
        emit(p, "vector", lambda v: v.tensor_scalar_add(lgs[:, :], pst0[0:8, :], gbc[:, 0:1]), [b_pst0, b_gbc], [b_lgs])
        for cc in range(4):
            emit_transpose(p, pm[0:128, 0:8], lgs[:, cc * 128:(cc + 1) * 128], idf[0:8, 0:8], [b_lgs, b_idf], [b_pm])
            emit(p, "vector", lambda v: v.tensor_copy(lgt[:], pm[0:128, 0:8]), [b_pm], [b_lgt])
            emit(p, "vector", lambda v: v.reduce_max(maxc[:], lgt[:], axis=AX.X), [b_lgt], [b_maxc])
            emit(p, "vector", lambda v: v.tensor_scalar(selt[:], lgt[:], maxc[:, 0:1], None, ALU.is_ge),
                 [b_lgt, b_maxc], [b_selt])
            emit_transpose(p, pm[0:8, 8:136], selt[:, :], idf[:, :], [b_selt, b_idf], [b_pm])
            emit(p, "vector", lambda v, o=o, cc=cc: v.tensor_copy(selfm[:, o + cc * 128: o + (cc + 1) * 128], pm[0:8, 8:136]),
                 [b_pm], [b_selfm])

    for e in range(8):
        for t4 in range(4):
            emit(p, "tensor", lambda t, e=e, t4=t4: t.matmul(
                pm[:, :], esel[:, e * 128:(e + 1) * 128], selfm[:, t4 * 512:(t4 + 1) * 512], start=True, stop=True),
                [b_esel, b_selfm], [b_pm])
            emit(p, "vector", lambda v, t4=t4: v.tensor_copy(mskb[:, t4 * 512:(t4 + 1) * 512], pm[:, :]),
                 [b_pm], [b_mskb])
        for jb in range(4):
            for jc in range(8):
                j = jb * 8 + jc
                wst, b_wst = (wst0, b_wst0) if jc % 2 == 0 else (wst1, b_wst1)
                wcb, b_wcb = (wcb0, b_wcb0) if jc % 2 == 0 else (wcb1, b_wcb1)
                p.dma("sync", wst[:, :], w1_d[(e * 32 + j) * 128:(e * 32 + j + 1) * 128, :], writes=[b_wst])
                emit(p, "vector", lambda v, wst=wst, wcb=wcb: v.tensor_copy(wcb[:], wst[:]), [b_wst], [b_wcb])
                p.dma("sync", b1st[:], b1_d[:, j * 128:(j + 1) * 128], writes=[b_b1st])
                emit(p, "vector", lambda v, e=e: v.tensor_scalar_mul(b1s[:], b1st[:], eself[:, e * 128:e * 128 + 1]),
                     [b_b1st, b_eself], [b_b1s])
                for t4 in range(4):
                    pw, b_pw = (pw0, b_pw0) if t4 % 2 == 0 else (pw1, b_pw1)
                    def mmw1b(t, wcb=wcb, t4=t4, pw=pw, e=e, j=j):
                        last = None
                        for kc in range(8):
                            last = t.matmul(pw[:, :], wcb[:, kc * 128:(kc + 1) * 128],
                                            hn2[:, kc * SH + t4 * 512: kc * SH + t4 * 512 + 512],
                                            start=(kc == 0), stop=False)
                        last = t.matmul(pw[:, :], b1s[0:8, :], ones8b[0:8, :],
                                        start=False, stop=True)
                        return last
                    _ = None
                    p.op("tensor", mmw1b, reads=[b_wcb, b_hn2, b_b1s, b_ones8b], writes=[b_pw])
                    emit(p, "vector", lambda v, jc=jc, t4=t4, pw=pw: v.scalar_tensor_tensor(
                        midb[:, jc * SH + t4 * 512: jc * SH + t4 * 512 + 512], pw[:, :], 0.0,
                        mskb[:, t4 * 512:(t4 + 1) * 512], ALU.max, ALU.mult),
                        [b_pw, b_mskb], [b_midb])
            for oc in range(8):
                wst, b_wst = (wst0, b_wst0) if oc % 2 == 0 else (wst1, b_wst1)
                wcb, b_wcb = (wcb0, b_wcb0) if oc % 2 == 0 else (wcb1, b_wcb1)
                p.dma("sync", wst[:, :], w2_d[e * 1024 + oc * 128: e * 1024 + (oc + 1) * 128, jb * 1024:(jb + 1) * 1024],
                      writes=[b_wst])
                emit(p, "vector", lambda v, wst=wst, wcb=wcb: v.tensor_copy(wcb[:], wst[:]), [b_wst], [b_wcb])
                for t4 in range(4):
                    pw, b_pw = (pw0, b_pw0) if t4 % 2 == 0 else (pw1, b_pw1)
                    def mmw2(t, wcb=wcb, t4=t4, pw=pw):
                        last = None
                        for kc in range(8):
                            last = t.matmul(pw[:, :], wcb[:, kc * 128:(kc + 1) * 128],
                                            midb[:, kc * SH + t4 * 512: kc * SH + t4 * 512 + 512],
                                            start=(kc == 0), stop=(kc == 7))
                        return last
                    p.op("tensor", mmw2, reads=[b_wcb, b_midb], writes=[b_pw])
                    emit(p, "vector", lambda v, oc=oc, t4=t4, pw=pw: v.tensor_tensor(
                        outacc[:, oc * SH + t4 * 512: oc * SH + t4 * 512 + 512], pw[:, :],
                        outacc[:, oc * SH + t4 * 512: oc * SH + t4 * 512 + 512], ALU.add),
                        [b_pw, b_outacc], [b_outacc])

    for ti in range(4):
        o = ti * 512
        for oc in range(8):
            emit(p, "tensor", lambda t, oc=oc, o=o: t.matmul(
                pm[:, :], b2f[:, oc * 128:(oc + 1) * 128], selfm[:, o:o + 512], start=True, stop=True),
                [b_b2f, b_selfm], [b_pm])
            p.dma("sync", xts[:], x1_d[oc * 128:(oc + 1) * 128, o:o + 512], writes=[b_xts])
            p.dma("sync", tv[:], ar_d[oc * 128:(oc + 1) * 128, o:o + 512], reads=[b_ar], writes=[b_tv])
            emit(p, "vector", lambda v, oc=oc, o=o: v.tensor_add(sq[:], pm[:, :],
                 outacc[:, oc * SH + o: oc * SH + o + 512]), [b_pm, b_outacc], [b_sq])
            emit(p, "vector", lambda v: v.tensor_add(sq[:], sq[:], tv[:]), [b_sq, b_tv], [b_sq])
            emit(p, "vector", lambda v: v.tensor_add(sq[:], sq[:], xts[:]), [b_sq, b_xts], [b_sq])
            p.dma("gpsimd", x2_d[oc * 128:(oc + 1) * 128, o:o + 512], sq[:], reads=[b_sq], writes=[b_x2], out_dram=True)
    ctx.io = None
    if standalone:
        p.finalize()
    return ctx


def stage_B2(I, b1_res, x1_parts):
    kf = {}
    vf = {}
    for b in range(2):
        kf[b] = np.concatenate([np.asarray(b1_res[b * 4 + k]["k"]) for k in range(4)], axis=1)
        vf[b] = np.concatenate([np.asarray(b1_res[b * 4 + k]["vtm"]) for k in range(4)], axis=0)
    wo = stage_ktiles(I["L0_ffn_wo"])
    boc = stage_vec_cols(I["L0_ffn_bo"], 8)
    g2c = stage_vec_cols(I["L0_ffn_ln2_g"], 8)
    b2c = stage_vec_cols(I["L0_ffn_ln2_b"], 8)
    gatew = _f32c(I["L0_ffn_gate_w"])
    gatebc = _f32c(np.asarray(I["L0_ffn_gate_b"], np.float32).reshape(8, 1))
    w1m = stage_ktiles3(I["L0_ffn_w1"], kc=8)
    b1m = _f32c(I["L0_ffn_b1"])
    w2m = stage_ktiles3(I["L0_ffn_w2"], kc=32)
    b2m = _f32c(I["L0_ffn_b2"])
    erow = np.zeros((8, 4096), np.float32)
    esel = np.zeros((8, 1024), np.float32)
    for e in range(8):
        erow[e, e * 512:(e + 1) * 512] = 1.0
        esel[e, e * 128:(e + 1) * 128] = 1.0
    idf2 = np.eye(128, dtype=np.float32)
    in_maps = []
    for c, (b, k) in enumerate(CORES):
        in_maps.append(dict(
            qin=np.asarray(b1_res[c]["q"]), kf=kf[b], vf=vf[b],
            lnbin=np.asarray(b1_res[c]["lnb"]), x1in2=np.ascontiguousarray(x1_parts[c]),
            wo=wo, boc=boc, g2c=g2c, b2c=b2c, gatew=gatew, gatebc=gatebc,
            w1m=w1m, b1m=b1m, w2m=w2m, b2m=b2m, esel=esel, idf2=idf2))
    return in_maps


def build_C1(ctx=None, io=None):
    standalone = ctx is None
    if standalone:
        ctx = LaunchCtx("C1")
        p = ctx.start()
    else:
        p = ctx.prog
    ctx.io = io
    nc = ctx.nc

    x_d = ctx.din("x2in", [1024, SH])
    wq_d = ctx.din("wqT", [1024, 1024])
    wk_d = ctx.din("wkT", [1024, 1024])
    wv_d = ctx.din("wvT", [1024, 1024])
    q_d, b_q = ctx.dout("q", [1024, SH], BF16)
    k_d, b_k = ctx.dout("k", [1024, SH], BF16)
    v_d, b_v = ctx.dout("vtm", [SH, 1024], BF16)

    xt, b_xt = ctx.sbuf("xt", [128, 8 * 512], F32)
    lnb, b_lnb = ctx.sbuf("lnb", [128, 8 * 512], BF16)
    sq, b_sq = ctx.sbuf("sq", [128, 512], F32)
    tv, b_tv = ctx.sbuf("tv", [128, 512], F32)
    wqb, b_wqb = ctx.sbuf("wqb", [128, 8 * 1024], BF16)
    wkb, b_wkb = ctx.sbuf("wkb", [128, 8 * 1024], BF16)
    wvb, b_wvb = ctx.sbuf("wvb", [128, 8 * 1024], BF16)
    wst0, b_wst0 = ctx.sbuf("wst0", [128, 1024], F32)
    wst1, b_wst1 = ctx.sbuf("wst1", [128, 1024], F32)
    onesc, b_onesc = ctx.sbuf("onesc", [128, 1], F32)
    ones1, b_ones1 = ctx.sbuf("ones1", [1, 128], F32)
    srow, b_srow = ctx.sbuf("srow", [1, 4 * 512], F32)
    outs, b_outs = ctx.sbuf("outs", [128, 512], BF16)
    vts, b_vts = ctx.sbuf("vts", [128, 512], BF16)

    ps1, b_ps1 = ctx.psum("ps1", [1, 512])
    ps2, b_ps2 = ctx.psum("ps2", [1, 512])
    pbm, b_pbm = ctx.psum("pbm", [128, 512])
    pbr, b_pbr = ctx.psum("pbr", [128, 512])
    pp0, b_pp0 = ctx.psum("pp0", [128, 512])
    pp1, b_pp1 = ctx.psum("pp1", [128, 512])

    emit(p, "vector", lambda v: v.memset(onesc[:], 1.0), [], [b_onesc])
    emit(p, "vector", lambda v: v.memset(ones1[:], 1.0), [], [b_ones1])
    for (wsb, b_wsb, wd) in ((wqb, b_wqb, wq_d), (wkb, b_wkb, wk_d), (wvb, b_wvb, wv_d)):
        for oc in range(8):
            wst, b_wst = (wst0, b_wst0) if oc % 2 == 0 else (wst1, b_wst1)
            p.dma("sync", wst[:, :], wd[oc * 128:(oc + 1) * 128, :], writes=[b_wst])
            emit(p, "vector", lambda v, wsb=wsb, oc=oc, wst=wst: v.tensor_copy(
                wsb[:, oc * 1024:(oc + 1) * 1024], wst[:]), [b_wst], [b_wsb])

    cb5, b_cb5 = ctx.cbias(1e-5, 1)

    for ti in range(4):
        o = ti * 512
        for c in range(8):
            p.dma("sync", xt[:, c * 512:(c + 1) * 512], x_d[c * 128:(c + 1) * 128, o:o + 512], writes=[b_xt])
        def mm1(t):
            last = None
            for c in range(8):
                last = t.matmul(ps1[:, :], onesc[:, :], xt[:, c * 512:(c + 1) * 512], start=(c == 0), stop=(c == 7))
            return last
        p.op("tensor", mm1, reads=[b_onesc, b_xt], writes=[b_ps1])
        for c in range(8):
            emit(p, "scalar", lambda s, c=c: s.activation(out=sq[:], in_=xt[:, c * 512:(c + 1) * 512], func=AF.Square),
                 [b_xt], [b_sq])
            emit(p, "tensor", lambda t, c=c: t.matmul(ps2[:, :], onesc[:, :], sq[:, :], start=(c == 0), stop=(c == 7)),
                 [b_onesc, b_sq], [b_ps2])
        M, R, T1 = srow[0:1, 0:512], srow[0:1, 512:1024], srow[0:1, 1024:1536]
        emit(p, "vector", lambda v, M=M: v.tensor_scalar_mul(M, ps1[0:1, :], 1.0 / 1024.0), [b_ps1], [b_srow])
        emit(p, "vector", lambda v, T1=T1: v.tensor_scalar_mul(T1, ps2[0:1, :], 1.0 / 1024.0), [b_ps2], [b_srow])
        emit(p, "scalar", lambda s, M=M, R=R: s.activation(out=R, in_=M, func=AF.Square), [b_srow], [b_srow])
        emit(p, "vector", lambda v, T1=T1, R=R: v.tensor_sub(T1, T1, R), [b_srow], [b_srow])
        emit(p, "scalar", lambda s, T1=T1, cb5=cb5: s.activation(out=T1, in_=T1, func=AF.Sqrt, bias=cb5), [b_srow, b_cb5], [b_srow])
        emit(p, "vector", lambda v, T1=T1, R=R: v.reciprocal(R, T1), [b_srow], [b_srow])
        emit(p, "tensor", lambda t, M=M: t.matmul(pbm[:, :], ones1[0:1, :], M, start=True, stop=True),
             [b_ones1, b_srow], [b_pbm])
        emit(p, "tensor", lambda t, R=R: t.matmul(pbr[:, :], ones1[0:1, :], R, start=True, stop=True),
             [b_ones1, b_srow], [b_pbr])
        for c in range(8):
            sl = slice(c * 512, (c + 1) * 512)
            emit(p, "vector", lambda v, sl=sl: v.tensor_sub(tv[:], xt[:, sl], pbm[:, :]), [b_xt, b_pbm], [b_tv])
            emit(p, "vector", lambda v, sl=sl: v.tensor_tensor(lnb[:, sl], tv[:], pbr[:, :], ALU.mult),
                 [b_tv, b_pbr], [b_lnb])
        for (wsb, b_wsb, dram, b_dram) in ((wqb, b_wqb, q_d, b_q), (wkb, b_wkb, k_d, b_k)):
            for oc in range(8):
                pp, b_pp = (pp0, b_pp0) if oc % 2 == 0 else (pp1, b_pp1)
                def mmp(t, wsb=wsb, oc=oc, pp=pp):
                    last = None
                    for kc in range(8):
                        last = t.matmul(pp[:, :], wsb[:, oc * 1024 + kc * 128: oc * 1024 + (kc + 1) * 128],
                                        lnb[:, kc * 512:(kc + 1) * 512], start=(kc == 0), stop=(kc == 7))
                    return last
                p.op("tensor", mmp, reads=[b_wsb, b_lnb], writes=[b_pp])
                emit(p, "vector", lambda v, pp=pp: v.tensor_copy(outs[:], pp[:, :]), [b_pp], [b_outs])
                p.dma("gpsimd", dram[oc * 128:(oc + 1) * 128, o:o + 512], outs[:], reads=[b_outs], writes=[b_dram], out_dram=True)
        for sc in range(4):
            so = o + sc * 128
            for oc in range(8):
                pp, b_pp = (pp0, b_pp0) if oc % 2 == 0 else (pp1, b_pp1)
                def mmv(t, sc=sc, oc=oc, pp=pp):
                    last = None
                    for kc in range(8):
                        last = t.matmul(pp[:, 0:128], lnb[:, kc * 512 + sc * 128: kc * 512 + (sc + 1) * 128],
                                        wvb[:, oc * 1024 + kc * 128: oc * 1024 + (kc + 1) * 128],
                                        start=(kc == 0), stop=(kc == 7))
                    return last
                p.op("tensor", mmv, reads=[b_lnb, b_wvb], writes=[b_pp])
                emit(p, "vector", lambda v, pp=pp: v.tensor_copy(vts[:, 0:128], pp[:, 0:128]), [b_pp], [b_vts])
                p.dma("gpsimd", v_d[so:so + 128, oc * 128:(oc + 1) * 128], vts[:, 0:128], reads=[b_vts], writes=[b_v], out_dram=True)
    ctx.io = None
    if standalone:
        p.finalize()
    return ctx


def stage_C1(I, x2_parts):
    wqT = stage_ktiles(np.asarray(I["L1_attn_Wq"], np.float32).T)
    wkT = stage_ktiles(np.asarray(I["L1_attn_Wk"], np.float32).T)
    wvT = stage_ktiles(np.asarray(I["L1_attn_Wv"], np.float32).T)
    in_maps = []
    for c, (b, k) in enumerate(CORES):
        in_maps.append(dict(x2in=np.ascontiguousarray(x2_parts[c]), wqT=wqT, wkT=wkT, wvT=wvT))
    return in_maps


def build_C2(ctx=None, io=None):
    standalone = ctx is None
    if standalone:
        ctx = LaunchCtx("C2")
        p = ctx.start()
    else:
        p = ctx.prog
    ctx.io = io
    nc = ctx.nc
    S = 8192

    q_dd = ctx.din("qn", [256, S], BF16)
    qp_d = ctx.din("qp", [256, S], BF16)
    kf_d = ctx.din("kn", [256, S], BF16)
    kp_d = ctx.din("kp", [256, S], BF16)
    vf_d = ctx.din("vf", [S, 256], BF16)
    cosk_d = ctx.din("cosk", [128, S], BF16)
    sink_d = ctx.din("sink", [128, S], BF16)
    dmask_d = ctx.din("dmask", [128, 4 * 512])
    o_d, b_o = ctx.dout("oh", [256, S], BF16)

    kx, b_kx = ctx.sbuf("kx", [65, S], BF16)
    vx, b_vx = ctx.sbuf("vx", [128, 64 * 65], BF16)
    qx, b_qx = ctx.sbuf("qx", [65, S], BF16)
    rp, b_rp = ctx.sbuf("rp", [64, S], BF16)
    rp2, b_rp2 = ctx.sbuf("rp2", [64, S], BF16)
    cosk, b_cosk = ctx.sbuf("cosk_s", [128, S], BF16)
    sink, b_sink = ctx.sbuf("sink_s", [128, S], BF16)
    dmask, b_dmask = ctx.sbuf("dmask_s", [128, 4 * 512], F32)
    ptb, b_ptb = ctx.sbuf("ptb", [128, 2 * 512], BF16)
    sq, b_sq = ctx.sbuf("sq", [128, 512], F32)
    tvm, b_tvm = ctx.sbuf("tvm", [128, 512], F32)
    lbuf, b_lbuf = ctx.sbuf("lbuf", [128, 512], F32)
    bcs, b_bcs = ctx.sbuf("bcs", [64, 512], F32)
    osb, b_osb = ctx.sbuf("osb", [64, 512], BF16)
    onesc, b_onesc = ctx.sbuf("onesc", [128, 1], F32)
    ones1, b_ones1 = ctx.sbuf("ones1", [1, 128], F32)
    ones64, b_ones64 = ctx.sbuf("ones64", [128, 64], F32)
    srow, b_srow = ctx.sbuf("srow", [1, 4 * 512], F32)

    pst0, b_pst0 = ctx.psum("pst0", [128, 512])
    pst1, b_pst1 = ctx.psum("pst1", [128, 512])
    po, b_po = ctx.psum("po", [65, 512])
    pbc, b_pbc = ctx.psum("pbc", [64, 512])
    ps1, b_ps1 = ctx.psum("ps1", [1, 512])

    emit(p, "vector", lambda v: v.memset(onesc[:], 1.0), [], [b_onesc])
    emit(p, "vector", lambda v: v.memset(ones1[:], 1.0), [], [b_ones1])
    emit(p, "vector", lambda v: v.memset(ones64[:], 1.0), [], [b_ones64])
    emit(p, "vector", lambda v: v.memset(kx[64:65, :], 1.0), [], [b_kx])
    emit(p, "vector", lambda v: v.memset(vx[:, 64::65], 1.0), [], [b_vx])
    p.dma("sync", cosk[:], cosk_d[:, :], writes=[b_cosk])
    p.dma("sync", sink[:], sink_d[:, :], writes=[b_sink])
    p.dma("sync", dmask[:], dmask_d[:, :], writes=[b_dmask])
    cb0125, b_cb0125 = ctx.cbias(0.125, 128)

    for h in range(4):
        hh = 0
        p.dma("sync", rp[:, :], kf_d[64 * h:64 * h + 64, :], writes=[b_rp])
        p.dma("sync", rp2[:, :], kp_d[64 * h:64 * h + 64, :], writes=[b_rp2])
        emit(p, "vector", lambda v: v.tensor_tensor(kx[0:64, :], rp[:, :], cosk[0:64, :], ALU.mult),
             [b_rp, b_cosk], [b_kx])
        emit(p, "vector", lambda v: v.tensor_tensor(rp2[:, :], rp2[:, :], sink[0:64, :], ALU.mult),
             [b_rp2, b_sink], [b_rp2])
        emit(p, "vector", lambda v: v.tensor_tensor(kx[0:64, :], kx[0:64, :], rp2[:, :], ALU.add),
             [b_kx, b_rp2], [b_kx])
        p.dma("sync", rp[:, :], q_dd[64 * h:64 * h + 64, :], writes=[b_rp])
        p.dma("sync", rp2[:, :], qp_d[64 * h:64 * h + 64, :], writes=[b_rp2])
        emit(p, "vector", lambda v: v.tensor_tensor(qx[0:64, :], rp[:, :], cosk[0:64, :], ALU.mult),
             [b_rp, b_cosk], [b_qx])
        emit(p, "vector", lambda v: v.tensor_tensor(rp2[:, :], rp2[:, :], sink[0:64, :], ALU.mult),
             [b_rp2, b_sink], [b_rp2])
        emit(p, "vector", lambda v: v.tensor_tensor(qx[0:64, :], qx[0:64, :], rp2[:, :], ALU.add),
             [b_qx, b_rp2], [b_qx])
        p.dma("sync", vx[:, :].rearrange("p (s c) -> p s c", c=65)[:, :, 0:64],
              vf_d[:, 64 * h:64 * h + 64].rearrange("(s p) c -> p s c", p=128), writes=[b_vx])
        emit(p, "vector", lambda v: v.memset(srow[0:1, 0:1], 0.0), [], [b_srow])
        for sc in range(16):
            emit(p, "scalar", lambda s, sc=sc: s.activation(out=sq[0:64, 0:512], in_=kx[0:64, sc * 512:(sc + 1) * 512], func=AF.Square),
                 [b_kx], [b_sq])
            emit(p, "tensor", lambda t: t.matmul(ps1[:, :], onesc[0:64, :], sq[0:64, 0:512], start=True, stop=True),
                 [b_onesc, b_sq], [b_ps1])
            emit(p, "vector", lambda v: v.reduce_max(srow[0:1, 2:3], ps1[0:1, :], axis=AX.X), [b_ps1], [b_srow])
            emit(p, "vector", lambda v: v.tensor_tensor(srow[0:1, 0:1], srow[0:1, 0:1], srow[0:1, 2:3], ALU.max),
                 [b_srow], [b_srow])
        emit(p, "scalar", lambda s: s.activation(out=srow[0:1, 0:1], in_=srow[0:1, 0:1], func=AF.Sqrt), [b_srow], [b_srow])
        QN = srow[0:1, 512:1024]
        for qt in range(16):
            emit(p, "scalar", lambda s, qt=qt: s.activation(out=sq[0:64, 0:512], in_=qx[0:64, qt * 512:(qt + 1) * 512], func=AF.Square),
                 [b_qx], [b_sq])
            emit(p, "tensor", lambda t: t.matmul(ps1[:, :], onesc[0:64, :], sq[0:64, 0:512], start=True, stop=True),
                 [b_onesc, b_sq], [b_ps1])
            emit(p, "scalar", lambda s, QN=QN: s.activation(out=QN, in_=ps1[0:1, :], func=AF.Sqrt), [b_ps1], [b_srow])
            emit(p, "vector", lambda v, QN=QN: v.tensor_single_scalar(QN, QN, srow[0:1, 0:1], ALU.mult), [b_srow], [b_srow])
            emit(p, "vector", lambda v, QN=QN: v.tensor_scalar_mul(QN, QN, -1.0), [b_srow], [b_srow])
            emit(p, "tensor", lambda t, QN=QN: t.matmul(po[64:65, :], ones1[0:1, 0:1], QN, start=True, stop=True),
                 [b_ones1, b_srow], [b_po])
            emit(p, "vector", lambda v, qt=qt: v.tensor_copy(qx[64:65, qt * 512:(qt + 1) * 512], po[64:65, :]),
                 [b_po], [b_qx])
            q0 = qt * 512
            nsc = (q0 + 512) // 128
            for sc in range(nsc):
                pst, b_pst = (pst0, b_pst0) if sc % 2 == 0 else (pst1, b_pst1)
                emit(p, "tensor", lambda t, sc=sc, qt=qt, pst=pst: t.matmul(
                    pst[:, :], kx[:, sc * 128:(sc + 1) * 128], qx[:, qt * 512:(qt + 1) * 512],
                    start=True, stop=True), [b_kx, b_qx], [b_pst])
                pt_sl = slice((sc % 2) * 512, (sc % 2) * 512 + 512)
                d = sc - q0 // 128
                if d >= 0:
                    emit(p, "vector", lambda v, pst=pst, d=d: v.tensor_tensor(
                        tvm[:], pst[:, :], dmask[:, d * 512:(d + 1) * 512], ALU.add), [b_pst, b_dmask], [b_tvm])
                    emit(p, "scalar", lambda s, pt_sl=pt_sl, cb0125=cb0125: s.activation(
                        out=ptb[:, pt_sl], in_=tvm[:], func=AF.Exp, scale=cb0125),
                        [b_tvm, b_cb0125], [b_ptb])
                else:
                    emit(p, "scalar", lambda s, pst=pst, pt_sl=pt_sl, cb0125=cb0125: s.activation(
                        out=ptb[:, pt_sl], in_=pst[:, :], func=AF.Exp, scale=cb0125),
                        [b_pst, b_cb0125], [b_ptb])
                emit(p, "tensor", lambda t, sc=sc, pt_sl=pt_sl, nsc=nsc: t.matmul(
                    po[:, :], vx[:, sc * 65:(sc + 1) * 65], ptb[:, pt_sl],
                    start=(sc == 0), stop=(sc == nsc - 1)), [b_vx, b_ptb], [b_po])
            emit(p, "vector", lambda v: v.reciprocal(lbuf[64:65, 0:512], po[64:65, :]), [b_po], [b_lbuf])
            emit(p, "tensor", lambda t: t.matmul(pbc[:, :], ones64[64:65, 0:64], lbuf[64:65, 0:512], start=True, stop=True),
                 [b_ones64, b_lbuf], [b_pbc])
            emit(p, "vector", lambda v: v.tensor_copy(bcs[:], pbc[:, :]), [b_pbc], [b_bcs])
            emit(p, "vector", lambda v: v.tensor_tensor(osb[:], po[0:64, :], bcs[:], ALU.mult), [b_po, b_bcs], [b_osb])
            p.dma("gpsimd", o_d[64 * h:64 * h + 64, qt * 512:(qt + 1) * 512], osb[:], reads=[b_osb], writes=[b_o], out_dram=True)
    ctx.io = None
    if standalone:
        p.finalize()
    return ctx


def build_C3(ctx=None, io=None):
    standalone = ctx is None
    if standalone:
        ctx = LaunchCtx("C3")
        p = ctx.start()
    else:
        p = ctx.prog
    ctx.io = io
    nc = ctx.nc
    oin_d = ctx.din("oin", [1024, SH], BF16)
    x2_d = ctx.din("x2in3", [1024, SH])
    wo_d = ctx.din("woT", [1024, 1024])
    x3_d, b_x3 = ctx.dout("x3", [1024, SH])

    ob, b_ob = ctx.sbuf("ob", [128, 8 * 512], BF16)
    wob, b_wob = ctx.sbuf("wob", [128, 8 * 1024], BF16)
    wst0, b_wst0 = ctx.sbuf("wst0", [128, 1024], F32)
    wst1, b_wst1 = ctx.sbuf("wst1", [128, 1024], F32)
    xts, b_xts = ctx.sbuf("xts", [128, 512], F32)
    x3s, b_x3s = ctx.sbuf("x3s", [128, 512], F32)
    pw0, b_pw0 = ctx.psum("pw0", [128, 512])
    pw1, b_pw1 = ctx.psum("pw1", [128, 512])

    for oc in range(8):
        wst, b_wst = (wst0, b_wst0) if oc % 2 == 0 else (wst1, b_wst1)
        p.dma("sync", wst[:, :], wo_d[oc * 128:(oc + 1) * 128, :], writes=[b_wst])
        emit(p, "vector", lambda v, oc=oc, wst=wst: v.tensor_copy(wob[:, oc * 1024:(oc + 1) * 1024], wst[:]),
             [b_wst], [b_wob])
    for ti in range(4):
        o = ti * 512
        for c in range(8):
            p.dma("sync", ob[:, c * 512:(c + 1) * 512], oin_d[c * 128:(c + 1) * 128, o:o + 512], writes=[b_ob])
        for oc in range(8):
            pw, b_pw = (pw0, b_pw0) if oc % 2 == 0 else (pw1, b_pw1)
            def mmwo(t, oc=oc, pw=pw):
                last = None
                for kc in range(8):
                    last = t.matmul(pw[:, :], wob[:, oc * 1024 + kc * 128: oc * 1024 + (kc + 1) * 128],
                                    ob[:, kc * 512:(kc + 1) * 512], start=(kc == 0), stop=(kc == 7))
                return last
            p.op("tensor", mmwo, reads=[b_wob, b_ob], writes=[b_pw])
            p.dma("sync", xts[:], x2_d[oc * 128:(oc + 1) * 128, o:o + 512], writes=[b_xts])
            emit(p, "vector", lambda v, pw=pw: v.tensor_add(x3s[:], pw[:, :], xts[:]), [b_pw, b_xts], [b_x3s])
            p.dma("gpsimd", x3_d[oc * 128:(oc + 1) * 128, o:o + 512], x3s[:], reads=[b_x3s], writes=[b_x3], out_dram=True)
    ctx.io = None
    if standalone:
        p.finalize()
    return ctx


def rope_tables(S):
    DC = 64
    pos = np.arange(S, dtype=np.float32)
    inv = 10000.0 ** (-np.arange(0, DC, 2, dtype=np.float32) / DC)
    ang = pos[:, None] * inv[None, :]
    cos = np.cos(ang)
    sin = np.sin(ang)
    C = np.zeros((128, S), np.float32)
    Sg = np.zeros((128, S), np.float32)
    for r in range(128):
        d = r % 64
        i = d // 2
        C[r] = cos[:, i]
        Sg[r] = -sin[:, i] if d % 2 == 0 else sin[:, i]
    return C, Sg


def swap_pairs_rows(a):
    b = a.reshape(-1, 2, a.shape[1])
    return np.ascontiguousarray(b[:, ::-1, :].reshape(a.shape))


def stage_C2(I, c1_res):
    S = 8192
    kf = {}
    vf = {}
    qf = {}
    for b in range(2):
        kf[b] = np.concatenate([np.asarray(c1_res[b * 4 + k]["k"]) for k in range(4)], axis=1)
        vf[b] = np.concatenate([np.asarray(c1_res[b * 4 + k]["vtm"]) for k in range(4)], axis=0)
        qf[b] = np.concatenate([np.asarray(c1_res[b * 4 + k]["q"]) for k in range(4)], axis=1)
    C, Sg = rope_tables(S)
    Cb = np.ascontiguousarray(C[0:128].astype(BF16NP))
    Sb = np.ascontiguousarray(Sg[0:128].astype(BF16NP))
    sqr = np.arange(128)
    dmask = np.zeros((128, 4 * 512), np.float32)
    for d in range(4):
        qcols = np.arange(512)
        m = (sqr[:, None] + 128 * d) <= qcols[None, :]
        dmask[:, d * 512:(d + 1) * 512] = np.where(m, 0.0, NEG)
    in_maps = []
    for c, (b, k) in enumerate(CORES):
        rows = slice(256 * k, 256 * k + 256)
        in_maps.append(dict(
            qn=np.ascontiguousarray(qf[b][rows]), qp=swap_pairs_rows(np.ascontiguousarray(qf[b][rows])),
            kn=np.ascontiguousarray(kf[b][rows]), kp=swap_pairs_rows(np.ascontiguousarray(kf[b][rows])),
            vf=np.ascontiguousarray(vf[b][:, rows]),
            cosk=Cb, sink=Sb, dmask=dmask))
    return in_maps


def stage_C3(I, c2_res, x2_parts):
    woT = stage_ktiles(np.asarray(I["L1_attn_Wo"], np.float32).T)
    ofull = {}
    for b in range(2):
        ofull[b] = np.concatenate([np.asarray(c2_res[b * 4 + k]["oh"]) for k in range(4)], axis=0)
    in_maps = []
    for c, (b, k) in enumerate(CORES):
        sl = slice(2048 * k, 2048 * k + 2048)
        in_maps.append(dict(oin=np.ascontiguousarray(ofull[b][:, sl]),
                            x2in3=np.ascontiguousarray(x2_parts[c]), woT=woT))
    return in_maps


def build_D(ctx=None, io=None):
    standalone = ctx is None
    if standalone:
        ctx = LaunchCtx("D")
        p = ctx.start()
    else:
        p = ctx.prog
    ctx.io = io
    nc = ctx.nc

    x_d = ctx.din("x3in", [1024, SH])
    wr_d = ctx.din("wrT", [1024, 16])
    gu_d = ctx.din("gup", [16 * 4096, 1024])
    dn_d = ctx.din("dnp", [16 * 1024, 2048])
    wg_d = ctx.din("wgT2", [2048, 1024])
    wu_d = ctx.din("wuT2", [2048, 1024])
    wd_d = ctx.din("wdT2", [1024, 2048])
    esel_d = ctx.din("esel16", [16, 2048])
    idf_d = ctx.din("idf3", [128, 128])
    out_d, b_out = ctx.dout("out", [1024, SH])

    xt, b_xt = ctx.sbuf("xt", [128, 8 * 512], F32)
    lnd, b_lnd = ctx.sbuf("lnd", [128, 8 * SH], BF16)
    lnf, b_lnf = ctx.sbuf("lnf", [128, 512], F32)
    sq, b_sq = ctx.sbuf("sq", [128, 512], F32)
    outacc, b_outacc = ctx.sbuf("outacc", [128, 8 * SH], BF16)
    midb, b_midb = ctx.sbuf("midb", [128, 8 * SH], BF16)
    xin, b_xin = ctx.sbuf("xin", [128, 8 * 512], BF16)
    gsil, b_gsil = ctx.sbuf("gsil", [128, 512], BF16)
    scfm, b_scfm = ctx.sbuf("scfm", [16, SH], F32)
    lg, b_lg = ctx.sbuf("lg", [16, 512], F32)
    sg, b_sg = ctx.sbuf("sg", [16, 512], F32)
    lgt, b_lgt = ctx.sbuf("lgt", [128, 16], F32)
    selt, b_selt = ctx.sbuf("selt", [128, 16], F32)
    maxc, b_maxc = ctx.sbuf("maxc", [128, 1], F32)
    msk, b_msk = ctx.sbuf("msk", [128, 512], BF16)
    wrb, b_wrb = ctx.sbuf("wrb", [128, 8 * 16], F32)
    esel, b_esel = ctx.sbuf("esel_s", [16, 2048], F32)
    idf, b_idf = ctx.sbuf("idf_s", [128, 128], F32)
    wst0, b_wst0 = ctx.sbuf("wst0", [128, 1024], F32)
    wst1, b_wst1 = ctx.sbuf("wst1", [128, 1024], F32)
    wcb0, b_wcb0 = ctx.sbuf("wcb0", [128, 1024], BF16)
    wcb1, b_wcb1 = ctx.sbuf("wcb1", [128, 1024], BF16)
    wcb2, b_wcb2 = ctx.sbuf("wcb2", [128, 1024], BF16)
    wcb3, b_wcb3 = ctx.sbuf("wcb3", [128, 1024], BF16)
    onesc, b_onesc = ctx.sbuf("onesc", [128, 1], F32)
    ones1, b_ones1 = ctx.sbuf("ones1", [1, 128], F32)
    srow, b_srow = ctx.sbuf("srow", [1, 4 * 512], F32)
    os_, b_os = ctx.sbuf("os_", [128, 512], F32)

    ps1, b_ps1 = ctx.psum("ps1", [1, 512])
    ps2, b_ps2 = ctx.psum("ps2", [1, 512])
    pbm, b_pbm = ctx.psum("pbm", [128, 512])
    pbr, b_pbr = ctx.psum("pbr", [128, 512])
    pp0, b_pp0 = ctx.psum("pp0", [128, 512])
    pp1, b_pp1 = ctx.psum("pp1", [128, 512])
    pmk, b_pmk = ctx.psum("pmk", [128, 512])

    emit(p, "vector", lambda v: v.memset(onesc[:], 1.0), [], [b_onesc])
    emit(p, "vector", lambda v: v.memset(ones1[:], 1.0), [], [b_ones1])
    emit(p, "vector", lambda v: v.memset(outacc[:], 0.0), [], [b_outacc])
    p.dma("sync", esel[:], esel_d[:, :], writes=[b_esel])
    p.dma("sync", idf[:], idf_d[:, :], writes=[b_idf])
    for kc in range(8):
        p.dma("sync", wrb[:, kc * 16:(kc + 1) * 16], wr_d[kc * 128:(kc + 1) * 128, :], writes=[b_wrb])
    cb5, b_cb5 = ctx.cbias(1e-5, 1)

    for ti in range(4):
        o = ti * 512
        for c in range(8):
            p.dma("sync", xt[:, c * 512:(c + 1) * 512], x_d[c * 128:(c + 1) * 128, o:o + 512], writes=[b_xt])
        def mm1(t):
            last = None
            for c in range(8):
                last = t.matmul(ps1[:, :], onesc[:, :], xt[:, c * 512:(c + 1) * 512], start=(c == 0), stop=(c == 7))
            return last
        p.op("tensor", mm1, reads=[b_onesc, b_xt], writes=[b_ps1])
        for c in range(8):
            emit(p, "scalar", lambda s, c=c: s.activation(out=sq[:], in_=xt[:, c * 512:(c + 1) * 512], func=AF.Square),
                 [b_xt], [b_sq])
            emit(p, "tensor", lambda t, c=c: t.matmul(ps2[:, :], onesc[:, :], sq[:, :], start=(c == 0), stop=(c == 7)),
                 [b_onesc, b_sq], [b_ps2])
        M, R, T1 = srow[0:1, 0:512], srow[0:1, 512:1024], srow[0:1, 1024:1536]
        emit(p, "vector", lambda v, M=M: v.tensor_scalar_mul(M, ps1[0:1, :], 1.0 / 1024.0), [b_ps1], [b_srow])
        emit(p, "vector", lambda v, T1=T1: v.tensor_scalar_mul(T1, ps2[0:1, :], 1.0 / 1024.0), [b_ps2], [b_srow])
        emit(p, "scalar", lambda s, M=M, R=R: s.activation(out=R, in_=M, func=AF.Square), [b_srow], [b_srow])
        emit(p, "vector", lambda v, T1=T1, R=R: v.tensor_sub(T1, T1, R), [b_srow], [b_srow])
        emit(p, "scalar", lambda s, T1=T1, cb5=cb5: s.activation(out=T1, in_=T1, func=AF.Sqrt, bias=cb5), [b_srow, b_cb5], [b_srow])
        emit(p, "vector", lambda v, T1=T1, R=R: v.reciprocal(R, T1), [b_srow], [b_srow])
        emit(p, "tensor", lambda t, M=M: t.matmul(pbm[:, :], ones1[0:1, :], M, start=True, stop=True),
             [b_ones1, b_srow], [b_pbm])
        emit(p, "tensor", lambda t, R=R: t.matmul(pbr[:, :], ones1[0:1, :], R, start=True, stop=True),
             [b_ones1, b_srow], [b_pbr])
        for c in range(8):
            sl = slice(c * 512, (c + 1) * 512)
            emit(p, "vector", lambda v, sl=sl: v.tensor_sub(lnf[:], xt[:, sl], pbm[:, :]), [b_xt, b_pbm], [b_lnf])
            emit(p, "vector", lambda v: v.tensor_mul(lnf[:], lnf[:], pbr[:, :]), [b_lnf, b_pbr], [b_lnf])
            emit(p, "vector", lambda v, c=c, o=o: v.tensor_copy(lnd[:, c * SH + o: c * SH + o + 512], lnf[:]),
                 [b_lnf], [b_lnd])
            emit(p, "tensor", lambda t, c=c: t.matmul(pp0[0:16, :], wrb[:, c * 16:(c + 1) * 16], lnf[:, :],
                                                      start=(c == 0), stop=(c == 7)), [b_wrb, b_lnf], [b_pp0])
        emit(p, "vector", lambda v: v.tensor_copy(lg[:], pp0[0:16, :]), [b_pp0], [b_lg])
        emit(p, "scalar", lambda s: s.activation(out=sg[:], in_=lg[:], func=AF.Sigmoid), [b_lg], [b_sg])
        for cc in range(4):
            emit_transpose(p, pmk[0:128, 0:16], lg[:, cc * 128:(cc + 1) * 128], idf[0:16, 0:16], [b_lg, b_idf], [b_pmk])
            emit(p, "vector", lambda v: v.tensor_copy(lgt[:], pmk[0:128, 0:16]), [b_pmk], [b_lgt])
            emit(p, "vector", lambda v: v.reduce_max(maxc[:], lgt[:], axis=AX.X), [b_lgt], [b_maxc])
            emit(p, "vector", lambda v: v.tensor_scalar(selt[:], lgt[:], maxc[:, 0:1], None, ALU.is_ge),
                 [b_lgt, b_maxc], [b_selt])
            emit_transpose(p, pmk[0:16, 16:144], selt[:, :], idf[:, :], [b_selt, b_idf], [b_pmk])
            emit(p, "vector", lambda v, o=o, cc=cc: v.tensor_copy(scfm[:, o + cc * 128: o + (cc + 1) * 128], pmk[0:16, 16:144]),
                 [b_pmk], [b_scfm])
        emit(p, "vector", lambda v, o=o: v.tensor_mul(scfm[:, o:o + 512], scfm[:, o:o + 512], sg[:, :]),
             [b_scfm, b_sg], [b_scfm])

    for e in range(17):
        shared = (e == 16)
        for t4 in range(4):
            if not shared:
                emit(p, "tensor", lambda t, e=e, t4=t4: t.matmul(
                    pmk[:, :], esel[:, e * 128:(e + 1) * 128], scfm[:, t4 * 512:(t4 + 1) * 512], start=True, stop=True),
                    [b_esel, b_scfm], [b_pmk])
                emit(p, "vector", lambda v, t4=t4: v.tensor_copy(msk[:], pmk[:, :]), [b_pmk], [b_msk])
                for c in range(8):
                    emit(p, "vector", lambda v, c=c, t4=t4: v.tensor_tensor(
                        xin[:, c * 512:(c + 1) * 512], lnd[:, c * SH + t4 * 512: c * SH + t4 * 512 + 512],
                        msk[:], ALU.mult), [b_lnd, b_msk], [b_xin])
            else:
                for c in range(8):
                    emit(p, "vector", lambda v, c=c, t4=t4: v.tensor_copy(
                        xin[:, c * 512:(c + 1) * 512], lnd[:, c * SH + t4 * 512: c * SH + t4 * 512 + 512]),
                        [b_lnd], [b_xin])
            for jb in range(2):
                for jc in range(8):
                    jg = jb * 8 + jc
                    ju = 16 + jg
                    for (wcb, b_wcb, wst, b_wst, col) in ((wcb0, b_wcb0, wst0, b_wst0, jg), (wcb1, b_wcb1, wst1, b_wst1, ju)):
                        if shared:
                            wsrc = wg_d if col < 16 else wu_d
                            ccol = col % 16
                            p.dma("sync", wst[:, :], wsrc[ccol * 128:(ccol + 1) * 128, :], writes=[b_wst])
                        else:
                            p.dma("sync", wst[:, :], gu_d[(e * 32 + col) * 128:(e * 32 + col + 1) * 128, :],
                                  writes=[b_wst])
                        emit(p, "vector", lambda v, wcb=wcb, wst=wst: v.tensor_copy(wcb[:], wst[:]), [b_wst], [b_wcb])
                    def mmg(t, wcb=wcb0, t4=t4):
                        last = None
                        for kc in range(8):
                            last = t.matmul(pp0[:, :], wcb[:, kc * 128:(kc + 1) * 128],
                                            xin[:, kc * 512:(kc + 1) * 512], start=(kc == 0), stop=(kc == 7))
                        return last
                    p.op("tensor", mmg, reads=[b_wcb0, b_xin], writes=[b_pp0])
                    def mmu(t, wcb=wcb1, t4=t4):
                        last = None
                        for kc in range(8):
                            last = t.matmul(pp1[:, :], wcb[:, kc * 128:(kc + 1) * 128],
                                            xin[:, kc * 512:(kc + 1) * 512], start=(kc == 0), stop=(kc == 7))
                        return last
                    p.op("tensor", mmu, reads=[b_wcb1, b_xin], writes=[b_pp1])
                    emit(p, "scalar", lambda s: s.activation(out=gsil[:], in_=pp0[:, :], func=AF.Silu), [b_pp0], [b_gsil])
                    emit(p, "vector", lambda v, jc=jc, t4=t4: v.tensor_tensor(
                        midb[:, jc * SH + t4 * 512: jc * SH + t4 * 512 + 512], pp1[:, :], gsil[:], ALU.mult),
                        [b_pp1, b_gsil], [b_midb])
                for oc in range(8):
                    wst, b_wst = (wst0, b_wst0) if oc % 2 == 0 else (wst1, b_wst1)
                    wcb, b_wcb = (wcb2, b_wcb2) if oc % 2 == 0 else (wcb3, b_wcb3)
                    if shared:
                        p.dma("sync", wst[:, :], wd_d[oc * 128:(oc + 1) * 128, jb * 1024:(jb + 1) * 1024],
                              writes=[b_wst])
                    else:
                        p.dma("sync", wst[:, :], dn_d[e * 1024 + oc * 128: e * 1024 + (oc + 1) * 128, jb * 1024:(jb + 1) * 1024],
                              writes=[b_wst])
                    emit(p, "vector", lambda v, wcb=wcb, wst=wst: v.tensor_copy(wcb[:], wst[:]), [b_wst], [b_wcb])
                    pw, b_pw = (pp0, b_pp0) if oc % 2 == 0 else (pp1, b_pp1)
                    def mmd(t, wcb=wcb, t4=t4, pw=pw):
                        last = None
                        for kc in range(8):
                            last = t.matmul(pw[:, :], wcb[:, kc * 128:(kc + 1) * 128],
                                            midb[:, kc * SH + t4 * 512: kc * SH + t4 * 512 + 512],
                                            start=(kc == 0), stop=(kc == 7))
                        return last
                    p.op("tensor", mmd, reads=[b_wcb, b_midb], writes=[b_pw])
                    emit(p, "vector", lambda v, oc=oc, t4=t4, pw=pw: v.tensor_tensor(
                        outacc[:, oc * SH + t4 * 512: oc * SH + t4 * 512 + 512], pw[:, :],
                        outacc[:, oc * SH + t4 * 512: oc * SH + t4 * 512 + 512], ALU.add),
                        [b_pw, b_outacc], [b_outacc])

    for ti in range(4):
        o = ti * 512
        for oc in range(8):
            p.dma("sync", xt[:, 0:512], x_d[oc * 128:(oc + 1) * 128, o:o + 512], writes=[b_xt])
            emit(p, "vector", lambda v, oc=oc, o=o: v.tensor_add(os_[:], xt[:, 0:512],
                 outacc[:, oc * SH + o: oc * SH + o + 512]), [b_xt, b_outacc], [b_os])
            p.dma("gpsimd", out_d[oc * 128:(oc + 1) * 128, o:o + 512], os_[:], reads=[b_os], writes=[b_out], out_dram=True)
    ctx.io = None
    if standalone:
        p.finalize()
    return ctx


def stage_D(I, x3_parts):
    wrT = _f32c(np.asarray(I["L1_ffn_w_router"], np.float32).T)
    gup = stage_ktiles3(I["L1_ffn_gate_up_proj"], kc=8)
    dnp = stage_ktiles3(I["L1_ffn_down_proj"], kc=16)
    wgT2 = stage_ktiles(np.asarray(I["L1_ffn_w_gate"], np.float32).T)
    wuT2 = stage_ktiles(np.asarray(I["L1_ffn_w_up"], np.float32).T)
    wdT2 = stage_ktiles(np.asarray(I["L1_ffn_w_down"], np.float32).T, kc=16)
    esel16 = np.zeros((16, 2048), np.float32)
    for e in range(16):
        esel16[e, e * 128:(e + 1) * 128] = 1.0
    idf3 = np.eye(128, dtype=np.float32)
    in_maps = []
    for c, (b, k) in enumerate(CORES):
        in_maps.append(dict(x3in=np.ascontiguousarray(x3_parts[c]), wrT=wrT, gup=gup, dnp=dnp,
                            wgT2=wgT2, wuT2=wuT2, wdT2=wdT2, esel16=esel16, idf3=idf3))
    return in_maps


_CTX_CACHE = {}


def _get_ctx(name, builder):
    if name not in _CTX_CACHE:
        _CTX_CACHE[name] = builder()
    return _CTX_CACHE[name]


def kernel(**I):
    return kernel_mega(**I)


def kernel_multilaunch(**I):
    ctx = _get_ctx("A1", build_A1)
    a1 = run_launch(ctx, stage_A1(I))
    a1_res = [dict(xbc=np.asarray(r["xbc"]), dt=np.asarray(r["dt"]),
                   z=np.asarray(r["z"]), astat=np.asarray(r["astat"])) for r in a1]
    ctx = _get_ctx("A2", build_A2)
    a2 = run_launch(ctx, stage_A2(I, a1_res))
    a2_res = [dict(y=np.asarray(r["y"])) for r in a2]
    ctx = _get_ctx("A3", build_A3)
    a3 = run_launch(ctx, stage_A3(I, a1_res, a2_res))
    x1_parts = [np.asarray(r["x1"]) for r in a3]
    ctx = _get_ctx("B1", build_B1)
    b1 = run_launch(ctx, stage_B1(I, x1_parts))
    b1_res = [dict(q=np.asarray(r["q"]), k=np.asarray(r["k"]),
                   vtm=np.asarray(r["vtm"]), lnb=np.asarray(r["lnb"])) for r in b1]
    ctx = _get_ctx("B2", build_B2)
    b2 = run_launch(ctx, stage_B2(I, b1_res, x1_parts))
    x2_parts = [np.asarray(r["x2"]) for r in b2]
    ctx = _get_ctx("C1", build_C1)
    c1 = run_launch(ctx, stage_C1(I, x2_parts))
    c1_res = [dict(q=np.asarray(r["q"]), k=np.asarray(r["k"]), vtm=np.asarray(r["vtm"])) for r in c1]
    ctx = _get_ctx("C2", build_C2)
    c2 = run_launch(ctx, stage_C2(I, c1_res))
    c2_res = [dict(oh=np.asarray(r["oh"])) for r in c2]
    ctx = _get_ctx("C3", build_C3)
    c3 = run_launch(ctx, stage_C3(I, c2_res, x2_parts))
    x3_parts = [np.asarray(r["x3"]) for r in c3]
    ctx = _get_ctx("D", build_D)
    d = run_launch(ctx, stage_D(I, x3_parts))
    full = np.zeros((2, 8192, 1024), np.float32)
    for c, (b, k) in enumerate(CORES):
        full[b, 2048 * k:2048 * k + 2048] = np.asarray(d[c]["out"]).T
    return full


class MegaCtx(LaunchCtx):
    NP = 52800

    def __init__(self, name):
        super().__init__(name)
        self.mega = False

    def init_pools(self):
        p = self.prog
        self.cbt = self.es.enter_context(self.nc.sbuf_tensor("CBIAS", [128, 8], F32))
        self.poolf = self.es.enter_context(self.nc.sbuf_tensor("POOLF", [128, self.NP], F32))
        self.poolb_full = self.poolf[:, :].bitcast(BF16)
        self.psf = [self.es.enter_context(self.nc.psum_tensor(f"PSF{i}", [128, 512], F32)) for i in range(7)]
        self.psb = self.es.enter_context(self.nc.psum_tensor("PSB", [128, 1024], BF16))
        self.mega = True
        self.new_phase(first=True)

    def new_phase(self, first=False):
        if not first:
            self.prog.barrier()
            for b in self.prog.dma_bufs:
                self.prog.sem_pool.append((b.dma_sem, b.dma_base + b.dma_writes))
            self.prog.dma_bufs = []
        self.off = 0
        self.ps_i = 0
        self.ps_b_used = False

    def cbias(self, val, parts=128):
        if val not in self._cb:
            i = len(self._cb)
            t = self.cbt[:, i:i + 1]
            b = self.prog.buf(f"cb_{i}")
            self.prog.op("vector", lambda v, t=t, val=val: v.memset(t, float(val)),
                         reads=[], writes=[b])
            self._cb[val] = (t, b)
        t, b = self._cb[val]
        return t[0:parts, 0:1], b

    def sbuf(self, name, shape, dtype):
        if not self.mega:
            return super().sbuf(name, shape, dtype)
        parts, cols = shape
        esz = 4 if dtype == F32 else 2
        nbytes = ceil_div(cols * esz, 512) * 512
        assert self.off + nbytes <= self.NP * 4, f"pool overflow at {name}: {self.off}+{nbytes}"
        if dtype == F32:
            t = self.poolf[0:parts, self.off // 4: self.off // 4 + cols]
        else:
            t = self.poolb_full[0:parts, self.off // 2: self.off // 2 + cols]
        self.off += nbytes
        return t, self.prog.buf(name)

    def psum(self, name, shape, dtype=F32):
        if not self.mega:
            return super().psum(name, shape, dtype)
        parts, cols = shape
        if dtype == F32:
            assert self.ps_i < 7, f"psum pool overflow at {name}"
            t = self.psf[self.ps_i][0:parts, 0:cols]
            self.ps_i += 1
        else:
            assert not self.ps_b_used, f"bf16 psum overflow at {name}"
            t = self.psb[0:parts, 0:cols]
            self.ps_b_used = True
        return t, self.prog.buf(name)


def _prog_barrier(self):
    for e in ENGINES:
        for f in ENGINES:
            if f != e and self.counts[f] > 0:
                self._wait(e, self.sems[f], self.counts[f])
        for b in self.dma_bufs:
            self._wait(e, b.dma_sem, 16 * (b.dma_base + b.dma_writes))


Prog.barrier = _prog_barrier


from concourse.aot_env import allow_ici_all_to_all

GRP = [[0, 1, 2, 3], [4, 5, 6, 7]]


def build_mega():
    ctx = MegaCtx("MEGA")
    p = ctx.start()
    nc = ctx.nc
    ctx.init_pools()
    S = 8192
    I32 = mybir.dt.int32

    def scr(name, shape, dtype=F32):
        return nc.dram_tensor(name, shape, dtype).ap()

    xin_ap = nc.dram_tensor("xin", [1024, 2176], F32, kind="ExternalInput").ap()
    ix_xs = nc.dram_tensor("ix_xs", [128, 16], I32, kind="ExternalInput").ap()
    ix_dt = nc.dram_tensor("ix_dt", [8, 4], I32, kind="ExternalInput").ap()
    ix_y = nc.dram_tensor("ix_y", [128, 16], I32, kind="ExternalInput").ap()
    ix_qk = nc.dram_tensor("ix_qk", [128, 8], I32, kind="ExternalInput").ap()
    ix_v = nc.dram_tensor("ix_v", [128, 64], I32, kind="ExternalInput").ap()
    ix_o = nc.dram_tensor("ix_o", [128, 8], I32, kind="ExternalInput").ap()
    ix_bc = nc.dram_tensor("ix_bc", [128, 8], I32, kind="ExternalInput").ap()
    ix_kf = nc.dram_tensor("ix_kf", [128, 32], I32, kind="ExternalInput").ap()
    ix_sw = nc.dram_tensor("ix_sw", [128, 2], I32, kind="ExternalInput").ap()

    g_xbc = scr("g_xbc", [2304, SH], BF16)
    g_dt = scr("g_dt", [32, SH])
    g_z = scr("g_z", [2048, SH])
    g_ast = scr("g_ast", [2, SH])
    g_xbcg = scr("g_xbcg", [4 * 2304, SH], BF16)
    g_dtg = scr("g_dtg", [128, SH])
    g_xs = scr("g_xs", [512, S], BF16)
    g_dth = scr("g_dth", [8, S])
    g_Bf = scr("g_Bf", [128, S], BF16)
    g_Cf = scr("g_Cf", [128, S], BF16)
    g_y = scr("g_y", [512, S], BF16)
    g_yg = scr("g_yg", [2048, S], BF16)
    g_yin = scr("g_yin", [2048, SH], BF16)
    g_x1 = scr("g_x1", [1024, SH])
    g_lnb = scr("g_lnb", [1024, SH])
    g_q = scr("g_q", [1024, SH], BF16)
    g_k = scr("g_k", [1024, SH], BF16)
    g_v = scr("g_v", [SH, 1024], BF16)
    g_kg = scr("g_kg", [4096, SH], BF16)
    g_kf = scr("g_kf", [1024, S], BF16)
    g_vg = scr("g_vg", [S, 1024], BF16)
    g_x2 = scr("g_x2", [1024, SH])
    g_qg = scr("g_qg", [4096, SH], BF16)
    g_kg2 = scr("g_kg2", [4096, SH], BF16)
    g_qn = scr("g_qn", [256, S], BF16)
    g_qp = scr("g_qp", [256, S], BF16)
    g_kn = scr("g_kn", [256, S], BF16)
    g_kp = scr("g_kp", [256, S], BF16)
    g_vf2 = scr("g_vf2", [S, 256], BF16)
    g_oh = scr("g_oh", [256, S], BF16)
    g_og = scr("g_og", [1024, S], BF16)
    g_oin = scr("g_oin", [1024, SH], BF16)
    g_x3 = scr("g_x3", [1024, SH])

    BREG = {}

    def gb(name):
        if name not in BREG:
            BREG[name] = p.buf(name)
        return BREG[name]

    def cc(kind, in_ap, out_ap, tag, src, dst):
        def fn(g_eng, in_ap=in_ap, out_ap=out_ap, kind=kind):
            with allow_ici_all_to_all():
                return g_eng.collective_compute(
                    kind, ALU.bypass, replica_groups=GRP,
                    ins=[in_ap.opt()], outs=[out_ap.opt()])
        return p.op("gpsimd", fn, reads=[gb(src)], writes=[gb(dst)], tag=tag)

    def dcp(dst_ap, src_ap, tag, src, dst):
        return p.dma("sync", dst_ap, src_ap, reads=[gb(src)], writes=[gb(dst)], tag=tag)

    ixs = {}
    for nm, ap_, shp in (("ix_xs", ix_xs, [128, 16]), ("ix_dt", ix_dt, [8, 4]),
                         ("ix_y", ix_y, [128, 16]), ("ix_qk", ix_qk, [128, 8]),
                         ("ix_v", ix_v, [128, 64]), ("ix_o", ix_o, [128, 8]),
                         ("ix_bc", ix_bc, [128, 8]), ("ix_kf", ix_kf, [128, 32]),
                         ("ix_sw", ix_sw, [128, 2])):
        t = ctx.es.enter_context(nc.sbuf_tensor("s" + nm, shp, I32))
        b = p.buf("s" + nm)
        p.dma("sync", t[:, :], ap_[:, :], writes=[b])
        ixs[nm] = (t, b)

    STG = {"t": None}

    def stg_tiles():
        sb, bsb = ctx.sbuf(f"stg_b{ctx.uniq()}", [128, 8192], BF16)
        sf, bsf = ctx.sbuf(f"stg_f{ctx.uniq()}", [128, 2048], F32)
        STG["t"] = (sb, bsb, sf, bsf)

    def _raw_idma(out_ap, src_view, t, col, nparts, tag, reads, wbuf):
        def fn(g_eng, out_ap=out_ap, src_view=src_view, t=t, col=col, nparts=nparts):
            return g_eng.indirect_dma_start(
                out_ap, None, src_view,
                bass.IndirectOffsetOnAxis(t[0:nparts, col:col + 1], 0))
        wb = wbuf
        if wb.dma_sem is None:
            if p.sem_pool:
                wb.dma_sem, wb.dma_base = p.sem_pool.pop()
            else:
                wb.dma_sem = p.new_sem(f"D{p._nsem}_{wb.name}")
                wb.dma_base = 0
            p.dma_bufs.append(wb)
        p._pre("gpsimd", reads, [wb], ())
        wb.dma_writes += 1
        val = 16 * (wb.dma_base + wb.dma_writes)
        p.items["gpsimd"].append(("emit", fn, [(wb.dma_sem, 16)], tag))
        op = Op("gpsimd", wb.dma_sem, val, "dma")
        p._post(op, reads, [wb])
        return op

    def igather(out_ap, src_view, ixname, col, nparts, tag, srcb, dstb, f32=False):
        t, bix = ixs[ixname]
        sb, bsb, sf, bsf = STG["t"]
        cols = out_ap.shape[-1]
        stg, bstg = (sf, bsf) if f32 else (sb, bsb)
        _raw_idma(stg[0:nparts, 0:cols], src_view, t, col, nparts, tag + "_g",
                  [gb(srcb), bix], bstg)
        p.dma("sync", out_ap, stg[0:nparts, 0:cols], reads=[bstg], writes=[gb(dstb)], tag=tag + "_s")

    build_A1(ctx=ctx, io={"xin": xin_ap, "xbc": g_xbc, "dt": g_dt, "z": g_z, "astat": g_ast})
    ctx.new_phase()
    stg_tiles()
    cc("AllGather", g_xbc[:, :], g_xbcg[:, :], "cc_xbc", "g_xbc", "g_xbcg")
    cc("AllGather", g_dt[:, :], g_dtg[:, :], "cc_dt", "g_dt", "g_dtg")
    for gg in range(4):
        for blk in range(4):
            igather(g_xs[blk * 128:(blk + 1) * 128, gg * 2048:(gg + 1) * 2048],
                    g_xbcg[:, :], "ix_xs", gg * 4 + blk, 128, f"ig_xs{gg}_{blk}", "g_xbcg", "g_xs")
        igather(g_dth[0:8, gg * 2048:(gg + 1) * 2048], g_dtg[:, :], "ix_dt", gg, 8,
                f"ig_dt{gg}", "g_dtg", "g_dth", f32=True)
        igather(g_Bf[:, gg * 2048:(gg + 1) * 2048], g_xbcg[:, :], "ix_bc", gg * 2, 128,
                f"m_B{gg}", "g_xbcg", "g_Bf")
        igather(g_Cf[:, gg * 2048:(gg + 1) * 2048], g_xbcg[:, :], "ix_bc", gg * 2 + 1, 128,
                f"m_C{gg}", "g_xbcg", "g_Cf")
    ctx.new_phase()
    build_A2(ctx=ctx, io={"xs": g_xs, "Bf": g_Bf, "Cf": g_Cf, "dth": g_dth, "y": g_y})
    ctx.new_phase()
    stg_tiles()
    cc("AllGather", g_y[:, :], g_yg[:, :], "cc_y", "g_y", "g_yg")
    yv = g_yg.rearrange("r (q c) -> (r q) c", c=2048)
    for blk in range(16):
        igather(g_yin[blk * 128:(blk + 1) * 128, :], yv, "ix_y", blk, 128,
                f"ig_y{blk}", "g_yg", "g_yin")
    ctx.new_phase()
    build_A3(ctx=ctx, io={"xin3": xin_ap[:, 128:2176], "astat3": g_ast, "zin": g_z,
                          "yin": g_yin, "x1": g_x1})
    ctx.new_phase()
    build_B1(ctx=ctx, io={"x1in": g_x1, "lnb": g_lnb, "q": g_q, "k": g_k, "vtm": g_v})
    ctx.new_phase()
    stg_tiles()
    cc("AllGather", g_k[:, :], g_kg[:, :], "cc_k", "g_k", "g_kg")
    cc("AllGather", g_v[:, :], g_vg[:, :], "cc_v", "g_v", "g_vg")
    for gg in range(4):
        for blk in range(8):
            igather(g_kf[blk * 128:(blk + 1) * 128, gg * 2048:(gg + 1) * 2048], g_kg[:, :],
                    "ix_kf", gg * 8 + blk, 128, f"m_k{gg}_{blk}", "g_kg", "g_kf")
    ctx.new_phase()
    build_B2(ctx=ctx, io={"qin": g_q, "kf": g_kf, "vf": g_vg, "lnbin": g_lnb,
                          "x1in2": g_x1, "x2": g_x2})
    ctx.new_phase()
    build_C1(ctx=ctx, io={"x2in": g_x2, "q": g_q, "k": g_k, "vtm": g_v})
    ctx.new_phase()
    stg_tiles()
    cc("AllGather", g_q[:, :], g_qg[:, :], "cc_q2", "g_q", "g_qg")
    cc("AllGather", g_k[:, :], g_kg2[:, :], "cc_k2", "g_k", "g_kg2")
    cc("AllGather", g_v[:, :], g_vg[:, :], "cc_v2", "g_v", "g_vg")
    for gg in range(4):
        for blk in range(2):
            igather(g_qn[blk * 128:(blk + 1) * 128, gg * 2048:(gg + 1) * 2048], g_qg[:, :],
                    "ix_qk", gg * 2 + blk, 128, f"ig_q{gg}_{blk}", "g_qg", "g_qn")
            igather(g_kn[blk * 128:(blk + 1) * 128, gg * 2048:(gg + 1) * 2048], g_kg2[:, :],
                    "ix_qk", gg * 2 + blk, 128, f"ig_k{gg}_{blk}", "g_kg2", "g_kn")
    vv = g_vg.rearrange("r (q c) -> (r q) c", c=256)
    for blk in range(64):
        igather(g_vf2[blk * 128:(blk + 1) * 128, :], vv, "ix_v", blk, 128,
                f"ig_v{blk}", "g_vg", "g_vf2")
    ctx.new_phase()
    stg_tiles()
    for blk in range(2):
        igather(g_qp[blk * 128:(blk + 1) * 128, :], g_qn[:, :], "ix_sw", blk, 128,
                f"sw_q{blk}", "g_qn", "g_qp")
        igather(g_kp[blk * 128:(blk + 1) * 128, :], g_kn[:, :], "ix_sw", blk, 128,
                f"sw_k{blk}", "g_kn", "g_kp")
    ctx.new_phase()
    build_C2(ctx=ctx, io={"qn": g_qn, "qp": g_qp, "kn": g_kn, "kp": g_kp, "vf": g_vf2,
                          "oh": g_oh})
    ctx.new_phase()
    stg_tiles()
    cc("AllGather", g_oh[:, :], g_og[:, :], "cc_o", "g_oh", "g_og")
    ov = g_og.rearrange("r (q c) -> (r q) c", c=2048)
    for blk in range(8):
        igather(g_oin[blk * 128:(blk + 1) * 128, :], ov, "ix_o", blk, 128,
                f"ig_o{blk}", "g_og", "g_oin")
    ctx.new_phase()
    build_C3(ctx=ctx, io={"oin": g_oin, "x2in3": g_x2, "x3": g_x3})
    ctx.new_phase()
    build_D(ctx=ctx, io={"x3in": g_x3})
    p.finalize()
    return ctx


def stage_ix(k):
    pq = np.arange(128)
    ix_xs = np.zeros((128, 16), np.int32)
    for gg in range(4):
        for blk in range(4):
            ix_xs[:, gg * 4 + blk] = gg * 2304 + 512 * k + blk * 128 + pq
    ix_dt = np.zeros((8, 4), np.int32)
    for gg in range(4):
        ix_dt[:, gg] = gg * 32 + 8 * k + np.arange(8)
    ix_y = np.zeros((128, 16), np.int32)
    for blk in range(16):
        ix_y[:, blk] = (blk * 128 + pq) * 4 + k
    ix_qk = np.zeros((128, 8), np.int32)
    for gg in range(4):
        for blk in range(2):
            ix_qk[:, gg * 2 + blk] = gg * 1024 + 256 * k + blk * 128 + pq
    ix_v = np.zeros((128, 64), np.int32)
    for blk in range(64):
        ix_v[:, blk] = (blk * 128 + pq) * 4 + k
    ix_o = np.zeros((128, 8), np.int32)
    for blk in range(8):
        ix_o[:, blk] = (blk * 128 + pq) * 4 + k
    ix_bc = np.zeros((128, 8), np.int32)
    for gg in range(4):
        ix_bc[:, gg * 2] = gg * 2304 + 2048 + pq
        ix_bc[:, gg * 2 + 1] = gg * 2304 + 2176 + pq
    ix_kf = np.zeros((128, 32), np.int32)
    for gg in range(4):
        for blk in range(8):
            ix_kf[:, gg * 8 + blk] = gg * 1024 + blk * 128 + pq
    ix_sw = np.zeros((128, 2), np.int32)
    for blk in range(2):
        ix_sw[:, blk] = blk * 128 + (pq ^ 1)
    return dict(ix_xs=ix_xs, ix_dt=ix_dt, ix_y=ix_y, ix_qk=ix_qk, ix_v=ix_v, ix_o=ix_o,
                ix_bc=ix_bc, ix_kf=ix_kf, ix_sw=ix_sw)


def stage_mega(I):
    x = I["L0_attn_hidden_states"]
    xfm = np.ascontiguousarray(x.transpose(0, 2, 1), dtype=np.float32)
    w_in_T = np.zeros((1024, 4480), np.float32)
    w_in_T[:, 0:4384] = np.asarray(I["L0_attn_w_in"], np.float32).T
    w_in_T = _f32c(stage_ktiles(np.concatenate([w_in_T], axis=0), kc=8)) if False else stage_ktiles(w_in_T)
    wln1c = stage_vec_cols(I["L0_attn_w_ln1"], 8)
    dtbc = _f32c(np.asarray(I["L0_attn_dt_bias"], np.float32).reshape(32, 1))
    bconvc = stage_vec_cols(I["L0_attn_b_conv"], 18)
    wc = np.asarray(I["L0_attn_w_conv"], np.float32)
    wconvc = _f32c(wc.reshape(18, 128, 4).transpose(1, 0, 2).reshape(128, 72))
    idf = np.eye(128, dtype=np.float32)
    idb = idf.astype(BF16NP)
    sq = np.arange(128)
    maskT = np.where(sq[:, None] <= sq[None, :], 0.0, NEG).astype(np.float32)
    selpair = np.zeros((8, 128), np.float32)
    for h in range(8):
        selpair[h, (h // 2) * 32:(h // 2) * 32 + 32] = 1.0
    sel8 = np.zeros((8, 128), np.float32)
    for h in range(8):
        sel8[h, (h % 2) * 64:(h % 2) * 64 + 64] = 1.0
    alog_full = np.asarray(I["L0_attn_A_log"], np.float32)
    d_full = np.asarray(I["L0_attn_D"], np.float32)
    wmnc = stage_vec_cols(I["L0_attn_w_mnorm"], 16)
    wout_T = stage_ktiles(np.asarray(I["L0_attn_w_out"], np.float32).T, kc=16)
    wln2c = stage_vec_cols(I["L0_attn_w_ln2"], 8)
    wg_T = stage_ktiles(np.asarray(I["L0_attn_w_gate"], np.float32).T)
    wu_T = stage_ktiles(np.asarray(I["L0_attn_w_up"], np.float32).T)
    wd_T = stage_ktiles(np.asarray(I["L0_attn_w_down"], np.float32).T, kc=32)
    wq = stage_ktiles(I["L0_ffn_wq"]); wk = stage_ktiles(I["L0_ffn_wk"]); wv = stage_ktiles(I["L0_ffn_wv"])
    bqc = stage_vec_cols(I["L0_ffn_bq"], 8); bkc = stage_vec_cols(I["L0_ffn_bk"], 8)
    bvr = _f32c(np.asarray(I["L0_ffn_bv"], np.float32).reshape(1, 1024))
    g1c = stage_vec_cols(I["L0_ffn_ln1_g"], 8); b1c = stage_vec_cols(I["L0_ffn_ln1_b"], 8)
    wo = stage_ktiles(I["L0_ffn_wo"]); boc = stage_vec_cols(I["L0_ffn_bo"], 8)
    g2c = stage_vec_cols(I["L0_ffn_ln2_g"], 8); b2c = stage_vec_cols(I["L0_ffn_ln2_b"], 8)
    gatew = _f32c(I["L0_ffn_gate_w"])
    gatebc = _f32c(np.asarray(I["L0_ffn_gate_b"], np.float32).reshape(8, 1))
    w1m = stage_ktiles3(I["L0_ffn_w1"], kc=8)
    b1m = _f32c(I["L0_ffn_b1"])
    w2m = stage_ktiles3(I["L0_ffn_w2"], kc=32)
    b2m = _f32c(I["L0_ffn_b2"])
    erow = np.zeros((8, 4096), np.float32)
    esel = np.zeros((8, 1024), np.float32)
    for e in range(8):
        erow[e, e * 512:(e + 1) * 512] = 1.0
        esel[e, e * 128:(e + 1) * 128] = 1.0
    wqT = stage_ktiles(np.asarray(I["L1_attn_Wq"], np.float32).T)
    wkT = stage_ktiles(np.asarray(I["L1_attn_Wk"], np.float32).T)
    wvT = stage_ktiles(np.asarray(I["L1_attn_Wv"], np.float32).T)
    woT = stage_ktiles(np.asarray(I["L1_attn_Wo"], np.float32).T)
    C, Sg = rope_tables(8192)
    Cb = np.ascontiguousarray(C.astype(BF16NP))
    Sb = np.ascontiguousarray(Sg.astype(BF16NP))
    dmask = np.zeros((128, 4 * 512), np.float32)
    for d in range(4):
        qcols = np.arange(512)
        m = (sq[:, None] + 128 * d) <= qcols[None, :]
        dmask[:, d * 512:(d + 1) * 512] = np.where(m, 0.0, NEG)
    wrT = _f32c(np.asarray(I["L1_ffn_w_router"], np.float32).T)
    gup = stage_ktiles3(I["L1_ffn_gate_up_proj"], kc=8)
    dnp = stage_ktiles3(I["L1_ffn_down_proj"], kc=16)
    wgT2 = stage_ktiles(np.asarray(I["L1_ffn_w_gate"], np.float32).T)
    wuT2 = stage_ktiles(np.asarray(I["L1_ffn_w_up"], np.float32).T)
    wdT2 = stage_ktiles(np.asarray(I["L1_ffn_w_down"], np.float32).T, kc=16)
    esel16 = np.zeros((16, 2048), np.float32)
    for e in range(16):
        esel16[e, e * 128:(e + 1) * 128] = 1.0
    in_maps = []
    for c, (b, k) in enumerate(CORES):
        lo = 2048 * k - 128
        if lo < 0:
            xin = np.concatenate([np.zeros((1024, 128), np.float32), xfm[b][:, 0:2048]], axis=1)
        else:
            xin = np.ascontiguousarray(xfm[b][:, lo:lo + 2176])
        in_maps.append(dict(
            xin=xin, w_in_T=w_in_T, wln1c=wln1c, dtbc=dtbc, bconvc=bconvc, wconvc=wconvc,
            alog=np.ascontiguousarray(alog_full[8 * k:8 * k + 8].reshape(8, 1)),
            dd=np.ascontiguousarray(d_full[8 * k:8 * k + 8].reshape(1, 8)),
            idf=idf, idb=idb, maskT=maskT, selpair=selpair, sel8=sel8,
            wmnc=wmnc, wout_T=wout_T, wln2c=wln2c, wg_T=wg_T, wu_T=wu_T, wd_T=wd_T,
            wq=wq, wk=wk, wv=wv, bqc=bqc, bkc=bkc, bvr=bvr, g1c=g1c, b1c=b1c,
            wo=wo, boc=boc, g2c=g2c, b2c=b2c, gatew=gatew, gatebc=gatebc,
            w1m=w1m, b1m=b1m, w2m=w2m, b2m=b2m, esel=esel, idf2=idf,
            wqT=wqT, wkT=wkT, wvT=wvT, woT=woT, cosk=Cb, sink=Sb, dmask=dmask,
            wrT=wrT, gup=gup, dnp=dnp, wgT2=wgT2, wuT2=wuT2, wdT2=wdT2,
            esel16=esel16, idf3=idf, **stage_ix(k),
        ))
    return in_maps


def kernel_mega(**I):
    ctx = _get_ctx("MEGA", build_mega)
    res = run_launch(ctx, stage_mega(I))
    full = np.zeros((2, 8192, 1024), np.float32)
    for c, (b, k) in enumerate(CORES):
        full[b, 2048 * k:2048 * k + 2048] = np.asarray(res[c]["out"]).T
    return full


# version 09 (final)
# speedup vs baseline: 1.0173x; 1.0052x over previous
import contextlib
import math

import numpy as np
import ml_dtypes

import concourse.bass as bass
import concourse.mybir as mybir
from concourse.bass_utils import run_bass_kernel_spmd

F32 = mybir.dt.float32
BF16 = mybir.dt.bfloat16
AF = mybir.ActivationFunctionType
ALU = mybir.AluOpType
AX = mybir.AxisListType
BF16NP = ml_dtypes.bfloat16

NEG = -3.0e38


class Op:
    __slots__ = ("engine", "sem", "val", "kind")

    def __init__(self, engine, sem, val, kind):
        self.engine = engine
        self.sem = sem
        self.val = val
        self.kind = kind


class Buf:
    __slots__ = ("name", "last_write", "readers", "dma_sem", "dma_writes", "dma_base")

    def __init__(self, name):
        self.name = name
        self.last_write = None
        self.readers = []
        self.dma_sem = None
        self.dma_writes = 0
        self.dma_base = 0


ENGINES = ("sync", "tensor", "vector", "scalar", "gpsimd")


class Prog:
    def __init__(self, nc, es):
        self.nc = nc
        self.es = es
        self.sems = {e: es.enter_context(nc.semaphore(f"S_{e}")) for e in ENGINES}
        self.counts = {e: 0 for e in ENGINES}
        self.items = {e: [] for e in ENGINES}
        self.clock = {e: {} for e in ENGINES}
        self.out_bufs = []
        self.dma_bufs = []
        self.sem_pool = []
        self._nsem = len(ENGINES)

    def new_sem(self, name):
        self._nsem += 1
        assert self._nsem < 200, "semaphore budget exceeded"
        return self.es.enter_context(self.nc.semaphore(name))

    def buf(self, name):
        return Buf(name)

    def _wait(self, engine, sem, val):
        ck = self.clock[engine]
        k = id(sem)
        if ck.get(k, -1) >= val:
            return
        ck[k] = val
        self.items[engine].append(("wait", sem, val))

    def _dep(self, engine, dep):
        if dep is None:
            return
        if dep.engine == engine and engine == "tensor" and dep.kind == "compute":
            return
        self._wait(engine, dep.sem, dep.val)

    def _pre(self, engine, reads, writes, extra):
        deps = []
        for b in reads:
            deps.append(b.last_write)
        for b in writes:
            deps.extend(b.readers)
            deps.append(b.last_write)
        deps.extend(extra)
        for d in deps:
            self._dep(engine, d)

    def _post(self, op, reads, writes):
        for b in reads:
            b.readers.append(op)
        for b in writes:
            b.last_write = op
            b.readers = []

    def op(self, engine, fn, reads=(), writes=(), extra=(), tag=""):
        self._pre(engine, reads, writes, extra)
        sem = self.sems[engine]
        self.counts[engine] += 1
        val = self.counts[engine]
        self.items[engine].append(("emit", fn, [(sem, 1)], tag))
        op = Op(engine, sem, val, "compute")
        self._post(op, reads, writes)
        return op

    def dma(self, engine, out_ap, in_ap, reads=(), writes=(), extra=(), tag="", out_dram=False):
        assert len(writes) == 1
        wb = writes[0]
        if wb.dma_sem is None:
            if self.sem_pool:
                wb.dma_sem, wb.dma_base = self.sem_pool.pop()
            else:
                wb.dma_sem = self.new_sem(f"D{self._nsem}_{wb.name}")
                wb.dma_base = 0
            self.dma_bufs.append(wb)
        self._pre(engine, reads, writes, extra)
        wb.dma_writes += 1
        val = 16 * (wb.dma_base + wb.dma_writes)
        sem = wb.dma_sem

        def fn(eng, out_ap=out_ap, in_ap=in_ap):
            return eng.dma_start(out=out_ap, in_=in_ap)

        self.items[engine].append(("emit", fn, [(sem, 16)], tag))
        op = Op(engine, sem, val, "dma")
        self._post(op, reads, writes)
        if out_dram and wb not in self.out_bufs:
            self.out_bufs.append(wb)
        return op

    def finalize(self):
        for e in ("sync", "gpsimd"):
            for b in self.out_bufs:
                if b.dma_sem is not None:
                    self._wait(e, b.dma_sem, 16 * (b.dma_base + b.dma_writes))
        with self.nc.Block() as block:
            for e in ENGINES:
                if not self.items[e]:
                    continue
                items = self.items[e]

                def mk(items=items):
                    def section(eng):
                        for it in items:
                            if it[0] == "wait":
                                eng.wait_ge(it[1], it[2])
                            else:
                                ins = it[1](eng)
                                for (sem, k) in it[2]:
                                    ins.then_inc(sem, k)
                    return section
                getattr(block, e)(mk())


def ceil_div(a, b):
    return (a + b - 1) // b


def tiles(total, step):
    out = []
    o = 0
    while o < total:
        out.append((o, min(step, total - o)))
        o += step
    return out


class LaunchCtx:

    def __init__(self, name):
        self.nc = bass.Bass("TRN2", target_bir_lowering=False, debug=False)
        self.es = contextlib.ExitStack()
        self.prog = None
        self.name = name
        self._uniq = 0
        self._cb = {}

    def cbias(self, val, parts=128):
        if val not in self._cb:
            t, b = self.sbuf(f"cb{len(self._cb)}", [128, 1], F32)
            self.prog.op("vector", lambda v, t=t, val=val: v.memset(t[:], float(val)),
                         reads=[], writes=[b])
            self._cb[val] = (t, b)
        t, b = self._cb[val]
        return t[0:parts, 0:1], b

    def start(self):
        self.prog = Prog(self.nc, self.es)
        return self.prog

    def uniq(self):
        self._uniq += 1
        return self._uniq

    def sbuf(self, name, shape, dtype):
        t = self.es.enter_context(self.nc.sbuf_tensor(name, shape, dtype))
        return t, self.prog.buf(name)

    def psum(self, name, shape, dtype=F32):
        t = self.es.enter_context(self.nc.psum_tensor(name, shape, dtype))
        return t, self.prog.buf(name)

    io = None

    def din(self, name, shape, dtype=F32):
        if self.io is not None and name in self.io:
            return self.io[name]
        return self.nc.dram_tensor(name, shape, dtype, kind="ExternalInput").ap()

    def dout(self, name, shape, dtype=F32):
        if self.io is not None and name in self.io:
            return self.io[name], self.prog.buf(name + f"_p{self.uniq()}")
        ap = self.nc.dram_tensor(name, shape, dtype, kind="ExternalOutput").ap()
        return ap, self.prog.buf(name)

    def close(self):
        self.es.close()


def run_launch(ctx, in_maps, n_cores=8):
    res = run_bass_kernel_spmd(ctx.nc, in_maps, core_ids=list(range(n_cores)))
    return res.results


def emit_mm_acc(prog, psum_ap, pairs, reads, writes, dt_note=""):
    def fn(t):
        last = None
        n = len(pairs)
        for i, (l, r) in enumerate(pairs):
            last = t.matmul(psum_ap, l, r, start=(i == 0), stop=(i == n - 1))
        return last
    return prog.op("tensor", fn, reads=reads, writes=writes, tag="mm" + dt_note)


def emit_transpose(prog, psum_ap, in_ap, ident_ap, reads, writes):
    def fn(t):
        return t.transpose(psum_ap, in_ap, ident_ap)
    return prog.op("tensor", fn, reads=reads, writes=writes, tag="tr")


def emit(prog, engine, fn, reads, writes, tag=""):
    return prog.op(engine, fn, reads=reads, writes=writes, tag=tag)


def _f32c(a):
    return np.ascontiguousarray(a, dtype=np.float32)


def stage_vec_cols(v, nchunks=None):
    v = np.asarray(v, np.float32)
    d = v.shape[0]
    if nchunks is None:
        nchunks = ceil_div(d, 128)
    pad = nchunks * 128 - d
    if pad:
        v = np.concatenate([v, np.zeros(pad, np.float32)])
    return _f32c(v.reshape(nchunks, 128).T)




def stage_ktiles(wT, kc=8):
    wT = np.asarray(wT, np.float32)
    K, N = wT.shape
    assert K == kc * 128 and N % 128 == 0
    a = wT.reshape(kc, 128, N // 128, 128).transpose(2, 1, 0, 3)
    return _f32c(a.reshape(N, kc * 128))


def stage_ktiles3(w, kc=8):
    w = np.asarray(w, np.float32)
    E, K, N = w.shape
    return _f32c(np.stack([stage_ktiles(w[e], kc) for e in range(E)]).reshape(E * N, kc * 128))


CORES = [(c // 4, c % 4) for c in range(8)]
SH = 2048


def build_A1(n_tiles=5, dbg_skip=(), ctx=None, io=None):
    standalone = ctx is None
    if standalone:
        ctx = LaunchCtx("A1")
        p = ctx.start()
    else:
        p = ctx.prog
    ctx.io = io
    nc = ctx.nc

    TIN = 2176
    xin = ctx.din("xin", [1024, TIN])
    w_in_T = ctx.din("w_in_T", [4480, 1024])
    wln1c = ctx.din("wln1c", [128, 8])
    dtbc = ctx.din("dtbc", [32, 1])
    bconvc = ctx.din("bconvc", [128, 18])
    wconvc = ctx.din("wconvc", [128, 18 * 4])
    d_xbc, b_xbc = ctx.dout("xbc", [2304, SH], BF16)
    d_dt, b_dt = ctx.dout("dt", [32, SH])
    d_z, b_z = ctx.dout("z", [2048, SH])
    d_astat, b_astat = ctx.dout("astat", [2, SH])

    xt, b_xt = ctx.sbuf("xt", [128, 8 * 512], F32)
    hf, b_hf = ctx.sbuf("hf", [128, 8 * 512], F32)
    hb, b_hb = ctx.sbuf("hb", [128, 8 * 512], BF16)
    wst0, b_wst0 = ctx.sbuf("wst0", [128, 1024], F32)
    wst1, b_wst1 = ctx.sbuf("wst1", [128, 1024], F32)
    wbuf, b_wbuf = ctx.sbuf("wbuf", [128, 35 * 1024], BF16)
    wdtf, b_wdtf = ctx.sbuf("wdtf", [128, 8 * 32], F32)
    wln1, b_wln1 = ctx.sbuf("wln1", [128, 8], F32)
    dtb, b_dtb = ctx.sbuf("dtb", [32, 1], F32)
    bconv, b_bconv = ctx.sbuf("bconv", [128, 18], F32)
    wconv, b_wconv = ctx.sbuf("wconv", [128, 72], F32)
    onesc, b_onesc = ctx.sbuf("onesc", [128, 1], F32)
    ones1, b_ones1 = ctx.sbuf("ones1", [1, 128], F32)
    xraw, b_xraw = ctx.sbuf("xraw", [128, 515], F32)
    carry, b_carry = ctx.sbuf("carry", [128, 64], F32)
    convt, b_convt = ctx.sbuf("convt", [128, 512], F32)
    convt2, b_convt2 = ctx.sbuf("convt2", [128, 512], F32)
    xbcs, b_xbcs = ctx.sbuf("xbcs", [128, 512], BF16)
    zt, b_zt = ctx.sbuf("zt", [128, 512], F32)
    dtt, b_dtt = ctx.sbuf("dtt", [32, 512], F32)
    dtt2, b_dtt2 = ctx.sbuf("dtt2", [32, 512], F32)
    dtt3, b_dtt3 = ctx.sbuf("dtt3", [32, 512], F32)
    sq, b_sq = ctx.sbuf("sq", [128, 512], F32)
    srow, b_srow = ctx.sbuf("srow", [1, 6 * 512], F32)

    ps1, b_ps1 = ctx.psum("ps1", [1, 512])
    ps2, b_ps2 = ctx.psum("ps2", [1, 512])
    pbm, b_pbm = ctx.psum("pbm", [128, 512])
    pbs, b_pbs = ctx.psum("pbs", [128, 512])
    pp0, b_pp0 = ctx.psum("pp0", [128, 512])
    pp1, b_pp1 = ctx.psum("pp1", [128, 512])
    pdt, b_pdt = ctx.psum("pdt", [32, 512])

    emit(p, "vector", lambda v: v.memset(onesc[:], 1.0), [], [b_onesc])
    emit(p, "vector", lambda v: v.memset(ones1[:], 1.0), [], [b_ones1])
    p.dma("sync", wln1[:], wln1c[:, :], writes=[b_wln1])
    p.dma("sync", dtb[:], dtbc[:, :], writes=[b_dtb])
    p.dma("sync", bconv[:], bconvc[:, :], writes=[b_bconv])
    p.dma("sync", wconv[:], wconvc[:, :], writes=[b_wconv])
    emit(p, "vector", lambda v: v.memset(carry[:], 0.0), [], [b_carry])

    for oc in range(35):
        wst, b_wst = (wst0, b_wst0) if oc % 2 == 0 else (wst1, b_wst1)
        p.dma("sync", wst[:, 0:1024], w_in_T[oc * 128:(oc + 1) * 128, :],
              writes=[b_wst], tag=f"wld{oc}")
        emit(p, "vector", lambda v, oc=oc, wst=wst: v.tensor_copy(
            wbuf[:, oc * 1024:(oc + 1) * 1024], wst[:, 0:1024]),
            [b_wst], [b_wbuf], tag=f"wcast{oc}")
        if oc == 34:
            emit(p, "vector", lambda v, wst=wst: v.tensor_copy(
                wdtf[:].rearrange("p (k c) -> p k c", k=8),
                wst[:, 0:1024].rearrange("p (k c) -> p k c", k=8)[:, :, 0:32]),
                [b_wst], [b_wdtf])

    tlist = ([(0, 128)] + [(128 + i * 512, 512) for i in range(4)])[:n_tiles]
    prev_tt = None
    for ti, (o, tt) in enumerate(tlist):
        halo = (ti == 0)
        oo = o - 128
        for c in range(8):
            p.dma("sync", xt[:, c * 512: c * 512 + tt], xin[c * 128:(c + 1) * 128, o:o + tt],
                  writes=[b_xt], tag=f"x{ti}c{c}")
        if "stats" in dbg_skip:
            continue
        def mm_s1(t, tt=tt):
            last = None
            for c in range(8):
                last = t.matmul(ps1[:, 0:tt], onesc[:, :], xt[:, c * 512: c * 512 + tt],
                                start=(c == 0), stop=(c == 7))
            return last
        p.op("tensor", mm_s1, reads=[b_onesc, b_xt], writes=[b_ps1])
        for c in range(8):
            emit(p, "scalar", lambda s, c=c, tt=tt: s.activation(
                out=sq[:, 0:tt], in_=xt[:, c * 512: c * 512 + tt], func=AF.Square),
                [b_xt], [b_sq])
            emit(p, "tensor", lambda t, c=c, tt=tt: t.matmul(
                ps2[:, 0:tt], onesc[:, :], sq[:, 0:tt], start=(c == 0), stop=(c == 7)),
                [b_onesc, b_sq], [b_ps2])
        M, R, SS, T1, T2, T3 = (srow[0:1, i * 512:i * 512 + tt] for i in range(6))
        emit(p, "vector", lambda v, M=M, tt=tt: v.tensor_scalar_mul(M, ps1[0:1, 0:tt], 1.0 / 1024.0), [b_ps1], [b_srow])
        emit(p, "vector", lambda v, T1=T1, tt=tt: v.tensor_scalar_mul(T1, ps2[0:1, 0:tt], 1.0 / 1024.0), [b_ps2], [b_srow])
        emit(p, "scalar", lambda s, M=M, T2=T2: s.activation(out=T2, in_=M, func=AF.Square), [b_srow], [b_srow])
        emit(p, "vector", lambda v, T1=T1, T2=T2: v.tensor_sub(T1, T1, T2), [b_srow], [b_srow])
        cb5, b_cb5 = ctx.cbias(1e-5, 1)
        emit(p, "scalar", lambda s, T1=T1, T2=T2, cb5=cb5: s.activation(out=T2, in_=T1, func=AF.Sqrt, bias=cb5), [b_srow, b_cb5], [b_srow])
        emit(p, "vector", lambda v, T2=T2, R=R: v.reciprocal(R, T2), [b_srow], [b_srow])
        emit(p, "scalar", lambda s, R=R, T2=T2: s.activation(out=T2, in_=R, func=AF.Square), [b_srow], [b_srow])
        emit(p, "vector", lambda v, T1=T1, T2=T2, T3=T3: v.tensor_mul(T3, T1, T2), [b_srow], [b_srow])
        cb6, b_cb6 = ctx.cbias(1e-6, 1)
        emit(p, "scalar", lambda s, T3=T3, cb6=cb6: s.activation(out=T3, in_=T3, func=AF.Sqrt, bias=cb6), [b_srow, b_cb6], [b_srow])
        emit(p, "vector", lambda v, T3=T3: v.reciprocal(T3, T3), [b_srow], [b_srow])
        emit(p, "vector", lambda v, R=R, T3=T3, SS=SS: v.tensor_mul(SS, R, T3), [b_srow], [b_srow])
        if not halo:
            p.dma("gpsimd", d_astat[0:1, oo:oo + tt], M, reads=[b_srow], writes=[b_astat], out_dram=True)
            p.dma("gpsimd", d_astat[1:2, oo:oo + tt], R, reads=[b_srow], writes=[b_astat], out_dram=True)
        emit(p, "tensor", lambda t, M=M, tt=tt: t.matmul(pbm[:, 0:tt], ones1[0:1, :], M, start=True, stop=True),
             [b_ones1, b_srow], [b_pbm])
        emit(p, "tensor", lambda t, SS=SS, tt=tt: t.matmul(pbs[:, 0:tt], ones1[0:1, :], SS, start=True, stop=True),
             [b_ones1, b_srow], [b_pbs])
        for c in range(8):
            sl = slice(c * 512, c * 512 + tt)
            emit(p, "vector", lambda v, c=c, sl=sl, tt=tt: v.tensor_sub(hf[:, sl], xt[:, sl], pbm[:, 0:tt]),
                 [b_xt, b_pbm], [b_hf])
            emit(p, "vector", lambda v, c=c, sl=sl, tt=tt: v.scalar_tensor_tensor(
                hb[:, sl], hf[:, sl], wln1[:, c:c + 1], pbs[:, 0:tt], ALU.mult, ALU.mult),
                [b_hf, b_wln1, b_pbs], [b_hb])
            emit(p, "vector", lambda v, c=c, sl=sl, tt=tt: v.scalar_tensor_tensor(
                hf[:, sl], hf[:, sl], wln1[:, c:c + 1], pbs[:, 0:tt], ALU.mult, ALU.mult),
                [b_hf, b_wln1, b_pbs], [b_hf])
        if "proj" in dbg_skip:
            prev_tt = tt
            continue
        oc_list = list(range(16, 34)) + [34] + ([] if halo else list(range(16)))
        if "z" in dbg_skip:
            oc_list = [oc for oc in oc_list if oc >= 16]
        if "dt" in dbg_skip:
            oc_list = [oc for oc in oc_list if oc != 34]
        if "xbc" in dbg_skip:
            oc_list = [oc for oc in oc_list if not (16 <= oc < 34)]
        for oc in oc_list:
            if oc == 34:
                def mmdt(t, tt=tt):
                    last = None
                    for kc in range(8):
                        last = t.matmul(pdt[:, 0:tt], wdtf[:, kc * 32:(kc + 1) * 32],
                                        hf[:, kc * 512: kc * 512 + tt], start=(kc == 0), stop=(kc == 7))
                    return last
                p.op("tensor", mmdt, reads=[b_wdtf, b_hf], writes=[b_pdt])
                emit(p, "vector", lambda v, tt=tt: v.tensor_scalar_add(dtt[:, 0:tt], pdt[:, 0:tt], dtb[:, 0:1]),
                     [b_pdt, b_dtb], [b_dtt])
                emit(p, "vector", lambda v, tt=tt: v.tensor_scalar_mul(dtt2[:, 0:tt], dtt[:, 0:tt], -1.0),
                     [b_dtt], [b_dtt2])
                emit(p, "vector", lambda v, tt=tt: v.tensor_tensor(dtt2[:, 0:tt], dtt2[:, 0:tt], dtt[:, 0:tt], ALU.min),
                     [b_dtt2, b_dtt], [b_dtt2])
                emit(p, "scalar", lambda s, tt=tt: s.activation(out=dtt2[:, 0:tt], in_=dtt2[:, 0:tt], func=AF.Exp),
                     [b_dtt2], [b_dtt2])
                cbo, b_cbo = ctx.cbias(1.0, 32)
                emit(p, "scalar", lambda s, tt=tt, cbo=cbo: s.activation(out=dtt2[:, 0:tt], in_=dtt2[:, 0:tt], func=AF.Ln, bias=cbo),
                     [b_dtt2, b_cbo], [b_dtt2])
                emit(p, "vector", lambda v, tt=tt: v.tensor_scalar(dtt3[:, 0:tt], dtt[:, 0:tt], 0.0, None, ALU.max),
                     [b_dtt], [b_dtt3])
                emit(p, "vector", lambda v, tt=tt: v.tensor_add(dtt3[:, 0:tt], dtt3[:, 0:tt], dtt2[:, 0:tt]),
                     [b_dtt3, b_dtt2], [b_dtt3])
                if not halo:
                    p.dma("gpsimd", d_dt[:, oo:oo + tt], dtt3[:, 0:tt], reads=[b_dtt3], writes=[b_dt], out_dram=True)
                continue
            pp, b_pp = (pp0, b_pp0) if oc % 2 == 0 else (pp1, b_pp1)
            def mmoc(t, oc=oc, tt=tt, pp=pp):
                last = None
                for kc in range(8):
                    last = t.matmul(pp[:, 0:tt], wbuf[:, oc * 1024 + kc * 128: oc * 1024 + (kc + 1) * 128],
                                    hb[:, kc * 512: kc * 512 + tt], start=(kc == 0), stop=(kc == 7))
                return last
            p.op("tensor", mmoc, reads=[b_wbuf, b_hb], writes=[b_pp])
            if oc < 16:
                emit(p, "vector", lambda v, pp=pp, tt=tt: v.tensor_copy(zt[:, 0:tt], pp[:, 0:tt]),
                     [b_pp], [b_zt])
                p.dma("gpsimd", d_z[oc * 128:(oc + 1) * 128, oo:oo + tt], zt[:, 0:tt],
                      reads=[b_zt], writes=[b_z], out_dram=True)
            else:
                c = oc - 16
                emit(p, "vector", lambda v, c=c: v.tensor_copy(xraw[:, 0:3], carry[:, 3 * c:3 * c + 3]),
                     [b_carry], [b_xraw])
                emit(p, "vector", lambda v, tt=tt, pp=pp: v.tensor_copy(xraw[:, 3:3 + tt], pp[:, 0:tt]),
                     [b_pp], [b_xraw])
                emit(p, "vector", lambda v, c=c, tt=tt: v.tensor_copy(carry[:, 3 * c:3 * c + 3], xraw[:, tt:tt + 3]),
                     [b_xraw], [b_carry])
                if not halo:
                    emit(p, "vector", lambda v, c=c, tt=tt: v.tensor_scalar_mul(
                        convt[:, 0:tt], xraw[:, 0:tt], wconv[:, c * 4: c * 4 + 1]),
                        [b_xraw, b_wconv], [b_convt])
                    emit(p, "vector", lambda v, c=c, tt=tt: v.scalar_tensor_tensor(
                        convt2[:, 0:tt], xraw[:, 1:1 + tt], wconv[:, c * 4 + 1: c * 4 + 2], convt[:, 0:tt],
                        ALU.mult, ALU.add), [b_xraw, b_wconv, b_convt], [b_convt2])
                    emit(p, "vector", lambda v, c=c, tt=tt: v.scalar_tensor_tensor(
                        convt[:, 0:tt], xraw[:, 2:2 + tt], wconv[:, c * 4 + 2: c * 4 + 3], convt2[:, 0:tt],
                        ALU.mult, ALU.add), [b_xraw, b_wconv, b_convt2], [b_convt])
                    emit(p, "vector", lambda v, c=c, tt=tt: v.scalar_tensor_tensor(
                        convt2[:, 0:tt], xraw[:, 3:3 + tt], wconv[:, c * 4 + 3: c * 4 + 4], convt[:, 0:tt],
                        ALU.mult, ALU.add), [b_xraw, b_wconv, b_convt], [b_convt2])
                    emit(p, "scalar", lambda s, c=c, tt=tt: s.activation(
                        out=xbcs[:, 0:tt], in_=convt2[:, 0:tt], func=AF.Silu, bias=bconv[:, c:c + 1]),
                        [b_convt2, b_bconv], [b_xbcs])
                    p.dma("gpsimd", d_xbc[c * 128:(c + 1) * 128, oo:oo + tt], xbcs[:, 0:tt],
                          reads=[b_xbcs], writes=[b_xbc], out_dram=True)
        prev_tt = tt
    ctx.io = None
    if standalone:
        p.finalize()
    return ctx


def stage_A1(I):
    x = I["L0_attn_hidden_states"]
    xfm = np.ascontiguousarray(x.transpose(0, 2, 1), dtype=np.float32)
    w_in_T = np.zeros((1024, 4480), np.float32)
    w_in_T[:, 0:4384] = np.asarray(I["L0_attn_w_in"], np.float32).T
    w_in_T = _f32c(stage_ktiles(np.concatenate([w_in_T], axis=0), kc=8)) if False else stage_ktiles(w_in_T)
    wln1c = stage_vec_cols(I["L0_attn_w_ln1"], 8)
    dtbc = _f32c(np.asarray(I["L0_attn_dt_bias"], np.float32).reshape(32, 1))
    bconvc = stage_vec_cols(I["L0_attn_b_conv"], 18)
    wc = np.asarray(I["L0_attn_w_conv"], np.float32)
    wconvc = _f32c(wc.reshape(18, 128, 4).transpose(1, 0, 2).reshape(128, 72))
    in_maps = []
    for (b, k) in CORES:
        lo = 2048 * k - 128
        if lo < 0:
            xin = np.concatenate([np.zeros((1024, 128), np.float32), xfm[b][:, 0:2048]], axis=1)
        else:
            xin = np.ascontiguousarray(xfm[b][:, lo:lo + 2176])
        in_maps.append(dict(xin=xin, w_in_T=w_in_T, wln1c=wln1c, dtbc=dtbc,
                            bconvc=bconvc, wconvc=wconvc))
    return in_maps


def build_A2(ctx=None, io=None):
    standalone = ctx is None
    if standalone:
        ctx = LaunchCtx("A2")
        p = ctx.start()
    else:
        p = ctx.prog
    ctx.io = io
    nc = ctx.nc
    S = 8192
    HS = 4096
    NCH = 32

    xs_d = ctx.din("xs", [512, S], BF16)
    B_d = ctx.din("Bf", [128, S], BF16)
    C_d = ctx.din("Cf", [128, S], BF16)
    dth_d = ctx.din("dth", [8, S])
    alog_d = ctx.din("alog", [8, 1])
    dd_d = ctx.din("dd", [1, 8])
    idf_d = ctx.din("idf", [128, 128])
    idb_d = ctx.din("idb", [128, 128], BF16)
    maskT_d = ctx.din("maskT", [128, 128])
    selpair_d = ctx.din("selpair", [8, 128])
    sel8_d = ctx.din("sel8", [8, 128])
    y_d, b_y = ctx.dout("y", [512, S], BF16)

    xs, b_xs = ctx.sbuf("s_xs", [128, 4 * HS], BF16)
    Bf, b_Bf = ctx.sbuf("s_Bf", [128, HS], BF16)
    Cf, b_Cf = ctx.sbuf("s_Cf", [128, HS], BF16)
    dth, b_dth = ctx.sbuf("s_dth", [8, HS], F32)
    acs, b_acs = ctx.sbuf("acs", [8, HS], F32)
    dtds, b_dtds = ctx.sbuf("dtds", [8, HS], F32)
    alast, b_alast = ctx.sbuf("s_alast", [8, NCH], F32)
    expal, b_expal = ctx.sbuf("expal", [8, NCH], F32)
    selrhs, b_selrhs = ctx.sbuf("selrhs", [8, 4 * NCH], F32)
    selpair, b_selpair = ctx.sbuf("s_selpair", [8, 4 * NCH], F32)
    sel8, b_sel8 = ctx.sbuf("s_sel8", [8, 128], F32)
    acol, b_acol = ctx.sbuf("acol", [8, 1], F32)
    ddr, b_ddr = ctx.sbuf("ddr", [1, 8], F32)
    dcols, b_dcols = ctx.sbuf("dcols", [128, 8], F32)
    deccols, b_deccols = ctx.sbuf("deccols", [128, 4 * NCH], F32)
    ones8, b_ones8 = ctx.sbuf("ones8", [8, 128], F32)
    ones1, b_ones1 = ctx.sbuf("ones1", [1, 128], F32)
    idf, b_idf = ctx.sbuf("s_idf", [128, 128], F32)
    idb, b_idb = ctx.sbuf("s_idb", [128, 128], BF16)
    maskT, b_maskT = ctx.sbuf("s_maskT", [128, 128], F32)
    Sst, b_Sst = ctx.sbuf("Sst", [128, 512], F32)
    Sbf, b_Sbf = ctx.sbuf("Sbf", [128, 512], BF16)
    acsT, b_acsT = ctx.sbuf("acsT", [128, 8], F32)
    dtT, b_dtT = ctx.sbuf("dtT", [128, 8], F32)
    dtdsT, b_dtdsT = ctx.sbuf("dtdsT", [128, 8], F32)
    rows, b_rows = ctx.sbuf("rows", [1, 8 * 128], F32)
    xtm, b_xtm = ctx.sbuf("xtm", [128, 128], BF16)
    btm, b_btm = ctx.sbuf("btm", [128, 128], BF16)
    x2, b_x2 = ctx.sbuf("x2", [128, 128], BF16)
    cbt, b_cbt = ctx.sbuf("cbt", [128, 128], F32)
    g1, b_g1 = ctx.sbuf("g1", [128, 128], F32)
    g2, b_g2 = ctx.sbuf("g2", [128, 128], F32)
    g3, b_g3 = ctx.sbuf("g3", [128, 128], F32)
    gb, b_gb = ctx.sbuf("gb", [128, 128], BF16)
    ebq, b_ebq = ctx.sbuf("ebq", [128, 128], F32)
    cph, b_cph = ctx.sbuf("cph", [128, 128], BF16)
    ysb, b_ysb = ctx.sbuf("ysb", [128, 2 * 128], BF16)

    pMISC, b_pMISC = ctx.psum("pMISC", [128, 512])
    pTb, b_pTb = ctx.psum("pTb", [128, 1024], BF16)
    pBCQ, b_pBCQ = ctx.psum("pBCQ", [128, 128])
    pCB, b_pCB = ctx.psum("pCB", [128, 128])
    pYa, b_pYa = ctx.psum("pYa", [128, 128])
    pYb, b_pYb = ctx.psum("pYb", [128, 128])
    pS, b_pS = ctx.psum("pS", [128, 128])

    p.dma("sync", acol[:], alog_d[:, :], writes=[b_acol])
    p.dma("sync", ddr[:], dd_d[:, :], writes=[b_ddr])
    p.dma("sync", idf[:], idf_d[:, :], writes=[b_idf])
    p.dma("sync", idb[:], idb_d[:, :], writes=[b_idb])
    p.dma("sync", maskT[:], maskT_d[:, :], writes=[b_maskT])
    p.dma("sync", selpair[:], selpair_d[:, :], writes=[b_selpair])
    p.dma("sync", sel8[:], sel8_d[:, :], writes=[b_sel8])
    emit(p, "vector", lambda v: v.memset(ones8[:], 1.0), [], [b_ones8])
    emit(p, "vector", lambda v: v.memset(ones1[:], 1.0), [], [b_ones1])
    emit(p, "vector", lambda v: v.memset(Sst[:], 0.0), [], [b_Sst])
    emit(p, "scalar", lambda s: s.activation(out=acol[:], in_=acol[:], func=AF.Exp), [b_acol], [b_acol])
    emit(p, "vector", lambda v: v.tensor_scalar_mul(acol[:], acol[:], -1.0), [b_acol], [b_acol])
    emit(p, "tensor", lambda t: t.matmul(pMISC[:, 0:8], ones1[0:1, :], ddr[0:1, :], start=True, stop=True),
         [b_ones1, b_ddr], [b_pMISC])
    emit(p, "vector", lambda v: v.tensor_copy(dcols[:], pMISC[:, 0:8]), [b_pMISC], [b_dcols])

    for half in range(2):
        HOFF = half * HS
        for pp in range(4):
            p.dma("sync", xs[:, pp * HS:(pp + 1) * HS], xs_d[pp * 128:(pp + 1) * 128, HOFF:HOFF + HS], writes=[b_xs])
        p.dma("sync", Bf[:], B_d[:, HOFF:HOFF + HS], writes=[b_Bf])
        p.dma("sync", Cf[:], C_d[:, HOFF:HOFF + HS], writes=[b_Cf])
        p.dma("sync", dth[:], dth_d[:, HOFF:HOFF + HS], writes=[b_dth])

        emit(p, "vector", lambda v: v.tensor_scalar_mul(dtds[:], dth[:], acol[:, 0:1]), [b_dth, b_acol], [b_dtds])
        for ci in range(NCH):
            sl = slice(ci * 128, (ci + 1) * 128)
            emit(p, "vector", lambda v, sl=sl: v.tensor_tensor_scan(
                acs[:, sl], ones8[:, :], dtds[:, sl], 0.0, ALU.mult, ALU.add), [b_ones8, b_dtds], [b_acs])
        for ci in range(NCH):
            sl = slice(ci * 128, (ci + 1) * 128)
            emit(p, "vector", lambda v, sl=sl, ci=ci: v.tensor_scalar(
                dtds[:, sl], acs[:, sl], -1.0, acs[:, ci * 128 + 127: ci * 128 + 128], ALU.mult, ALU.add),
                [b_acs], [b_dtds])
        emit(p, "scalar", lambda s: s.activation(out=dtds[:], in_=dtds[:], func=AF.Exp), [b_dtds], [b_dtds])
        emit(p, "vector", lambda v: v.tensor_mul(dtds[:], dth[:], dtds[:]), [b_dth, b_dtds], [b_dtds])
        emit(p, "vector", lambda v: v.tensor_copy(alast[:], acs[:, 127::128]), [b_acs], [b_alast])
        emit(p, "scalar", lambda s: s.activation(out=expal[:], in_=alast[:], func=AF.Exp), [b_alast], [b_expal])
        for pp in range(4):
            emit(p, "vector", lambda v, pp=pp: v.tensor_copy(selrhs[:, pp * NCH:(pp + 1) * NCH], expal[:]),
                 [b_expal], [b_selrhs])
        emit(p, "vector", lambda v: v.tensor_mul(selrhs[:], selrhs[:], selpair[:]), [b_selrhs, b_selpair], [b_selrhs])
        emit(p, "tensor", lambda t: t.matmul(pMISC[:, 256:256 + 4 * NCH], sel8[:, :], selrhs[:, :], start=True, stop=True),
             [b_sel8, b_selrhs], [b_pMISC])
        emit(p, "vector", lambda v: v.tensor_copy(deccols[:], pMISC[:, 256:256 + 4 * NCH]), [b_pMISC], [b_deccols])

        for ci in range(NCH):
            sl = slice(ci * 128, (ci + 1) * 128)
            gci = HOFF + ci * 128
            for (src, b_src, dst, b_dst) in ((acs, b_acs, acsT, b_acsT), (dth, b_dth, dtT, b_dtT), (dtds, b_dtds, dtdsT, b_dtdsT)):
                emit_transpose(p, pMISC[0:128, 0:8], src[:, sl], idf[0:8, 0:8], [b_src, b_idf], [b_pMISC])
                emit(p, "vector", lambda v, dst=dst: v.tensor_copy(dst[:], pMISC[0:128, 0:8]), [b_pMISC], [b_dst])
            for h in range(8):
                emit_transpose(p, pMISC[0:1, 128:256], acsT[:, h:h + 1], idf[:, :], [b_acsT, b_idf], [b_pMISC])
                emit(p, "vector", lambda v, h=h: v.tensor_copy(rows[0:1, h * 128:(h + 1) * 128], pMISC[0:1, 128:256]),
                     [b_pMISC], [b_rows])
            emit(p, "tensor", lambda t, sl=sl: t.matmul(pCB[:, :], Bf[:, sl], Cf[:, sl], start=True, stop=True),
                 [b_Bf, b_Cf], [b_pCB])
            emit(p, "vector", lambda v: v.tensor_copy(cbt[:], pCB[:, :]), [b_pCB], [b_cbt])
            emit_transpose(p, pTb[:, 0:128], Bf[:, sl], idb[:, :], [b_Bf, b_idb], [b_pTb])
            emit(p, "vector", lambda v: v.tensor_copy(btm[:], pTb[:, 0:128]), [b_pTb], [b_btm])
            for pp in range(4):
                emit_transpose(p, pMISC[0:128, 128:256], Sst[:, pp * 128:(pp + 1) * 128], idf[:, :],
                               [b_Sst, b_idf], [b_pMISC])
                emit(p, "vector", lambda v, pp=pp: v.tensor_copy(Sbf[:, pp * 128:(pp + 1) * 128], pMISC[0:128, 128:256]),
                     [b_pMISC], [b_Sbf])
                xsl = slice(pp * HS + ci * 128, pp * HS + (ci + 1) * 128)
                emit_transpose(p, pTb[:, 128:256], xs[:, xsl], idb[:, :], [b_xs, b_idb], [b_pTb])
                emit(p, "vector", lambda v: v.tensor_copy(xtm[:], pTb[:, 128:256]), [b_pTb], [b_xtm])
                for hf in range(2):
                    h = pp * 2 + hf
                    emit(p, "vector", lambda v, hf=hf, h=h: v.tensor_scalar_mul(
                        x2[:, hf * 64:(hf + 1) * 64], xtm[:, hf * 64:(hf + 1) * 64], dtdsT[:, h:h + 1]),
                        [b_xtm, b_dtdsT], [b_x2])
                pY, b_pY = (pYa, b_pYa) if pp % 2 == 0 else (pYb, b_pYb)
                for hf in range(2):
                    h = pp * 2 + hf
                    emit(p, "tensor", lambda t, h=h: t.matmul(
                        pBCQ[:, :], ones1[0:1, :], rows[0:1, h * 128:(h + 1) * 128], start=True, stop=True),
                        [b_ones1, b_rows], [b_pBCQ])
                    emit(p, "vector", lambda v, h=h: v.scalar_tensor_tensor(
                        g1[:], pBCQ[:, :], acsT[:, h:h + 1], maskT[:], ALU.subtract, ALU.min),
                        [b_pBCQ, b_acsT, b_maskT], [b_g1])
                    emit(p, "scalar", lambda s: s.activation(out=g2[:], in_=g1[:], func=AF.Exp), [b_g1], [b_g2])
                    emit(p, "vector", lambda v, h=h: v.scalar_tensor_tensor(
                        g3[:], g2[:], dtT[:, h:h + 1], cbt[:], ALU.mult, ALU.mult),
                        [b_g2, b_dtT, b_cbt], [b_g3])
                    emit(p, "vector", lambda v, h=h: v.scalar_tensor_tensor(
                        gb[:], idf[:, :], dcols[:, h:h + 1], g3[:], ALU.mult, ALU.add),
                        [b_idf, b_dcols, b_g3], [b_gb])
                    emit(p, "tensor", lambda t, hf=hf, pY=pY: t.matmul(
                        pY[hf * 64:(hf + 1) * 64, :], xtm[:, hf * 64:(hf + 1) * 64], gb[:, :],
                        start=True, stop=False),
                        [b_xtm, b_gb], [b_pY])
                    emit(p, "scalar", lambda s: s.activation(out=ebq[:], in_=pBCQ[:, :], func=AF.Exp), [b_pBCQ], [b_ebq])
                    emit(p, "vector", lambda v, sl=sl: v.tensor_tensor(cph[:], Cf[:, sl], ebq[:], ALU.mult),
                         [b_Cf, b_ebq], [b_cph])
                    emit(p, "tensor", lambda t, pp=pp, hf=hf, pY=pY: t.matmul(
                        pY[hf * 64:(hf + 1) * 64, :], Sbf[:, pp * 128 + hf * 64: pp * 128 + (hf + 1) * 64], cph[:, :],
                        start=False, stop=(hf == 1)),
                        [b_Sbf, b_cph], [b_pY])
                emit(p, "tensor", lambda t: t.matmul(pS[:, :], x2[:, :], btm[:, :], start=True, stop=True),
                     [b_btm, b_x2], [b_pS])
                emit(p, "vector", lambda v, pp=pp, ci=ci: v.scalar_tensor_tensor(
                    Sst[:, pp * 128:(pp + 1) * 128], Sst[:, pp * 128:(pp + 1) * 128],
                    deccols[:, pp * NCH + ci: pp * NCH + ci + 1], pS[:, :], ALU.mult, ALU.add),
                    [b_Sst, b_deccols, b_pS], [b_Sst])
                ysl = slice((pp % 2) * 128, (pp % 2) * 128 + 128)
                emit(p, "vector", lambda v, ysl=ysl, pY=pY: v.tensor_copy(ysb[:, ysl], pY[:, :]), [b_pY], [b_ysb])
                p.dma("gpsimd", y_d[pp * 128:(pp + 1) * 128, gci:gci + 128], ysb[:, ysl],
                      reads=[b_ysb], writes=[b_y], out_dram=True)
    ctx.io = None
    if standalone:
        p.finalize()
    return ctx


def stage_A2(I, a1_res):
    NEGv = NEG
    idf = np.eye(128, dtype=np.float32)
    idb = idf.astype(BF16NP)
    sq = np.arange(128)
    maskT = np.where(sq[:, None] <= sq[None, :], 0.0, NEGv).astype(np.float32)
    selpair = np.zeros((8, 128), np.float32)
    for h in range(8):
        selpair[h, (h // 2) * 32:(h // 2) * 32 + 32] = 1.0
    sel8 = np.zeros((8, 128), np.float32)
    for h in range(8):
        sel8[h, (h % 2) * 64:(h % 2) * 64 + 64] = 1.0
    alog_full = np.asarray(I["L0_attn_A_log"], np.float32)
    d_full = np.asarray(I["L0_attn_D"], np.float32)
    xall = {}
    dtall = {}
    for b in range(2):
        xall[b] = np.concatenate([np.asarray(a1_res[b * 4 + k]["xbc"]) for k in range(4)], axis=1)
        dtall[b] = np.concatenate([np.asarray(a1_res[b * 4 + k]["dt"]) for k in range(4)], axis=1)
    in_maps = []
    for (b, k) in CORES:
        in_maps.append(dict(
            xs=np.ascontiguousarray(xall[b][512 * k:512 * k + 512]),
            Bf=np.ascontiguousarray(xall[b][2048:2176]),
            Cf=np.ascontiguousarray(xall[b][2176:2304]),
            dth=np.ascontiguousarray(dtall[b][8 * k:8 * k + 8]),
            alog=np.ascontiguousarray(alog_full[8 * k:8 * k + 8].reshape(8, 1)),
            dd=np.ascontiguousarray(d_full[8 * k:8 * k + 8].reshape(1, 8)),
            idf=idf, idb=idb, maskT=maskT, selpair=selpair, sel8=sel8,
        ))
    return in_maps


def build_A3(ctx=None, io=None):
    standalone = ctx is None
    if standalone:
        ctx = LaunchCtx("A3")
        p = ctx.start()
    else:
        p = ctx.prog
    ctx.io = io
    nc = ctx.nc

    x_d = ctx.din("xin3", [1024, SH])
    astat_d = ctx.din("astat3", [2, SH])
    z_d = ctx.din("zin", [2048, SH])
    y_d = ctx.din("yin", [2048, SH], BF16)
    wmn_d = ctx.din("wmnc", [128, 16])
    wout_d = ctx.din("wout_T", [1024, 2048])
    wln2_d = ctx.din("wln2c", [128, 8])
    wg_d = ctx.din("wg_T", [4096, 1024])
    wu_d = ctx.din("wu_T", [4096, 1024])
    wd_d = ctx.din("wd_T", [1024, 4096])
    x1_d, b_x1 = ctx.dout("x1", [1024, SH])

    xt, b_xt = ctx.sbuf("xt", [128, 8 * 512], F32)
    asm, b_asm = ctx.sbuf("asm", [1, 512], F32)
    asr, b_asr = ctx.sbuf("asr", [1, 512], F32)
    zt0, b_zt0 = ctx.sbuf("zt0", [128, 512], F32)
    zt1, b_zt1 = ctx.sbuf("zt1", [128, 512], F32)
    yt, b_yt = ctx.sbuf("yt", [128, 16 * 512], BF16)
    zs, b_zs = ctx.sbuf("zs", [128, 512], F32)
    ygf, b_ygf = ctx.sbuf("ygf", [128, 512], F32)
    sq, b_sq = ctx.sbuf("sq", [128, 512], F32)
    ygb, b_ygb = ctx.sbuf("ygb", [128, 16 * 512], BF16)
    ynb, b_ynb = ctx.sbuf("ynb", [128, 16 * 512], BF16)
    hid, b_hid = ctx.sbuf("hid", [128, 8 * 512], F32)
    h2, b_h2 = ctx.sbuf("h2", [128, 8 * 512], BF16)
    h2_scr = ctx.nc.dram_tensor("a3h2scr", [1024, SH], BF16).ap()
    b_h2s = p.buf("a3h2scr")
    hid_scr = ctx.nc.dram_tensor("a3hidscr", [1024, SH], F32).ap()
    b_hids = p.buf("a3hidscr")
    wmn, b_wmn = ctx.sbuf("wmn", [128, 16], F32)
    wln2, b_wln2 = ctx.sbuf("wln2", [128, 8], F32)
    wob, b_wob = ctx.sbuf("wob", [128, 16 * 1024], BF16)
    wst0, b_wst0 = ctx.sbuf("wst0", [128, 1024], F32)
    wst1, b_wst1 = ctx.sbuf("wst1", [128, 1024], F32)
    wcb0, b_wcb0 = ctx.sbuf("wcb0", [128, 1024], BF16)
    wcb1, b_wcb1 = ctx.sbuf("wcb1", [128, 1024], BF16)
    onesc, b_onesc = ctx.sbuf("onesc", [128, 1], F32)
    ones1, b_ones1 = ctx.sbuf("ones1", [1, 128], F32)
    srow, b_srow = ctx.sbuf("srow", [1, 6 * 512], F32)
    x1s, b_x1s = ctx.sbuf("x1s", [128, 512], F32)
    tvec, b_tvec = ctx.sbuf("tvec", [128, 512], F32)

    ps2, b_ps2 = ctx.psum("ps2", [1, 512])
    pbm, b_pbm = ctx.psum("pbm", [128, 512])
    pbr, b_pbr = ctx.psum("pbr", [128, 512])
    pbn, b_pbn = ctx.psum("pbn", [128, 512])
    pp0, b_pp0 = ctx.psum("pp0", [128, 512])
    pp1, b_pp1 = ctx.psum("pp1", [128, 512])

    emit(p, "vector", lambda v: v.memset(onesc[:], 1.0), [], [b_onesc])
    emit(p, "vector", lambda v: v.memset(ones1[:], 1.0), [], [b_ones1])
    p.dma("sync", wmn[:], wmn_d[:, :], writes=[b_wmn])
    p.dma("sync", wln2[:], wln2_d[:, :], writes=[b_wln2])
    for oc in range(8):
        wst, b_wst = (wst0, b_wst0) if oc % 2 == 0 else (wst1, b_wst1)
        for hh in range(2):
            p.dma("sync", wst[:, :], wout_d[oc * 128:(oc + 1) * 128, hh * 1024:(hh + 1) * 1024],
                  writes=[b_wst])
            emit(p, "vector", lambda v, oc=oc, hh=hh, wst=wst: v.tensor_copy(
                wob[:, oc * 2048 + hh * 1024: oc * 2048 + (hh + 1) * 1024], wst[:, :]),
                [b_wst], [b_wob])

    cb5, b_cb5 = ctx.cbias(1e-5, 1)
    cb6, b_cb6 = ctx.cbias(1e-6, 1)

    for ti in range(4):
        o = ti * 512
        for c in range(8):
            p.dma("sync", xt[:, c * 512:(c + 1) * 512], x_d[c * 128:(c + 1) * 128, o:o + 512],
                  writes=[b_xt])
        p.dma("sync", asm[:], astat_d[0:1, o:o + 512], writes=[b_asm])
        p.dma("sync", asr[:], astat_d[1:2, o:o + 512], writes=[b_asr])
        for c in range(16):
            p.dma("sync", yt[:, c * 512:(c + 1) * 512], y_d[c * 128:(c + 1) * 128, o:o + 512],
                  writes=[b_yt])
        for c in range(16):
            zt, b_zt = (zt0, b_zt0) if c % 2 == 0 else (zt1, b_zt1)
            p.dma("sync", zt[:], z_d[c * 128:(c + 1) * 128, o:o + 512], writes=[b_zt])
            emit(p, "scalar", lambda s, zt=zt: s.activation(out=zs[:], in_=zt[:], func=AF.Silu),
                 [b_zt], [b_zs])
            emit(p, "vector", lambda v, c=c: v.tensor_tensor(ygf[:], yt[:, c * 512:(c + 1) * 512], zs[:], ALU.mult),
                 [b_yt, b_zs], [b_ygf])
            emit(p, "vector", lambda v, c=c: v.tensor_copy(ygb[:, c * 512:(c + 1) * 512], ygf[:]),
                 [b_ygf], [b_ygb])
            emit(p, "scalar", lambda s: s.activation(out=sq[:], in_=ygf[:], func=AF.Square), [b_ygf], [b_sq])
            emit(p, "tensor", lambda t, c=c: t.matmul(ps2[:, :], onesc[:, :], sq[:, :],
                                                      start=(c == 0), stop=(c == 15)),
                 [b_onesc, b_sq], [b_ps2])
        R0 = srow[0:1, 0:512]
        emit(p, "vector", lambda v: v.tensor_scalar_mul(R0, ps2[0:1, :], 1.0 / 2048.0), [b_ps2], [b_srow])
        emit(p, "scalar", lambda s, cb5=cb5: s.activation(out=R0, in_=R0, func=AF.Sqrt, bias=cb5), [b_srow, b_cb5], [b_srow])
        emit(p, "vector", lambda v: v.reciprocal(R0, R0), [b_srow], [b_srow])
        emit(p, "tensor", lambda t: t.matmul(pbn[:, :], ones1[0:1, :], R0, start=True, stop=True),
             [b_ones1, b_srow], [b_pbn])
        for c in range(16):
            emit(p, "vector", lambda v, c=c: v.scalar_tensor_tensor(
                ynb[:, c * 512:(c + 1) * 512], ygb[:, c * 512:(c + 1) * 512], wmn[:, c:c + 1], pbn[:, :],
                ALU.mult, ALU.mult), [b_ygb, b_wmn, b_pbn], [b_ynb])
        emit(p, "tensor", lambda t: t.matmul(pbm[:, :], ones1[0:1, :], asm[0:1, :], start=True, stop=True),
             [b_ones1, b_asm], [b_pbm])
        emit(p, "tensor", lambda t: t.matmul(pbr[:, :], ones1[0:1, :], asr[0:1, :], start=True, stop=True),
             [b_ones1, b_asr], [b_pbr])
        for oc in range(8):
            pp, b_pp = (pp0, b_pp0) if oc % 2 == 0 else (pp1, b_pp1)
            def mmwo(t, oc=oc, pp=pp):
                last = None
                for kc in range(16):
                    last = t.matmul(pp[:, :], wob[:, oc * 2048 + kc * 128:(oc * 2048 + (kc + 1) * 128)],
                                    ynb[:, kc * 512:(kc + 1) * 512], start=(kc == 0), stop=(kc == 15))
                return last
            p.op("tensor", mmwo, reads=[b_wob, b_ynb], writes=[b_pp])
            emit(p, "vector", lambda v, oc=oc: v.tensor_sub(tvec[:], xt[:, oc * 512:(oc + 1) * 512], pbm[:, :]),
                 [b_xt, b_pbm], [b_tvec])
            emit(p, "vector", lambda v: v.tensor_mul(tvec[:], tvec[:], pbr[:, :]), [b_tvec, b_pbr], [b_tvec])
            emit(p, "vector", lambda v, oc=oc, pp=pp: v.tensor_add(hid[:, oc * 512:(oc + 1) * 512], tvec[:], pp[:, :]),
                 [b_tvec, b_pp], [b_hid])
        for oc in range(8):
            emit(p, "scalar", lambda s, oc=oc: s.activation(out=sq[:], in_=hid[:, oc * 512:(oc + 1) * 512], func=AF.Square),
                 [b_hid], [b_sq])
            emit(p, "tensor", lambda t, oc=oc: t.matmul(ps2[:, :], onesc[:, :], sq[:, :],
                                                        start=(oc == 0), stop=(oc == 7)),
                 [b_onesc, b_sq], [b_ps2])
        R1 = srow[0:1, 512:1024]
        emit(p, "vector", lambda v: v.tensor_scalar_mul(R1, ps2[0:1, :], 1.0 / 1024.0), [b_ps2], [b_srow])
        emit(p, "scalar", lambda s, cb6=cb6: s.activation(out=R1, in_=R1, func=AF.Sqrt, bias=cb6), [b_srow, b_cb6], [b_srow])
        emit(p, "vector", lambda v: v.reciprocal(R1, R1), [b_srow], [b_srow])
        emit(p, "tensor", lambda t: t.matmul(pbn[:, :], ones1[0:1, :], R1, start=True, stop=True),
             [b_ones1, b_srow], [b_pbn])
        for oc in range(8):
            emit(p, "vector", lambda v, oc=oc: v.scalar_tensor_tensor(
                h2[:, oc * 512:(oc + 1) * 512], hid[:, oc * 512:(oc + 1) * 512], wln2[:, oc:oc + 1], pbn[:, :],
                ALU.mult, ALU.mult), [b_hid, b_wln2, b_pbn], [b_h2])
            p.dma("gpsimd", h2_scr[oc * 128:(oc + 1) * 128, o:o + 512], h2[:, oc * 512:(oc + 1) * 512],
                  reads=[b_h2], writes=[b_h2s])
            p.dma("gpsimd", hid_scr[oc * 128:(oc + 1) * 128, o:o + 512], hid[:, oc * 512:(oc + 1) * 512],
                  reads=[b_hid], writes=[b_hids])

    if ctx.__class__.__name__ == "MegaCtx":
        ctx.new_phase()
    h2r, b_h2r = ctx.sbuf("h2r", [128, 8 * SH], BF16)
    midb, b_midb = ctx.sbuf("midb3", [128, 8 * SH], BF16)
    outac, b_outac = ctx.sbuf("outac3", [128, 8 * SH], BF16)
    gsil, b_gsil = ctx.sbuf("gsil3", [128, 512], BF16)
    w2st0, b_w2st0 = ctx.sbuf("w2st0", [128, 1024], F32)
    w2st1, b_w2st1 = ctx.sbuf("w2st1", [128, 1024], F32)
    w2cb0, b_w2cb0 = ctx.sbuf("w2cb0", [128, 1024], BF16)
    w2cb1, b_w2cb1 = ctx.sbuf("w2cb1", [128, 1024], BF16)
    w2cb2, b_w2cb2 = ctx.sbuf("w2cb2", [128, 1024], BF16)
    w2cb3, b_w2cb3 = ctx.sbuf("w2cb3", [128, 1024], BF16)
    w2cb4, b_w2cb4 = ctx.sbuf("w2cb4", [128, 1024], BF16)
    w2cb5, b_w2cb5 = ctx.sbuf("w2cb5", [128, 1024], BF16)
    w2st2, b_w2st2 = ctx.sbuf("w2st2", [128, 1024], F32)
    w2st3, b_w2st3 = ctx.sbuf("w2st3", [128, 1024], F32)
    xl2, b_xl2 = ctx.sbuf("xl2", [128, 512], F32)
    hl2, b_hl2 = ctx.sbuf("hl2", [128, 512], F32)
    x1s2, b_x1s2 = ctx.sbuf("x1s2", [128, 512], F32)
    if ctx.__class__.__name__ == "MegaCtx":
        pq0, b_pq0 = ctx.psum("pq0", [128, 512])
        pq1, b_pq1 = ctx.psum("pq1", [128, 512])
    else:
        pq0, b_pq0, pq1, b_pq1 = pp0, b_pp0, pp1, b_pp1
    for c in range(8):
        for t4 in range(4):
            p.dma("sync", h2r[:, c * SH + t4 * 512: c * SH + t4 * 512 + 512],
                  h2_scr[c * 128:(c + 1) * 128, t4 * 512:(t4 + 1) * 512], reads=[b_h2s], writes=[b_h2r])
    emit(p, "vector", lambda v: v.memset(outac[:], 0.0), [], [b_outac])
    for jb in range(4):
        for jc in range(8):
            ocg = jb * 8 + jc
            if jc % 2 == 0:
                stg_, b_stg, cbg_, b_cbg = w2st0, b_w2st0, w2cb0, b_w2cb0
                stu_, b_stu, cbu_, b_cbu = w2st1, b_w2st1, w2cb1, b_w2cb1
            else:
                stg_, b_stg, cbg_, b_cbg = w2st2, b_w2st2, w2cb4, b_w2cb4
                stu_, b_stu, cbu_, b_cbu = w2st3, b_w2st3, w2cb5, b_w2cb5
            p.dma("sync", stg_[:, :], wg_d[ocg * 128:(ocg + 1) * 128, :], writes=[b_stg])
            emit(p, "vector", lambda v, stg_=stg_, cbg_=cbg_: v.tensor_copy(cbg_[:], stg_[:]), [b_stg], [b_cbg])
            p.dma("sync", stu_[:, :], wu_d[ocg * 128:(ocg + 1) * 128, :], writes=[b_stu])
            emit(p, "vector", lambda v, stu_=stu_, cbu_=cbu_: v.tensor_copy(cbu_[:], stu_[:]), [b_stu], [b_cbu])
            for t4 in range(4):
                def mmg(t, t4=t4, cbg_=cbg_):
                    last = None
                    for kc in range(8):
                        last = t.matmul(pq0[:, :], cbg_[:, kc * 128:(kc + 1) * 128],
                                        h2r[:, kc * SH + t4 * 512: kc * SH + t4 * 512 + 512],
                                        start=(kc == 0), stop=(kc == 7))
                    return last
                p.op("tensor", mmg, reads=[b_cbg, b_h2r], writes=[b_pq0])
                def mmu(t, t4=t4, cbu_=cbu_):
                    last = None
                    for kc in range(8):
                        last = t.matmul(pq1[:, :], cbu_[:, kc * 128:(kc + 1) * 128],
                                        h2r[:, kc * SH + t4 * 512: kc * SH + t4 * 512 + 512],
                                        start=(kc == 0), stop=(kc == 7))
                    return last
                p.op("tensor", mmu, reads=[b_cbu, b_h2r], writes=[b_pq1])
                emit(p, "scalar", lambda s: s.activation(out=gsil[:], in_=pq0[:, :], func=AF.Silu), [b_pq0], [b_gsil])
                emit(p, "vector", lambda v, jc=jc, t4=t4: v.tensor_tensor(
                    midb[:, jc * SH + t4 * 512: jc * SH + t4 * 512 + 512], pq1[:, :], gsil[:], ALU.mult),
                    [b_pq1, b_gsil], [b_midb])
        for oc in range(8):
            w2st, b_w2st = (w2st0, b_w2st0) if oc % 2 == 0 else (w2st1, b_w2st1)
            w2cb, b_w2cb = (w2cb2, b_w2cb2) if oc % 2 == 0 else (w2cb3, b_w2cb3)
            p.dma("sync", w2st[:, :], wd_d[oc * 128:(oc + 1) * 128, jb * 1024:(jb + 1) * 1024], writes=[b_w2st])
            emit(p, "vector", lambda v, w2st=w2st, w2cb=w2cb: v.tensor_copy(w2cb[:], w2st[:]), [b_w2st], [b_w2cb])
            for t4 in range(4):
                pq, b_pq = (pq0, b_pq0) if t4 % 2 == 0 else (pq1, b_pq1)
                def mmd(t, w2cb=w2cb, t4=t4, pq=pq):
                    last = None
                    for kc in range(8):
                        last = t.matmul(pq[:, :], w2cb[:, kc * 128:(kc + 1) * 128],
                                        midb[:, kc * SH + t4 * 512: kc * SH + t4 * 512 + 512],
                                        start=(kc == 0), stop=(kc == 7))
                    return last
                p.op("tensor", mmd, reads=[b_w2cb, b_midb], writes=[b_pq])
                emit(p, "vector", lambda v, oc=oc, t4=t4, pq=pq: v.tensor_tensor(
                    outac[:, oc * SH + t4 * 512: oc * SH + t4 * 512 + 512], pq[:, :],
                    outac[:, oc * SH + t4 * 512: oc * SH + t4 * 512 + 512], ALU.add),
                    [b_pq, b_outac], [b_outac])
    for ti in range(4):
        o = ti * 512
        for oc in range(8):
            p.dma("sync", xl2[:], x_d[oc * 128:(oc + 1) * 128, o:o + 512], writes=[b_xl2])
            p.dma("sync", hl2[:], hid_scr[oc * 128:(oc + 1) * 128, o:o + 512], reads=[b_hids], writes=[b_hl2])
            emit(p, "vector", lambda v, oc=oc, o=o: v.tensor_add(
                x1s2[:], hl2[:], outac[:, oc * SH + o: oc * SH + o + 512]), [b_hl2, b_outac], [b_x1s2])
            emit(p, "vector", lambda v: v.tensor_add(x1s2[:], x1s2[:], xl2[:]), [b_x1s2, b_xl2], [b_x1s2])
            p.dma("gpsimd", x1_d[oc * 128:(oc + 1) * 128, o:o + 512], x1s2[:], reads=[b_x1s2], writes=[b_x1], out_dram=True)
    ctx.io = None
    if standalone:
        p.finalize()
    return ctx


def stage_A3(I, a1_res, a2_res):
    wmnc = stage_vec_cols(I["L0_attn_w_mnorm"], 16)
    wout_T = stage_ktiles(np.asarray(I["L0_attn_w_out"], np.float32).T, kc=16)
    wln2c = stage_vec_cols(I["L0_attn_w_ln2"], 8)
    wg_T = stage_ktiles(np.asarray(I["L0_attn_w_gate"], np.float32).T)
    wu_T = stage_ktiles(np.asarray(I["L0_attn_w_up"], np.float32).T)
    wd_T = stage_ktiles(np.asarray(I["L0_attn_w_down"], np.float32).T, kc=32)
    x = I["L0_attn_hidden_states"]
    xfm = np.ascontiguousarray(x.transpose(0, 2, 1), dtype=np.float32)
    yfm = {}
    for b in range(2):
        yfm[b] = np.concatenate([np.asarray(a2_res[b * 4 + k]["y"]) for k in range(4)], axis=0)
    in_maps = []
    for c, (b, k) in enumerate(CORES):
        sl = slice(2048 * k, 2048 * k + 2048)
        in_maps.append(dict(
            xin3=np.ascontiguousarray(xfm[b][:, sl]),
            astat3=np.ascontiguousarray(np.asarray(a1_res[c]["astat"])),
            zin=np.ascontiguousarray(np.asarray(a1_res[c]["z"])),
            yin=np.ascontiguousarray(yfm[b][:, sl]),
            wmnc=wmnc, wout_T=wout_T, wln2c=wln2c, wg_T=wg_T, wu_T=wu_T, wd_T=wd_T,
        ))
    return in_maps


def build_B1(ctx=None, io=None):
    standalone = ctx is None
    if standalone:
        ctx = LaunchCtx("B1")
        p = ctx.start()
    else:
        p = ctx.prog
    ctx.io = io
    nc = ctx.nc

    x_d = ctx.din("x1in", [1024, SH])
    wq_d = ctx.din("wq", [1024, 1024])
    wk_d = ctx.din("wk", [1024, 1024])
    wv_d = ctx.din("wv", [1024, 1024])
    bq_d = ctx.din("bqc", [128, 8])
    bk_d = ctx.din("bkc", [128, 8])
    bv_d = ctx.din("bvr", [1, 1024])
    g1_d = ctx.din("g1c", [128, 8])
    b1_d = ctx.din("b1c", [128, 8])
    lnb_d, b_lnb = ctx.dout("lnb", [1024, SH])
    q_d, b_q = ctx.dout("q", [1024, SH], BF16)
    k_d, b_k = ctx.dout("k", [1024, SH], BF16)
    v_d, b_v = ctx.dout("vtm", [SH, 1024], BF16)

    xt, b_xt = ctx.sbuf("xt", [128, 8 * 512], F32)
    lnf, b_lnf = ctx.sbuf("lnf", [128, 8 * 512], F32)
    hnb, b_hnb = ctx.sbuf("hnb", [128, 8 * 512], BF16)
    sq, b_sq = ctx.sbuf("sq", [128, 512], F32)
    tv, b_tv = ctx.sbuf("tv", [128, 512], F32)
    wqb, b_wqb = ctx.sbuf("wqb", [128, 8 * 1024], BF16)
    wkb, b_wkb = ctx.sbuf("wkb", [128, 8 * 1024], BF16)
    wvb, b_wvb = ctx.sbuf("wvb", [128, 8 * 1024], BF16)
    wst0, b_wst0 = ctx.sbuf("wst0", [128, 1024], F32)
    wst1, b_wst1 = ctx.sbuf("wst1", [128, 1024], F32)
    bqc, b_bqc = ctx.sbuf("bqc_s", [128, 8], F32)
    bkc, b_bkc = ctx.sbuf("bkc_s", [128, 8], F32)
    bvr, b_bvr = ctx.sbuf("bvr_s", [1, 1024], F32)
    g1c, b_g1c = ctx.sbuf("g1c_s", [128, 8], F32)
    b1c, b_b1c = ctx.sbuf("b1c_s", [128, 8], F32)
    onesc, b_onesc = ctx.sbuf("onesc", [128, 1], F32)
    ones1, b_ones1 = ctx.sbuf("ones1", [1, 128], F32)
    srow, b_srow = ctx.sbuf("srow", [1, 6 * 512], F32)
    outs, b_outs = ctx.sbuf("outs", [128, 512], BF16)
    vts, b_vts = ctx.sbuf("vts", [128, 512], BF16)

    ps1, b_ps1 = ctx.psum("ps1", [1, 512])
    ps2, b_ps2 = ctx.psum("ps2", [1, 512])
    pbm, b_pbm = ctx.psum("pbm", [128, 512])
    pbr, b_pbr = ctx.psum("pbr", [128, 512])
    pp0, b_pp0 = ctx.psum("pp0", [128, 512])
    pp1, b_pp1 = ctx.psum("pp1", [128, 512])

    emit(p, "vector", lambda v: v.memset(onesc[:], 1.0), [], [b_onesc])
    emit(p, "vector", lambda v: v.memset(ones1[:], 1.0), [], [b_ones1])
    for (dst, b_dst, src) in ((bqc, b_bqc, bq_d), (bkc, b_bkc, bk_d), (g1c, b_g1c, g1_d), (b1c, b_b1c, b1_d)):
        p.dma("sync", dst[:], src[:, :], writes=[b_dst])
    p.dma("sync", bvr[:], bv_d[:, :], writes=[b_bvr])
    for (wsb, b_wsb, wd) in ((wqb, b_wqb, wq_d), (wkb, b_wkb, wk_d), (wvb, b_wvb, wv_d)):
        for oc in range(8):
            wst, b_wst = (wst0, b_wst0) if oc % 2 == 0 else (wst1, b_wst1)
            p.dma("sync", wst[:, :], wd[oc * 128:(oc + 1) * 128, :], writes=[b_wst])
            emit(p, "vector", lambda v, wsb=wsb, oc=oc, wst=wst: v.tensor_copy(
                wsb[:, oc * 1024:(oc + 1) * 1024], wst[:]), [b_wst], [b_wsb])

    cb5, b_cb5 = ctx.cbias(1e-5, 1)

    for ti in range(4):
        o = ti * 512
        for c in range(8):
            p.dma("sync", xt[:, c * 512:(c + 1) * 512], x_d[c * 128:(c + 1) * 128, o:o + 512], writes=[b_xt])

        def norm_pass(src_sb, b_src, R0off):
            def mm1(t):
                last = None
                for c in range(8):
                    last = t.matmul(ps1[:, :], onesc[:, :], src_sb[:, c * 512:(c + 1) * 512],
                                    start=(c == 0), stop=(c == 7))
                return last
            p.op("tensor", mm1, reads=[b_onesc, b_src], writes=[b_ps1])
            for c in range(8):
                emit(p, "scalar", lambda s, c=c: s.activation(out=sq[:], in_=src_sb[:, c * 512:(c + 1) * 512], func=AF.Square),
                     [b_src], [b_sq])
                emit(p, "tensor", lambda t, c=c: t.matmul(ps2[:, :], onesc[:, :], sq[:, :], start=(c == 0), stop=(c == 7)),
                     [b_onesc, b_sq], [b_ps2])
            M = srow[0:1, R0off:R0off + 512]
            R = srow[0:1, R0off + 512:R0off + 1024]
            T1 = srow[0:1, R0off + 1024:R0off + 1536]
            emit(p, "vector", lambda v, M=M: v.tensor_scalar_mul(M, ps1[0:1, :], 1.0 / 1024.0), [b_ps1], [b_srow])
            emit(p, "vector", lambda v, T1=T1: v.tensor_scalar_mul(T1, ps2[0:1, :], 1.0 / 1024.0), [b_ps2], [b_srow])
            emit(p, "scalar", lambda s, M=M, R=R: s.activation(out=R, in_=M, func=AF.Square), [b_srow], [b_srow])
            emit(p, "vector", lambda v, T1=T1, R=R: v.tensor_sub(T1, T1, R), [b_srow], [b_srow])
            emit(p, "scalar", lambda s, T1=T1, cb5=cb5: s.activation(out=T1, in_=T1, func=AF.Sqrt, bias=cb5), [b_srow, b_cb5], [b_srow])
            emit(p, "vector", lambda v, T1=T1, R=R: v.reciprocal(R, T1), [b_srow], [b_srow])
            emit(p, "tensor", lambda t, M=M: t.matmul(pbm[:, :], ones1[0:1, :], M, start=True, stop=True),
                 [b_ones1, b_srow], [b_pbm])
            emit(p, "tensor", lambda t, R=R: t.matmul(pbr[:, :], ones1[0:1, :], R, start=True, stop=True),
                 [b_ones1, b_srow], [b_pbr])

        norm_pass(xt, b_xt, 0)
        for c in range(8):
            sl = slice(c * 512, (c + 1) * 512)
            emit(p, "vector", lambda v, sl=sl: v.tensor_sub(lnf[:, sl], xt[:, sl], pbm[:, :]), [b_xt, b_pbm], [b_lnf])
            emit(p, "vector", lambda v, sl=sl: v.tensor_mul(lnf[:, sl], lnf[:, sl], pbr[:, :]), [b_lnf, b_pbr], [b_lnf])
            p.dma("gpsimd", lnb_d[c * 128:(c + 1) * 128, o:o + 512], lnf[:, sl], reads=[b_lnf], writes=[b_lnb], out_dram=True)
        norm_pass(lnf, b_lnf, 1536)
        for c in range(8):
            sl = slice(c * 512, (c + 1) * 512)
            emit(p, "vector", lambda v, sl=sl: v.tensor_sub(tv[:], lnf[:, sl], pbm[:, :]), [b_lnf, b_pbm], [b_tv])
            emit(p, "vector", lambda v: v.tensor_mul(tv[:], tv[:], pbr[:, :]), [b_tv, b_pbr], [b_tv])
            emit(p, "vector", lambda v, c=c, sl=sl: v.tensor_scalar(
                hnb[:, sl], tv[:], g1c[:, c:c + 1], b1c[:, c:c + 1], ALU.mult, ALU.add),
                [b_tv, b_g1c, b_b1c], [b_hnb])
        for (wsb, b_wsb, bc, b_bc, dram, b_dram) in ((wqb, b_wqb, bqc, b_bqc, q_d, b_q), (wkb, b_wkb, bkc, b_bkc, k_d, b_k)):
            for oc in range(8):
                pp, b_pp = (pp0, b_pp0) if oc % 2 == 0 else (pp1, b_pp1)
                def mmp(t, wsb=wsb, oc=oc, pp=pp):
                    last = None
                    for kc in range(8):
                        last = t.matmul(pp[:, :], wsb[:, oc * 1024 + kc * 128: oc * 1024 + (kc + 1) * 128],
                                        hnb[:, kc * 512:(kc + 1) * 512], start=(kc == 0), stop=(kc == 7))
                    return last
                p.op("tensor", mmp, reads=[b_wsb, b_hnb], writes=[b_pp])
                emit(p, "vector", lambda v, oc=oc, pp=pp, bc=bc: v.tensor_scalar_add(outs[:], pp[:, :], bc[:, oc:oc + 1]),
                     [b_pp, b_bc], [b_outs])
                p.dma("gpsimd", dram[oc * 128:(oc + 1) * 128, o:o + 512], outs[:], reads=[b_outs], writes=[b_dram], out_dram=True)
        for sc in range(4):
            so = o + sc * 128
            for oc in range(8):
                pp, b_pp = (pp0, b_pp0) if oc % 2 == 0 else (pp1, b_pp1)
                def mmv(t, sc=sc, oc=oc, pp=pp):
                    last = None
                    for kc in range(8):
                        last = t.matmul(pp[:, 0:128], hnb[:, kc * 512 + sc * 128: kc * 512 + (sc + 1) * 128],
                                        wvb[:, oc * 1024 + kc * 128: oc * 1024 + (kc + 1) * 128],
                                        start=(kc == 0), stop=False)
                    last = t.matmul(pp[:, 0:128], ones1[0:1, 0:128], bvr[0:1, oc * 128:(oc + 1) * 128],
                                    start=False, stop=True)
                    return last
                p.op("tensor", mmv, reads=[b_hnb, b_wvb, b_ones1, b_bvr], writes=[b_pp])
                emit(p, "vector", lambda v, pp=pp: v.tensor_copy(vts[:, 0:128], pp[:, 0:128]), [b_pp], [b_vts])
                p.dma("gpsimd", v_d[so:so + 128, oc * 128:(oc + 1) * 128], vts[:, 0:128], reads=[b_vts], writes=[b_v], out_dram=True)
    ctx.io = None
    if standalone:
        p.finalize()
    return ctx


def stage_B1(I, x1_parts):
    wq = stage_ktiles(I["L0_ffn_wq"])
    wk = stage_ktiles(I["L0_ffn_wk"])
    wv = stage_ktiles(I["L0_ffn_wv"])
    bqc = stage_vec_cols(I["L0_ffn_bq"], 8)
    bkc = stage_vec_cols(I["L0_ffn_bk"], 8)
    bvr = _f32c(np.asarray(I["L0_ffn_bv"], np.float32).reshape(1, 1024))
    g1c = stage_vec_cols(I["L0_ffn_ln1_g"], 8)
    b1c = stage_vec_cols(I["L0_ffn_ln1_b"], 8)
    in_maps = []
    for c, (b, k) in enumerate(CORES):
        in_maps.append(dict(x1in=np.ascontiguousarray(x1_parts[c]),
                            wq=wq, wk=wk, wv=wv, bqc=bqc, bkc=bkc, bvr=bvr, g1c=g1c, b1c=b1c))
    return in_maps


def build_B2(ctx=None, io=None):
    standalone = ctx is None
    if standalone:
        ctx = LaunchCtx("B2")
        p = ctx.start()
    else:
        p = ctx.prog
    ctx.io = io
    nc = ctx.nc
    S = 8192

    q_dd = ctx.din("qin", [1024, SH], BF16)
    kf_d = ctx.din("kf", [1024, S], BF16)
    vf_d = ctx.din("vf", [S, 1024], BF16)
    lnb_d = ctx.din("lnbin", [1024, SH])
    x1_d = ctx.din("x1in2", [1024, SH])
    wo_d = ctx.din("wo", [1024, 1024])
    bo_d = ctx.din("boc", [128, 8])
    g2_d = ctx.din("g2c", [128, 8])
    b2_d = ctx.din("b2c", [128, 8])
    gw_d = ctx.din("gatew", [1024, 8])
    gb_d = ctx.din("gatebc", [8, 1])
    w1_d = ctx.din("w1m", [8 * 4096, 1024])
    b1_d = ctx.din("b1m", [8, 4096])
    w2_d = ctx.din("w2m", [8 * 1024, 4096])
    b2m_d = ctx.din("b2m", [8, 1024])
    esel_d = ctx.din("esel", [8, 1024])
    idf_d = ctx.din("idf2", [128, 128])
    x2_d, b_x2 = ctx.dout("x2", [1024, SH])
    od_d = ctx.nc.dram_tensor("oscr", [1024, SH], BF16, kind="Internal").ap()
    b_od = p.buf("oscr")
    ar_d = ctx.nc.dram_tensor("arscr", [1024, SH], F32, kind="Internal").ap()
    b_ar = p.buf("arscr")

    kx, b_kx = ctx.sbuf("kx", [65, S], BF16)
    vx, b_vx = ctx.sbuf("vx", [128, 64 * 65], BF16)
    qx, b_qx = ctx.sbuf("qx", [65, SH], BF16)
    ptb, b_ptb = ctx.sbuf("ptb", [128, 2 * 512], BF16)
    sq, b_sq = ctx.sbuf("sq", [128, 512], F32)
    lbuf, b_lbuf = ctx.sbuf("lbuf", [128, 512], F32)
    bcs, b_bcs = ctx.sbuf("bcs", [64, 512], F32)
    osb, b_osb = ctx.sbuf("osb", [64, 512], BF16)
    ob0, b_ob0 = ctx.sbuf("ob0", [128, 512], BF16)
    ob1, b_ob1 = ctx.sbuf("ob1", [128, 512], BF16)
    art, b_art = ctx.sbuf("art", [128, 8 * 512], F32)
    hn2, b_hn2 = ctx.sbuf("hn2", [128, 8 * SH], BF16)
    hn2f, b_hn2f = ctx.sbuf("hn2f", [128, 512], F32)
    selfm, b_selfm = ctx.sbuf("selfm", [8, SH], BF16)
    lgs, b_lgs = ctx.sbuf("lgs", [8, 512], F32)
    lgt, b_lgt = ctx.sbuf("lgt", [128, 8], F32)
    selt, b_selt = ctx.sbuf("selt", [128, 8], F32)
    maxc, b_maxc = ctx.sbuf("maxc", [128, 1], F32)
    midb, b_midb = ctx.sbuf("midb", [128, 8 * SH], BF16)
    outacc, b_outacc = ctx.sbuf("outacc", [128, 8 * SH], BF16)
    mskb, b_mskb = ctx.sbuf("mskb", [128, 4 * 512], BF16)
    wst0, b_wst0 = ctx.sbuf("wst0", [128, 1024], F32)
    wst1, b_wst1 = ctx.sbuf("wst1", [128, 1024], F32)
    wcb0, b_wcb0 = ctx.sbuf("wcb0", [128, 1024], BF16)
    wcb1, b_wcb1 = ctx.sbuf("wcb1", [128, 1024], BF16)
    boc, b_boc = ctx.sbuf("boc_s", [128, 8], F32)
    g2c, b_g2c = ctx.sbuf("g2c_s", [128, 8], F32)
    b2c, b_b2c = ctx.sbuf("b2c_s", [128, 8], F32)
    gwc, b_gwc = ctx.sbuf("gwc", [128, 8 * 8], F32)
    gbc, b_gbc = ctx.sbuf("gbc", [8, 1], F32)
    b1s, b_b1s = ctx.sbuf("b1s", [8, 128], BF16)
    b1st, b_b1st = ctx.sbuf("b1st", [8, 128], F32)
    ones8b, b_ones8b = ctx.sbuf("ones8b", [8, 512], BF16)
    eself, b_eself = ctx.sbuf("eself", [8, 1024], F32)
    b2f, b_b2f = ctx.sbuf("b2f", [8, 1024], BF16)
    esel, b_esel = ctx.sbuf("esel_s", [8, 1024], BF16)
    idf, b_idf = ctx.sbuf("idf_s", [128, 128], F32)
    onesc, b_onesc = ctx.sbuf("onesc", [128, 1], F32)
    ones1, b_ones1 = ctx.sbuf("ones1", [1, 128], F32)
    ones64, b_ones64 = ctx.sbuf("ones64", [128, 64], F32)
    srow, b_srow = ctx.sbuf("srow", [1, 4 * 512], F32)
    tv, b_tv = ctx.sbuf("tv", [128, 512], F32)
    xts, b_xts = ctx.sbuf("xts", [128, 512], F32)

    pst0, b_pst0 = ctx.psum("pst0", [128, 512])
    pst1, b_pst1 = ctx.psum("pst1", [128, 512])
    po, b_po = ctx.psum("po", [65, 512])
    pbc, b_pbc = ctx.psum("pbc", [64, 512])
    pm, b_pm = ctx.psum("pm", [128, 512])
    pw0, b_pw0 = ctx.psum("pw0", [128, 512])
    pw1, b_pw1 = ctx.psum("pw1", [128, 512])
    ps1, b_ps1 = pbc[0:1, :], b_pbc

    emit(p, "vector", lambda v: v.memset(onesc[:], 1.0), [], [b_onesc])
    emit(p, "vector", lambda v: v.memset(ones1[:], 1.0), [], [b_ones1])
    emit(p, "vector", lambda v: v.memset(ones64[:], 1.0), [], [b_ones64])
    emit(p, "vector", lambda v: v.memset(kx[64:65, :], 1.0), [], [b_kx])
    emit(p, "vector", lambda v: v.memset(vx[:, 64::65], 1.0), [], [b_vx])
    emit(p, "vector", lambda v: v.memset(outacc[:], 0.0), [], [b_outacc])
    for (dst, b_dst, src) in ((boc, b_boc, bo_d), (g2c, b_g2c, g2_d), (b2c, b_b2c, b2_d),
                              (gbc, b_gbc, gb_d), (idf, b_idf, idf_d)):
        p.dma("sync", dst[:], src[:, :], writes=[b_dst])
    p.dma("sync", wst0[0:8, 0:1024], b2m_d[:, :], writes=[b_wst0])
    emit(p, "vector", lambda v: v.tensor_copy(b2f[:], wst0[0:8, 0:1024]), [b_wst0], [b_b2f])
    p.dma("sync", wst1[0:8, 0:1024], esel_d[:, :], writes=[b_wst1])
    emit(p, "vector", lambda v: v.tensor_copy(esel[:], wst1[0:8, 0:1024]), [b_wst1], [b_esel])
    for kc in range(8):
        p.dma("sync", gwc[:, kc * 8:(kc + 1) * 8], gw_d[kc * 128:(kc + 1) * 128, :], writes=[b_gwc])
    emit(p, "vector", lambda v: v.memset(ones8b[:], 1.0), [], [b_ones8b])
    p.dma("sync", eself[:], esel_d[:, :], writes=[b_eself])

    cb5, b_cb5 = ctx.cbias(1e-5, 1)
    cb0125, b_cb0125 = ctx.cbias(0.125, 128)

    for h in range(16):
        p.dma("sync", kx[0:64, :], kf_d[64 * h:64 * h + 64, :], writes=[b_kx])
        p.dma("sync", vx[:, :].rearrange("p (s c) -> p s c", c=65)[:, :, 0:64],
              vf_d[:, 64 * h:64 * h + 64].rearrange("(s p) c -> p s c", p=128), writes=[b_vx])
        p.dma("sync", qx[0:64, :], q_dd[64 * h:64 * h + 64, :], writes=[b_qx])
        emit(p, "vector", lambda v: v.memset(srow[0:1, 0:1], 0.0), [], [b_srow])
        for sc in range(16):
            emit(p, "scalar", lambda s, sc=sc: s.activation(out=sq[0:64, 0:512], in_=kx[0:64, sc * 512:(sc + 1) * 512], func=AF.Square),
                 [b_kx], [b_sq])
            emit(p, "tensor", lambda t, sc=sc: t.matmul(ps1[:, :], onesc[0:64, :], sq[0:64, 0:512], start=True, stop=True),
                 [b_onesc, b_sq], [b_ps1])
            emit(p, "vector", lambda v, sc=sc: v.reduce_max(srow[0:1, 2:3], ps1[0:1, :], axis=AX.X),
                 [b_ps1], [b_srow])
            emit(p, "vector", lambda v: v.tensor_tensor(srow[0:1, 0:1], srow[0:1, 0:1], srow[0:1, 2:3], ALU.max),
                 [b_srow], [b_srow])
        emit(p, "scalar", lambda s: s.activation(out=srow[0:1, 0:1], in_=srow[0:1, 0:1], func=AF.Sqrt), [b_srow], [b_srow])
        QN = srow[0:1, 512:1024]
        for qt in range(4):
            emit(p, "scalar", lambda s, qt=qt: s.activation(out=sq[0:64, 0:512], in_=qx[0:64, qt * 512:(qt + 1) * 512], func=AF.Square),
                 [b_qx], [b_sq])
            emit(p, "tensor", lambda t: t.matmul(ps1[:, :], onesc[0:64, :], sq[0:64, 0:512], start=True, stop=True),
                 [b_onesc, b_sq], [b_ps1])
            emit(p, "scalar", lambda s, QN=QN: s.activation(out=QN, in_=ps1[0:1, :], func=AF.Sqrt),
                 [b_ps1], [b_srow])
            emit(p, "vector", lambda v, QN=QN: v.tensor_single_scalar(QN, QN, srow[0:1, 0:1], ALU.mult),
                 [b_srow], [b_srow])
            emit(p, "vector", lambda v, QN=QN: v.tensor_scalar_mul(QN, QN, -1.0), [b_srow], [b_srow])
            emit(p, "tensor", lambda t, QN=QN: t.matmul(po[64:65, :], ones1[0:1, 0:1], QN,
                                                        start=True, stop=True), [b_ones1, b_srow], [b_po])
            emit(p, "vector", lambda v, qt=qt: v.tensor_copy(qx[64:65, qt * 512:(qt + 1) * 512], po[64:65, :]),
                 [b_po], [b_qx])
        for qt in range(4):
            for sc in range(64):
                pst, b_pst = (pst0, b_pst0) if sc % 2 == 0 else (pst1, b_pst1)
                emit(p, "tensor", lambda t, sc=sc, qt=qt, pst=pst: t.matmul(
                    pst[:, :], kx[:, sc * 128:(sc + 1) * 128], qx[:, qt * 512:(qt + 1) * 512],
                    start=True, stop=True), [b_kx, b_qx], [b_pst])
                pt_sl = slice((sc % 2) * 512, (sc % 2) * 512 + 512)
                emit(p, "scalar", lambda s, pst=pst, pt_sl=pt_sl, cb0125=cb0125: s.activation(
                    out=ptb[:, pt_sl], in_=pst[:, :], func=AF.Exp, scale=cb0125),
                    [b_pst, b_cb0125], [b_ptb])
                emit(p, "tensor", lambda t, sc=sc, pt_sl=pt_sl: t.matmul(
                    po[:, :], vx[:, sc * 65:(sc + 1) * 65], ptb[:, pt_sl],
                    start=(sc == 0), stop=(sc == 63)), [b_vx, b_ptb], [b_po])
            emit(p, "vector", lambda v: v.reciprocal(lbuf[64:65, 0:512], po[64:65, :]), [b_po], [b_lbuf])
            emit(p, "tensor", lambda t: t.matmul(pbc[:, :], ones64[64:65, 0:64], lbuf[64:65, 0:512], start=True, stop=True),
                 [b_ones64, b_lbuf], [b_pbc])
            emit(p, "vector", lambda v: v.tensor_copy(bcs[:], pbc[:, :]), [b_pbc], [b_bcs])
            emit(p, "vector", lambda v: v.tensor_tensor(osb[:], po[0:64, :], bcs[:], ALU.mult), [b_po, b_bcs], [b_osb])
            p.dma("gpsimd", od_d[64 * h:64 * h + 64, qt * 512:(qt + 1) * 512], osb[:], reads=[b_osb], writes=[b_od])

    for ti in range(4):
        o = ti * 512
        for oc in range(8):
            pw, b_pw = (pw0, b_pw0) if oc % 2 == 0 else (pw1, b_pw1)
            wst, b_wst = (wst0, b_wst0) if oc % 2 == 0 else (wst1, b_wst1)
            wcb, b_wcb = (wcb0, b_wcb0) if oc % 2 == 0 else (wcb1, b_wcb1)
            p.dma("sync", wst[:, :], wo_d[oc * 128:(oc + 1) * 128, :], writes=[b_wst])
            emit(p, "vector", lambda v, wst=wst, wcb=wcb: v.tensor_copy(wcb[:], wst[:]), [b_wst], [b_wcb])
            for kc in range(8):
                obx, b_obx = (ob0, b_ob0) if kc % 2 == 0 else (ob1, b_ob1)
                p.dma("sync", obx[:], od_d[kc * 128:(kc + 1) * 128, o:o + 512],
                      reads=[b_od], writes=[b_obx])
                emit(p, "tensor", lambda t, kc=kc, pw=pw, wcb=wcb, obx=obx: t.matmul(
                    pw[:, :], wcb[:, kc * 128:(kc + 1) * 128], obx[:],
                    start=(kc == 0), stop=(kc == 7)), [b_wcb, b_obx], [b_pw])
            p.dma("sync", xts[:], lnb_d[oc * 128:(oc + 1) * 128, o:o + 512], writes=[b_xts])
            emit(p, "vector", lambda v, oc=oc, pw=pw: v.tensor_scalar_add(tv[:], pw[:, :], boc[:, oc:oc + 1]),
                 [b_pw, b_boc], [b_tv])
            emit(p, "vector", lambda v, oc=oc: v.tensor_add(art[:, oc * 512:(oc + 1) * 512], tv[:], xts[:]),
                 [b_tv, b_xts], [b_art])
            p.dma("gpsimd", ar_d[oc * 128:(oc + 1) * 128, o:o + 512], art[:, oc * 512:(oc + 1) * 512],
                  reads=[b_art], writes=[b_ar])
        def mm1(t):
            last = None
            for c in range(8):
                last = t.matmul(ps1[:, :], onesc[:, :], art[:, c * 512:(c + 1) * 512], start=(c == 0), stop=(c == 7))
            return last
        p.op("tensor", mm1, reads=[b_onesc, b_art], writes=[b_ps1])
        M = srow[0:1, 512:1024]
        emit(p, "vector", lambda v, M=M: v.tensor_scalar_mul(M, ps1[0:1, :], 1.0 / 1024.0), [b_ps1], [b_srow])
        for c in range(8):
            emit(p, "scalar", lambda s, c=c: s.activation(out=sq[:], in_=art[:, c * 512:(c + 1) * 512], func=AF.Square),
                 [b_art], [b_sq])
            emit(p, "tensor", lambda t, c=c: t.matmul(ps1[:, :], onesc[:, :], sq[:, :], start=(c == 0), stop=(c == 7)),
                 [b_onesc, b_sq], [b_ps1])
        T1 = srow[0:1, 1024:1536]
        R = srow[0:1, 1536:2048]
        emit(p, "vector", lambda v, T1=T1: v.tensor_scalar_mul(T1, ps1[0:1, :], 1.0 / 1024.0), [b_ps1], [b_srow])
        emit(p, "scalar", lambda s, M=M, R=R: s.activation(out=R, in_=M, func=AF.Square), [b_srow], [b_srow])
        emit(p, "vector", lambda v, T1=T1, R=R: v.tensor_sub(T1, T1, R), [b_srow], [b_srow])
        emit(p, "scalar", lambda s, T1=T1, cb5=cb5: s.activation(out=T1, in_=T1, func=AF.Sqrt, bias=cb5), [b_srow, b_cb5], [b_srow])
        emit(p, "vector", lambda v, T1=T1, R=R: v.reciprocal(R, T1), [b_srow], [b_srow])
        emit(p, "tensor", lambda t, M=M: t.matmul(pm[:, :], ones1[0:1, :], M, start=True, stop=True),
             [b_ones1, b_srow], [b_pm])
        emit(p, "tensor", lambda t, R=R: t.matmul(pw0[:, :], ones1[0:1, :], R, start=True, stop=True),
             [b_ones1, b_srow], [b_pw0])
        for c in range(8):
            emit(p, "vector", lambda v, c=c: v.tensor_sub(hn2f[:], art[:, c * 512:(c + 1) * 512], pm[:, :]),
                 [b_art, b_pm], [b_hn2f])
            emit(p, "vector", lambda v: v.tensor_mul(hn2f[:], hn2f[:], pw0[:, :]), [b_hn2f, b_pw0], [b_hn2f])
            emit(p, "vector", lambda v, c=c: v.tensor_scalar(hn2f[:], hn2f[:], g2c[:, c:c + 1], b2c[:, c:c + 1],
                                                             ALU.mult, ALU.add), [b_hn2f, b_g2c, b_b2c], [b_hn2f])
            emit(p, "vector", lambda v, c=c, o=o: v.tensor_copy(hn2[:, c * SH + o: c * SH + o + 512], hn2f[:]),
                 [b_hn2f], [b_hn2])
            emit(p, "tensor", lambda t, c=c: t.matmul(pst0[0:8, :], gwc[:, c * 8:(c + 1) * 8], hn2f[:, :],
                                                      start=(c == 0), stop=(c == 7)), [b_gwc, b_hn2f], [b_pst0])
        emit(p, "vector", lambda v: v.tensor_scalar_add(lgs[:, :], pst0[0:8, :], gbc[:, 0:1]), [b_pst0, b_gbc], [b_lgs])
        for cc in range(4):
            emit_transpose(p, pm[0:128, 0:8], lgs[:, cc * 128:(cc + 1) * 128], idf[0:8, 0:8], [b_lgs, b_idf], [b_pm])
            emit(p, "vector", lambda v: v.tensor_copy(lgt[:], pm[0:128, 0:8]), [b_pm], [b_lgt])
            emit(p, "vector", lambda v: v.reduce_max(maxc[:], lgt[:], axis=AX.X), [b_lgt], [b_maxc])
            emit(p, "vector", lambda v: v.tensor_scalar(selt[:], lgt[:], maxc[:, 0:1], None, ALU.is_ge),
                 [b_lgt, b_maxc], [b_selt])
            emit_transpose(p, pm[0:8, 8:136], selt[:, :], idf[:, :], [b_selt, b_idf], [b_pm])
            emit(p, "vector", lambda v, o=o, cc=cc: v.tensor_copy(selfm[:, o + cc * 128: o + (cc + 1) * 128], pm[0:8, 8:136]),
                 [b_pm], [b_selfm])

    for e in range(8):
        for t4 in range(4):
            emit(p, "tensor", lambda t, e=e, t4=t4: t.matmul(
                pm[:, :], esel[:, e * 128:(e + 1) * 128], selfm[:, t4 * 512:(t4 + 1) * 512], start=True, stop=True),
                [b_esel, b_selfm], [b_pm])
            emit(p, "vector", lambda v, t4=t4: v.tensor_copy(mskb[:, t4 * 512:(t4 + 1) * 512], pm[:, :]),
                 [b_pm], [b_mskb])
        for jb in range(4):
            for jc in range(8):
                j = jb * 8 + jc
                wst, b_wst = (wst0, b_wst0) if jc % 2 == 0 else (wst1, b_wst1)
                wcb, b_wcb = (wcb0, b_wcb0) if jc % 2 == 0 else (wcb1, b_wcb1)
                p.dma("sync", wst[:, :], w1_d[(e * 32 + j) * 128:(e * 32 + j + 1) * 128, :], writes=[b_wst])
                emit(p, "vector", lambda v, wst=wst, wcb=wcb: v.tensor_copy(wcb[:], wst[:]), [b_wst], [b_wcb])
                p.dma("sync", b1st[:], b1_d[:, j * 128:(j + 1) * 128], writes=[b_b1st])
                emit(p, "vector", lambda v, e=e: v.tensor_scalar_mul(b1s[:], b1st[:], eself[:, e * 128:e * 128 + 1]),
                     [b_b1st, b_eself], [b_b1s])
                for t4 in range(4):
                    pw, b_pw = (pw0, b_pw0) if t4 % 2 == 0 else (pw1, b_pw1)
                    def mmw1b(t, wcb=wcb, t4=t4, pw=pw, e=e, j=j):
                        last = None
                        for kc in range(8):
                            last = t.matmul(pw[:, :], wcb[:, kc * 128:(kc + 1) * 128],
                                            hn2[:, kc * SH + t4 * 512: kc * SH + t4 * 512 + 512],
                                            start=(kc == 0), stop=False)
                        last = t.matmul(pw[:, :], b1s[0:8, :], ones8b[0:8, :],
                                        start=False, stop=True)
                        return last
                    _ = None
                    p.op("tensor", mmw1b, reads=[b_wcb, b_hn2, b_b1s, b_ones8b], writes=[b_pw])
                    emit(p, "vector", lambda v, jc=jc, t4=t4, pw=pw: v.scalar_tensor_tensor(
                        midb[:, jc * SH + t4 * 512: jc * SH + t4 * 512 + 512], pw[:, :], 0.0,
                        mskb[:, t4 * 512:(t4 + 1) * 512], ALU.max, ALU.mult),
                        [b_pw, b_mskb], [b_midb])
            for oc in range(8):
                wst, b_wst = (wst0, b_wst0) if oc % 2 == 0 else (wst1, b_wst1)
                wcb, b_wcb = (wcb0, b_wcb0) if oc % 2 == 0 else (wcb1, b_wcb1)
                p.dma("sync", wst[:, :], w2_d[e * 1024 + oc * 128: e * 1024 + (oc + 1) * 128, jb * 1024:(jb + 1) * 1024],
                      writes=[b_wst])
                emit(p, "vector", lambda v, wst=wst, wcb=wcb: v.tensor_copy(wcb[:], wst[:]), [b_wst], [b_wcb])
                for t4 in range(4):
                    pw, b_pw = (pw0, b_pw0) if t4 % 2 == 0 else (pw1, b_pw1)
                    def mmw2(t, wcb=wcb, t4=t4, pw=pw):
                        last = None
                        for kc in range(8):
                            last = t.matmul(pw[:, :], wcb[:, kc * 128:(kc + 1) * 128],
                                            midb[:, kc * SH + t4 * 512: kc * SH + t4 * 512 + 512],
                                            start=(kc == 0), stop=(kc == 7))
                        return last
                    p.op("tensor", mmw2, reads=[b_wcb, b_midb], writes=[b_pw])
                    emit(p, "vector", lambda v, oc=oc, t4=t4, pw=pw: v.tensor_tensor(
                        outacc[:, oc * SH + t4 * 512: oc * SH + t4 * 512 + 512], pw[:, :],
                        outacc[:, oc * SH + t4 * 512: oc * SH + t4 * 512 + 512], ALU.add),
                        [b_pw, b_outacc], [b_outacc])

    for ti in range(4):
        o = ti * 512
        for oc in range(8):
            emit(p, "tensor", lambda t, oc=oc, o=o: t.matmul(
                pm[:, :], b2f[:, oc * 128:(oc + 1) * 128], selfm[:, o:o + 512], start=True, stop=True),
                [b_b2f, b_selfm], [b_pm])
            p.dma("sync", xts[:], x1_d[oc * 128:(oc + 1) * 128, o:o + 512], writes=[b_xts])
            p.dma("sync", tv[:], ar_d[oc * 128:(oc + 1) * 128, o:o + 512], reads=[b_ar], writes=[b_tv])
            emit(p, "vector", lambda v, oc=oc, o=o: v.tensor_add(sq[:], pm[:, :],
                 outacc[:, oc * SH + o: oc * SH + o + 512]), [b_pm, b_outacc], [b_sq])
            emit(p, "vector", lambda v: v.tensor_add(sq[:], sq[:], tv[:]), [b_sq, b_tv], [b_sq])
            emit(p, "vector", lambda v: v.tensor_add(sq[:], sq[:], xts[:]), [b_sq, b_xts], [b_sq])
            p.dma("gpsimd", x2_d[oc * 128:(oc + 1) * 128, o:o + 512], sq[:], reads=[b_sq], writes=[b_x2], out_dram=True)
    ctx.io = None
    if standalone:
        p.finalize()
    return ctx


def stage_B2(I, b1_res, x1_parts):
    kf = {}
    vf = {}
    for b in range(2):
        kf[b] = np.concatenate([np.asarray(b1_res[b * 4 + k]["k"]) for k in range(4)], axis=1)
        vf[b] = np.concatenate([np.asarray(b1_res[b * 4 + k]["vtm"]) for k in range(4)], axis=0)
    wo = stage_ktiles(I["L0_ffn_wo"])
    boc = stage_vec_cols(I["L0_ffn_bo"], 8)
    g2c = stage_vec_cols(I["L0_ffn_ln2_g"], 8)
    b2c = stage_vec_cols(I["L0_ffn_ln2_b"], 8)
    gatew = _f32c(I["L0_ffn_gate_w"])
    gatebc = _f32c(np.asarray(I["L0_ffn_gate_b"], np.float32).reshape(8, 1))
    w1m = stage_ktiles3(I["L0_ffn_w1"], kc=8)
    b1m = _f32c(I["L0_ffn_b1"])
    w2m = stage_ktiles3(I["L0_ffn_w2"], kc=32)
    b2m = _f32c(I["L0_ffn_b2"])
    erow = np.zeros((8, 4096), np.float32)
    esel = np.zeros((8, 1024), np.float32)
    for e in range(8):
        erow[e, e * 512:(e + 1) * 512] = 1.0
        esel[e, e * 128:(e + 1) * 128] = 1.0
    idf2 = np.eye(128, dtype=np.float32)
    in_maps = []
    for c, (b, k) in enumerate(CORES):
        in_maps.append(dict(
            qin=np.asarray(b1_res[c]["q"]), kf=kf[b], vf=vf[b],
            lnbin=np.asarray(b1_res[c]["lnb"]), x1in2=np.ascontiguousarray(x1_parts[c]),
            wo=wo, boc=boc, g2c=g2c, b2c=b2c, gatew=gatew, gatebc=gatebc,
            w1m=w1m, b1m=b1m, w2m=w2m, b2m=b2m, esel=esel, idf2=idf2))
    return in_maps


def build_C1(ctx=None, io=None):
    standalone = ctx is None
    if standalone:
        ctx = LaunchCtx("C1")
        p = ctx.start()
    else:
        p = ctx.prog
    ctx.io = io
    nc = ctx.nc

    x_d = ctx.din("x2in", [1024, SH])
    wq_d = ctx.din("wqT", [1024, 1024])
    wk_d = ctx.din("wkT", [1024, 1024])
    wv_d = ctx.din("wvT", [1024, 1024])
    q_d, b_q = ctx.dout("q", [1024, SH], BF16)
    k_d, b_k = ctx.dout("k", [1024, SH], BF16)
    v_d, b_v = ctx.dout("vtm", [SH, 1024], BF16)

    xt, b_xt = ctx.sbuf("xt", [128, 8 * 512], F32)
    lnb, b_lnb = ctx.sbuf("lnb", [128, 8 * 512], BF16)
    sq, b_sq = ctx.sbuf("sq", [128, 512], F32)
    tv, b_tv = ctx.sbuf("tv", [128, 512], F32)
    wqb, b_wqb = ctx.sbuf("wqb", [128, 8 * 1024], BF16)
    wkb, b_wkb = ctx.sbuf("wkb", [128, 8 * 1024], BF16)
    wvb, b_wvb = ctx.sbuf("wvb", [128, 8 * 1024], BF16)
    wst0, b_wst0 = ctx.sbuf("wst0", [128, 1024], F32)
    wst1, b_wst1 = ctx.sbuf("wst1", [128, 1024], F32)
    onesc, b_onesc = ctx.sbuf("onesc", [128, 1], F32)
    ones1, b_ones1 = ctx.sbuf("ones1", [1, 128], F32)
    srow, b_srow = ctx.sbuf("srow", [1, 4 * 512], F32)
    outs, b_outs = ctx.sbuf("outs", [128, 512], BF16)
    vts, b_vts = ctx.sbuf("vts", [128, 512], BF16)

    ps1, b_ps1 = ctx.psum("ps1", [1, 512])
    ps2, b_ps2 = ctx.psum("ps2", [1, 512])
    pbm, b_pbm = ctx.psum("pbm", [128, 512])
    pbr, b_pbr = ctx.psum("pbr", [128, 512])
    pp0, b_pp0 = ctx.psum("pp0", [128, 512])
    pp1, b_pp1 = ctx.psum("pp1", [128, 512])

    emit(p, "vector", lambda v: v.memset(onesc[:], 1.0), [], [b_onesc])
    emit(p, "vector", lambda v: v.memset(ones1[:], 1.0), [], [b_ones1])
    for (wsb, b_wsb, wd) in ((wqb, b_wqb, wq_d), (wkb, b_wkb, wk_d), (wvb, b_wvb, wv_d)):
        for oc in range(8):
            wst, b_wst = (wst0, b_wst0) if oc % 2 == 0 else (wst1, b_wst1)
            p.dma("sync", wst[:, :], wd[oc * 128:(oc + 1) * 128, :], writes=[b_wst])
            emit(p, "vector", lambda v, wsb=wsb, oc=oc, wst=wst: v.tensor_copy(
                wsb[:, oc * 1024:(oc + 1) * 1024], wst[:]), [b_wst], [b_wsb])

    cb5, b_cb5 = ctx.cbias(1e-5, 1)

    for ti in range(4):
        o = ti * 512
        for c in range(8):
            p.dma("sync", xt[:, c * 512:(c + 1) * 512], x_d[c * 128:(c + 1) * 128, o:o + 512], writes=[b_xt])
        def mm1(t):
            last = None
            for c in range(8):
                last = t.matmul(ps1[:, :], onesc[:, :], xt[:, c * 512:(c + 1) * 512], start=(c == 0), stop=(c == 7))
            return last
        p.op("tensor", mm1, reads=[b_onesc, b_xt], writes=[b_ps1])
        for c in range(8):
            emit(p, "scalar", lambda s, c=c: s.activation(out=sq[:], in_=xt[:, c * 512:(c + 1) * 512], func=AF.Square),
                 [b_xt], [b_sq])
            emit(p, "tensor", lambda t, c=c: t.matmul(ps2[:, :], onesc[:, :], sq[:, :], start=(c == 0), stop=(c == 7)),
                 [b_onesc, b_sq], [b_ps2])
        M, R, T1 = srow[0:1, 0:512], srow[0:1, 512:1024], srow[0:1, 1024:1536]
        emit(p, "vector", lambda v, M=M: v.tensor_scalar_mul(M, ps1[0:1, :], 1.0 / 1024.0), [b_ps1], [b_srow])
        emit(p, "vector", lambda v, T1=T1: v.tensor_scalar_mul(T1, ps2[0:1, :], 1.0 / 1024.0), [b_ps2], [b_srow])
        emit(p, "scalar", lambda s, M=M, R=R: s.activation(out=R, in_=M, func=AF.Square), [b_srow], [b_srow])
        emit(p, "vector", lambda v, T1=T1, R=R: v.tensor_sub(T1, T1, R), [b_srow], [b_srow])
        emit(p, "scalar", lambda s, T1=T1, cb5=cb5: s.activation(out=T1, in_=T1, func=AF.Sqrt, bias=cb5), [b_srow, b_cb5], [b_srow])
        emit(p, "vector", lambda v, T1=T1, R=R: v.reciprocal(R, T1), [b_srow], [b_srow])
        emit(p, "tensor", lambda t, M=M: t.matmul(pbm[:, :], ones1[0:1, :], M, start=True, stop=True),
             [b_ones1, b_srow], [b_pbm])
        emit(p, "tensor", lambda t, R=R: t.matmul(pbr[:, :], ones1[0:1, :], R, start=True, stop=True),
             [b_ones1, b_srow], [b_pbr])
        for c in range(8):
            sl = slice(c * 512, (c + 1) * 512)
            emit(p, "vector", lambda v, sl=sl: v.tensor_sub(tv[:], xt[:, sl], pbm[:, :]), [b_xt, b_pbm], [b_tv])
            emit(p, "vector", lambda v, sl=sl: v.tensor_tensor(lnb[:, sl], tv[:], pbr[:, :], ALU.mult),
                 [b_tv, b_pbr], [b_lnb])
        for (wsb, b_wsb, dram, b_dram) in ((wqb, b_wqb, q_d, b_q), (wkb, b_wkb, k_d, b_k)):
            for oc in range(8):
                pp, b_pp = (pp0, b_pp0) if oc % 2 == 0 else (pp1, b_pp1)
                def mmp(t, wsb=wsb, oc=oc, pp=pp):
                    last = None
                    for kc in range(8):
                        last = t.matmul(pp[:, :], wsb[:, oc * 1024 + kc * 128: oc * 1024 + (kc + 1) * 128],
                                        lnb[:, kc * 512:(kc + 1) * 512], start=(kc == 0), stop=(kc == 7))
                    return last
                p.op("tensor", mmp, reads=[b_wsb, b_lnb], writes=[b_pp])
                emit(p, "vector", lambda v, pp=pp: v.tensor_copy(outs[:], pp[:, :]), [b_pp], [b_outs])
                p.dma("gpsimd", dram[oc * 128:(oc + 1) * 128, o:o + 512], outs[:], reads=[b_outs], writes=[b_dram], out_dram=True)
        for sc in range(4):
            so = o + sc * 128
            for oc in range(8):
                pp, b_pp = (pp0, b_pp0) if oc % 2 == 0 else (pp1, b_pp1)
                def mmv(t, sc=sc, oc=oc, pp=pp):
                    last = None
                    for kc in range(8):
                        last = t.matmul(pp[:, 0:128], lnb[:, kc * 512 + sc * 128: kc * 512 + (sc + 1) * 128],
                                        wvb[:, oc * 1024 + kc * 128: oc * 1024 + (kc + 1) * 128],
                                        start=(kc == 0), stop=(kc == 7))
                    return last
                p.op("tensor", mmv, reads=[b_lnb, b_wvb], writes=[b_pp])
                emit(p, "vector", lambda v, pp=pp: v.tensor_copy(vts[:, 0:128], pp[:, 0:128]), [b_pp], [b_vts])
                p.dma("gpsimd", v_d[so:so + 128, oc * 128:(oc + 1) * 128], vts[:, 0:128], reads=[b_vts], writes=[b_v], out_dram=True)
    ctx.io = None
    if standalone:
        p.finalize()
    return ctx


def stage_C1(I, x2_parts):
    wqT = stage_ktiles(np.asarray(I["L1_attn_Wq"], np.float32).T)
    wkT = stage_ktiles(np.asarray(I["L1_attn_Wk"], np.float32).T)
    wvT = stage_ktiles(np.asarray(I["L1_attn_Wv"], np.float32).T)
    in_maps = []
    for c, (b, k) in enumerate(CORES):
        in_maps.append(dict(x2in=np.ascontiguousarray(x2_parts[c]), wqT=wqT, wkT=wkT, wvT=wvT))
    return in_maps


def build_C2(ctx=None, io=None):
    standalone = ctx is None
    if standalone:
        ctx = LaunchCtx("C2")
        p = ctx.start()
    else:
        p = ctx.prog
    ctx.io = io
    nc = ctx.nc
    S = 8192

    q_dd = ctx.din("qn", [256, S], BF16)
    qp_d = ctx.din("qp", [256, S], BF16)
    kf_d = ctx.din("kn", [256, S], BF16)
    kp_d = ctx.din("kp", [256, S], BF16)
    vf_d = ctx.din("vf", [S, 256], BF16)
    cosk_d = ctx.din("cosk", [128, S], BF16)
    sink_d = ctx.din("sink", [128, S], BF16)
    dmask_d = ctx.din("dmask", [128, 4 * 512])
    o_d, b_o = ctx.dout("oh", [256, S], BF16)

    kx, b_kx = ctx.sbuf("kx", [65, S], BF16)
    vx, b_vx = ctx.sbuf("vx", [128, 64 * 65], BF16)
    qx, b_qx = ctx.sbuf("qx", [65, S], BF16)
    rp, b_rp = ctx.sbuf("rp", [64, S], BF16)
    rp2, b_rp2 = ctx.sbuf("rp2", [64, S], BF16)
    cosk, b_cosk = ctx.sbuf("cosk_s", [128, S], BF16)
    sink, b_sink = ctx.sbuf("sink_s", [128, S], BF16)
    dmask, b_dmask = ctx.sbuf("dmask_s", [128, 4 * 512], F32)
    ptb, b_ptb = ctx.sbuf("ptb", [128, 2 * 512], BF16)
    sq, b_sq = ctx.sbuf("sq", [128, 512], F32)
    tvm, b_tvm = ctx.sbuf("tvm", [128, 512], F32)
    lbuf, b_lbuf = ctx.sbuf("lbuf", [128, 512], F32)
    bcs, b_bcs = ctx.sbuf("bcs", [64, 512], F32)
    osb, b_osb = ctx.sbuf("osb", [64, 512], BF16)
    onesc, b_onesc = ctx.sbuf("onesc", [128, 1], F32)
    ones1, b_ones1 = ctx.sbuf("ones1", [1, 128], F32)
    ones64, b_ones64 = ctx.sbuf("ones64", [128, 64], F32)
    srow, b_srow = ctx.sbuf("srow", [1, 4 * 512], F32)

    pst0, b_pst0 = ctx.psum("pst0", [128, 512])
    pst1, b_pst1 = ctx.psum("pst1", [128, 512])
    po, b_po = ctx.psum("po", [65, 512])
    pbc, b_pbc = ctx.psum("pbc", [64, 512])
    ps1, b_ps1 = ctx.psum("ps1", [1, 512])

    emit(p, "vector", lambda v: v.memset(onesc[:], 1.0), [], [b_onesc])
    emit(p, "vector", lambda v: v.memset(ones1[:], 1.0), [], [b_ones1])
    emit(p, "vector", lambda v: v.memset(ones64[:], 1.0), [], [b_ones64])
    emit(p, "vector", lambda v: v.memset(kx[64:65, :], 1.0), [], [b_kx])
    emit(p, "vector", lambda v: v.memset(vx[:, 64::65], 1.0), [], [b_vx])
    p.dma("sync", cosk[:], cosk_d[:, :], writes=[b_cosk])
    p.dma("sync", sink[:], sink_d[:, :], writes=[b_sink])
    p.dma("sync", dmask[:], dmask_d[:, :], writes=[b_dmask])
    cb0125, b_cb0125 = ctx.cbias(0.125, 128)

    for h in range(4):
        hh = 0
        p.dma("sync", rp[:, :], kf_d[64 * h:64 * h + 64, :], writes=[b_rp])
        p.dma("sync", rp2[:, :], kp_d[64 * h:64 * h + 64, :], writes=[b_rp2])
        emit(p, "vector", lambda v: v.tensor_tensor(kx[0:64, :], rp[:, :], cosk[0:64, :], ALU.mult),
             [b_rp, b_cosk], [b_kx])
        emit(p, "vector", lambda v: v.tensor_tensor(rp2[:, :], rp2[:, :], sink[0:64, :], ALU.mult),
             [b_rp2, b_sink], [b_rp2])
        emit(p, "vector", lambda v: v.tensor_tensor(kx[0:64, :], kx[0:64, :], rp2[:, :], ALU.add),
             [b_kx, b_rp2], [b_kx])
        p.dma("sync", rp[:, :], q_dd[64 * h:64 * h + 64, :], writes=[b_rp])
        p.dma("sync", rp2[:, :], qp_d[64 * h:64 * h + 64, :], writes=[b_rp2])
        emit(p, "vector", lambda v: v.tensor_tensor(qx[0:64, :], rp[:, :], cosk[0:64, :], ALU.mult),
             [b_rp, b_cosk], [b_qx])
        emit(p, "vector", lambda v: v.tensor_tensor(rp2[:, :], rp2[:, :], sink[0:64, :], ALU.mult),
             [b_rp2, b_sink], [b_rp2])
        emit(p, "vector", lambda v: v.tensor_tensor(qx[0:64, :], qx[0:64, :], rp2[:, :], ALU.add),
             [b_qx, b_rp2], [b_qx])
        p.dma("sync", vx[:, :].rearrange("p (s c) -> p s c", c=65)[:, :, 0:64],
              vf_d[:, 64 * h:64 * h + 64].rearrange("(s p) c -> p s c", p=128), writes=[b_vx])
        emit(p, "vector", lambda v: v.memset(srow[0:1, 0:1], 0.0), [], [b_srow])
        for sc in range(16):
            emit(p, "scalar", lambda s, sc=sc: s.activation(out=sq[0:64, 0:512], in_=kx[0:64, sc * 512:(sc + 1) * 512], func=AF.Square),
                 [b_kx], [b_sq])
            emit(p, "tensor", lambda t: t.matmul(ps1[:, :], onesc[0:64, :], sq[0:64, 0:512], start=True, stop=True),
                 [b_onesc, b_sq], [b_ps1])
            emit(p, "vector", lambda v: v.reduce_max(srow[0:1, 2:3], ps1[0:1, :], axis=AX.X), [b_ps1], [b_srow])
            emit(p, "vector", lambda v: v.tensor_tensor(srow[0:1, 0:1], srow[0:1, 0:1], srow[0:1, 2:3], ALU.max),
                 [b_srow], [b_srow])
        emit(p, "scalar", lambda s: s.activation(out=srow[0:1, 0:1], in_=srow[0:1, 0:1], func=AF.Sqrt), [b_srow], [b_srow])
        QN = srow[0:1, 512:1024]
        for qt in range(16):
            emit(p, "scalar", lambda s, qt=qt: s.activation(out=sq[0:64, 0:512], in_=qx[0:64, qt * 512:(qt + 1) * 512], func=AF.Square),
                 [b_qx], [b_sq])
            emit(p, "tensor", lambda t: t.matmul(ps1[:, :], onesc[0:64, :], sq[0:64, 0:512], start=True, stop=True),
                 [b_onesc, b_sq], [b_ps1])
            emit(p, "scalar", lambda s, QN=QN: s.activation(out=QN, in_=ps1[0:1, :], func=AF.Sqrt), [b_ps1], [b_srow])
            emit(p, "vector", lambda v, QN=QN: v.tensor_single_scalar(QN, QN, srow[0:1, 0:1], ALU.mult), [b_srow], [b_srow])
            emit(p, "vector", lambda v, QN=QN: v.tensor_scalar_mul(QN, QN, -1.0), [b_srow], [b_srow])
            emit(p, "tensor", lambda t, QN=QN: t.matmul(po[64:65, :], ones1[0:1, 0:1], QN, start=True, stop=True),
                 [b_ones1, b_srow], [b_po])
            emit(p, "vector", lambda v, qt=qt: v.tensor_copy(qx[64:65, qt * 512:(qt + 1) * 512], po[64:65, :]),
                 [b_po], [b_qx])
            q0 = qt * 512
            nsc = (q0 + 512) // 128
            for sc in range(nsc):
                pst, b_pst = (pst0, b_pst0) if sc % 2 == 0 else (pst1, b_pst1)
                emit(p, "tensor", lambda t, sc=sc, qt=qt, pst=pst: t.matmul(
                    pst[:, :], kx[:, sc * 128:(sc + 1) * 128], qx[:, qt * 512:(qt + 1) * 512],
                    start=True, stop=True), [b_kx, b_qx], [b_pst])
                pt_sl = slice((sc % 2) * 512, (sc % 2) * 512 + 512)
                d = sc - q0 // 128
                if d >= 0:
                    emit(p, "vector", lambda v, pst=pst, d=d: v.tensor_tensor(
                        tvm[:], pst[:, :], dmask[:, d * 512:(d + 1) * 512], ALU.add), [b_pst, b_dmask], [b_tvm])
                    emit(p, "scalar", lambda s, pt_sl=pt_sl, cb0125=cb0125: s.activation(
                        out=ptb[:, pt_sl], in_=tvm[:], func=AF.Exp, scale=cb0125),
                        [b_tvm, b_cb0125], [b_ptb])
                else:
                    emit(p, "scalar", lambda s, pst=pst, pt_sl=pt_sl, cb0125=cb0125: s.activation(
                        out=ptb[:, pt_sl], in_=pst[:, :], func=AF.Exp, scale=cb0125),
                        [b_pst, b_cb0125], [b_ptb])
                emit(p, "tensor", lambda t, sc=sc, pt_sl=pt_sl, nsc=nsc: t.matmul(
                    po[:, :], vx[:, sc * 65:(sc + 1) * 65], ptb[:, pt_sl],
                    start=(sc == 0), stop=(sc == nsc - 1)), [b_vx, b_ptb], [b_po])
            emit(p, "vector", lambda v: v.reciprocal(lbuf[64:65, 0:512], po[64:65, :]), [b_po], [b_lbuf])
            emit(p, "tensor", lambda t: t.matmul(pbc[:, :], ones64[64:65, 0:64], lbuf[64:65, 0:512], start=True, stop=True),
                 [b_ones64, b_lbuf], [b_pbc])
            emit(p, "vector", lambda v: v.tensor_copy(bcs[:], pbc[:, :]), [b_pbc], [b_bcs])
            emit(p, "vector", lambda v: v.tensor_tensor(osb[:], po[0:64, :], bcs[:], ALU.mult), [b_po, b_bcs], [b_osb])
            p.dma("gpsimd", o_d[64 * h:64 * h + 64, qt * 512:(qt + 1) * 512], osb[:], reads=[b_osb], writes=[b_o], out_dram=True)
    ctx.io = None
    if standalone:
        p.finalize()
    return ctx


def build_C3(ctx=None, io=None):
    standalone = ctx is None
    if standalone:
        ctx = LaunchCtx("C3")
        p = ctx.start()
    else:
        p = ctx.prog
    ctx.io = io
    nc = ctx.nc
    oin_d = ctx.din("oin", [1024, SH], BF16)
    x2_d = ctx.din("x2in3", [1024, SH])
    wo_d = ctx.din("woT", [1024, 1024])
    x3_d, b_x3 = ctx.dout("x3", [1024, SH])

    ob, b_ob = ctx.sbuf("ob", [128, 8 * 512], BF16)
    wob, b_wob = ctx.sbuf("wob", [128, 8 * 1024], BF16)
    wst0, b_wst0 = ctx.sbuf("wst0", [128, 1024], F32)
    wst1, b_wst1 = ctx.sbuf("wst1", [128, 1024], F32)
    xts, b_xts = ctx.sbuf("xts", [128, 512], F32)
    x3s, b_x3s = ctx.sbuf("x3s", [128, 512], F32)
    pw0, b_pw0 = ctx.psum("pw0", [128, 512])
    pw1, b_pw1 = ctx.psum("pw1", [128, 512])

    for oc in range(8):
        wst, b_wst = (wst0, b_wst0) if oc % 2 == 0 else (wst1, b_wst1)
        p.dma("sync", wst[:, :], wo_d[oc * 128:(oc + 1) * 128, :], writes=[b_wst])
        emit(p, "vector", lambda v, oc=oc, wst=wst: v.tensor_copy(wob[:, oc * 1024:(oc + 1) * 1024], wst[:]),
             [b_wst], [b_wob])
    for ti in range(4):
        o = ti * 512
        for c in range(8):
            p.dma("sync", ob[:, c * 512:(c + 1) * 512], oin_d[c * 128:(c + 1) * 128, o:o + 512], writes=[b_ob])
        for oc in range(8):
            pw, b_pw = (pw0, b_pw0) if oc % 2 == 0 else (pw1, b_pw1)
            def mmwo(t, oc=oc, pw=pw):
                last = None
                for kc in range(8):
                    last = t.matmul(pw[:, :], wob[:, oc * 1024 + kc * 128: oc * 1024 + (kc + 1) * 128],
                                    ob[:, kc * 512:(kc + 1) * 512], start=(kc == 0), stop=(kc == 7))
                return last
            p.op("tensor", mmwo, reads=[b_wob, b_ob], writes=[b_pw])
            p.dma("sync", xts[:], x2_d[oc * 128:(oc + 1) * 128, o:o + 512], writes=[b_xts])
            emit(p, "vector", lambda v, pw=pw: v.tensor_add(x3s[:], pw[:, :], xts[:]), [b_pw, b_xts], [b_x3s])
            p.dma("gpsimd", x3_d[oc * 128:(oc + 1) * 128, o:o + 512], x3s[:], reads=[b_x3s], writes=[b_x3], out_dram=True)
    ctx.io = None
    if standalone:
        p.finalize()
    return ctx


def rope_tables(S):
    DC = 64
    pos = np.arange(S, dtype=np.float32)
    inv = 10000.0 ** (-np.arange(0, DC, 2, dtype=np.float32) / DC)
    ang = pos[:, None] * inv[None, :]
    cos = np.cos(ang)
    sin = np.sin(ang)
    C = np.zeros((128, S), np.float32)
    Sg = np.zeros((128, S), np.float32)
    for r in range(128):
        d = r % 64
        i = d // 2
        C[r] = cos[:, i]
        Sg[r] = -sin[:, i] if d % 2 == 0 else sin[:, i]
    return C, Sg


def swap_pairs_rows(a):
    b = a.reshape(-1, 2, a.shape[1])
    return np.ascontiguousarray(b[:, ::-1, :].reshape(a.shape))


def stage_C2(I, c1_res):
    S = 8192
    kf = {}
    vf = {}
    qf = {}
    for b in range(2):
        kf[b] = np.concatenate([np.asarray(c1_res[b * 4 + k]["k"]) for k in range(4)], axis=1)
        vf[b] = np.concatenate([np.asarray(c1_res[b * 4 + k]["vtm"]) for k in range(4)], axis=0)
        qf[b] = np.concatenate([np.asarray(c1_res[b * 4 + k]["q"]) for k in range(4)], axis=1)
    C, Sg = rope_tables(S)
    Cb = np.ascontiguousarray(C[0:128].astype(BF16NP))
    Sb = np.ascontiguousarray(Sg[0:128].astype(BF16NP))
    sqr = np.arange(128)
    dmask = np.zeros((128, 4 * 512), np.float32)
    for d in range(4):
        qcols = np.arange(512)
        m = (sqr[:, None] + 128 * d) <= qcols[None, :]
        dmask[:, d * 512:(d + 1) * 512] = np.where(m, 0.0, NEG)
    in_maps = []
    for c, (b, k) in enumerate(CORES):
        rows = slice(256 * k, 256 * k + 256)
        in_maps.append(dict(
            qn=np.ascontiguousarray(qf[b][rows]), qp=swap_pairs_rows(np.ascontiguousarray(qf[b][rows])),
            kn=np.ascontiguousarray(kf[b][rows]), kp=swap_pairs_rows(np.ascontiguousarray(kf[b][rows])),
            vf=np.ascontiguousarray(vf[b][:, rows]),
            cosk=Cb, sink=Sb, dmask=dmask))
    return in_maps


def stage_C3(I, c2_res, x2_parts):
    woT = stage_ktiles(np.asarray(I["L1_attn_Wo"], np.float32).T)
    ofull = {}
    for b in range(2):
        ofull[b] = np.concatenate([np.asarray(c2_res[b * 4 + k]["oh"]) for k in range(4)], axis=0)
    in_maps = []
    for c, (b, k) in enumerate(CORES):
        sl = slice(2048 * k, 2048 * k + 2048)
        in_maps.append(dict(oin=np.ascontiguousarray(ofull[b][:, sl]),
                            x2in3=np.ascontiguousarray(x2_parts[c]), woT=woT))
    return in_maps


def build_D(ctx=None, io=None):
    standalone = ctx is None
    if standalone:
        ctx = LaunchCtx("D")
        p = ctx.start()
    else:
        p = ctx.prog
    ctx.io = io
    nc = ctx.nc

    x_d = ctx.din("x3in", [1024, SH])
    wr_d = ctx.din("wrT", [1024, 16])
    gu_d = ctx.din("gup", [16 * 4096, 1024])
    dn_d = ctx.din("dnp", [16 * 1024, 2048])
    wg_d = ctx.din("wgT2", [2048, 1024])
    wu_d = ctx.din("wuT2", [2048, 1024])
    wd_d = ctx.din("wdT2", [1024, 2048])
    esel_d = ctx.din("esel16", [16, 2048])
    idf_d = ctx.din("idf3", [128, 128])
    out_d, b_out = ctx.dout("out", [1024, SH])

    xt, b_xt = ctx.sbuf("xt", [128, 8 * 512], F32)
    lnd, b_lnd = ctx.sbuf("lnd", [128, 8 * SH], BF16)
    lnf, b_lnf = ctx.sbuf("lnf", [128, 512], F32)
    sq, b_sq = ctx.sbuf("sq", [128, 512], F32)
    outacc, b_outacc = ctx.sbuf("outacc", [128, 8 * SH], BF16)
    midb, b_midb = ctx.sbuf("midb", [128, 8 * SH], BF16)
    xin, b_xin = ctx.sbuf("xin", [128, 8 * 512], BF16)
    gsil, b_gsil = ctx.sbuf("gsil", [128, 512], BF16)
    scfm, b_scfm = ctx.sbuf("scfm", [16, SH], F32)
    lg, b_lg = ctx.sbuf("lg", [16, 512], F32)
    sg, b_sg = ctx.sbuf("sg", [16, 512], F32)
    lgt, b_lgt = ctx.sbuf("lgt", [128, 16], F32)
    selt, b_selt = ctx.sbuf("selt", [128, 16], F32)
    maxc, b_maxc = ctx.sbuf("maxc", [128, 1], F32)
    msk, b_msk = ctx.sbuf("msk", [128, 512], BF16)
    wrb, b_wrb = ctx.sbuf("wrb", [128, 8 * 16], F32)
    esel, b_esel = ctx.sbuf("esel_s", [16, 2048], F32)
    idf, b_idf = ctx.sbuf("idf_s", [128, 128], F32)
    wst0, b_wst0 = ctx.sbuf("wst0", [128, 1024], F32)
    wst1, b_wst1 = ctx.sbuf("wst1", [128, 1024], F32)
    wcb0, b_wcb0 = ctx.sbuf("wcb0", [128, 1024], BF16)
    wcb1, b_wcb1 = ctx.sbuf("wcb1", [128, 1024], BF16)
    wcb2, b_wcb2 = ctx.sbuf("wcb2", [128, 1024], BF16)
    wcb3, b_wcb3 = ctx.sbuf("wcb3", [128, 1024], BF16)
    wcb4, b_wcb4 = ctx.sbuf("wcb4", [128, 1024], BF16)
    wcb5, b_wcb5 = ctx.sbuf("wcb5", [128, 1024], BF16)
    wst2, b_wst2 = ctx.sbuf("wst2", [128, 1024], F32)
    wst3, b_wst3 = ctx.sbuf("wst3", [128, 1024], F32)
    onesc, b_onesc = ctx.sbuf("onesc", [128, 1], F32)
    ones1, b_ones1 = ctx.sbuf("ones1", [1, 128], F32)
    srow, b_srow = ctx.sbuf("srow", [1, 4 * 512], F32)
    os_, b_os = ctx.sbuf("os_", [128, 512], F32)

    ps1, b_ps1 = ctx.psum("ps1", [1, 512])
    ps2, b_ps2 = ctx.psum("ps2", [1, 512])
    pbm, b_pbm = ctx.psum("pbm", [128, 512])
    pbr, b_pbr = ctx.psum("pbr", [128, 512])
    pp0, b_pp0 = ctx.psum("pp0", [128, 512])
    pp1, b_pp1 = ctx.psum("pp1", [128, 512])
    pmk, b_pmk = ctx.psum("pmk", [128, 512])

    emit(p, "vector", lambda v: v.memset(onesc[:], 1.0), [], [b_onesc])
    emit(p, "vector", lambda v: v.memset(ones1[:], 1.0), [], [b_ones1])
    emit(p, "vector", lambda v: v.memset(outacc[:], 0.0), [], [b_outacc])
    p.dma("sync", esel[:], esel_d[:, :], writes=[b_esel])
    p.dma("sync", idf[:], idf_d[:, :], writes=[b_idf])
    for kc in range(8):
        p.dma("sync", wrb[:, kc * 16:(kc + 1) * 16], wr_d[kc * 128:(kc + 1) * 128, :], writes=[b_wrb])
    cb5, b_cb5 = ctx.cbias(1e-5, 1)

    for ti in range(4):
        o = ti * 512
        for c in range(8):
            p.dma("sync", xt[:, c * 512:(c + 1) * 512], x_d[c * 128:(c + 1) * 128, o:o + 512], writes=[b_xt])
        def mm1(t):
            last = None
            for c in range(8):
                last = t.matmul(ps1[:, :], onesc[:, :], xt[:, c * 512:(c + 1) * 512], start=(c == 0), stop=(c == 7))
            return last
        p.op("tensor", mm1, reads=[b_onesc, b_xt], writes=[b_ps1])
        for c in range(8):
            emit(p, "scalar", lambda s, c=c: s.activation(out=sq[:], in_=xt[:, c * 512:(c + 1) * 512], func=AF.Square),
                 [b_xt], [b_sq])
            emit(p, "tensor", lambda t, c=c: t.matmul(ps2[:, :], onesc[:, :], sq[:, :], start=(c == 0), stop=(c == 7)),
                 [b_onesc, b_sq], [b_ps2])
        M, R, T1 = srow[0:1, 0:512], srow[0:1, 512:1024], srow[0:1, 1024:1536]
        emit(p, "vector", lambda v, M=M: v.tensor_scalar_mul(M, ps1[0:1, :], 1.0 / 1024.0), [b_ps1], [b_srow])
        emit(p, "vector", lambda v, T1=T1: v.tensor_scalar_mul(T1, ps2[0:1, :], 1.0 / 1024.0), [b_ps2], [b_srow])
        emit(p, "scalar", lambda s, M=M, R=R: s.activation(out=R, in_=M, func=AF.Square), [b_srow], [b_srow])
        emit(p, "vector", lambda v, T1=T1, R=R: v.tensor_sub(T1, T1, R), [b_srow], [b_srow])
        emit(p, "scalar", lambda s, T1=T1, cb5=cb5: s.activation(out=T1, in_=T1, func=AF.Sqrt, bias=cb5), [b_srow, b_cb5], [b_srow])
        emit(p, "vector", lambda v, T1=T1, R=R: v.reciprocal(R, T1), [b_srow], [b_srow])
        emit(p, "tensor", lambda t, M=M: t.matmul(pbm[:, :], ones1[0:1, :], M, start=True, stop=True),
             [b_ones1, b_srow], [b_pbm])
        emit(p, "tensor", lambda t, R=R: t.matmul(pbr[:, :], ones1[0:1, :], R, start=True, stop=True),
             [b_ones1, b_srow], [b_pbr])
        for c in range(8):
            sl = slice(c * 512, (c + 1) * 512)
            emit(p, "vector", lambda v, sl=sl: v.tensor_sub(lnf[:], xt[:, sl], pbm[:, :]), [b_xt, b_pbm], [b_lnf])
            emit(p, "vector", lambda v: v.tensor_mul(lnf[:], lnf[:], pbr[:, :]), [b_lnf, b_pbr], [b_lnf])
            emit(p, "vector", lambda v, c=c, o=o: v.tensor_copy(lnd[:, c * SH + o: c * SH + o + 512], lnf[:]),
                 [b_lnf], [b_lnd])
            emit(p, "tensor", lambda t, c=c: t.matmul(pp0[0:16, :], wrb[:, c * 16:(c + 1) * 16], lnf[:, :],
                                                      start=(c == 0), stop=(c == 7)), [b_wrb, b_lnf], [b_pp0])
        emit(p, "vector", lambda v: v.tensor_copy(lg[:], pp0[0:16, :]), [b_pp0], [b_lg])
        emit(p, "scalar", lambda s: s.activation(out=sg[:], in_=lg[:], func=AF.Sigmoid), [b_lg], [b_sg])
        for cc in range(4):
            emit_transpose(p, pmk[0:128, 0:16], lg[:, cc * 128:(cc + 1) * 128], idf[0:16, 0:16], [b_lg, b_idf], [b_pmk])
            emit(p, "vector", lambda v: v.tensor_copy(lgt[:], pmk[0:128, 0:16]), [b_pmk], [b_lgt])
            emit(p, "vector", lambda v: v.reduce_max(maxc[:], lgt[:], axis=AX.X), [b_lgt], [b_maxc])
            emit(p, "vector", lambda v: v.tensor_scalar(selt[:], lgt[:], maxc[:, 0:1], None, ALU.is_ge),
                 [b_lgt, b_maxc], [b_selt])
            emit_transpose(p, pmk[0:16, 16:144], selt[:, :], idf[:, :], [b_selt, b_idf], [b_pmk])
            emit(p, "vector", lambda v, o=o, cc=cc: v.tensor_copy(scfm[:, o + cc * 128: o + (cc + 1) * 128], pmk[0:16, 16:144]),
                 [b_pmk], [b_scfm])
        emit(p, "vector", lambda v, o=o: v.tensor_mul(scfm[:, o:o + 512], scfm[:, o:o + 512], sg[:, :]),
             [b_scfm, b_sg], [b_scfm])

    for e in range(17):
        shared = (e == 16)
        for t4 in range(4):
            if not shared:
                emit(p, "tensor", lambda t, e=e, t4=t4: t.matmul(
                    pmk[:, :], esel[:, e * 128:(e + 1) * 128], scfm[:, t4 * 512:(t4 + 1) * 512], start=True, stop=True),
                    [b_esel, b_scfm], [b_pmk])
                emit(p, "vector", lambda v, t4=t4: v.tensor_copy(msk[:], pmk[:, :]), [b_pmk], [b_msk])
                for c in range(8):
                    emit(p, "vector", lambda v, c=c, t4=t4: v.tensor_tensor(
                        xin[:, c * 512:(c + 1) * 512], lnd[:, c * SH + t4 * 512: c * SH + t4 * 512 + 512],
                        msk[:], ALU.mult), [b_lnd, b_msk], [b_xin])
            else:
                for c in range(8):
                    emit(p, "vector", lambda v, c=c, t4=t4: v.tensor_copy(
                        xin[:, c * 512:(c + 1) * 512], lnd[:, c * SH + t4 * 512: c * SH + t4 * 512 + 512]),
                        [b_lnd], [b_xin])
            for jb in range(2):
                for jc in range(8):
                    jg = jb * 8 + jc
                    ju = 16 + jg
                    if jc % 2 == 0:
                        gsel = ((wcb0, b_wcb0, wst0, b_wst0, jg), (wcb1, b_wcb1, wst1, b_wst1, ju))
                    else:
                        gsel = ((wcb4, b_wcb4, wst2, b_wst2, jg), (wcb5, b_wcb5, wst3, b_wst3, ju))
                    for (wcb, b_wcb, wst, b_wst, col) in gsel:
                        if shared:
                            wsrc = wg_d if col < 16 else wu_d
                            ccol = col % 16
                            p.dma("sync", wst[:, :], wsrc[ccol * 128:(ccol + 1) * 128, :], writes=[b_wst])
                        else:
                            p.dma("sync", wst[:, :], gu_d[(e * 32 + col) * 128:(e * 32 + col + 1) * 128, :],
                                  writes=[b_wst])
                        emit(p, "vector", lambda v, wcb=wcb, wst=wst: v.tensor_copy(wcb[:], wst[:]), [b_wst], [b_wcb])
                    wcbg, b_wcbg = gsel[0][0], gsel[0][1]
                    wcbu, b_wcbu = gsel[1][0], gsel[1][1]
                    def mmg(t, wcb=wcbg, t4=t4):
                        last = None
                        for kc in range(8):
                            last = t.matmul(pp0[:, :], wcb[:, kc * 128:(kc + 1) * 128],
                                            xin[:, kc * 512:(kc + 1) * 512], start=(kc == 0), stop=(kc == 7))
                        return last
                    p.op("tensor", mmg, reads=[b_wcbg, b_xin], writes=[b_pp0])
                    def mmu(t, wcb=wcbu, t4=t4):
                        last = None
                        for kc in range(8):
                            last = t.matmul(pp1[:, :], wcb[:, kc * 128:(kc + 1) * 128],
                                            xin[:, kc * 512:(kc + 1) * 512], start=(kc == 0), stop=(kc == 7))
                        return last
                    p.op("tensor", mmu, reads=[b_wcbu, b_xin], writes=[b_pp1])
                    emit(p, "scalar", lambda s: s.activation(out=gsil[:], in_=pp0[:, :], func=AF.Silu), [b_pp0], [b_gsil])
                    emit(p, "vector", lambda v, jc=jc, t4=t4: v.tensor_tensor(
                        midb[:, jc * SH + t4 * 512: jc * SH + t4 * 512 + 512], pp1[:, :], gsil[:], ALU.mult),
                        [b_pp1, b_gsil], [b_midb])
                for oc in range(8):
                    wst, b_wst = (wst0, b_wst0) if oc % 2 == 0 else (wst1, b_wst1)
                    wcb, b_wcb = (wcb2, b_wcb2) if oc % 2 == 0 else (wcb3, b_wcb3)
                    if shared:
                        p.dma("sync", wst[:, :], wd_d[oc * 128:(oc + 1) * 128, jb * 1024:(jb + 1) * 1024],
                              writes=[b_wst])
                    else:
                        p.dma("sync", wst[:, :], dn_d[e * 1024 + oc * 128: e * 1024 + (oc + 1) * 128, jb * 1024:(jb + 1) * 1024],
                              writes=[b_wst])
                    emit(p, "vector", lambda v, wcb=wcb, wst=wst: v.tensor_copy(wcb[:], wst[:]), [b_wst], [b_wcb])
                    pw, b_pw = (pp0, b_pp0) if oc % 2 == 0 else (pp1, b_pp1)
                    def mmd(t, wcb=wcb, t4=t4, pw=pw):
                        last = None
                        for kc in range(8):
                            last = t.matmul(pw[:, :], wcb[:, kc * 128:(kc + 1) * 128],
                                            midb[:, kc * SH + t4 * 512: kc * SH + t4 * 512 + 512],
                                            start=(kc == 0), stop=(kc == 7))
                        return last
                    p.op("tensor", mmd, reads=[b_wcb, b_midb], writes=[b_pw])
                    emit(p, "vector", lambda v, oc=oc, t4=t4, pw=pw: v.tensor_tensor(
                        outacc[:, oc * SH + t4 * 512: oc * SH + t4 * 512 + 512], pw[:, :],
                        outacc[:, oc * SH + t4 * 512: oc * SH + t4 * 512 + 512], ALU.add),
                        [b_pw, b_outacc], [b_outacc])

    for ti in range(4):
        o = ti * 512
        for oc in range(8):
            p.dma("sync", xt[:, 0:512], x_d[oc * 128:(oc + 1) * 128, o:o + 512], writes=[b_xt])
            emit(p, "vector", lambda v, oc=oc, o=o: v.tensor_add(os_[:], xt[:, 0:512],
                 outacc[:, oc * SH + o: oc * SH + o + 512]), [b_xt, b_outacc], [b_os])
            p.dma("gpsimd", out_d[oc * 128:(oc + 1) * 128, o:o + 512], os_[:], reads=[b_os], writes=[b_out], out_dram=True)
    ctx.io = None
    if standalone:
        p.finalize()
    return ctx


def stage_D(I, x3_parts):
    wrT = _f32c(np.asarray(I["L1_ffn_w_router"], np.float32).T)
    gup = stage_ktiles3(I["L1_ffn_gate_up_proj"], kc=8)
    dnp = stage_ktiles3(I["L1_ffn_down_proj"], kc=16)
    wgT2 = stage_ktiles(np.asarray(I["L1_ffn_w_gate"], np.float32).T)
    wuT2 = stage_ktiles(np.asarray(I["L1_ffn_w_up"], np.float32).T)
    wdT2 = stage_ktiles(np.asarray(I["L1_ffn_w_down"], np.float32).T, kc=16)
    esel16 = np.zeros((16, 2048), np.float32)
    for e in range(16):
        esel16[e, e * 128:(e + 1) * 128] = 1.0
    idf3 = np.eye(128, dtype=np.float32)
    in_maps = []
    for c, (b, k) in enumerate(CORES):
        in_maps.append(dict(x3in=np.ascontiguousarray(x3_parts[c]), wrT=wrT, gup=gup, dnp=dnp,
                            wgT2=wgT2, wuT2=wuT2, wdT2=wdT2, esel16=esel16, idf3=idf3))
    return in_maps


_CTX_CACHE = {}


def _get_ctx(name, builder):
    if name not in _CTX_CACHE:
        _CTX_CACHE[name] = builder()
    return _CTX_CACHE[name]


def kernel(**I):
    return kernel_mega(**I)


def kernel_multilaunch(**I):
    ctx = _get_ctx("A1", build_A1)
    a1 = run_launch(ctx, stage_A1(I))
    a1_res = [dict(xbc=np.asarray(r["xbc"]), dt=np.asarray(r["dt"]),
                   z=np.asarray(r["z"]), astat=np.asarray(r["astat"])) for r in a1]
    ctx = _get_ctx("A2", build_A2)
    a2 = run_launch(ctx, stage_A2(I, a1_res))
    a2_res = [dict(y=np.asarray(r["y"])) for r in a2]
    ctx = _get_ctx("A3", build_A3)
    a3 = run_launch(ctx, stage_A3(I, a1_res, a2_res))
    x1_parts = [np.asarray(r["x1"]) for r in a3]
    ctx = _get_ctx("B1", build_B1)
    b1 = run_launch(ctx, stage_B1(I, x1_parts))
    b1_res = [dict(q=np.asarray(r["q"]), k=np.asarray(r["k"]),
                   vtm=np.asarray(r["vtm"]), lnb=np.asarray(r["lnb"])) for r in b1]
    ctx = _get_ctx("B2", build_B2)
    b2 = run_launch(ctx, stage_B2(I, b1_res, x1_parts))
    x2_parts = [np.asarray(r["x2"]) for r in b2]
    ctx = _get_ctx("C1", build_C1)
    c1 = run_launch(ctx, stage_C1(I, x2_parts))
    c1_res = [dict(q=np.asarray(r["q"]), k=np.asarray(r["k"]), vtm=np.asarray(r["vtm"])) for r in c1]
    ctx = _get_ctx("C2", build_C2)
    c2 = run_launch(ctx, stage_C2(I, c1_res))
    c2_res = [dict(oh=np.asarray(r["oh"])) for r in c2]
    ctx = _get_ctx("C3", build_C3)
    c3 = run_launch(ctx, stage_C3(I, c2_res, x2_parts))
    x3_parts = [np.asarray(r["x3"]) for r in c3]
    ctx = _get_ctx("D", build_D)
    d = run_launch(ctx, stage_D(I, x3_parts))
    full = np.zeros((2, 8192, 1024), np.float32)
    for c, (b, k) in enumerate(CORES):
        full[b, 2048 * k:2048 * k + 2048] = np.asarray(d[c]["out"]).T
    return full


class MegaCtx(LaunchCtx):
    NP = 52800

    def __init__(self, name):
        super().__init__(name)
        self.mega = False

    def init_pools(self):
        p = self.prog
        self.cbt = self.es.enter_context(self.nc.sbuf_tensor("CBIAS", [128, 8], F32))
        self.poolf = self.es.enter_context(self.nc.sbuf_tensor("POOLF", [128, self.NP], F32))
        self.poolb_full = self.poolf[:, :].bitcast(BF16)
        self.psf = [self.es.enter_context(self.nc.psum_tensor(f"PSF{i}", [128, 512], F32)) for i in range(7)]
        self.psb = self.es.enter_context(self.nc.psum_tensor("PSB", [128, 1024], BF16))
        self.mega = True
        self.new_phase(first=True)

    def new_phase(self, first=False):
        if not first:
            self.prog.barrier()
            for b in self.prog.dma_bufs:
                self.prog.sem_pool.append((b.dma_sem, b.dma_base + b.dma_writes))
            self.prog.dma_bufs = []
        self.off = 0
        self.ps_i = 0
        self.ps_b_used = False

    def cbias(self, val, parts=128):
        if val not in self._cb:
            i = len(self._cb)
            t = self.cbt[:, i:i + 1]
            b = self.prog.buf(f"cb_{i}")
            self.prog.op("vector", lambda v, t=t, val=val: v.memset(t, float(val)),
                         reads=[], writes=[b])
            self._cb[val] = (t, b)
        t, b = self._cb[val]
        return t[0:parts, 0:1], b

    def sbuf(self, name, shape, dtype):
        if not self.mega:
            return super().sbuf(name, shape, dtype)
        parts, cols = shape
        esz = 4 if dtype == F32 else 2
        nbytes = ceil_div(cols * esz, 512) * 512
        assert self.off + nbytes <= self.NP * 4, f"pool overflow at {name}: {self.off}+{nbytes}"
        if dtype == F32:
            t = self.poolf[0:parts, self.off // 4: self.off // 4 + cols]
        else:
            t = self.poolb_full[0:parts, self.off // 2: self.off // 2 + cols]
        self.off += nbytes
        return t, self.prog.buf(name)

    def psum(self, name, shape, dtype=F32):
        if not self.mega:
            return super().psum(name, shape, dtype)
        parts, cols = shape
        if dtype == F32:
            assert self.ps_i < 7, f"psum pool overflow at {name}"
            t = self.psf[self.ps_i][0:parts, 0:cols]
            self.ps_i += 1
        else:
            assert not self.ps_b_used, f"bf16 psum overflow at {name}"
            t = self.psb[0:parts, 0:cols]
            self.ps_b_used = True
        return t, self.prog.buf(name)


def _prog_barrier(self):
    for e in ENGINES:
        for f in ENGINES:
            if f != e and self.counts[f] > 0:
                self._wait(e, self.sems[f], self.counts[f])
        for b in self.dma_bufs:
            self._wait(e, b.dma_sem, 16 * (b.dma_base + b.dma_writes))


Prog.barrier = _prog_barrier


from concourse.aot_env import allow_ici_all_to_all

GRP = [[0, 1, 2, 3], [4, 5, 6, 7]]


def build_mega():
    ctx = MegaCtx("MEGA")
    p = ctx.start()
    nc = ctx.nc
    ctx.init_pools()
    S = 8192
    I32 = mybir.dt.int32

    def scr(name, shape, dtype=F32):
        return nc.dram_tensor(name, shape, dtype).ap()

    xin_ap = nc.dram_tensor("xin", [1024, 2176], F32, kind="ExternalInput").ap()
    ix_xs = nc.dram_tensor("ix_xs", [128, 16], I32, kind="ExternalInput").ap()
    ix_dt = nc.dram_tensor("ix_dt", [8, 4], I32, kind="ExternalInput").ap()
    ix_y = nc.dram_tensor("ix_y", [128, 16], I32, kind="ExternalInput").ap()
    ix_qk = nc.dram_tensor("ix_qk", [128, 8], I32, kind="ExternalInput").ap()
    ix_v = nc.dram_tensor("ix_v", [128, 64], I32, kind="ExternalInput").ap()
    ix_o = nc.dram_tensor("ix_o", [128, 8], I32, kind="ExternalInput").ap()
    ix_bc = nc.dram_tensor("ix_bc", [128, 8], I32, kind="ExternalInput").ap()
    ix_kf = nc.dram_tensor("ix_kf", [128, 32], I32, kind="ExternalInput").ap()
    ix_sw = nc.dram_tensor("ix_sw", [128, 2], I32, kind="ExternalInput").ap()

    g_xbc = scr("g_xbc", [2304, SH], BF16)
    g_dt = scr("g_dt", [32, SH])
    g_z = scr("g_z", [2048, SH])
    g_ast = scr("g_ast", [2, SH])
    g_xbcg = scr("g_xbcg", [4 * 2304, SH], BF16)
    g_dtg = scr("g_dtg", [128, SH])
    g_xs = scr("g_xs", [512, S], BF16)
    g_dth = scr("g_dth", [8, S])
    g_Bf = scr("g_Bf", [128, S], BF16)
    g_Cf = scr("g_Cf", [128, S], BF16)
    g_y = scr("g_y", [512, S], BF16)
    g_yg = scr("g_yg", [2048, S], BF16)
    g_yin = scr("g_yin", [2048, SH], BF16)
    g_x1 = scr("g_x1", [1024, SH])
    g_lnb = scr("g_lnb", [1024, SH])
    g_q = scr("g_q", [1024, SH], BF16)
    g_k = scr("g_k", [1024, SH], BF16)
    g_v = scr("g_v", [SH, 1024], BF16)
    g_kg = scr("g_kg", [4096, SH], BF16)
    g_kf = scr("g_kf", [1024, S], BF16)
    g_vg = scr("g_vg", [S, 1024], BF16)
    g_x2 = scr("g_x2", [1024, SH])
    g_qg = scr("g_qg", [4096, SH], BF16)
    g_kg2 = scr("g_kg2", [4096, SH], BF16)
    g_qn = scr("g_qn", [256, S], BF16)
    g_qp = scr("g_qp", [256, S], BF16)
    g_kn = scr("g_kn", [256, S], BF16)
    g_kp = scr("g_kp", [256, S], BF16)
    g_vf2 = scr("g_vf2", [S, 256], BF16)
    g_oh = scr("g_oh", [256, S], BF16)
    g_og = scr("g_og", [1024, S], BF16)
    g_oin = scr("g_oin", [1024, SH], BF16)
    g_x3 = scr("g_x3", [1024, SH])

    BREG = {}

    def gb(name):
        if name not in BREG:
            BREG[name] = p.buf(name)
        return BREG[name]

    def cc(kind, in_ap, out_ap, tag, src, dst):
        def fn(g_eng, in_ap=in_ap, out_ap=out_ap, kind=kind):
            with allow_ici_all_to_all():
                return g_eng.collective_compute(
                    kind, ALU.bypass, replica_groups=GRP,
                    ins=[in_ap.opt()], outs=[out_ap.opt()])
        return p.op("gpsimd", fn, reads=[gb(src)], writes=[gb(dst)], tag=tag)

    def dcp(dst_ap, src_ap, tag, src, dst):
        return p.dma("sync", dst_ap, src_ap, reads=[gb(src)], writes=[gb(dst)], tag=tag)

    ixs = {}
    for nm, ap_, shp in (("ix_xs", ix_xs, [128, 16]), ("ix_dt", ix_dt, [8, 4]),
                         ("ix_y", ix_y, [128, 16]), ("ix_qk", ix_qk, [128, 8]),
                         ("ix_v", ix_v, [128, 64]), ("ix_o", ix_o, [128, 8]),
                         ("ix_bc", ix_bc, [128, 8]), ("ix_kf", ix_kf, [128, 32]),
                         ("ix_sw", ix_sw, [128, 2])):
        t = ctx.es.enter_context(nc.sbuf_tensor("s" + nm, shp, I32))
        b = p.buf("s" + nm)
        p.dma("sync", t[:, :], ap_[:, :], writes=[b])
        ixs[nm] = (t, b)

    STG = {"t": None}

    def stg_tiles():
        sb, bsb = ctx.sbuf(f"stg_b{ctx.uniq()}", [128, 8192], BF16)
        sf, bsf = ctx.sbuf(f"stg_f{ctx.uniq()}", [128, 2048], F32)
        STG["t"] = (sb, bsb, sf, bsf)

    def _raw_idma(out_ap, src_view, t, col, nparts, tag, reads, wbuf):
        def fn(g_eng, out_ap=out_ap, src_view=src_view, t=t, col=col, nparts=nparts):
            return g_eng.indirect_dma_start(
                out_ap, None, src_view,
                bass.IndirectOffsetOnAxis(t[0:nparts, col:col + 1], 0))
        wb = wbuf
        if wb.dma_sem is None:
            if p.sem_pool:
                wb.dma_sem, wb.dma_base = p.sem_pool.pop()
            else:
                wb.dma_sem = p.new_sem(f"D{p._nsem}_{wb.name}")
                wb.dma_base = 0
            p.dma_bufs.append(wb)
        p._pre("gpsimd", reads, [wb], ())
        wb.dma_writes += 1
        val = 16 * (wb.dma_base + wb.dma_writes)
        p.items["gpsimd"].append(("emit", fn, [(wb.dma_sem, 16)], tag))
        op = Op("gpsimd", wb.dma_sem, val, "dma")
        p._post(op, reads, [wb])
        return op

    def igather(out_ap, src_view, ixname, col, nparts, tag, srcb, dstb, f32=False):
        t, bix = ixs[ixname]
        sb, bsb, sf, bsf = STG["t"]
        cols = out_ap.shape[-1]
        stg, bstg = (sf, bsf) if f32 else (sb, bsb)
        _raw_idma(stg[0:nparts, 0:cols], src_view, t, col, nparts, tag + "_g",
                  [gb(srcb), bix], bstg)
        p.dma("sync", out_ap, stg[0:nparts, 0:cols], reads=[bstg], writes=[gb(dstb)], tag=tag + "_s")

    build_A1(ctx=ctx, io={"xin": xin_ap, "xbc": g_xbc, "dt": g_dt, "z": g_z, "astat": g_ast})
    ctx.new_phase()
    stg_tiles()
    cc("AllGather", g_xbc[:, :], g_xbcg[:, :], "cc_xbc", "g_xbc", "g_xbcg")
    cc("AllGather", g_dt[:, :], g_dtg[:, :], "cc_dt", "g_dt", "g_dtg")
    for gg in range(4):
        for blk in range(4):
            igather(g_xs[blk * 128:(blk + 1) * 128, gg * 2048:(gg + 1) * 2048],
                    g_xbcg[:, :], "ix_xs", gg * 4 + blk, 128, f"ig_xs{gg}_{blk}", "g_xbcg", "g_xs")
        igather(g_dth[0:8, gg * 2048:(gg + 1) * 2048], g_dtg[:, :], "ix_dt", gg, 8,
                f"ig_dt{gg}", "g_dtg", "g_dth", f32=True)
        igather(g_Bf[:, gg * 2048:(gg + 1) * 2048], g_xbcg[:, :], "ix_bc", gg * 2, 128,
                f"m_B{gg}", "g_xbcg", "g_Bf")
        igather(g_Cf[:, gg * 2048:(gg + 1) * 2048], g_xbcg[:, :], "ix_bc", gg * 2 + 1, 128,
                f"m_C{gg}", "g_xbcg", "g_Cf")
    ctx.new_phase()
    build_A2(ctx=ctx, io={"xs": g_xs, "Bf": g_Bf, "Cf": g_Cf, "dth": g_dth, "y": g_y})
    ctx.new_phase()
    stg_tiles()
    cc("AllGather", g_y[:, :], g_yg[:, :], "cc_y", "g_y", "g_yg")
    yv = g_yg.rearrange("r (q c) -> (r q) c", c=2048)
    for blk in range(16):
        igather(g_yin[blk * 128:(blk + 1) * 128, :], yv, "ix_y", blk, 128,
                f"ig_y{blk}", "g_yg", "g_yin")
    ctx.new_phase()
    build_A3(ctx=ctx, io={"xin3": xin_ap[:, 128:2176], "astat3": g_ast, "zin": g_z,
                          "yin": g_yin, "x1": g_x1})
    ctx.new_phase()
    build_B1(ctx=ctx, io={"x1in": g_x1, "lnb": g_lnb, "q": g_q, "k": g_k, "vtm": g_v})
    ctx.new_phase()
    stg_tiles()
    cc("AllGather", g_k[:, :], g_kg[:, :], "cc_k", "g_k", "g_kg")
    cc("AllGather", g_v[:, :], g_vg[:, :], "cc_v", "g_v", "g_vg")
    for gg in range(4):
        for blk in range(8):
            igather(g_kf[blk * 128:(blk + 1) * 128, gg * 2048:(gg + 1) * 2048], g_kg[:, :],
                    "ix_kf", gg * 8 + blk, 128, f"m_k{gg}_{blk}", "g_kg", "g_kf")
    ctx.new_phase()
    build_B2(ctx=ctx, io={"qin": g_q, "kf": g_kf, "vf": g_vg, "lnbin": g_lnb,
                          "x1in2": g_x1, "x2": g_x2})
    ctx.new_phase()
    build_C1(ctx=ctx, io={"x2in": g_x2, "q": g_q, "k": g_k, "vtm": g_v})
    ctx.new_phase()
    stg_tiles()
    cc("AllGather", g_q[:, :], g_qg[:, :], "cc_q2", "g_q", "g_qg")
    cc("AllGather", g_k[:, :], g_kg2[:, :], "cc_k2", "g_k", "g_kg2")
    cc("AllGather", g_v[:, :], g_vg[:, :], "cc_v2", "g_v", "g_vg")
    for gg in range(4):
        for blk in range(2):
            igather(g_qn[blk * 128:(blk + 1) * 128, gg * 2048:(gg + 1) * 2048], g_qg[:, :],
                    "ix_qk", gg * 2 + blk, 128, f"ig_q{gg}_{blk}", "g_qg", "g_qn")
            igather(g_kn[blk * 128:(blk + 1) * 128, gg * 2048:(gg + 1) * 2048], g_kg2[:, :],
                    "ix_qk", gg * 2 + blk, 128, f"ig_k{gg}_{blk}", "g_kg2", "g_kn")
    vv = g_vg.rearrange("r (q c) -> (r q) c", c=256)
    for blk in range(64):
        igather(g_vf2[blk * 128:(blk + 1) * 128, :], vv, "ix_v", blk, 128,
                f"ig_v{blk}", "g_vg", "g_vf2")
    ctx.new_phase()
    stg_tiles()
    for blk in range(2):
        igather(g_qp[blk * 128:(blk + 1) * 128, :], g_qn[:, :], "ix_sw", blk, 128,
                f"sw_q{blk}", "g_qn", "g_qp")
        igather(g_kp[blk * 128:(blk + 1) * 128, :], g_kn[:, :], "ix_sw", blk, 128,
                f"sw_k{blk}", "g_kn", "g_kp")
    ctx.new_phase()
    build_C2(ctx=ctx, io={"qn": g_qn, "qp": g_qp, "kn": g_kn, "kp": g_kp, "vf": g_vf2,
                          "oh": g_oh})
    ctx.new_phase()
    stg_tiles()
    cc("AllGather", g_oh[:, :], g_og[:, :], "cc_o", "g_oh", "g_og")
    ov = g_og.rearrange("r (q c) -> (r q) c", c=2048)
    for blk in range(8):
        igather(g_oin[blk * 128:(blk + 1) * 128, :], ov, "ix_o", blk, 128,
                f"ig_o{blk}", "g_og", "g_oin")
    ctx.new_phase()
    build_C3(ctx=ctx, io={"oin": g_oin, "x2in3": g_x2, "x3": g_x3})
    ctx.new_phase()
    build_D(ctx=ctx, io={"x3in": g_x3})
    p.finalize()
    return ctx


def stage_ix(k):
    pq = np.arange(128)
    ix_xs = np.zeros((128, 16), np.int32)
    for gg in range(4):
        for blk in range(4):
            ix_xs[:, gg * 4 + blk] = gg * 2304 + 512 * k + blk * 128 + pq
    ix_dt = np.zeros((8, 4), np.int32)
    for gg in range(4):
        ix_dt[:, gg] = gg * 32 + 8 * k + np.arange(8)
    ix_y = np.zeros((128, 16), np.int32)
    for blk in range(16):
        ix_y[:, blk] = (blk * 128 + pq) * 4 + k
    ix_qk = np.zeros((128, 8), np.int32)
    for gg in range(4):
        for blk in range(2):
            ix_qk[:, gg * 2 + blk] = gg * 1024 + 256 * k + blk * 128 + pq
    ix_v = np.zeros((128, 64), np.int32)
    for blk in range(64):
        ix_v[:, blk] = (blk * 128 + pq) * 4 + k
    ix_o = np.zeros((128, 8), np.int32)
    for blk in range(8):
        ix_o[:, blk] = (blk * 128 + pq) * 4 + k
    ix_bc = np.zeros((128, 8), np.int32)
    for gg in range(4):
        ix_bc[:, gg * 2] = gg * 2304 + 2048 + pq
        ix_bc[:, gg * 2 + 1] = gg * 2304 + 2176 + pq
    ix_kf = np.zeros((128, 32), np.int32)
    for gg in range(4):
        for blk in range(8):
            ix_kf[:, gg * 8 + blk] = gg * 1024 + blk * 128 + pq
    ix_sw = np.zeros((128, 2), np.int32)
    for blk in range(2):
        ix_sw[:, blk] = blk * 128 + (pq ^ 1)
    return dict(ix_xs=ix_xs, ix_dt=ix_dt, ix_y=ix_y, ix_qk=ix_qk, ix_v=ix_v, ix_o=ix_o,
                ix_bc=ix_bc, ix_kf=ix_kf, ix_sw=ix_sw)


def stage_mega(I):
    x = I["L0_attn_hidden_states"]
    xfm = np.ascontiguousarray(x.transpose(0, 2, 1), dtype=np.float32)
    w_in_T = np.zeros((1024, 4480), np.float32)
    w_in_T[:, 0:4384] = np.asarray(I["L0_attn_w_in"], np.float32).T
    w_in_T = _f32c(stage_ktiles(np.concatenate([w_in_T], axis=0), kc=8)) if False else stage_ktiles(w_in_T)
    wln1c = stage_vec_cols(I["L0_attn_w_ln1"], 8)
    dtbc = _f32c(np.asarray(I["L0_attn_dt_bias"], np.float32).reshape(32, 1))
    bconvc = stage_vec_cols(I["L0_attn_b_conv"], 18)
    wc = np.asarray(I["L0_attn_w_conv"], np.float32)
    wconvc = _f32c(wc.reshape(18, 128, 4).transpose(1, 0, 2).reshape(128, 72))
    idf = np.eye(128, dtype=np.float32)
    idb = idf.astype(BF16NP)
    sq = np.arange(128)
    maskT = np.where(sq[:, None] <= sq[None, :], 0.0, NEG).astype(np.float32)
    selpair = np.zeros((8, 128), np.float32)
    for h in range(8):
        selpair[h, (h // 2) * 32:(h // 2) * 32 + 32] = 1.0
    sel8 = np.zeros((8, 128), np.float32)
    for h in range(8):
        sel8[h, (h % 2) * 64:(h % 2) * 64 + 64] = 1.0
    alog_full = np.asarray(I["L0_attn_A_log"], np.float32)
    d_full = np.asarray(I["L0_attn_D"], np.float32)
    wmnc = stage_vec_cols(I["L0_attn_w_mnorm"], 16)
    wout_T = stage_ktiles(np.asarray(I["L0_attn_w_out"], np.float32).T, kc=16)
    wln2c = stage_vec_cols(I["L0_attn_w_ln2"], 8)
    wg_T = stage_ktiles(np.asarray(I["L0_attn_w_gate"], np.float32).T)
    wu_T = stage_ktiles(np.asarray(I["L0_attn_w_up"], np.float32).T)
    wd_T = stage_ktiles(np.asarray(I["L0_attn_w_down"], np.float32).T, kc=32)
    wq = stage_ktiles(I["L0_ffn_wq"]); wk = stage_ktiles(I["L0_ffn_wk"]); wv = stage_ktiles(I["L0_ffn_wv"])
    bqc = stage_vec_cols(I["L0_ffn_bq"], 8); bkc = stage_vec_cols(I["L0_ffn_bk"], 8)
    bvr = _f32c(np.asarray(I["L0_ffn_bv"], np.float32).reshape(1, 1024))
    g1c = stage_vec_cols(I["L0_ffn_ln1_g"], 8); b1c = stage_vec_cols(I["L0_ffn_ln1_b"], 8)
    wo = stage_ktiles(I["L0_ffn_wo"]); boc = stage_vec_cols(I["L0_ffn_bo"], 8)
    g2c = stage_vec_cols(I["L0_ffn_ln2_g"], 8); b2c = stage_vec_cols(I["L0_ffn_ln2_b"], 8)
    gatew = _f32c(I["L0_ffn_gate_w"])
    gatebc = _f32c(np.asarray(I["L0_ffn_gate_b"], np.float32).reshape(8, 1))
    w1m = stage_ktiles3(I["L0_ffn_w1"], kc=8)
    b1m = _f32c(I["L0_ffn_b1"])
    w2m = stage_ktiles3(I["L0_ffn_w2"], kc=32)
    b2m = _f32c(I["L0_ffn_b2"])
    erow = np.zeros((8, 4096), np.float32)
    esel = np.zeros((8, 1024), np.float32)
    for e in range(8):
        erow[e, e * 512:(e + 1) * 512] = 1.0
        esel[e, e * 128:(e + 1) * 128] = 1.0
    wqT = stage_ktiles(np.asarray(I["L1_attn_Wq"], np.float32).T)
    wkT = stage_ktiles(np.asarray(I["L1_attn_Wk"], np.float32).T)
    wvT = stage_ktiles(np.asarray(I["L1_attn_Wv"], np.float32).T)
    woT = stage_ktiles(np.asarray(I["L1_attn_Wo"], np.float32).T)
    C, Sg = rope_tables(8192)
    Cb = np.ascontiguousarray(C.astype(BF16NP))
    Sb = np.ascontiguousarray(Sg.astype(BF16NP))
    dmask = np.zeros((128, 4 * 512), np.float32)
    for d in range(4):
        qcols = np.arange(512)
        m = (sq[:, None] + 128 * d) <= qcols[None, :]
        dmask[:, d * 512:(d + 1) * 512] = np.where(m, 0.0, NEG)
    wrT = _f32c(np.asarray(I["L1_ffn_w_router"], np.float32).T)
    gup = stage_ktiles3(I["L1_ffn_gate_up_proj"], kc=8)
    dnp = stage_ktiles3(I["L1_ffn_down_proj"], kc=16)
    wgT2 = stage_ktiles(np.asarray(I["L1_ffn_w_gate"], np.float32).T)
    wuT2 = stage_ktiles(np.asarray(I["L1_ffn_w_up"], np.float32).T)
    wdT2 = stage_ktiles(np.asarray(I["L1_ffn_w_down"], np.float32).T, kc=16)
    esel16 = np.zeros((16, 2048), np.float32)
    for e in range(16):
        esel16[e, e * 128:(e + 1) * 128] = 1.0
    in_maps = []
    for c, (b, k) in enumerate(CORES):
        lo = 2048 * k - 128
        if lo < 0:
            xin = np.concatenate([np.zeros((1024, 128), np.float32), xfm[b][:, 0:2048]], axis=1)
        else:
            xin = np.ascontiguousarray(xfm[b][:, lo:lo + 2176])
        in_maps.append(dict(
            xin=xin, w_in_T=w_in_T, wln1c=wln1c, dtbc=dtbc, bconvc=bconvc, wconvc=wconvc,
            alog=np.ascontiguousarray(alog_full[8 * k:8 * k + 8].reshape(8, 1)),
            dd=np.ascontiguousarray(d_full[8 * k:8 * k + 8].reshape(1, 8)),
            idf=idf, idb=idb, maskT=maskT, selpair=selpair, sel8=sel8,
            wmnc=wmnc, wout_T=wout_T, wln2c=wln2c, wg_T=wg_T, wu_T=wu_T, wd_T=wd_T,
            wq=wq, wk=wk, wv=wv, bqc=bqc, bkc=bkc, bvr=bvr, g1c=g1c, b1c=b1c,
            wo=wo, boc=boc, g2c=g2c, b2c=b2c, gatew=gatew, gatebc=gatebc,
            w1m=w1m, b1m=b1m, w2m=w2m, b2m=b2m, esel=esel, idf2=idf,
            wqT=wqT, wkT=wkT, wvT=wvT, woT=woT, cosk=Cb, sink=Sb, dmask=dmask,
            wrT=wrT, gup=gup, dnp=dnp, wgT2=wgT2, wuT2=wuT2, wdT2=wdT2,
            esel16=esel16, idf3=idf, **stage_ix(k),
        ))
    return in_maps


def kernel_mega(**I):
    ctx = _get_ctx("MEGA", build_mega)
    res = run_launch(ctx, stage_mega(I))
    full = np.zeros((2, 8192, 1024), np.float32)
    for c, (b, k) in enumerate(CORES):
        full[b, 2048 * k:2048 * k + 2048] = np.asarray(res[c]["out"]).T
    return full
